# version 01
# baseline (speedup 1.0000x reference)
import contextlib
import math
import numpy as np
import concourse.bass as bass
import concourse.mybir as mybir
from concourse.bass_utils import run_bass_kernel_spmd

F32 = mybir.dt.float32
BF16 = mybir.dt.bfloat16
AF = mybir.ActivationFunctionType
ALU = mybir.AluOpType
AX = mybir.AxisListType

TENSOR, VECTOR, SCALAR, GPSIMD, SYNC = "tensor", "vector", "scalar", "gpsimd", "sync"
COMPUTE_ENGINES = (TENSOR, VECTOR, SCALAR, GPSIMD)
DMA_QUEUES = (SYNC, GPSIMD, SCALAR)

S = 2048
D = 1024
NCH = D // 128
NT = S // 128
NT4 = S // 512
GAMMA = 0.96875
FH = 8
FDH = 64
FWIN = 512
FHID = 2730
FI = 4096
AH = 16
ADH = 64
LNG = math.log(GAMMA)
LN2 = math.log(2.0)
NEGB = -1.0e38



class Step:
    __slots__ = ("eng", "fn", "waits", "inc", "tok", "dma", "tag")

    def __init__(self, eng, fn, waits, inc, dma, tag):
        self.eng, self.fn, self.waits, self.inc, self.dma, self.tag = (
            eng, fn, list(waits), inc, dma, tag)
        self.tok = None


class Prog:
    def __init__(self, nc):
        self.nc = nc
        self.steps = []
        self.counts = {e: 0 for e in COMPUTE_ENGINES}
        self.dcounts = {q: 0 for q in DMA_QUEUES}

    pending_fence = None
    _fence_seen = None

    def _merge_fence(self, stream, waits):
        if self.pending_fence is None:
            return waits
        if stream in self._fence_seen:
            return waits
        self._fence_seen.add(stream)
        own = stream if stream in self.counts else None
        extra = [t for t in self.pending_fence
                 if t[0] != stream and t[0] != "dma_" + stream]
        if len(self._fence_seen) >= 5:
            self.pending_fence = None
        return list(waits) + extra

    def phase_fence(self):
        self.pending_fence = self.fence()
        self._fence_seen = set()

    def emit(self, eng, fn, waits=(), inc=True, tag=""):
        waits = self._merge_fence(eng, waits)
        s = Step(eng, fn, [t for t in waits if t], inc, False, tag)
        if inc:
            self.counts[eng] += 1
            s.tok = (eng, self.counts[eng])
        self.steps.append(s)
        return s.tok

    def dma(self, queue, out, in_, waits=(), inc=True, tag=""):
        def fn(eng):
            return eng.dma_start(out=out, in_=in_)
        return self.dma_fn(queue, fn, waits, inc, tag)

    def dma_fn(self, queue, fn, waits=(), inc=True, tag=""):
        waits = self._merge_fence(queue, waits)
        s = Step(queue, fn, [t for t in waits if t], inc, True, tag)
        if inc:
            self.dcounts[queue] += 16
            s.tok = ("dma_" + queue, self.dcounts[queue])
        self.steps.append(s)
        return s.tok

    def fence(self):
        toks = [(e, self.counts[e]) for e in COMPUTE_ENGINES if self.counts[e]]
        toks += [("dma_" + q, self.dcounts[q]) for q in DMA_QUEUES if self.dcounts[q]]
        return toks

    def check_deadlock(self):
        qs = {}
        for s in self.steps:
            qs.setdefault(s.eng, []).append(s)
        pos = {e: 0 for e in qs}
        counts = {}
        progress = True
        while progress:
            progress = False
            for e, lst in qs.items():
                while pos[e] < len(lst):
                    s = lst[pos[e]]
                    if all(counts.get(sm, 0) >= v for sm, v in s.waits):
                        if s.tok is not None:
                            counts[s.tok[0]] = s.tok[1]
                        pos[e] += 1
                        progress = True
                    else:
                        break
        bad = []
        for e, lst in qs.items():
            if pos[e] < len(lst):
                s = lst[pos[e]]
                bad.append(f"{e}[{pos[e]}/{len(lst)}] tag={s.tag} waits={s.waits}")
        if bad:
            raise RuntimeError("DEADLOCK:\n" + "\n".join(bad))

    def install(self, stack):
        self.check_deadlock()
        nc = self.nc
        sems = {}
        for e in COMPUTE_ENGINES:
            sems[e] = stack.enter_context(nc.semaphore("sem_" + e))
        for q in DMA_QUEUES:
            sems["dma_" + q] = stack.enter_context(nc.semaphore("sdma_" + q))
        block = stack.enter_context(nc.Block())
        bysec = {}
        for s in self.steps:
            bysec.setdefault(s.eng, []).append(s)

        def emitter(name):
            def run(eng):
                for s in bysec.get(name, []):
                    for sm, val in s.waits:
                        eng.wait_ge(sems[sm], val)
                    inst = s.fn(eng)
                    if s.tok is not None:
                        inst.then_inc(sems[s.tok[0]], 16 if s.dma else 1)
            return run

        block.tensor(emitter(TENSOR))
        block.vector(emitter(VECTOR))
        block.scalar(emitter(SCALAR))
        block.gpsimd(emitter(GPSIMD))
        block.sync(emitter(SYNC))


class Pool:

    def __init__(self, items):
        self.items = list(items)
        self.free_tok = [None] * len(self.items)
        self.i = 0

    def acquire(self):
        j = self.i % len(self.items)
        self.i += 1
        tok = self.free_tok[j]
        return j, self.items[j], ([tok] if tok else [])

    def release(self, j, tok):
        self.free_tok[j] = tok


class K:

    def __init__(self):
        self.nc = bass.Bass("TRN2", target_bir_lowering=False, debug=False, detect_race_conditions=False)
        self.stack = contextlib.ExitStack()
        self.scopes = [self.stack]
        self.p = Prog(self.nc)
        self._uid = 0
        self.dbg = {}

    def uid(self):
        self._uid += 1
        return self._uid

    def push_scope(self):
        sc = contextlib.ExitStack()
        self.scopes.append(sc)
        return sc

    def pop_scope(self):
        sc = self.scopes.pop()
        sc.close()
        self.p.phase_fence()

    def sb(self, name, shape, dt):
        import os
        h = self.scopes[-1].enter_context(
            self.nc.sbuf_tensor(f"{name}{self.uid()}", list(shape), dt))
        if os.environ.get("SBUF_LOG"):
            import numpy as _np
            nb = int(_np.prod(shape[1:])) * (2 if dt == BF16 else 4)
            print(f"SBUF {name} {shape} {nb}B rem={self.nc.sbuf_bytes_remaining}")
        return h

    _consts = None

    def c(self, value):
        if self._consts is None:
            self._consts = {}
        if value not in self._consts:
            t = self.scopes[0].enter_context(
                self.nc.sbuf_tensor(f"cst{self.uid()}", [128, 1], F32))
            tok = self.p.emit(GPSIMD, (lambda t, value: lambda g: g.memset(
                t[:, :], float(value)))(t, value), tag="const")
            self._consts[value] = (t[:, 0:1], tok)
        return self._consts[value]

    def din(self, name, shape, dt=F32):
        return self.nc.dram_tensor(name, list(shape), dt, kind="ExternalInput").ap()

    def dout(self, name, shape, dt=F32):
        return self.nc.dram_tensor(name, list(shape), dt, kind="ExternalOutput").ap()

    def dram(self, name, shape, dt):
        return self.nc.dram_tensor(f"{name}{self.uid()}", list(shape), dt,
                                   kind="Internal").ap()

    def dbg_dump(self, name, aps, dt=F32):
        total = sum(a.shape[1] for a in aps)
        d = self.dout("dbg_" + name, [128, total], dt)
        toks = self.p.fence()
        off = 0
        dts = []
        for a in aps:
            dts.append(self.p.dma(SYNC, d[0:a.shape[0], off:off + a.shape[1]], a,
                                  waits=toks, tag="dbg"))
            off += a.shape[1]
        self.p.emit(SYNC, lambda e: e.nop(), waits=[dts[-1]], inc=False, tag="dbgw")
        self.dbg[name] = d



def affine_mask(k, t, ncols, compare_op, base, cmul, fill, set_to=1.0, tag="am",
                piece=512, step=1):
    p = k.p
    def fn(g):
        last = None
        for c0 in range(0, ncols, piece):
            wdt = min(piece, ncols - c0)
            g.memset(t[:, c0:c0 + wdt], float(set_to))
            last = g.affine_select(out=t[:, c0:c0 + wdt], in_=t[:, c0:c0 + wdt],
                                   pattern=[[step, wdt]], compare_op=compare_op,
                                   fill=float(fill), base=int(base + step * c0),
                                   channel_multiplier=int(cmul))
        return last
    return p.emit(GPSIMD, fn, tag=tag)


def build_identity(k, dt, tag):
    idn = k.sb("idn" + tag, [128, 128], dt)
    def fn(g):
        g.memset(idn[:, :], 1.0)
        return g.affine_select(out=idn[:, :], in_=idn[:, :], pattern=[[-1, 128]],
                               compare_op=ALU.is_equal, fill=0.0, base=0,
                               channel_multiplier=1)
    tok = k.p.emit(GPSIMD, fn, tag="idn")
    return idn, tok


def bcast_row(k, row_ap, out_ap, psum_pool, waits, tag, dtype_bf=False):
    p = k.p
    if not hasattr(k, "_ones_row"):
        k._ones_row = k.scopes[0].enter_context(
            k.nc.sbuf_tensor(f"onesr{k.uid()}", [1, 128], F32))
        k._ones_row_tok = p.emit(GPSIMD, lambda g: g.memset(k._ones_row[:, :], 1.0),
                                 tag="onesr")
    n = out_ap.shape[1]
    last = None
    for t0 in range(0, n, 512):
        wdt = min(512, n - t0)
        j, bank, wb = psum_pool.acquire()
        mm = p.emit(TENSOR, (lambda bank, t0, wdt: lambda te: te.matmul(
            out=bank[:, 0:wdt], lhsT=k._ones_row[0:1, :],
            rhs=row_ap[0:1, t0:t0 + wdt], start=True, stop=True))(bank, t0, wdt),
            waits=list(waits) + wb + [k._ones_row_tok], tag=tag + "_mm")
        ev = p.emit(VECTOR, (lambda bank, t0, wdt: lambda v: v.tensor_copy(
            out=out_ap[:, t0:t0 + wdt], in_=bank[:, 0:wdt]))(bank, t0, wdt),
            waits=[mm], tag=tag + "_ev")
        psum_pool.release(j, ev)
        last = ev
    return last


def load_bias_cols(k, hbm_row, n=D, waits=(), tag="bias"):
    ncol = n // 128
    t = k.sb(tag, [128, ncol], F32)
    def fn(eng):
        return eng.dma_start(out=t[:, :],
                             in_=hbm_row[0:1, :].rearrange("o (c q) -> (o q) c", q=128),
                             allow_slow_non_contiguous=True)
    tok = k.p.dma_fn(SYNC, fn, waits, True, tag)
    return t, tok


class LNStats:

    def __init__(self, rho_bc, om_bc, toks, mu_row=None, r_row=None):
        self.rho_bc, self.om_bc, self.toks = rho_bc, om_bc, toks
        self.mu_row, self.r_row = mu_row, r_row


def ln_stats(k, chunk_aps, chunk_toks, ones1, ones1_tok, psum_pool, eps=1e-5,
             tag="ln", out=None, upto=99, scratch=None):
    p = k.p
    nc = NCH
    if scratch is not None:
        sq_ring, sum_ps, sq_ps, om_row = scratch
    else:
        sum_ps = k.sb(tag + "_sum", [1, S], F32)
        sq_ps = k.sb(tag + "_sq", [1, S], F32)
        sq_ring = k.sb(tag + "_sqr", [128, 2 * 512], F32)
    sq_ring_toks = [None, None]
    ones_c, ones_c_tok = ones1, ones1_tok
    if chunk_aps[0].dtype == BF16:
        ones_c, ones_c_tok = k._ones1b, k._ones1b_tok
    ev_toks = []
    for ts in range(NT4):
        j, bank, wb = psum_pool.acquire()
        mm = None
        for c in range(nc):
            mm = p.emit(TENSOR, (lambda bank, c, ts, first, last: lambda te: te.matmul(
                out=bank[0:1, :], lhsT=ones_c[:, 0:1],
                rhs=chunk_aps[c][:, ts * 512:(ts + 1) * 512],
                start=first, stop=last))(bank, c, ts, c == 0, c == nc - 1),
                waits=([ones_c_tok] + wb + list(chunk_toks) if c == 0 else []),
                tag=f"{tag}_sum{ts}")
        ev = p.emit(VECTOR, (lambda bank, ts: lambda v: v.tensor_copy(
            out=sum_ps[0:1, ts * 512:(ts + 1) * 512], in_=bank[0:1, :]))(bank, ts),
            waits=[mm], tag=f"{tag}_sumev{ts}")
        ev_toks.append(ev)
        psum_pool.release(j, ev)
        import os
        if os.environ.get("SKIPSQ"):
            continue
        j2, bank2, wb2 = psum_pool.acquire()
        mm2 = None
        for c in range(nc):
            ri = c % 2
            sqt = p.emit(SCALAR, (lambda c, ts, ri: lambda s: s.activation(
                out=sq_ring[:, ri * 512:(ri + 1) * 512],
                in_=chunk_aps[c][:, ts * 512:(ts + 1) * 512], func=AF.Square))(c, ts, ri),
                waits=(list(chunk_toks) if (ts == 0 and c < 2) else []) +
                      ([sq_ring_toks[ri]] if sq_ring_toks[ri] else []),
                tag=f"{tag}_sq{ts}_{c}")
            mm2 = p.emit(TENSOR, (lambda bank2, ri, first, last: lambda te: te.matmul(
                out=bank2[0:1, :], lhsT=ones1[:, 0:1],
                rhs=sq_ring[:, ri * 512:(ri + 1) * 512],
                start=first, stop=last))(bank2, ri, c == 0, c == nc - 1),
                waits=[sqt] + (wb2 + [ones1_tok] if c == 0 else []),
                tag=f"{tag}_sqmm{ts}")
            sq_ring_toks[ri] = mm2
        ev2 = p.emit(VECTOR, (lambda bank2, ts: lambda v: v.tensor_copy(
            out=sq_ps[0:1, ts * 512:(ts + 1) * 512], in_=bank2[0:1, :]))(bank2, ts),
            waits=[mm2], tag=f"{tag}_sqev{ts}")
        ev_toks.append(ev2)
        psum_pool.release(j2, ev2)
    mu = sum_ps
    r = sq_ps
    om = om_row if scratch is not None else k.sb(tag + "_om", [1, S], F32)
    res0 = LNStats(None, None, [], mu, r)
    res0.sum_ps, res0.sq_ps, res0.om = sum_ps, sq_ps, om
    if upto <= 0:
        return res0
    t1 = p.emit(VECTOR, lambda v: v.tensor_scalar(
        out=mu[0:1, :], in0=sum_ps[0:1, :], scalar1=1.0 / D, scalar2=None,
        op0=ALU.mult), waits=ev_toks, tag=tag + "_mu")
    t2 = p.emit(VECTOR, lambda v: v.tensor_tensor(
        out=om[0:1, :], in0=mu[0:1, :], in1=mu[0:1, :], op=ALU.mult),
        waits=[t1], tag=tag + "_mu2")
    t3 = p.emit(VECTOR, lambda v: v.scalar_tensor_tensor(
        out=r[0:1, :], in0=sq_ps[0:1, :], scalar=1.0 / D, in1=om[0:1, :],
        op0=ALU.mult, op1=ALU.subtract), waits=[t2], tag=tag + "_var")
    if upto <= 1:
        return res0
    epsc, epstok = k.c(float(eps))
    t4 = p.emit(SCALAR, lambda s: s.activation(
        out=r[0:1, :], in_=r[0:1, :], func=AF.Sqrt, bias=epsc[0:1, :]),
        waits=[t3, epstok], tag=tag + "_sqrt")
    t5 = p.emit(VECTOR, lambda v: v.reciprocal(out=r[0:1, :], in_=r[0:1, :]),
                waits=[t4], tag=tag + "_recip")
    t6 = p.emit(VECTOR, lambda v: v.scalar_tensor_tensor(
        out=om[0:1, :], in0=mu[0:1, :], scalar=-1.0, in1=r[0:1, :],
        op0=ALU.mult, op1=ALU.mult), waits=[t5, t1], tag=tag + "_om")
    if out is not None:
        rho_bc, om_bc = out
    else:
        rho_bc = k.sb(tag + "_rho_bc", [128, S], F32)
        om_bc = k.sb(tag + "_om_bc", [128, S], F32)
    if upto <= 2:
        res0.toks = [t6]
        return res0
    t7 = bcast_row(k, r[0:1, :], rho_bc[:, :], psum_pool, [t5], tag + "_bcr")
    t8 = bcast_row(k, om[0:1, :], om_bc[:, :], psum_pool, [t6], tag + "_bco")
    res = LNStats(rho_bc, om_bc, [t7, t8], mu, r)
    res.sum_ps, res.sq_ps, res.om = sum_ps, sq_ps, om
    return res



class MK:
    def __init__(self):
        self.k = K()
        k, p = self.k, self.k.p
        self.x_in = k.din("x", [S, D])
        self.y_out = k.dout("y", [S, D])
        self.win = {}
        self.banks = [k.stack.enter_context(
            k.nc.psum_tensor(f"bank{i}", [128, 512], F32)) for i in range(8)]
        self.ppool = Pool(self.banks)
        self.idn_f32, self.idn_f32_tok = build_identity(k, F32, "f")
        self.idn_bf, self.idn_bf_tok = build_identity(k, BF16, "b")
        self.ones1 = k.sb("ones1", [128, 1], F32)
        self.ones1_tok = p.emit(GPSIMD, lambda g: g.memset(self.ones1[:, :], 1.0),
                                tag="ones1")
        self.ones1b = k.sb("ones1b", [128, 1], BF16)
        self.ones1b_tok = p.emit(VECTOR, lambda v: v.memset(self.ones1b[:, :], 1.0),
                                 tag="ones1b")
        k._ones1b, k._ones1b_tok = self.ones1b, self.ones1b_tok
        for v in (1e-5, 1e-8, 0.5, -16.0, 1.0, -1.0, 25.6 / 89.6,
                  3.0 * LN2, -3.0 * LN2):
            k.c(v)
        k._ones_row = k.scopes[0].enter_context(
            k.nc.sbuf_tensor(f"onesr{k.uid()}", [1, 128], F32))
        k._ones_row_tok = p.emit(GPSIMD, lambda g: g.memset(k._ones_row[:, :], 1.0),
                                 tag="onesr")
        self.xT = k.sb("xT", [128, NCH * S], F32)
        self.xTc = [self.xT[:, c * S:(c + 1) * S] for c in range(NCH)]
        self.xTc_tok = [None] * NCH
        self.SW = S + 1
        self.strm = k.sb("strm", [128, NCH * self.SW], F32)
        self.strmc = [self.strm[:, c * self.SW:(c + 1) * self.SW] for c in range(NCH)]
        self.strm_tok = [None] * NCH

    def din(self, name, shape):
        ap = self.k.din(name, shape)
        self.win[name] = ap
        return ap

    def p0_load(self, do_ln=True, upto=99):
        k, p = self.k, self.k.p
        k.push_scope()
        rho_bc = k.sb("ln0_rho_bc", [128, S], BF16)
        om_bc = k.sb("ln0_om_bc", [128, S], BF16)
        k.push_scope()
        x_tok = self.strm[:, 0:NT * D]
        ld = []
        for tt in range(NT):
            ld.append(p.dma(SYNC, x_tok[:, tt * D:(tt + 1) * D],
                            self.x_in[tt * 128:(tt + 1) * 128, :], tag="ldx"))
        ch_done = [[] for _ in range(NCH)]
        for tt in range(NT):
            for c in range(NCH):
                j, bank, wb = self.ppool.acquire()
                src = x_tok[:, tt * D + c * 128: tt * D + (c + 1) * 128]
                tk = p.emit(TENSOR, (lambda src, bank: lambda te: te.transpose(
                    out=bank[:, 0:128], in_=src, identity=self.idn_f32[:, :]))(src, bank),
                    waits=[ld[tt], self.idn_f32_tok] + wb, tag=f"tpx{tt}_{c}")
                dst = self.xTc[c][:, tt * 128:(tt + 1) * 128]
                tv = p.emit(VECTOR, (lambda bank, dst: lambda v: v.tensor_copy(
                    out=dst, in_=bank[:, 0:128]))(bank, dst),
                    waits=[tk], tag=f"tpc{tt}_{c}")
                ch_done[c].append(tv)
                self.ppool.release(j, tv)
        self.x_ld_toks = [t for lst in ch_done for t in lst]
        for c in range(NCH):
            self.xTc_tok[c] = ch_done[c][-1]
        if not do_ln:
            self.ln0 = None
            return
        self.ln0 = ln_stats(k, self.xTc, self.x_ld_toks, self.ones1,
                            self.ones1_tok, self.ppool, tag="ln0",
                            out=(rho_bc, om_bc), upto=upto)

    def close_p0(self):
        self.k.pop_scope()

    def xn_tile(self, ring_f32, ring_bf, c, a0, wdt, waits, ln=None):
        k, p = self.k, self.k.p
        ln = ln or self.ln0
        t1 = p.emit(VECTOR, lambda v: v.tensor_tensor(
            out=ring_f32[:, 0:wdt], in0=self.xTc[c][:, a0:a0 + wdt],
            in1=ln.rho_bc[:, a0:a0 + wdt], op=ALU.mult),
            waits=list(waits) + ln.toks + ([self.xTc_tok[c]] if self.xTc_tok[c] else []),
            tag="xnt1")
        t2 = p.emit(VECTOR, lambda v: v.tensor_tensor(
            out=ring_bf[:, 0:wdt], in0=ring_f32[:, 0:wdt],
            in1=ln.om_bc[:, a0:a0 + wdt], op=ALU.add), waits=[t1], tag="xnt2")
        return t2

    def p1_retention(self):
        k, p = self.k, self.k.p
        import os as _os
        wq = k.din("l0a_wq", [D, D]); bq = k.din("l0a_bq", [1, D])
        wk = k.din("l0a_wk", [D, D]); bk = k.din("l0a_bk", [1, D])
        wv = k.din("l0a_wv", [D, D])
        self.attn0 = k.dram("attn0", [128, NCH * S], BF16)

        k.push_scope()
        if not _os.environ.get("SKIP_BIAS"):
            bq_col, bq_tok = load_bias_cols(k, bq, D, tag="bqc")
            bk_col, bk_tok = load_bias_cols(k, bk, D, tag="bkc")
        else:
            bq_col = k.sb("bqc", [128, 8], F32); bq_tok = None
            bk_col = k.sb("bkc", [128, 8], F32); bk_tok = None
        gi_bc = k.sb("gi_bc", [128, S], BF16)
        gj_bc = k.sb("gj_bc", [128, S], BF16)
        k.push_scope()
        ti = k.sb("ti", [1, S], F32)
        t_iota = p.emit(GPSIMD, lambda g: g.iota(
            out=ti[0:1, :], pattern=[[1, S]], base=0, channel_multiplier=0,
            allow_small_or_imprecise_dtypes=True), tag="ti")
        gi_row = k.sb("gi_row", [1, S], F32)
        gj_row = k.sb("gj_row", [1, S], F32)
        tg1 = p.emit(SCALAR, lambda s: s.activation(
            out=gi_row[0:1, :], in_=ti[0:1, :], func=AF.Exp, scale=LNG),
            waits=[t_iota], tag="gi")
        tg2 = p.emit(SCALAR, lambda s: s.activation(
            out=gj_row[0:1, :], in_=ti[0:1, :], func=AF.Exp, scale=-LNG),
            waits=[t_iota], tag="gj")
        if not _os.environ.get("SKIP_DECAYBC"):
            tg3 = bcast_row(k, gi_row[0:1, :], gi_bc[:, :], self.ppool, [tg1], "gibc")
            tg4 = bcast_row(k, gj_row[0:1, :], gj_bc[:, :], self.ppool, [tg2], "gjbc")
        else:
            tg3 = tg4 = tg2
        k.pop_scope()
        b01 = k.sb("b01", [128, 1024], F32)
        tb01 = affine_mask(k, b01, 1024, ALU.is_ge, -512, -1, 0.0, tag="b01")

        bcv = self.strm[:, :].bitcast(BF16)
        kT = bcv[:, 0:NCH * S]
        v_tok = bcv[:, NCH * S:2 * NCH * S]
        mir_f = k.sb("mir_f", [128, 512], F32)
        mir = k.sb("mir", [128, 8 * 512], BF16)
        wstg = k.sb("wstg", [128, D], F32)
        wbf = k.sb("wbf", [128, D], BF16)
        sT = k.sb("sT", [128, 16 * 512], BF16)
        q_slab = k.sb("q_slab", [128, 8 * 512], BF16)
        out_ring = k.sb("out_ring", [128, 2 * 512], BF16)
        out_free = [None] * 2
        self._mir_free = [None] * 8
        self._wstg_free = None

        def prep_wT(hbm, dst, dst_toks, tagp):
            for o in range(NCH):
                ld = p.dma(SYNC, wstg[:, :], hbm[o * 128:(o + 1) * 128, :],
                           waits=([self._wstg_free] if self._wstg_free else []),
                           tag=tagp + "ld")
                ct = p.emit(SCALAR, lambda s: s.copy(out=wbf[:, :], in_=wstg[:, :]),
                            waits=[ld], tag=tagp + "cast")
                last_ev = None
                for h in range(NCH):
                    j, bank, wb = self.ppool.acquire()
                    tt = p.emit(TENSOR, (lambda h, bank: lambda te: te.transpose(
                        out=bank.bitcast(BF16)[:, 0:128],
                        in_=wbf[:, h * 128:(h + 1) * 128],
                        identity=self.idn_bf[:, :]))(h, bank),
                        waits=[ct, self.idn_bf_tok] + wb, tag=tagp + "tp")
                    ev = p.emit(VECTOR, (lambda h, o, bank: lambda v: v.tensor_copy(
                        out=dst[:, h * D + o * 128: h * D + (o + 1) * 128],
                        in_=bank.bitcast(BF16)[:, 0:128]))(h, o, bank),
                        waits=[tt], tag=tagp + "ev")
                    self.ppool.release(j, ev)
                    dst_toks[h][o] = ev
                    last_ev = tt
                self._wstg_free = last_ev

        def build_mirror(a0):
            toks = []
            for h in range(NCH):
                w = [self._mir_free[h]] if self._mir_free[h] else []
                t2 = self.xn_tile(mir_f, mir[:, h * 512:(h + 1) * 512], h, a0,
                                  512, w)
                toks.append(t2)
            return toks

        kT_tok = [[None] * NT4 for _ in range(NCH)]
        v_tok_tok = [[None, None] for _ in range(NT)]
        import os
        _phases = os.environ.get("RET_PHASES", "kvq")
        _stop = os.environ.get("RET_STOP", "")

        k.push_scope()
        wkT = k.sb("wkT", [128, NCH * D], BF16)
        wkT_toks = [[None] * NCH for _ in range(NCH)]
        if _stop == "early1":
            self._ret_dbg = dict(kT=kT, v_tok=v_tok, sT=sT, q_slab=q_slab,
                                 gi_bc=gi_bc, gj_bc=gj_bc, mir=mir, b01=b01, wkT=wkT)
            self._p1_done = p.fence()
            return
        prep_wT(wk, wkT, wkT_toks, "wk")
        if _stop == "early2":
            self._ret_dbg = dict(kT=kT, v_tok=v_tok, sT=sT, q_slab=q_slab,
                                 gi_bc=gi_bc, gj_bc=gj_bc, mir=mir, b01=b01, wkT=wkT)
            self._p1_done = p.fence()
            return
        for ts in range(NT4):
            a0 = ts * 512
            mtoks = build_mirror(a0)
            last_mm = None
            for o in range(NCH):
                j, bank, wb = self.ppool.acquire()
                mm = None
                for h in range(NCH):
                    mm = p.emit(TENSOR, (lambda o, h, bank, first, last:
                        lambda te: te.matmul(
                            out=bank[:, :],
                            lhsT=wkT[:, h * D + o * 128: h * D + (o + 1) * 128],
                            rhs=mir[:, h * 512:(h + 1) * 512],
                            start=first, stop=last))(o, h, bank, h == 0, h == NCH - 1),
                        waits=(wb + mtoks + [wkT_toks[hh][o] for hh in range(NCH)]
                               if h == 0 else []),
                        tag=f"kmm{ts}_{o}")
                last_mm = mm
                ev = p.emit(VECTOR, (lambda o, a0, bank: lambda v:
                    v.scalar_tensor_tensor(
                        out=kT[:, o * S + a0: o * S + a0 + 512],
                        in0=bank[:, :], scalar=bk_col[:, o:o + 1],
                        in1=gj_bc[:, a0:a0 + 512],
                        op0=ALU.add, op1=ALU.mult))(o, a0, bank),
                    waits=[mm, bk_tok, tg4], tag=f"kev{ts}_{o}")
                self.ppool.release(j, ev)
                kT_tok[o][ts] = ev
            for h in range(NCH):
                self._mir_free[h] = last_mm
        k.pop_scope()

        if "v" not in _phases:
            self._ret_dbg = dict(kT=kT, v_tok=v_tok, sT=sT, q_slab=q_slab,
                                 gi_bc=gi_bc, gj_bc=gj_bc, mir=mir, b01=b01)
            self._p1_done = p.fence()
            return
        k.push_scope()
        wvT = k.sb("wvT", [128, NCH * D], BF16)
        wvT_toks = [[None] * NCH for _ in range(NCH)]
        prep_wT(wv, wvT, wvT_toks, "wv")
        for ts in range(NT4):
            a0 = ts * 512
            mtoks = build_mirror(a0)
            last_mm = None
            for q4 in range(4):
                tt_idx = ts * 4 + q4
                for oh in range(2):
                    j, bank, wb = self.ppool.acquire()
                    mm = None
                    for h in range(NCH):
                        mm = p.emit(TENSOR, (lambda h, q4, oh, bank, first, last:
                            lambda te: te.matmul(
                                out=bank[:, :],
                                lhsT=mir[:, h * 512 + q4 * 128: h * 512 + (q4 + 1) * 128],
                                rhs=wvT[:, h * D + oh * 512: h * D + (oh + 1) * 512],
                                start=first, stop=last))(h, q4, oh, bank, h == 0,
                                                         h == NCH - 1),
                            waits=(wb + mtoks +
                                   [wvT_toks[hh][o2] for hh in range(NCH)
                                    for o2 in range(oh * 4, oh * 4 + 4)]
                                   if h == 0 else []),
                            tag=f"vmm{tt_idx}_{oh}")
                    last_mm = mm
                    ev = p.emit(SCALAR, (lambda tt_idx, oh, bank: lambda s: s.copy(
                        out=v_tok[:, tt_idx * D + oh * 512: tt_idx * D + (oh + 1) * 512],
                        in_=bank[:, :]))(tt_idx, oh, bank),
                        waits=[mm], tag=f"vev{tt_idx}_{oh}")
                    self.ppool.release(j, ev)
                    v_tok_tok[tt_idx][oh] = ev
            for h in range(NCH):
                self._mir_free[h] = last_mm
        k.pop_scope()

        if "q" not in _phases:
            self._ret_dbg = dict(kT=kT, v_tok=v_tok, sT=sT, q_slab=q_slab,
                                 gi_bc=gi_bc, gj_bc=gj_bc, mir=mir, b01=b01)
            self._p1_done = p.fence()
            return
        k.push_scope()
        wqT = k.sb("wqT", [128, NCH * D], BF16)
        wqT_toks = [[None] * NCH for _ in range(NCH)]
        prep_wT(wq, wqT, wqT_toks, "wq")
        q_free = [None] * NCH
        sT_free = None
        for T in range(NT4):
            i0 = T * 512
            mtoks = build_mirror(i0)
            q_toks = []
            last_q_mm = None
            for o in range(NCH):
                j, bank, wb = self.ppool.acquire()
                mm = None
                for h in range(NCH):
                    mm = p.emit(TENSOR, (lambda o, h, bank, first, last:
                        lambda te: te.matmul(
                            out=bank[:, :],
                            lhsT=wqT[:, h * D + o * 128: h * D + (o + 1) * 128],
                            rhs=mir[:, h * 512:(h + 1) * 512],
                            start=first, stop=last))(o, h, bank, h == 0, h == NCH - 1),
                        waits=(wb + mtoks + [wqT_toks[hh][o] for hh in range(NCH)]
                               + ([q_free[o]] if q_free[o] else [])
                               if h == 0 else []),
                        tag=f"qmm{T}_{o}")
                last_q_mm = mm
                ev = p.emit(VECTOR, (lambda o, i0, bank: lambda v:
                    v.scalar_tensor_tensor(
                        out=q_slab[:, o * 512:(o + 1) * 512],
                        in0=bank[:, :], scalar=bq_col[:, o:o + 1],
                        in1=gi_bc[:, i0:i0 + 512],
                        op0=ALU.add, op1=ALU.mult))(o, i0, bank),
                    waits=[mm, bq_tok, tg3], tag=f"qev{T}_{o}")
                self.ppool.release(j, ev)
                q_toks.append(ev)
            for h in range(NCH):
                self._mir_free[h] = last_q_mm
            nj = 4 * T + 4
            s_toks = []
            for c in range(nj):
                ts_of_c = c // 4
                j, bank, wb = self.ppool.acquire()
                mm = None
                for o in range(NCH):
                    mm = p.emit(TENSOR, (lambda c, o, bank, first, last:
                        lambda te: te.matmul(
                            out=bank[:, :],
                            lhsT=kT[:, o * S + c * 128: o * S + (c + 1) * 128],
                            rhs=q_slab[:, o * 512:(o + 1) * 512],
                            start=first, stop=last))(c, o, bank, o == 0, o == NCH - 1),
                        waits=(wb + q_toks + [kT_tok[oo][ts_of_c] for oo in range(NCH)]
                               + ([sT_free] if sT_free else [])
                               if o == 0 else []),
                        tag=f"smm{T}_{c}")
                m = c - 4 * T
                if m >= 0:
                    ev = p.emit(VECTOR, (lambda c, m, bank: lambda v: v.tensor_tensor(
                        out=sT[:, c * 512:(c + 1) * 512], in0=bank[:, :],
                        in1=b01[:, 512 - 128 * m: 1024 - 128 * m], op=ALU.mult))(c, m, bank),
                        waits=[mm, tb01], tag=f"sev{T}_{c}")
                else:
                    ev = p.emit(SCALAR, (lambda c, bank: lambda s: s.copy(
                        out=sT[:, c * 512:(c + 1) * 512], in_=bank[:, :]))(c, bank),
                        waits=[mm], tag=f"sev{T}_{c}")
                self.ppool.release(j, ev)
                s_toks.append(ev)
            last_av = None
            for o in range(NCH):
                j, bank, wb = self.ppool.acquire()
                ri = o % 2
                mm = None
                for c in range(nj):
                    mm = p.emit(TENSOR, (lambda o, c, bank, first, last:
                        lambda te: te.matmul(
                            out=bank[:, :],
                            lhsT=v_tok[:, c * D + o * 128: c * D + o * 128 + 128],
                            rhs=sT[:, c * 512:(c + 1) * 512],
                            start=first, stop=last))(o, c, bank, c == 0, c == nj - 1),
                        waits=(wb + s_toks +
                               [v_tok_tok[c2][ohh] for c2 in range(nj) for ohh in range(2)]
                               if c == 0 else []),
                        tag=f"avmm{T}_{o}")
                last_av = mm
                w_or = [out_free[ri]] if out_free[ri] else []
                ev = p.emit(VECTOR, (lambda ri, bank: lambda v: v.tensor_copy(
                    out=out_ring[:, ri * 512:(ri + 1) * 512], in_=bank[:, :]))(ri, bank),
                    waits=[mm] + w_or, tag=f"avev{T}_{o}")
                self.ppool.release(j, ev)
                dm = p.dma(SYNC, self.attn0[:, o * S + i0: o * S + i0 + 512],
                           out_ring[:, ri * 512:(ri + 1) * 512], waits=[ev],
                           tag=f"avdma{T}_{o}")
                out_free[ri] = dm
            sT_free = last_av
            for o in range(NCH):
                q_free[o] = last_av
        if os.environ.get("KEEP_RET"):
            self._ret_dbg = dict(kT=kT, v_tok=v_tok, sT=sT, q_slab=q_slab,
                                 gi_bc=gi_bc, gj_bc=gj_bc, mir=mir, wqT=wqT, b01=b01)
            self._p1_done = p.fence()
            return
        k.pop_scope()
        self._p1_done = p.fence()
        k.pop_scope()

    def p1_apply_attn(self, extra_waits=()):
        k, p = self.k, self.k.p
        k.push_scope()
        ld_ring = k.sb("a0ld", [128, 2 * S], BF16)
        free_t = [None, None]
        for c in range(NCH):
            ri = c % 2
            ld = p.dma(SYNC, ld_ring[:, ri * S:(ri + 1) * S],
                       self.attn0[:, c * S:(c + 1) * S],
                       waits=list(extra_waits) + ([free_t[ri]] if free_t[ri] else []),
                       tag="a0ld")
            ad = p.emit(VECTOR, (lambda c, ri: lambda v: v.tensor_tensor(
                out=self.xTc[c], in0=self.xTc[c],
                in1=ld_ring[:, ri * S:(ri + 1) * S], op=ALU.add))(c, ri),
                waits=[ld] + ([self.xTc_tok[c]] if self.xTc_tok[c] else []),
                tag="a0add")
            free_t[ri] = ad
            self.xTc_tok[c] = ad
        k.pop_scope()

    SW = S + 1

    def p2_stream_init(self, waits=()):
        k, p = self.k, self.k.p
        k.push_scope()
        scr = k.sb("si_scr", [128, 512], F32)
        _w0 = list(waits)
        for c in range(NCH):
            zt = p.emit(VECTOR, (lambda c: lambda v: v.memset(
                self.strmc[c][:, 0:1], 0.0))(c), waits=_w0, tag="spad")
            last = zt
            for ts in range(NT4):
                a0 = ts * 512
                t1 = p.emit(VECTOR, (lambda c, a0: lambda v: v.tensor_tensor(
                    out=scr[:, :], in0=self.xTc[c][:, a0:a0 + 512],
                    in1=self.ln0.rho_bc[:, a0:a0 + 512], op=ALU.mult))(c, a0),
                    waits=self.ln0.toks + [self.xTc_tok[c]], tag="si1")
                t2 = p.emit(VECTOR, (lambda c, a0: lambda v: v.tensor_tensor(
                    out=self.strmc[c][:, a0 + 1:a0 + 513], in0=scr[:, :],
                    in1=self.ln0.om_bc[:, a0:a0 + 512], op=ALU.add))(c, a0),
                    waits=[t1], tag="si2")
                last = t2
            self.strm_tok[c] = last
        k.pop_scope()

    def p2_spill_xT(self):
        k, p = self.k, self.k.p
        self.xT_spill = k.dram("xTsp", [128, NCH * S], F32)
        self.xT_spill_toks = []
        for c in range(NCH):
            t = p.dma(SYNC, self.xT_spill[:, c * S:(c + 1) * S], self.xTc[c],
                      waits=[self.strm_tok[c]] + ([self.xTc_tok[c]] if self.xTc_tok[c] else []),
                      tag="spill")
            self.xT_spill_toks.append(t)

    def p2_reload_xT(self, waits=()):
        k, p = self.k, self.k.p
        for c in range(NCH):
            t = p.dma(SYNC, self.xTc[c], self.xT_spill[:, c * S:(c + 1) * S],
                      waits=list(waits) + [self.xT_spill_toks[c]], tag="unspill")
            self.xTc_tok[c] = t

    def p2_tables_global(self):
        k, p = self.k, self.k.p
        self.csq = k.sb("csq", [128, NT * 64], F32)
        self.ssq = k.sb("ssq", [128, NT * 64], F32)
        self.csk = k.sb("csk", [128, NT * 64], F32)
        self.ssk = k.sb("ssk", [128, NT * 64], F32)
        k.push_scope()
        invf_r = k.sb("invf", [1, 64], F32)
        lbase_r = k.sb("lbase", [1, 64], F32)
        t1 = p.emit(GPSIMD, lambda g: g.iota(
            out=invf_r[0:1, :], pattern=[[0, 2], [2, 32]], base=0,
            channel_multiplier=0, allow_small_or_imprecise_dtypes=True), tag="ivf0")
        t2 = p.emit(SCALAR, lambda s: s.activation(
            out=lbase_r[0:1, :], in_=invf_r[0:1, :], func=AF.Identity,
            scale=1.0 / 89.6, bias=k.c(25.6 / 89.6)[0][0:1, :]),
            waits=[t1, k.c(25.6 / 89.6)[1]], tag="base")
        t3 = p.emit(SCALAR, lambda s: s.activation(
            out=lbase_r[0:1, :], in_=lbase_r[0:1, :], func=AF.Ln), waits=[t2],
            tag="lbase")
        t4 = p.emit(SCALAR, lambda s: s.activation(
            out=invf_r[0:1, :], in_=invf_r[0:1, :], func=AF.Exp,
            scale=-math.log(10000.0) / 64.0), waits=[t1, t2], tag="invf")
        invf_bc = k.sb("invfbc", [128, 64], F32)
        lbase_bc = k.sb("lbasebc", [128, 64], F32)
        t5 = bcast_row(k, invf_r[0:1, :], invf_bc[:, :], self.ppool, [t4], "ivbc")
        t6 = bcast_row(k, lbase_r[0:1, :], lbase_bc[:, :], self.ppool, [t3], "lbbc")
        tcol = k.sb("tcol", [128, 2], F32)
        fr = k.sb("fr", [128, 64 * 5], F32)
        TWO_PI = 2.0 * math.pi
        prev_tls = None
        for tt in range(NT):
            w0 = [t5, t6] + ([prev_tls] if prev_tls else [])
            ti = p.emit(GPSIMD, (lambda tt: lambda g: g.iota(
                out=tcol[:, 0:1], pattern=[[0, 1]], base=tt * 128,
                channel_multiplier=1, allow_small_or_imprecise_dtypes=True))(tt),
                waits=w0, tag="tc")
            tpw = p.emit(VECTOR, lambda v: v.tensor_scalar(
                out=tcol[:, 1:2], in0=tcol[:, 0:1], scalar1=-1024.0, scalar2=1.0 / 256,
                op0=ALU.add, op1=ALU.mult), waits=[ti], tag="pw")
            tf = p.emit(VECTOR, lambda v: v.tensor_scalar(
                out=fr[:, 0:64], in0=invf_bc[:, :], scalar1=tcol[:, 0:1], scalar2=None,
                op0=ALU.mult), waits=[ti, t5], tag="fq")
            tu = p.emit(VECTOR, lambda v: v.tensor_scalar(
                out=fr[:, 64:128], in0=fr[:, 0:64], scalar1=1.0 / TWO_PI, scalar2=None,
                op0=ALU.mult), waits=[tf], tag="u")
            last = tu
            for kk in range(8, -1, -1):
                step = float(2 ** kk)
                m1 = p.emit(VECTOR, (lambda step: lambda v: v.tensor_scalar(
                    out=fr[:, 128:192], in0=fr[:, 64:128], scalar1=step, scalar2=None,
                    op0=ALU.is_ge))(step), waits=[last], tag="lad")
                last = p.emit(VECTOR, (lambda step: lambda v: v.scalar_tensor_tensor(
                    out=fr[:, 64:128], in0=fr[:, 128:192], scalar=-step,
                    in1=fr[:, 64:128], op0=ALU.mult, op1=ALU.add))(step),
                    waits=[m1], tag="lad2")
            m1 = p.emit(VECTOR, lambda v: v.tensor_scalar(
                out=fr[:, 128:192], in0=fr[:, 64:128], scalar1=0.5, scalar2=None,
                op0=ALU.is_ge), waits=[last], tag="sw")
            tsin = p.emit(VECTOR, lambda v: v.scalar_tensor_tensor(
                out=fr[:, 128:192], in0=fr[:, 128:192], scalar=-1.0,
                in1=fr[:, 64:128], op0=ALU.mult, op1=ALU.add), waits=[m1], tag="sa")
            tc1 = p.emit(VECTOR, lambda v: v.tensor_scalar(
                out=fr[:, 192:256], in0=fr[:, 64:128], scalar1=0.25, scalar2=None,
                op0=ALU.add), waits=[tsin], tag="ca")
            tc2 = p.emit(VECTOR, lambda v: v.tensor_scalar(
                out=fr[:, 0:64], in0=fr[:, 192:256], scalar1=0.5, scalar2=None,
                op0=ALU.is_ge), waits=[tc1], tag="cb")
            tcos = p.emit(VECTOR, lambda v: v.scalar_tensor_tensor(
                out=fr[:, 192:256], in0=fr[:, 0:64], scalar=-1.0,
                in1=fr[:, 192:256], op0=ALU.mult, op1=ALU.add), waits=[tc2], tag="cc")
            tsn = p.emit(SCALAR, (lambda tt: lambda s: s.activation(
                out=self.ssq[:, tt * 64:(tt + 1) * 64], in_=fr[:, 128:192],
                func=AF.Sin, scale=TWO_PI))(tt), waits=[tsin], tag="sin")
            tcs = p.emit(SCALAR, (lambda tt: lambda s: s.activation(
                out=self.csq[:, tt * 64:(tt + 1) * 64], in_=fr[:, 192:256],
                func=AF.Sin, scale=TWO_PI))(tt), waits=[tcos], tag="cos")
            tls = p.emit(VECTOR, lambda v: v.tensor_scalar(
                out=fr[:, 0:64], in0=lbase_bc[:, :], scalar1=tcol[:, 1:2],
                scalar2=None, op0=ALU.mult), waits=[tpw, t6, tcs], tag="ls")
            prev_tls = tls
            tsc = p.emit(SCALAR, lambda s: s.activation(
                out=fr[:, 64:128], in_=fr[:, 0:64], func=AF.Exp), waits=[tls],
                tag="sc")
            tisc = p.emit(SCALAR, lambda s: s.activation(
                out=fr[:, 256:320], in_=fr[:, 0:64], func=AF.Exp, scale=-1.0),
                waits=[tls], tag="isc")
            a = self.csq[:, tt * 64:(tt + 1) * 64]
            b = self.ssq[:, tt * 64:(tt + 1) * 64]
            c_ = self.csk[:, tt * 64:(tt + 1) * 64]
            d_ = self.ssk[:, tt * 64:(tt + 1) * 64]
            e1 = p.emit(VECTOR, (lambda a, c_: lambda v: v.tensor_tensor(
                out=c_, in0=a, in1=fr[:, 256:320], op=ALU.mult))(a, c_),
                waits=[tsc, tisc, tcs, tsn], tag="csk")
            e2 = p.emit(VECTOR, (lambda b, d_: lambda v: v.tensor_tensor(
                out=d_, in0=b, in1=fr[:, 256:320], op=ALU.mult))(b, d_),
                waits=[e1], tag="ssk")
            e3 = p.emit(VECTOR, (lambda a: lambda v: v.tensor_tensor(
                out=a, in0=a, in1=fr[:, 64:128], op=ALU.mult))(a),
                waits=[e2], tag="csq")
            e4 = p.emit(VECTOR, (lambda b: lambda v: v.tensor_tensor(
                out=b, in0=b, in1=fr[:, 64:128], op=ALU.mult))(b),
                waits=[e3], tag="ssq")
            self._tbl_tok = e4
        k.pop_scope()

    def p2_layer(self, l):
        k, p = self.k, self.k.p
        pre = f"f{l}_"
        wqkv = k.din(pre + "wqkv", [D, 1536])
        qs_h = k.din(pre + "qs", [1, 64]); ks_h = k.din(pre + "ks", [1, 64])
        wo_h = k.din(pre + "wo", [512, D])
        lng_h = k.din(pre + "lng", [1, D]); lnb_h = k.din(pre + "lnb", [1, D])
        w1_h = k.din(pre + "w1", [D, 5460]); b1_h = k.din(pre + "b1", [1, 5460])
        w2_h = k.din(pre + "w2", [2730, D]); b2_h = k.din(pre + "b2", [1, D])
        SW = self.SW
        k.push_scope()
        self.p2_tables_global()

        rows = k.sb("qr", [1, 4 * 64], F32)
        l1 = p.dma(SYNC, rows[0:1, 0:64], qs_h[0:1, :], tag="qsld")
        l2 = p.dma(SYNC, rows[0:1, 128:192], ks_h[0:1, :], tag="ksld")
        t1 = p.emit(VECTOR, lambda v: v.tensor_scalar(
            out=rows[0:1, 0:64], in0=rows[0:1, 0:64], scalar1=8.0, scalar2=None,
            op0=ALU.mult), waits=[l1], tag="qs8")
        t2 = p.emit(VECTOR, lambda v: v.tensor_scalar(
            out=rows[0:1, 64:96], in0=rows[0:1, 32:64], scalar1=-1.0, scalar2=None,
            op0=ALU.mult), waits=[t1], tag="qr1")
        t3 = p.emit(VECTOR, lambda v: v.tensor_copy(
            out=rows[0:1, 96:128], in_=rows[0:1, 0:32]), waits=[t2], tag="qr2")
        t4 = p.emit(VECTOR, lambda v: v.tensor_scalar(
            out=rows[0:1, 192:224], in0=rows[0:1, 160:192], scalar1=-1.0,
            scalar2=None, op0=ALU.mult), waits=[l2], tag="kr1")
        t5 = p.emit(VECTOR, lambda v: v.tensor_copy(
            out=rows[0:1, 224:256], in_=rows[0:1, 128:160]), waits=[t4], tag="kr2")
        nb = k.sb("nb", [1, 2], F32)
        t6 = p.emit(VECTOR, lambda v: v.reduce_max(
            out=nb[0:1, 0:1], in_=rows[0:1, 0:64], axis=AX.X,
            apply_absolute_value=True), waits=[t1], tag="m1")
        t7 = p.emit(VECTOR, lambda v: v.reduce_max(
            out=nb[0:1, 1:2], in_=rows[0:1, 128:192], axis=AX.X,
            apply_absolute_value=True), waits=[l2], tag="m2")
        t8 = p.emit(VECTOR, lambda v: v.scalar_tensor_tensor(
            out=nb[0:1, 0:1], in0=nb[0:1, 0:1], scalar=-1.0, in1=nb[0:1, 1:2],
            op0=ALU.mult, op1=ALU.mult), waits=[t6, t7], tag="nb0")
        nb0_col = k.sb("nb0c", [128, 1], F32)
        t9 = bcast_row(k, nb[0:1, 0:1], nb0_col[:, 0:1], self.ppool, [t8], "nbbc")
        rbc = k.sb("rbc", [128, 4 * 64], F32)
        rb_toks = []
        for i, tk in enumerate([t3, t3, t5, t5]):
            rb_toks.append(bcast_row(k, rows[0:1, i * 64:(i + 1) * 64],
                                     rbc[:, i * 64:(i + 1) * 64], self.ppool,
                                     [t3, t5], f"rbc{i}"))
        tbl = k.sb("tbl", [128, 4 * NT * 64], BF16)
        TL = NT * 64
        tbl_toks = []
        for i, (srct, rbi) in enumerate([(self.csq, 0), (self.ssq, 1),
                                         (self.csk, 2), (self.ssk, 3)]):
            tt_ = p.emit(VECTOR, (lambda i, srct, rbi: lambda v: v.tensor_tensor(
                out=tbl[:, i * TL:(i + 1) * TL].rearrange("p (t d) -> p t d", d=64),
                in0=srct[:, :].rearrange("p (t d) -> p t d", d=64),
                in1=rbc[:, rbi * 64:(rbi + 1) * 64].unsqueeze(1).broadcast_to(
                    [128, NT, 64]), op=ALU.mult))(i, srct, rbi),
                waits=rb_toks + [self._tbl_tok], tag=f"tbl{i}")
            tbl_toks.append(tt_)
        T1q = tbl[:, 0 * TL:1 * TL]; T2q = tbl[:, 1 * TL:2 * TL]
        T1k = tbl[:, 2 * TL:3 * TL]; T2k = tbl[:, 3 * TL:4 * TL]
        mAC = k.sb("mAC", [128, 256], F32)
        tmA = affine_mask(k, mAC[:, 0:128], 128, ALU.is_ge, 0, 1, NEGB, 0.0, "mA",
                          step=-1)
        tmC = affine_mask(k, mAC[:, 128:256], 128, ALU.is_ge, 0, -1, NEGB, 0.0, "mC",
                          step=1)
        mA = mAC[:, 0:128]; mC = mAC[:, 128:256]

        xbf = self.xT[:, :].bitcast(BF16)
        kT = xbf[:, 0:4 * S]
        qT = xbf[:, 4 * S:8 * S]
        v64 = xbf[:, 8 * S:8 * S + NT * 512]
        wo_sb = xbf[:, 8 * S + NT * 512:8 * S + NT * 512 + 8 * D]
        slab = k.sb("fslab", [128, 8 * 512], BF16)
        wstg = k.sb("fwstg", [128, D], F32)
        mir_f = k.sb("fmirf", [128, 512], F32)
        mir = k.sb("fmir", [128, 8 * 512], BF16)
        qn_sb = k.sb("fqn", [128, 512], F32)
        sq_sb = k.sb("fsq", [128, 512], F32)
        ra = k.sb("fra", [128, 512], F32)
        rot_bf = k.sb("frot", [128, 512], BF16)
        ssr = k.sb("fssr", [128, 24], F32)
        pT = k.sb("fpT", [128, 8 * 512], BF16)
        attn_sl = k.sb("fattn", [64, 8 * 512], BF16)
        den_sb = k.sb("fden", [64, 512], F32)
        den_r = k.sb("fdenr", [1, 512], F32)
        self._mirQ_free = [None] * 8
        wstg_free = [None]

        zt_toks = []
        def ztf(v):
            last = None
            for slot in range(1):
                for oo in range(1, 4):
                    last = v.memset(pT[:, slot * 4096 + (4 + oo) * 512:
                                       slot * 4096 + (4 + oo) * 512 + 128 * oo], 0.0)
                for pp in range(0, 3):
                    off = 128 * (pp + 1)
                    last = v.memset(pT[:, slot * 4096 + pp * 512 + off:
                                       slot * 4096 + (pp + 1) * 512], 0.0)
            return last
        zt = p.emit(VECTOR, ztf, tag="pTz")

        wo_toks = []
        for h in range(FH):
            ld = p.dma(SYNC, wstg[0:64, 0:D], wo_h[h * 64:(h + 1) * 64, :],
                       waits=[wstg_free[0]] if wstg_free[0] else [], tag="wold")
            ct = p.emit(SCALAR, (lambda h: lambda s: s.copy(
                out=wo_sb[0:64, h * D:(h + 1) * D], in_=wstg[0:64, 0:D]))(h),
                waits=[ld], tag="woc")
            wstg_free[0] = ct
            wo_toks.append(ct)

        def strm_shift_ap(c, a0, wdt):
            if c < 4:
                return self.strmc[c][:, a0 + 1:a0 + 1 + wdt]
            return self.strmc[c][:, a0:a0 + wdt]

        def build_mirror_q(a0):
            toks = []
            for h in range(NCH):
                w = [self._mirQ_free[h]] if self._mirQ_free[h] else []
                tq = p.emit(VECTOR, (lambda h, a0: lambda v: v.tensor_copy(
                    out=mir[:, h * 512:(h + 1) * 512],
                    in_=strm_shift_ap(h, a0, 512)))(h, a0),
                    waits=w + [self.strm_tok[h]], tag="mirq")
                toks.append(tq)
            return toks

        def rotv(ap512):
            return bass.AP(ap512.tensor, ap512.offset + 32,
                           [list(ap512.ap[0]), [64, 8], [-32, 2], [1, 32]])

        kT_tok = [None] * NT
        qT_tok = [None] * NT
        v_tok2 = [None] * NT
        self._slab_free = None
        def proj_pass(fpart, kind):
            sl_toks = []
            for h in range(NCH):
                ld = p.dma(SYNC, wstg[:, 0:512],
                           wqkv[h * 128:(h + 1) * 128, fpart * 512:(fpart + 1) * 512],
                           waits=[wstg_free[0]] if wstg_free[0] else [], tag="qkvld")
                ct = p.emit(SCALAR, (lambda h: lambda s: s.copy(
                    out=slab[:, h * 512:(h + 1) * 512], in_=wstg[:, 0:512]))(h),
                    waits=[ld] + ([self._slab_free] if (h == 0 and self._slab_free)
                                  else []),
                    tag="qkvc")
                wstg_free[0] = ct
                sl_toks.append(ct)
            for ts4 in range(NT4):
                a0 = ts4 * 512
                mtoks = build_mirror_q(a0)
                last_mm_slice = None
                for q4 in range(4):
                    tt = ts4 * 4 + q4
                    j, bank, wb = self.ppool.acquire()
                    mm = None
                    for h in range(NCH):
                        mm = p.emit(TENSOR, (lambda h, q4, bank, first, last:
                            lambda te: te.matmul(
                                out=bank[:, :],
                                lhsT=mir[:, h * 512 + q4 * 128: h * 512 + (q4 + 1) * 128],
                                rhs=slab[:, h * 512:(h + 1) * 512],
                                start=first, stop=last))(h, q4, bank, h == 0,
                                                         h == NCH - 1),
                            waits=(wb + mtoks + sl_toks if h == 0 else []),
                            tag=f"pj{kind}{tt}")
                    last_mm_slice = mm
                    if kind == "v":
                        ev = p.emit(VECTOR, (lambda tt, bank: lambda v: v.tensor_copy(
                            out=v64[:, tt * 512:(tt + 1) * 512], in_=bank[:, :]))(tt, bank),
                            waits=[mm], tag=f"vev{tt}")
                        self.ppool.release(j, ev)
                        v_tok2[tt] = ev
                        continue
                    T1 = T1q if kind == "q" else T1k
                    T2 = T2q if kind == "q" else T2k
                    dstT = qT if kind == "q" else kT
                    s1 = p.emit(SCALAR, (lambda bank: lambda s: s.activation(
                        out=sq_sb[:, :], in_=bank[:, :], func=AF.Square))(bank),
                        waits=[mm], tag="l2sq")
                    s2 = p.emit(VECTOR, lambda v: v.tensor_reduce(
                        out=ssr[:, 0:8], in_=sq_sb[:, :].rearrange(
                            "p (h d) -> p h d", d=64), axis=AX.X, op=ALU.add),
                        waits=[s1], tag="l2red")
                    s3 = p.emit(SCALAR, lambda s: s.activation(
                        out=ssr[:, 8:16], in_=ssr[:, 0:8], func=AF.Sqrt),
                        waits=[s2], tag="l2rt")
                    s4 = p.emit(VECTOR, lambda v: v.tensor_scalar(
                        out=ssr[:, 8:16], in0=ssr[:, 8:16], scalar1=1e-12,
                        scalar2=None, op0=ALU.max), waits=[s3], tag="l2mx")
                    s5 = p.emit(VECTOR, lambda v: v.reciprocal(
                        out=ssr[:, 16:24], in_=ssr[:, 8:16]), waits=[s4], tag="l2rc")
                    s6 = p.emit(VECTOR, (lambda bank: lambda v: v.tensor_tensor(
                        out=qn_sb[:, :].rearrange("p (h d) -> p h d", d=64),
                        in0=bank[:, :].rearrange("p (h d) -> p h d", d=64),
                        in1=ssr[:, 16:24].unsqueeze(2).broadcast_to([128, 8, 64]),
                        op=ALU.mult))(bank), waits=[s5], tag="l2ap")
                    self.ppool.release(j, s6)
                    s7 = p.emit(VECTOR, (lambda tt, T1: lambda v: v.tensor_tensor(
                        out=ra[:, :].rearrange("p (h d) -> p h d", d=64),
                        in0=qn_sb[:, :].rearrange("p (h d) -> p h d", d=64),
                        in1=T1[:, tt * 64:(tt + 1) * 64].unsqueeze(1).broadcast_to(
                            [128, 8, 64]), op=ALU.mult))(tt, T1),
                        waits=[s6] + tbl_toks, tag="rt1")
                    s8 = p.emit(VECTOR, (lambda tt, T2: lambda v: v.tensor_tensor(
                        out=sq_sb[:, :].rearrange("p (h d) -> p h d", d=64),
                        in0=rotv(qn_sb[:, :]),
                        in1=T2[:, tt * 64:(tt + 1) * 64].unsqueeze(1).broadcast_to(
                            [128, 8, 64]), op=ALU.mult))(tt, T2),
                        waits=[s7], tag="rt2")
                    s9 = p.emit(VECTOR, lambda v: v.tensor_tensor(
                        out=rot_bf[:, :], in0=ra[:, :], in1=sq_sb[:, :], op=ALU.add),
                        waits=[s8], tag="rt3")
                    evs = []
                    for r in range(4):
                        j2, bank2, wb2 = self.ppool.acquire()
                        tt2 = p.emit(TENSOR, (lambda r, bank2: lambda te: te.transpose(
                            out=bank2.bitcast(BF16)[:, 0:128],
                            in_=rot_bf[:, r * 128:(r + 1) * 128],
                            identity=self.idn_bf[:, :]))(r, bank2),
                            waits=[s9, self.idn_bf_tok] + wb2, tag="rtp")
                        ev = p.emit(VECTOR, (lambda r, tt, bank2, dstT: lambda v:
                            v.tensor_copy(
                                out=dstT[:, r * S + tt * 128: r * S + (tt + 1) * 128],
                                in_=bank2.bitcast(BF16)[:, 0:128]))(r, tt, bank2, dstT),
                            waits=[tt2], tag="rtc")
                        self.ppool.release(j2, ev)
                        evs.append(ev)
                    if kind == "q":
                        qT_tok[tt] = evs[-1]
                    else:
                        kT_tok[tt] = evs[-1]
                for h in range(NCH):
                    self._mirQ_free[h] = last_mm_slice
                self._slab_free = last_mm_slice
        proj_pass(1, "k")
        proj_pass(2, "v")
        proj_pass(0, "q")

        pT_free = [None, None]
        attn_free = [None] * 8
        for w in range(4):
            wo_evs = []
            for h in range(FH):
                slot = 0
                pbase = 0
                chunks = []
                if w > 0:
                    for pp in range(4):
                        chunks.append(("prev", pp, 4 * (w - 1) + pp, pp))
                for oo in range(4):
                    chunks.append(("own", oo, 4 * w + oo, 4 + oo))
                hp = (h % 2) * 64
                r = h // 2
                exp_toks = []
                pw = [pT_free[slot]] if pT_free[slot] else []
                for kindc, idx, jtile, ci in chunks:
                    if kindc == "prev":
                        off, N = 0, 128 * (idx + 1)
                    else:
                        off, N = 128 * idx, 512 - 128 * idx
                    j, bank, wb = self.ppool.acquire()
                    mm = p.emit(TENSOR, (lambda hp, r, jtile, w, off, N, bank:
                        lambda te: te.matmul(
                            out=bank[:, off:off + N],
                            lhsT=kT[hp:hp + 64, r * S + jtile * 128: r * S + (jtile + 1) * 128],
                            rhs=qT[hp:hp + 64, r * S + w * 512 + off: r * S + w * 512 + off + N],
                            start=True, stop=True,
                            tile_position=(hp, 0)))(hp, r, jtile, w, off, N, bank),
                        waits=wb + [kT_tok[jtile], qT_tok[4 * w + 0], qT_tok[4 * w + 1],
                                    qT_tok[4 * w + 2], qT_tok[4 * w + 3]],
                        tag=f"sc{w}_{h}_{ci}")
                    dq = 128 * idx
                    md = p.emit(VECTOR, (lambda bank, dq, kindc: lambda v:
                        v.tensor_tensor(out=bank[:, dq:dq + 128],
                                        in0=bank[:, dq:dq + 128],
                                        in1=(mA if kindc == "prev" else mC),
                                        op=ALU.add))(bank, dq, kindc),
                        waits=[mm, tmA, tmC], tag=f"msk{w}_{h}_{ci}")
                    ex = p.emit(SCALAR, (lambda bank, pbase, ci, off, N: lambda s:
                        s.activation(out=pT[:, pbase + ci * 512 + off:
                                            pbase + ci * 512 + off + N],
                                     in_=bank[:, off:off + N], func=AF.Exp,
                                     bias=nb0_col[:, 0:1]))(bank, pbase, ci, off, N),
                        waits=[md, t9, zt] + pw, tag=f"ex{w}_{h}_{ci}")
                    self.ppool.release(j, ex)
                    exp_toks.append(ex)
                j, bank, wb = self.ppool.acquire()
                mm = None
                for ii, (kindc, idx, jtile, ci) in enumerate(chunks):
                    mm = p.emit(TENSOR, (lambda h, jtile, ci, pbase, bank, first, last:
                        lambda te: te.matmul(
                            out=bank[0:64, :],
                            lhsT=v64[:, jtile * 512 + h * 64: jtile * 512 + (h + 1) * 64],
                            rhs=pT[:, pbase + ci * 512: pbase + (ci + 1) * 512],
                            start=first, stop=last))(h, jtile, ci, pbase, bank,
                                                     ii == 0, ii == len(chunks) - 1),
                        waits=(wb + exp_toks + [v_tok2[jt] for _, _, jt, _ in chunks]
                               if ii == 0 else []),
                        tag=f"av{w}_{h}")
                j2, bank2, wb2 = self.ppool.acquire()
                dmm = None
                for ii, (kindc, idx, jtile, ci) in enumerate(chunks):
                    dmm = p.emit(TENSOR, (lambda ci, pbase, bank2, first, last:
                        lambda te: te.matmul(
                            out=bank2[0:1, :], lhsT=self.ones1b[:, 0:1],
                            rhs=pT[:, pbase + ci * 512: pbase + (ci + 1) * 512],
                            start=first, stop=last))(ci, pbase, bank2,
                                                     ii == 0, ii == len(chunks) - 1),
                        waits=(wb2 + exp_toks + [self.ones1b_tok] if ii == 0 else []),
                        tag=f"dn{w}_{h}")
                pT_free[slot] = dmm
                d1 = p.emit(VECTOR, (lambda bank2: lambda v: v.reciprocal(
                    out=den_r[0:1, :], in_=bank2[0:1, :]))(bank2),
                    waits=[dmm], tag=f"dr{w}_{h}")
                self.ppool.release(j2, d1)
                j3, bank3, wb3 = self.ppool.acquire()
                d2 = p.emit(TENSOR, (lambda bank3: lambda te: te.matmul(
                    out=bank3[0:64, :], lhsT=k._ones_row[0:1, 0:64],
                    rhs=den_r[0:1, :], start=True, stop=True))(bank3),
                    waits=wb3 + [d1, k._ones_row_tok], tag=f"db{w}_{h}")
                d3 = p.emit(VECTOR, (lambda bank3: lambda v: v.tensor_copy(
                    out=den_sb[0:64, :], in_=bank3[0:64, :]))(bank3),
                    waits=[d2], tag=f"dc{w}_{h}")
                self.ppool.release(j3, d3)
                aw = [attn_free[h]] if attn_free[h] else []
                a1 = p.emit(VECTOR, (lambda h, bank: lambda v: v.tensor_tensor(
                    out=attn_sl[0:64, h * 512:(h + 1) * 512], in0=bank[0:64, :],
                    in1=den_sb[0:64, :], op=ALU.mult))(h, bank),
                    waits=[d3] + aw, tag=f"at{w}_{h}")
                self.ppool.release(j, a1)
                attn_free[h] = a1
                wo_evs.append(a1)
            for f in range(NCH):
                j, bank, wb = self.ppool.acquire()
                mm = None
                for h in range(FH):
                    mm = p.emit(TENSOR, (lambda f, h, bank, first, last:
                        lambda te: te.matmul(
                            out=bank[:, :],
                            lhsT=wo_sb[0:64, h * D + f * 128: h * D + (f + 1) * 128],
                            rhs=attn_sl[0:64, h * 512:(h + 1) * 512],
                            start=first, stop=last))(f, h, bank, h == 0, h == FH - 1),
                        waits=(wb + wo_evs + wo_toks if h == 0 else []),
                        tag=f"wo{w}_{f}")
                ev = p.emit(VECTOR, (lambda f, w, bank: lambda v: v.tensor_tensor(
                    out=self.strmc[f][:, w * 512 + 1:(w + 1) * 512 + 1],
                    in0=self.strmc[f][:, w * 512 + 1:(w + 1) * 512 + 1],
                    in1=bank[:, :], op=ALU.add))(f, w, bank),
                    waits=[mm, self.strm_tok[f]], tag=f"woev{w}_{f}")
                self.ppool.release(j, ev)
                self.strm_tok[f] = ev
            attn_free = [mm] * 8
        self._p2a_done = p.fence()
        import os as _os
        if _os.environ.get("P2_KEEP"):
            self._p2_dbg = dict(kT=kT, qT=qT, v64=v64, tbl=tbl[:, :], pT=pT[:, :],
                                attn_sl=attn_sl[:, :], nb0_col=nb0_col[:, :],
                                rbc=rbc[:, :])
            return
        k.pop_scope()
        if _os.environ.get("P2_NO_FFN"):
            return
        self.p2_ffn(l, lng_h, lnb_h, w1_h, b1_h, w2_h, b2_h)

    def p2_ffn(self, l, lng_h, lnb_h, w1_h, b1_h, w2_h, b2_h):
        k, p = self.k, self.k.p
        SW = self.SW
        HID = 2730
        k.push_scope()
        sh_aps = [self.strmc[c][:, (1 if c < 4 else 0):(1 if c < 4 else 0) + S]
                  for c in range(NCH)]
        xbf = self.xT[:, :].bitcast(BF16)
        h_half = xbf[:, 0:22 * 1024]
        lnh_h = xbf[:, 22 * 1024:22 * 1024 + NCH * 1024]
        lng_col, lg_tok = load_bias_cols(k, lng_h, D, tag="lngc")
        lnb_col, lb_tok = load_bias_cols(k, lnb_h, D, tag="lnbc")
        b2_col, b2_tok = load_bias_cols(k, b2_h, D, tag="b2c")
        b1a = k.sb("b1a", [128, 22], F32)
        b1g = k.sb("b1g", [128, 22], F32)
        def ld_ragged(dst, base):
            t1_ = k.p.dma_fn(SYNC, (lambda dst, base: lambda eng: eng.dma_start(
                out=dst[:, 0:21],
                in_=b1_h[0:1, base:base + 2688].rearrange("o (c q) -> (o q) c", q=128),
                allow_slow_non_contiguous=True))(dst, base), (), True, "b1r")
            t2_ = k.p.dma_fn(SYNC, (lambda dst, base: lambda eng: eng.dma_start(
                out=dst[0:42, 21:22],
                in_=b1_h[0:1, base + 2688:base + 2730].rearrange("o q -> q o"),
                allow_slow_non_contiguous=True))(dst, base), (), True, "b1r2")
            return [t1_, t2_]
        b1a_tok = ld_ragged(b1a, 0)
        b1g_tok = ld_ragged(b1g, HID)
        sv = k.sb("svcol", [128, 8], F32)
        sv_toks = []
        for c in range(4, NCH):
            t = p.emit(VECTOR, (lambda c: lambda v: v.tensor_copy(
                out=sv[:, c:c + 1], in_=self.strmc[c][:, 1024:1025]))(c),
                waits=[self.strm_tok[c]], tag="svc")
            sv_toks.append(t)
        rho2 = k.sb("rho2", [128, S], BF16)
        om2 = k.sb("om2", [128, S], BF16)
        k.push_scope()
        ln2 = ln_stats(k, sh_aps, list(self.strm_tok), self.ones1, self.ones1_tok,
                       self.ppool, tag=f"l2s{l}", out=(rho2, om2))
        k.pop_scope()
        w2res = k.sb("w2res", [128, 22 * D], BF16)
        wstg = k.sb("w2stg", [128, D], F32)
        stg_free = None
        w2_toks = []
        for oc in range(22):
            M = 128 if oc < 21 else 42
            ld = p.dma(SYNC, wstg[0:M, :], w2_h[oc * 128: oc * 128 + M, :],
                       waits=[stg_free] if stg_free else [], tag="w2ld")
            ct = p.emit(SCALAR, (lambda oc, M: lambda s: s.copy(
                out=w2res[0:M, oc * D:(oc + 1) * D], in_=wstg[0:M, :]))(oc, M),
                waits=[ld], tag="w2c")
            stg_free = ct
            w2_toks.append(ct)
        w1a_sl = k.sb("w1a", [128, 8 * 256], BF16)
        w1g_sl = k.sb("w1g", [128, 8 * 256], BF16)
        gring = k.sb("gring", [128, 2 * 512], F32)
        gring_free = [None, None]
        lf = k.sb("lf", [128, 2 * 512], F32)
        slabs = [(i * 256, 256) for i in range(10)] + [(2560, 170)]
        w1_free = [None, None]
        h_toks = {}
        w2_last = None
        for th in range(2):
            lnh_last = None
            lf_free = [None, None]
            prev_th = [w2_last] if w2_last else []
            for c in range(NCH):
                for t5i in range(2):
                    a0 = th * 1024 + t5i * 512
                    ri = (c * 2 + t5i) % 2
                    w_ = [lf_free[ri]] if lf_free[ri] else []
                    u1 = p.emit(VECTOR, (lambda c, a0, ri: lambda v: v.tensor_tensor(
                        out=lf[:, ri * 512:(ri + 1) * 512],
                        in0=sh_aps[c][:, a0:a0 + 512],
                        in1=rho2[:, a0:a0 + 512], op=ALU.mult))(c, a0, ri),
                        waits=ln2.toks + [self.strm_tok[c]] + w_ + prev_th,
                        tag="lnh1")
                    u2 = p.emit(VECTOR, (lambda c, a0, ri: lambda v: v.tensor_tensor(
                        out=lf[:, ri * 512:(ri + 1) * 512],
                        in0=lf[:, ri * 512:(ri + 1) * 512],
                        in1=om2[:, a0:a0 + 512], op=ALU.add))(c, a0, ri),
                        waits=[u1], tag="lnh2")
                    u3 = p.emit(VECTOR, (lambda c, a0, t5i, ri: lambda v: v.tensor_scalar(
                        out=lnh_h[:, c * 1024 + t5i * 512: c * 1024 + t5i * 512 + 512],
                        in0=lf[:, ri * 512:(ri + 1) * 512],
                        scalar1=lng_col[:, c:c + 1], scalar2=lnb_col[:, c:c + 1],
                        op0=ALU.mult, op1=ALU.add))(c, a0, t5i, ri),
                        waits=[u2, lg_tok, lb_tok], tag="lnh3")
                    lf_free[ri] = u3
                    lnh_last = u3
                if th == 1 and c >= 4:
                    u1 = p.emit(VECTOR, (lambda c: lambda v: v.tensor_tensor(
                        out=lf[:, 0:1], in0=sv[:, c:c + 1],
                        in1=rho2[:, 1024:1025], op=ALU.mult))(c),
                        waits=sv_toks + [lnh_last], tag="lnp1")
                    u2 = p.emit(VECTOR, (lambda c: lambda v: v.tensor_tensor(
                        out=lf[:, 0:1], in0=lf[:, 0:1],
                        in1=om2[:, 1024:1025], op=ALU.add))(c),
                        waits=[u1], tag="lnp2")
                    u3 = p.emit(VECTOR, (lambda c: lambda v: v.tensor_scalar(
                        out=lnh_h[:, c * 1024: c * 1024 + 1], in0=lf[:, 0:1],
                        scalar1=lng_col[:, c:c + 1], scalar2=lnb_col[:, c:c + 1],
                        op0=ALU.mult, op1=ALU.add))(c),
                        waits=[u2], tag="lnp3")
                    lnh_last = u3
                    lf_free[0] = u3
            for snum, (aoff, wdt) in enumerate(slabs):
                a_toks, g_toks = [], []
                for half, (sl, coff) in enumerate([(w1a_sl, 0), (w1g_sl, HID)]):
                    w_ = [w1_free[half]] if w1_free[half] else []
                    for hh in range(NCH):
                        ld = p.dma(SYNC, wstg[:, 0:wdt],
                                   w1_h[hh * 128:(hh + 1) * 128,
                                        coff + aoff: coff + aoff + wdt],
                                   waits=([stg_free] if stg_free else []) +
                                         (w_ if hh == 0 else []),
                                   tag="w1ld")
                        ct = p.emit(SCALAR, (lambda hh, sl, wdt: lambda s: s.copy(
                            out=sl[:, hh * 256: hh * 256 + wdt],
                            in_=wstg[:, 0:wdt]))(hh, sl, wdt),
                            waits=[ld], tag="w1c")
                        stg_free = ct
                        (a_toks if half == 0 else g_toks).append(ct)
                nsub = (wdt + 127) // 128
                last_g_mm = None
                for t5i in range(2):
                    tcols = th * 1024 + t5i * 512
                    for sub in range(nsub):
                        M = min(128, wdt - 128 * sub)
                        ci = (aoff + 128 * sub) // 128
                        ja, banka, wba = self.ppool.acquire()
                        mma = None
                        for hh in range(NCH):
                            mma = p.emit(TENSOR, (lambda hh, sub, M, t5i, banka, first, last:
                                lambda te: te.matmul(
                                    out=banka[0:M, :],
                                    lhsT=w1a_sl[:, hh * 256 + sub * 128: hh * 256 + sub * 128 + M],
                                    rhs=lnh_h[:, hh * 1024 + t5i * 512: hh * 1024 + t5i * 512 + 512],
                                    start=first, stop=last))(hh, sub, M, t5i, banka,
                                                             hh == 0, hh == NCH - 1),
                                waits=(wba + [a_toks[-1], lnh_last] if hh == 0 else []),
                                tag=f"w1a{th}_{snum}_{t5i}_{sub}")
                        jg, bankg, wbg = self.ppool.acquire()
                        mmg = None
                        for hh in range(NCH):
                            mmg = p.emit(TENSOR, (lambda hh, sub, M, t5i, bankg, first, last:
                                lambda te: te.matmul(
                                    out=bankg[0:M, :],
                                    lhsT=w1g_sl[:, hh * 256 + sub * 128: hh * 256 + sub * 128 + M],
                                    rhs=lnh_h[:, hh * 1024 + t5i * 512: hh * 1024 + t5i * 512 + 512],
                                    start=first, stop=last))(hh, sub, M, t5i, bankg,
                                                             hh == 0, hh == NCH - 1),
                                waits=(wbg + [g_toks[-1]] if hh == 0 else []),
                                tag=f"w1g{th}_{snum}_{t5i}_{sub}")
                        last_g_mm = mmg
                        gr = t5i % 2
                        ge = p.emit(SCALAR, (lambda bankg, M, ci, gr: lambda s:
                            s.activation(out=gring[0:M, gr * 512:(gr + 1) * 512],
                                         in_=bankg[0:M, :], func=AF.Gelu,
                                         bias=b1g[0:M, ci:ci + 1]))(bankg, M, ci, gr),
                            waits=[mmg] + b1g_tok +
                                  ([gring_free[gr]] if gring_free[gr] else []),
                            tag="gelu")
                        self.ppool.release(jg, ge)
                        he = p.emit(VECTOR, (lambda banka, M, ci, gr, t5i: lambda v:
                            v.scalar_tensor_tensor(
                                out=h_half[0:M, ci * 1024 + t5i * 512:
                                           ci * 1024 + t5i * 512 + 512],
                                in0=banka[0:M, :], scalar=b1a[0:M, ci:ci + 1],
                                in1=gring[0:M, gr * 512:(gr + 1) * 512],
                                op0=ALU.add, op1=ALU.mult))(banka, M, ci, gr, t5i),
                            waits=[ge, mma] + b1a_tok + prev_th, tag="hmul")
                        self.ppool.release(ja, he)
                        gring_free[gr] = he
                        h_toks[(th, t5i)] = he
                w1_free[0] = last_g_mm
                w1_free[1] = last_g_mm
            for f in range(NCH):
                for t5i in range(2):
                    tcols = th * 1024 + t5i * 512
                    j, bank, wb = self.ppool.acquire()
                    mm = None
                    for oc in range(22):
                        M = 128 if oc < 21 else 42
                        mm = p.emit(TENSOR, (lambda f, oc, M, t5i, bank, first, last:
                            lambda te: te.matmul(
                                out=bank[:, :],
                                lhsT=w2res[0:M, oc * D + f * 128: oc * D + (f + 1) * 128],
                                rhs=h_half[0:M, oc * 1024 + t5i * 512:
                                           oc * 1024 + t5i * 512 + 512],
                                start=first, stop=last))(f, oc, M, t5i, bank,
                                                         oc == 0, oc == 21),
                            waits=(wb + [w2_toks[-1], h_toks[(th, t5i)]]
                                   if oc == 0 else []),
                            tag=f"w2{th}_{f}_{t5i}")
                    ev = p.emit(VECTOR, (lambda f, tcols, bank: lambda v:
                        v.scalar_tensor_tensor(
                            out=self.strmc[f][:, tcols + 1: tcols + 513],
                            in0=bank[:, :], scalar=b2_col[:, f:f + 1],
                            in1=self.strmc[f][:, tcols + 1: tcols + 513],
                            op0=ALU.add, op1=ALU.add))(f, tcols, bank),
                        waits=[mm, b2_tok, self.strm_tok[f]], tag=f"w2ev{th}_{f}")
                    self.ppool.release(j, ev)
                    self.strm_tok[f] = ev
                    w2_last = mm
        import os as _os
        if _os.environ.get("P2_KEEP_FFN") == str(l):
            self._ffn_dbg = dict(lnh=lnh_h, h=h_half, rho2=rho2[:, :], om2=om2[:, :])
            return
        k.pop_scope()

    def p2_apply(self):
        k, p = self.k, self.k.p
        self.p2_reload_xT()
        for c in range(NCH):
            t = p.emit(VECTOR, (lambda c: lambda v: v.tensor_tensor(
                out=self.xTc[c], in0=self.xTc[c],
                in1=self.strmc[c][:, 1:S + 1], op=ALU.add))(c),
                waits=[self.xTc_tok[c], self.strm_tok[c]], tag="xps")
            self.xTc_tok[c] = t
        self.p1_apply_attn()

    def p3_ln1(self):
        k, p = self.k, self.k.p
        k.push_scope()
        self.rho1 = k.sb("rho1", [128, S], BF16)
        self.om1 = k.sb("om1", [128, S], BF16)
        k.push_scope()
        self.ln1 = ln_stats(k, self.xTc, [t for t in self.xTc_tok if t],
                            self.ones1, self.ones1_tok, self.ppool, tag="ln1",
                            out=(self.rho1, self.om1))
        k.pop_scope()

    def xn1_tile(self, ring_f32, out_bf, c, a0, wdt, waits):
        k, p = self.k, self.k.p
        t1 = p.emit(VECTOR, lambda v: v.tensor_tensor(
            out=ring_f32[:, 0:wdt], in0=self.xTc[c][:, a0:a0 + wdt],
            in1=self.rho1[:, a0:a0 + wdt], op=ALU.mult),
            waits=list(waits) + self.ln1.toks +
                  ([self.xTc_tok[c]] if self.xTc_tok[c] else []), tag="x1a")
        t2 = p.emit(VECTOR, lambda v: v.tensor_tensor(
            out=out_bf, in0=ring_f32[:, 0:wdt],
            in1=self.om1[:, a0:a0 + wdt], op=ALU.add), waits=[t1], tag="x1b")
        return t2

    def p4_l1attn(self):
        k, p = self.k, self.k.p
        wq = k.din("l1a_wq", [D, D]); bq = k.din("l1a_bq", [1, D])
        wk = k.din("l1a_wk", [D, D]); bk = k.din("l1a_bk", [1, D])
        wv = k.din("l1a_wv", [D, D]); bv = k.din("l1a_bv", [1, D])
        self.attn1 = k.dram("attn1", [D, S], BF16)
        k.push_scope()
        bq_col, bq_tok = load_bias_cols(k, bq, D, tag="bq1")
        bk_col, bk_tok = load_bias_cols(k, bk, D, tag="bk1")
        bv_bc = k.sb("bv_bc", [128, D], BF16)
        k.push_scope()
        bv_row = k.sb("bv_row", [1, D], F32)
        tbv0 = p.dma(SYNC, bv_row[0:1, :], bv[0:1, :], tag="bvld")
        tbv = bcast_row(k, bv_row[0:1, :], bv_bc[:, :], self.ppool, [tbv0], "bvbc")
        k.pop_scope()
        mC_t = k.sb("mC1", [128, 128], F32)
        tmC = affine_mask(k, mC_t, 128, ALU.is_ge, 0, -1, NEGB, 0.0, "mC1", step=1)
        mC = mC_t[:, :]
        bcv = self.strm[:, :].bitcast(BF16)
        KT = bcv[:, 0:NCH * S]
        VT = bcv[:, NCH * S:2 * NCH * S]
        wsb = k.sb("wsb1", [128, NCH * D], BF16)
        wstg = k.sb("wstg1", [128, D], F32)
        mirf = k.sb("mirf1", [128, 512], F32)
        mir = k.sb("mir1", [128, 8 * 512], BF16)
        q_slab = k.sb("qs1", [128, 8 * 512], BF16)
        pT = k.sb("pT1", [128, 16 * 512], BF16)
        offs_col = k.sb("offs1", [128, 16], F32)
        km_row = k.sb("kmr1", [1, 2 * S], BF16)
        den_r = k.sb("denr1", [1, 512], F32)
        den_sb = k.sb("dens1", [64, 512], BF16)
        to_ring = k.sb("tor1", [64, 2 * 512], BF16)
        to_free = [None, None]
        self._mir1_free = [None] * 8
        self._w1stg_free = None
        self._w1sb_free = None

        def load_w(hbm, toks_out, tagp):
            for c in range(NCH):
                ld = p.dma(SYNC, wstg[:, :], hbm[c * 128:(c + 1) * 128, :],
                           waits=[self._w1stg_free] if self._w1stg_free else [],
                           tag=tagp + "ld")
                ct = p.emit(SCALAR, (lambda c: lambda s: s.copy(
                    out=wsb[:, c * D:(c + 1) * D], in_=wstg[:, :]))(c),
                    waits=[ld] + ([self._w1sb_free] if (c == 0 and self._w1sb_free)
                                  else []),
                    tag=tagp + "c")
                self._w1stg_free = ct
                toks_out.append(ct)

        def build_mir1(a0):
            toks = []
            for h in range(NCH):
                w_ = [self._mir1_free[h]] if self._mir1_free[h] else []
                toks.append(self.xn1_tile(mirf, mir[:, h * 512:(h + 1) * 512],
                                          h, a0, 512, w_))
            return toks

        KT_tok = [[None] * NT4 for _ in range(NCH)]
        km_toks = []
        wk_toks = []
        load_w(wk, wk_toks, "wk1")
        last_mm = None
        for ts4 in range(NT4):
            a0 = ts4 * 512
            mtoks = build_mir1(a0)
            for u in range(NCH):
                j, bank, wb = self.ppool.acquire()
                mm = None
                for c in range(NCH):
                    mm = p.emit(TENSOR, (lambda u, c, bank, first, last:
                        lambda te: te.matmul(
                            out=bank[:, :],
                            lhsT=wsb[:, c * D + u * 128: c * D + (u + 1) * 128],
                            rhs=mir[:, c * 512:(c + 1) * 512],
                            start=first, stop=last))(u, c, bank, c == 0, c == NCH - 1),
                        waits=(wb + mtoks + [wk_toks[-1]] if c == 0 else []),
                        tag=f"k1mm{ts4}_{u}")
                last_mm = mm
                ev = p.emit(SCALAR, (lambda u, a0, bank: lambda s: s.activation(
                    out=KT[:, u * S + a0: u * S + a0 + 512], in_=bank[:, :],
                    func=AF.Relu, bias=bk_col[:, u:u + 1]))(u, a0, bank),
                    waits=[mm, bk_tok], tag=f"k1ev{ts4}_{u}")
                self.ppool.release(j, ev)
                KT_tok[u][ts4] = ev
            j, bank, wb = self.ppool.acquire()
            mm = None
            for c in range(NCH):
                mm = p.emit(TENSOR, (lambda c, bank, first, last: lambda te: te.matmul(
                    out=bank[0:1, :], lhsT=self.ones1b[:, 0:1],
                    rhs=mir[:, c * 512:(c + 1) * 512],
                    start=first, stop=last))(c, bank, c == 0, c == NCH - 1),
                    waits=(wb + mtoks + [self.ones1b_tok] if c == 0 else []),
                    tag=f"km{ts4}")
            last_mm = mm
            ev = p.emit(VECTOR, (lambda a0, bank: lambda v: v.tensor_copy(
                out=km_row[0:1, a0:a0 + 512], in_=bank[0:1, :]))(a0, bank),
                waits=[mm], tag=f"kmev{ts4}")
            self.ppool.release(j, ev)
            km_toks.append(ev)
            for h in range(NCH):
                self._mir1_free[h] = last_mm
        self._w1sb_free = last_mm
        t1 = p.emit(VECTOR, lambda v: v.tensor_scalar(
            out=km_row[0:1, S:2 * S], in0=km_row[0:1, 0:S], scalar1=0.0,
            scalar2=None, op0=ALU.is_equal), waits=km_toks, tag="kmeq")
        t2 = p.emit(VECTOR, lambda v: v.tensor_scalar(
            out=km_row[0:1, 0:S], in0=km_row[0:1, S:2 * S],
            scalar1=-(2.0 ** 32 - 1.0), scalar2=-16.0, op0=ALU.mult, op1=ALU.add),
            waits=[t1], tag="kmoffs")
        t3 = p.emit(VECTOR, lambda v: v.tensor_scalar(
            out=km_row[0:1, S:2 * S], in0=km_row[0:1, S:2 * S], scalar1=-1.0,
            scalar2=1.0, op0=ALU.mult, op1=ALU.add), waits=[t2], tag="kmqm")
        off_toks = []
        for tt in range(NT):
            j, bank, wb = self.ppool.acquire()
            tk = p.emit(TENSOR, (lambda tt, bank: lambda te: te.transpose(
                out=bank.bitcast(BF16)[0:128, 0:1],
                in_=km_row[0:1, tt * 128:(tt + 1) * 128],
                identity=self.idn_bf[0:1, 0:1]))(tt, bank),
                waits=[t2, self.idn_bf_tok] + wb, tag=f"otp{tt}")
            ev = p.emit(VECTOR, (lambda tt, bank: lambda v: v.tensor_copy(
                out=offs_col[:, tt:tt + 1],
                in_=bank.bitcast(BF16)[0:128, 0:1]))(tt, bank),
                waits=[tk], tag=f"oev{tt}")
            self.ppool.release(j, ev)
            off_toks.append(ev)
        VT_tok = [None] * NT
        wv_toks = []
        load_w(wv, wv_toks, "wv1")
        for ts4 in range(NT4):
            a0 = ts4 * 512
            mtoks = build_mir1(a0)
            last_mm = None
            for q4 in range(4):
                tt = ts4 * 4 + q4
                for oh in range(2):
                    j, bank, wb = self.ppool.acquire()
                    mm = None
                    for c in range(NCH):
                        mm = p.emit(TENSOR, (lambda c, q4, oh, bank, first, last:
                            lambda te: te.matmul(
                                out=bank[:, :],
                                lhsT=mir[:, c * 512 + q4 * 128: c * 512 + (q4 + 1) * 128],
                                rhs=wsb[:, c * D + oh * 512: c * D + (oh + 1) * 512],
                                start=first, stop=last))(c, q4, oh, bank, c == 0,
                                                         c == NCH - 1),
                            waits=(wb + mtoks + [wv_toks[-1]] if c == 0 else []),
                            tag=f"v1mm{tt}_{oh}")
                    last_mm = mm
                    e1 = p.emit(VECTOR, (lambda oh, bank: lambda v: v.tensor_tensor(
                        out=bank[:, :], in0=bank[:, :],
                        in1=bv_bc[:, oh * 512:(oh + 1) * 512], op=ALU.add))(oh, bank),
                        waits=[mm, tbv], tag=f"v1b{tt}_{oh}")
                    e2 = p.emit(VECTOR, (lambda tt, oh, bank: lambda v: v.tensor_scalar(
                        out=VT[:, tt * D + oh * 512: tt * D + (oh + 1) * 512],
                        in0=bank[:, :], scalar1=0.0, scalar2=None,
                        op0=ALU.max))(tt, oh, bank), waits=[e1], tag=f"v1r{tt}_{oh}")
                    self.ppool.release(j, e2)
                    VT_tok[tt] = e2
            for h in range(NCH):
                self._mir1_free[h] = last_mm
        self._w1sb_free = last_mm
        wq_toks = []
        load_w(wq, wq_toks, "wq1")
        q_free = [None] * NCH
        pT_free = None
        for T in range(NT4):
            i0 = T * 512
            mtoks = build_mir1(i0)
            q_toks = []
            last_mm = None
            for u in range(NCH):
                j, bank, wb = self.ppool.acquire()
                mm = None
                for c in range(NCH):
                    mm = p.emit(TENSOR, (lambda u, c, bank, first, last:
                        lambda te: te.matmul(
                            out=bank[:, :],
                            lhsT=wsb[:, c * D + u * 128: c * D + (u + 1) * 128],
                            rhs=mir[:, c * 512:(c + 1) * 512],
                            start=first, stop=last))(u, c, bank, c == 0, c == NCH - 1),
                        waits=(wb + mtoks + [wq_toks[-1]] +
                               ([q_free[u]] if q_free[u] else [])
                               if c == 0 else []),
                        tag=f"q1mm{T}_{u}")
                last_mm = mm
                ev = p.emit(SCALAR, (lambda u, bank: lambda s: s.activation(
                    out=q_slab[:, u * 512:(u + 1) * 512], in_=bank[:, :],
                    func=AF.Relu, bias=bq_col[:, u:u + 1]))(u, bank),
                    waits=[mm, bq_tok], tag=f"q1ev{T}_{u}")
                self.ppool.release(j, ev)
                q_toks.append(ev)
            for h in range(NCH):
                self._mir1_free[h] = last_mm
            nj = 4 * T + 4
            zT = None
            for c in range(nj - 3, nj):
                trim = c - 4 * T
                if trim > 0:
                    zT = p.emit(VECTOR, (lambda c, trim: lambda v: v.memset(
                        pT[:, c * 512: c * 512 + 128 * trim], 0.0))(c, trim),
                        waits=([pT_free] if pT_free else []), tag=f"zT{T}_{c}")
            zTl = [zT] if zT else []
            for hh in range(AH):
                hp = (hh % 2) * 64
                r = hh // 2
                exp_toks = []
                pw = [pT_free] if pT_free else []
                for c in range(nj):
                    trim = c - 4 * T
                    off = 128 * trim if trim >= 0 else 0
                    N = 512 - off
                    j, bank, wb = self.ppool.acquire()
                    mm = p.emit(TENSOR, (lambda hp, r, c, i0, off, N, bank:
                        lambda te: te.matmul(
                            out=bank[:, off:off + N],
                            lhsT=KT[hp:hp + 64, r * S + c * 128: r * S + (c + 1) * 128],
                            rhs=q_slab[hp:hp + 64, r * 512 + off: r * 512 + off + N],
                            start=True, stop=True,
                            tile_position=(hp, 0)))(hp, r, c, i0, off, N, bank),
                        waits=wb + [q_toks[-1], KT_tok[NCH - 1][NT4 - 1]],
                        tag=f"s1{T}_{hh}_{c}")
                    w_ = [mm, tmC]
                    if trim >= 0:
                        md = p.emit(VECTOR, (lambda bank, off: lambda v:
                            v.tensor_tensor(out=bank[:, off:off + 128],
                                            in0=bank[:, off:off + 128],
                                            in1=mC, op=ALU.add))(bank, off),
                            waits=w_, tag=f"m1{T}_{hh}_{c}")
                    else:
                        md = mm
                    ex = p.emit(SCALAR, (lambda bank, c, off, N: lambda s:
                        s.activation(out=pT[:, c * 512 + off: c * 512 + off + N],
                                     in_=bank[:, off:off + N], func=AF.Exp,
                                     scale=0.125,
                                     bias=offs_col[:, c // 4 * 4 + c % 4:
                                                   c // 4 * 4 + c % 4 + 1][:, 0:1]
                                     if False else offs_col[:, c:c + 1]))(bank, c, off, N),
                        waits=[md, off_toks[c]] + pw, tag=f"e1{T}_{hh}_{c}")
                    self.ppool.release(j, ex)
                    exp_toks.append(ex)
                ztl = zTl
                j, bank, wb = self.ppool.acquire()
                mm = None
                for ii in range(nj):
                    mm = p.emit(TENSOR, (lambda hh, ii, bank, first, last:
                        lambda te: te.matmul(
                            out=bank[0:64, :],
                            lhsT=VT[:, ii * D + hh * 64: ii * D + (hh + 1) * 64],
                            rhs=pT[:, ii * 512:(ii + 1) * 512],
                            start=first, stop=last))(hh, ii, bank,
                                                     ii == 0, ii == nj - 1),
                        waits=(wb + [exp_toks[-1], VT_tok[NT - 1]] + ztl
                               if ii == 0 else []),
                        tag=f"av1{T}_{hh}")
                j2, bank2, wb2 = self.ppool.acquire()
                dmm = None
                for ii in range(nj):
                    dmm = p.emit(TENSOR, (lambda ii, bank2, first, last:
                        lambda te: te.matmul(
                            out=bank2[0:1, :], lhsT=self.ones1b[:, 0:1],
                            rhs=pT[:, ii * 512:(ii + 1) * 512],
                            start=first, stop=last))(ii, bank2, ii == 0, ii == nj - 1),
                        waits=(wb2 + [exp_toks[-1]] + ztl + [self.ones1b_tok]
                               if ii == 0 else []),
                        tag=f"dn1{T}_{hh}")
                pT_free = dmm
                d1 = p.emit(VECTOR, (lambda bank2: lambda v: v.reciprocal(
                    out=den_r[0:1, :], in_=bank2[0:1, :]))(bank2),
                    waits=[dmm], tag=f"dr1{T}_{hh}")
                self.ppool.release(j2, d1)
                d1b = p.emit(VECTOR, (lambda i0: lambda v: v.tensor_tensor(
                    out=den_r[0:1, :], in0=den_r[0:1, :],
                    in1=km_row[0:1, S + i0: S + i0 + 512], op=ALU.mult))(i0),
                    waits=[d1, t3], tag=f"dq1{T}_{hh}")
                j3, bank3, wb3 = self.ppool.acquire()
                d2 = p.emit(TENSOR, (lambda bank3: lambda te: te.matmul(
                    out=bank3[0:64, :], lhsT=k._ones_row[0:1, 0:64],
                    rhs=den_r[0:1, :], start=True, stop=True))(bank3),
                    waits=wb3 + [d1b, k._ones_row_tok], tag=f"db1{T}_{hh}")
                d3 = p.emit(VECTOR, (lambda bank3: lambda v: v.tensor_copy(
                    out=den_sb[0:64, :], in_=bank3[0:64, :]))(bank3),
                    waits=[d2], tag=f"dc1{T}_{hh}")
                self.ppool.release(j3, d3)
                ri = hh % 2
                w_ = [to_free[ri]] if to_free[ri] else []
                a1 = p.emit(VECTOR, (lambda ri, bank: lambda v: v.tensor_tensor(
                    out=to_ring[0:64, ri * 512:(ri + 1) * 512], in0=bank[0:64, :],
                    in1=den_sb[0:64, :], op=ALU.mult))(ri, bank),
                    waits=[d3] + w_, tag=f"ao1{T}_{hh}")
                self.ppool.release(j, a1)
                dm = p.dma(SYNC, self.attn1[hh * 64:(hh + 1) * 64, i0:i0 + 512],
                           to_ring[0:64, ri * 512:(ri + 1) * 512], waits=[a1],
                           tag=f"ad1{T}_{hh}")
                to_free[ri] = dm
            for u in range(NCH):
                q_free[u] = mm
        self._p4_done = p.fence()
        k.pop_scope()

    def quant_init_scratch(self):
        k = self.k
        self.qsc_blk = k.sb("qblk", [128, 64 * 6], F32)
        self.qsc_el = k.sb("qel", [128, 3 * 1024], F32)
        self._quant_free = None

    def quant_tile(self, src, dst, gs_col, g6_col, ginv2_col, gtoks, waits,
                   tag="qt"):
        k, p = self.k, self.k.p
        F = src.shape[-1]
        nb = F // 16
        B = self.qsc_blk
        E = self.qsc_el
        qf = [self._quant_free] if self._quant_free else []
        amax = B[:, 0:nb]; sc = B[:, nb:2 * nb]; ee = B[:, 2 * nb:3 * nb]
        qq = B[:, 3 * nb:4 * nb]; vv = B[:, 4 * nb:5 * nb]; yy = B[:, 5 * nb:6 * nb]
        y = E[:, 0:F]; acc = E[:, F:2 * F]; tmp = E[:, 2 * F:3 * F]
        w0 = list(waits) + list(gtoks) + qf
        t = p.emit(VECTOR, lambda v: v.tensor_reduce(
            out=amax, in_=src.rearrange("p (b z) -> p b z", z=16), axis=AX.X,
            op=ALU.max, apply_absolute_value=True), waits=w0, tag=tag + "am")
        t = p.emit(VECTOR, lambda v: v.tensor_scalar(
            out=sc, in0=amax, scalar1=g6_col, scalar2=448.0, op0=ALU.mult,
            op1=ALU.min), waits=[t], tag=tag + "sc")
        t = p.emit(VECTOR, lambda v: v.tensor_scalar(
            out=sc, in0=sc, scalar1=1e-30, scalar2=None, op0=ALU.max),
            waits=[t], tag=tag + "scm")
        tl = p.emit(SCALAR, lambda s: s.activation(
            out=ee, in_=sc, func=AF.Ln), waits=[t], tag=tag + "ln")
        t = p.emit(VECTOR, lambda v: v.tensor_scalar(
            out=acc[:, 0:nb], in0=ee, scalar1=-5.0 * LN2, scalar2=-6.0,
            op0=ALU.is_ge, op1=ALU.add), waits=[tl], tag=tag + "e0")
        for kk in range(-4, 9):
            t = p.emit(VECTOR, (lambda kk: lambda v: v.scalar_tensor_tensor(
                out=acc[:, 0:nb], in0=ee, scalar=float(kk) * LN2,
                in1=acc[:, 0:nb], op0=ALU.is_ge, op1=ALU.add))(kk),
                waits=[t], tag=tag + "el")
        c3a, tk3a = k.c(3.0 * LN2)
        c3b, tk3b = k.c(-3.0 * LN2)
        tqi = p.emit(SCALAR, lambda s: s.activation(
            out=qq, in_=acc[:, 0:nb], func=AF.Exp, scale=-LN2, bias=c3a),
            waits=[t, tk3a], tag=tag + "qi")
        tq2 = p.emit(SCALAR, lambda s: s.activation(
            out=ee, in_=acc[:, 0:nb], func=AF.Exp, scale=LN2, bias=c3b),
            waits=[t, tk3b], tag=tag + "q2")
        t = p.emit(VECTOR, lambda v: v.tensor_tensor(
            out=vv, in0=sc, in1=qq, op=ALU.mult), waits=[tqi], tag=tag + "v")
        t = p.emit(VECTOR, lambda v: v.tensor_scalar(
            out=vv, in0=vv, scalar1=0.5, scalar2=None, op0=ALU.add),
            waits=[t], tag=tag + "v5")
        tz = p.emit(VECTOR, lambda v: v.tensor_scalar(
            out=yy, in0=vv, scalar1=1.0, scalar2=None, op0=ALU.is_ge),
            waits=[t], tag=tag + "f0")
        for kk in range(2, 17):
            tz = p.emit(VECTOR, (lambda kk: lambda v: v.scalar_tensor_tensor(
                out=yy, in0=vv, scalar=float(kk), in1=yy,
                op0=ALU.is_ge, op1=ALU.add))(kk), waits=[tz], tag=tag + "fl")
        ty = p.emit(VECTOR, lambda v: v.tensor_tensor(
            out=yy, in0=yy, in1=ee, op=ALU.mult), waits=[tz, tq2], tag=tag + "y")
        t = p.emit(VECTOR, lambda v: v.tensor_scalar(
            out=vv, in0=yy, scalar1=0.0, scalar2=None, op0=ALU.is_equal),
            waits=[ty], tag=tag + "z")
        t = p.emit(VECTOR, lambda v: v.tensor_tensor(
            out=vv, in0=vv, in1=yy, op=ALU.add), waits=[t], tag=tag + "zs")
        t = p.emit(VECTOR, lambda v: v.reciprocal(out=vv, in_=vv),
                   waits=[t], tag=tag + "rc")
        trdq = p.emit(VECTOR, lambda v: v.tensor_scalar(
            out=vv, in0=vv, scalar1=gs_col, scalar2=None, op0=ALU.mult),
            waits=[t], tag=tag + "rq")
        tdq = p.emit(VECTOR, lambda v: v.tensor_scalar(
            out=yy, in0=yy, scalar1=ginv2_col, scalar2=None, op0=ALU.mult),
            waits=[ty, trdq], tag=tag + "dq")
        def bc(col):
            return col.rearrange("p (b o) -> p b o", o=1).broadcast_to([128, nb, 16])
        sv3 = src.rearrange("p (b z) -> p b z", z=16)
        t = p.emit(VECTOR, lambda v: v.tensor_tensor(
            out=y.rearrange("p (b z) -> p b z", z=16), in0=sv3, in1=bc(vv),
            op=ALU.mult), waits=[trdq], tag=tag + "ye")
        t = p.emit(SCALAR, lambda s: s.activation(
            out=y, in_=y, func=AF.Abs), waits=[t], tag=tag + "abs")
        t0 = p.emit(VECTOR, lambda v: v.tensor_scalar(
            out=acc, in0=y, scalar1=0.25, scalar2=None, op0=ALU.is_ge),
            waits=[t], tag=tag + "n1")
        cum = t0
        for thr, cof in [(0.75, 1.0), (1.25, 1.0), (1.75, 1.0), (2.5, 2.0),
                         (3.5, 2.0), (5.0, 4.0)]:
            t1 = p.emit(VECTOR, (lambda thr: lambda v: v.tensor_scalar(
                out=tmp, in0=y, scalar1=float(thr), scalar2=None,
                op0=ALU.is_ge))(thr), waits=[cum], tag=tag + "n")
            cum = p.emit(VECTOR, (lambda cof: lambda v: v.scalar_tensor_tensor(
                out=acc, in0=tmp, scalar=float(cof), in1=acc,
                op0=ALU.mult, op1=ALU.add))(cof), waits=[t1], tag=tag + "na")
        tsg = p.emit(VECTOR, lambda v: v.tensor_scalar(
            out=tmp, in0=src, scalar1=0.0, scalar2=2.0, op0=ALU.is_ge,
            op1=ALU.mult), waits=[cum], tag=tag + "sg")
        tsg2 = p.emit(VECTOR, lambda v: v.scalar_tensor_tensor(
            out=tmp, in0=tmp, scalar=-1.0, in1=acc, op0=ALU.add,
            op1=ALU.mult), waits=[tsg], tag=tag + "sg2")
        tout = p.emit(VECTOR, lambda v: v.tensor_tensor(
            out=dst.rearrange("p (b z) -> p b z", z=16),
            in0=tmp.rearrange("p (b z) -> p b z", z=16), in1=bc(yy),
            op=ALU.mult), waits=[tsg2, tdq], tag=tag + "out")
        self._quant_free = tout
        return tout

    def p5_l1attn_apply(self):
        k, p = self.k, self.k.p
        g_h = k.din("l1a_g", [1, D]); b_h = k.din("l1a_b", [1, D])
        k.push_scope()
        g_col, g_tok = load_bias_cols(k, g_h, D, tag="g1c")
        b_col, b_tok = load_bias_cols(k, b_h, D, tag="b1c")
        tob = self.strm[:, :].bitcast(BF16)[:, 0:NCH * S]
        rho3 = k.sb("rho3", [128, S], BF16)
        om3 = k.sb("om3", [128, S], BF16)
        ldr = k.sb("ldr1", [128, S], BF16)
        fr = k.sb("fr1", [128, 512], F32)
        ld_free = [None]
        tob_toks = []
        for c in range(NCH):
            ri = 0
            ld = p.dma(SYNC, ldr[:, ri * S:(ri + 1) * S],
                       self.attn1[c * 128:(c + 1) * 128, :],
                       waits=self._p4_done + ([ld_free[ri]] if ld_free[ri] else []),
                       tag="a1ld")
            last = None
            for ts4 in range(NT4):
                a0 = ts4 * 512
                t1 = p.emit(VECTOR, (lambda c, a0: lambda v: v.tensor_tensor(
                    out=fr[:, :], in0=self.xTc[c][:, a0:a0 + 512],
                    in1=self.rho1[:, a0:a0 + 512], op=ALU.mult))(c, a0),
                    waits=self.ln1.toks +
                          ([self.xTc_tok[c]] if self.xTc_tok[c] else []) +
                          ([last] if last else []), tag="t1")
                t2 = p.emit(VECTOR, (lambda c, a0: lambda v: v.tensor_tensor(
                    out=fr[:, :], in0=fr[:, :],
                    in1=self.om1[:, a0:a0 + 512], op=ALU.add))(c, a0),
                    waits=[t1], tag="t2")
                t3 = p.emit(VECTOR, (lambda c, a0, ri: lambda v: v.tensor_tensor(
                    out=tob[:, c * S + a0: c * S + a0 + 512], in0=fr[:, :],
                    in1=ldr[:, ri * S + a0: ri * S + a0 + 512], op=ALU.add))(c, a0, ri),
                    waits=[t2, ld], tag="t3")
                last = t3
            ld_free[ri] = last
            tob_toks.append(last)
        tob_c = [tob[:, c * S:(c + 1) * S] for c in range(NCH)]
        scr = (self.strm[:, 8196:9220], self.strm[0:1, 9220:11268],
               self.strm[0:1, 11268:13316], self.strm[0:1, 13316:15364])
        ln3 = ln_stats(k, tob_c, tob_toks, self.ones1, self.ones1_tok, self.ppool,
                       eps=1e-8, tag="ln3", out=(rho3, om3), scratch=scr)
        for c in range(NCH):
            last = None
            for ts4 in range(NT4):
                a0 = ts4 * 512
                t1 = p.emit(VECTOR, (lambda c, a0: lambda v: v.tensor_tensor(
                    out=fr[:, :], in0=tob[:, c * S + a0: c * S + a0 + 512],
                    in1=rho3[:, a0:a0 + 512], op=ALU.mult))(c, a0),
                    waits=ln3.toks + [tob_toks[c]] + ([last] if last else []),
                    tag="a3a")
                t2 = p.emit(VECTOR, (lambda c, a0: lambda v: v.tensor_tensor(
                    out=fr[:, :], in0=fr[:, :], in1=om3[:, a0:a0 + 512],
                    op=ALU.add))(c, a0), waits=[t1], tag="a3b")
                t3 = p.emit(VECTOR, (lambda c: lambda v: v.tensor_scalar(
                    out=fr[:, :], in0=fr[:, :], scalar1=g_col[:, c:c + 1],
                    scalar2=b_col[:, c:c + 1], op0=ALU.mult, op1=ALU.add))(c),
                    waits=[t2, g_tok, b_tok], tag="a3c")
                t4 = p.emit(VECTOR, (lambda c, a0: lambda v: v.tensor_tensor(
                    out=self.xTc[c][:, a0:a0 + 512],
                    in0=self.xTc[c][:, a0:a0 + 512], in1=fr[:, :],
                    op=ALU.add))(c, a0),
                    waits=[t3] + ([self.xTc_tok[c]] if self.xTc_tok[c] else []),
                    tag="a3d")
                last = t4
            self.xTc_tok[c] = last
        k.pop_scope()

    def p6_prepare(self):
        k, p = self.k, self.k.p
        gw = k.din("l1f_gw", [FI, D])
        uw = k.din("l1f_uw", [FI, D])
        dw = k.din("l1f_dw", [D, FI])
        sin_h = k.din("l1f_sin", [1, 1])
        sind_h = k.din("l1f_sind", [1, 1])
        self.gT = k.dram("gT", [D, FI], BF16)
        self.uT = k.dram("uT", [D, FI], BF16)
        self.dT = k.dram("dT", [FI, D], BF16)
        self.quant_init_scratch()
        gcols = k.sb("gcols", [128, 12], F32)
        gcols2 = k.sb("gcols2", [128, 3], F32)
        k.push_scope()
        wstg = k.sb("wstg6", [128, 2 * FI], F32)
        wq = k.sb("wq6", [128, FI], BF16)
        stg_free = [None, None]
        wq_free = None

        def scalar_cols(val_row, toks, idx, tagp):
            r2 = k.sb(tagp + "r", [1, 3], F32)
            t1 = p.emit(VECTOR, lambda v: v.tensor_copy(out=r2[0:1, 0:1],
                        in_=val_row[0:1, 0:1]), waits=toks, tag=tagp + "a")
            t2 = p.emit(VECTOR, lambda v: v.tensor_scalar(
                out=r2[0:1, 1:2], in0=r2[0:1, 0:1], scalar1=1.0 / 6.0,
                scalar2=None, op0=ALU.mult), waits=[t1], tag=tagp + "b")
            t3 = p.emit(VECTOR, lambda v: v.reciprocal(
                out=r2[0:1, 2:3], in_=r2[0:1, 0:1]), waits=[t1], tag=tagp + "c")
            t4 = p.emit(VECTOR, lambda v: v.tensor_scalar(
                out=r2[0:1, 2:3], in0=r2[0:1, 2:3], scalar1=0.5, scalar2=None,
                op0=ALU.mult), waits=[t3], tag=tagp + "d")
            tk = bcast_row(k, r2[0:1, 0:3], gcols[:, idx:idx + 3], self.ppool,
                           [t2, t4], tagp + "bc")
            return tk

        sin_r = k.sb("sinr", [1, 2], F32)
        tsi = p.dma(SYNC, sin_r[0:1, 0:1], sin_h[0:1, :], tag="sinld")
        tsd = p.dma(SYNC, sin_r[0:1, 1:2], sind_h[0:1, :], tag="sindld")
        tk_x = scalar_cols(sin_r[0:1, 0:1], [tsi], 0, "gx")
        tk_h = scalar_cols(sin_r[0:1, 1:2], [tsd], 3, "gh")

        mxrow = k.sb("mxrow", [1, 128], F32)

        def weight_gs2(hbm, nrow, ncol, idx, tagp):
            mx = k.sb(tagp + "mx2", [128, 2], F32)
            t0 = p.emit(VECTOR, lambda v: v.memset(mx[:, :], 0.0), tag=tagp + "z")
            last = t0
            nch = nrow // 128
            for c in range(nch):
                ri = c % 2
                ld = p.dma(SYNC, wstg[:, ri * FI:ri * FI + ncol],
                           hbm[c * 128:(c + 1) * 128, :],
                           waits=[stg_free[ri]] if stg_free[ri] else [],
                           tag=tagp + "ld")
                t1 = p.emit(VECTOR, (lambda ri, ncol: lambda v: v.tensor_reduce(
                    out=mx[:, 1:2], in_=wstg[:, ri * FI:ri * FI + ncol], axis=AX.X,
                    op=ALU.max, apply_absolute_value=True))(ri, ncol),
                    waits=[ld, last], tag=tagp + "r")
                t2 = p.emit(VECTOR, lambda v: v.tensor_tensor(
                    out=mx[:, 0:1], in0=mx[:, 0:1], in1=mx[:, 1:2], op=ALU.max),
                    waits=[t1], tag=tagp + "m")
                stg_free[ri] = t1
                last = t2
            j, bank, wb = self.ppool.acquire()
            tt = p.emit(TENSOR, (lambda bank: lambda te: te.transpose(
                out=bank[0:1, 0:128], in_=mx[:, 0:1],
                identity=self.idn_f32[:, :]))(bank),
                waits=[last, self.idn_f32_tok] + wb, tag=tagp + "tp")
            ev = p.emit(VECTOR, (lambda bank: lambda v: v.reduce_max(
                out=mxrow[0:1, 0:1], in_=bank[0:1, 0:128], axis=AX.X))(bank),
                waits=[tt], tag=tagp + "rx")
            self.ppool.release(j, ev)
            t3 = p.emit(VECTOR, lambda v: v.reciprocal(
                out=mxrow[0:1, 1:2], in_=mxrow[0:1, 0:1]), waits=[ev], tag=tagp + "rc")
            t4 = p.emit(VECTOR, lambda v: v.tensor_scalar(
                out=mxrow[0:1, 1:2], in0=mxrow[0:1, 1:2], scalar1=2688.0,
                scalar2=None, op0=ALU.mult), waits=[t3], tag=tagp + "gs")
            return scalar_cols(mxrow[0:1, 1:2], [t4], idx, tagp + "sc")

        tk_g = weight_gs2(gw, FI, D, 6, "gw")
        tk_u = weight_gs2(uw, FI, D, 9, "uw")
        def weight_gs_d(hbm, tagp):
            mx = k.sb(tagp + "mx2", [128, 2], F32)
            t0 = p.emit(VECTOR, lambda v: v.memset(mx[:, :], 0.0), tag=tagp + "z")
            last = t0
            for c in range(D // 128):
                ri = c % 2
                ld = p.dma(SYNC, wstg[:, ri * FI:(ri + 1) * FI],
                           hbm[c * 128:(c + 1) * 128, :],
                           waits=[stg_free[ri]] if stg_free[ri] else [],
                           tag=tagp + "ld")
                t1 = p.emit(VECTOR, (lambda ri: lambda v: v.tensor_reduce(
                    out=mx[:, 1:2], in_=wstg[:, ri * FI:(ri + 1) * FI], axis=AX.X,
                    op=ALU.max, apply_absolute_value=True))(ri),
                    waits=[ld, last], tag=tagp + "r")
                t2 = p.emit(VECTOR, lambda v: v.tensor_tensor(
                    out=mx[:, 0:1], in0=mx[:, 0:1], in1=mx[:, 1:2], op=ALU.max),
                    waits=[t1], tag=tagp + "m")
                stg_free[ri] = t1
                last = t2
            j, bank, wb = self.ppool.acquire()
            tt = p.emit(TENSOR, (lambda bank: lambda te: te.transpose(
                out=bank[0:1, 0:128], in_=mx[:, 0:1],
                identity=self.idn_f32[:, :]))(bank),
                waits=[last, self.idn_f32_tok] + wb, tag=tagp + "tp")
            ev = p.emit(VECTOR, (lambda bank: lambda v: v.reduce_max(
                out=mxrow[0:1, 0:1], in_=bank[0:1, 0:128], axis=AX.X))(bank),
                waits=[tt], tag=tagp + "rx")
            self.ppool.release(j, ev)
            t3 = p.emit(VECTOR, lambda v: v.reciprocal(
                out=mxrow[0:1, 1:2], in_=mxrow[0:1, 0:1]), waits=[ev], tag=tagp + "rc")
            t4 = p.emit(VECTOR, lambda v: v.tensor_scalar(
                out=mxrow[0:1, 1:2], in0=mxrow[0:1, 1:2], scalar1=2688.0,
                scalar2=None, op0=ALU.mult), waits=[t3], tag=tagp + "gs")
            r2 = k.sb(tagp + "r", [1, 3], F32)
            t1b = p.emit(VECTOR, lambda v: v.tensor_copy(out=r2[0:1, 0:1],
                         in_=mxrow[0:1, 1:2]), waits=[t4], tag=tagp + "a")
            t2b = p.emit(VECTOR, lambda v: v.tensor_scalar(
                out=r2[0:1, 1:2], in0=r2[0:1, 0:1], scalar1=1.0 / 6.0,
                scalar2=None, op0=ALU.mult), waits=[t1b], tag=tagp + "b")
            t3b = p.emit(VECTOR, lambda v: v.reciprocal(
                out=r2[0:1, 2:3], in_=r2[0:1, 0:1]), waits=[t1b], tag=tagp + "c")
            t4b = p.emit(VECTOR, lambda v: v.tensor_scalar(
                out=r2[0:1, 2:3], in0=r2[0:1, 2:3], scalar1=0.5, scalar2=None,
                op0=ALU.mult), waits=[t3b], tag=tagp + "d")
            return bcast_row(k, r2[0:1, 0:3], gcols2[:, 0:3], self.ppool,
                             [t2b, t4b], tagp + "bc")
        tk_dw = weight_gs_d(dw, "dww")

        def quant_spill(hbm, scratch, nrow, ncol, gidx, gc, gtok, tagp):
            nch = nrow // 128
            for c in range(nch):
                ri = c % 2
                ld = p.dma(SYNC, wstg[:, ri * FI:ri * FI + ncol],
                           hbm[c * 128:(c + 1) * 128, :],
                           waits=[stg_free[ri]] if stg_free[ri] else [],
                           tag=tagp + "l2")
                qts = []
                for f0 in range(0, ncol, 1024):
                    qt = self.quant_tile(
                        wstg[:, ri * FI + f0: ri * FI + f0 + 1024],
                        wq[:, f0:f0 + 1024],
                        gc[:, gidx:gidx + 1], gc[:, gidx + 1:gidx + 2],
                        gc[:, gidx + 2:gidx + 3], [gtok],
                        [ld] + ([wq_free[0]] if wq_free[0] else []),
                        tag=tagp + f"q{c}_{f0}")
                    qts.append(qt)
                stg_free[ri] = qts[-1]
                last_tt = None
                for r in range(ncol // 128):
                    j, bank, wb = self.ppool.acquire()
                    tt = p.emit(TENSOR, (lambda r, bank: lambda te: te.transpose(
                        out=bank.bitcast(BF16)[:, 0:128],
                        in_=wq[:, r * 128:(r + 1) * 128],
                        identity=self.idn_bf[:, :]))(r, bank),
                        waits=[qts[-1], self.idn_bf_tok] + wb, tag=tagp + "tp")
                    ri2 = r % 4
                    ev = p.emit(VECTOR, (lambda r, ri2, bank: lambda v: v.tensor_copy(
                        out=wqt[:, ri2 * 128:(ri2 + 1) * 128],
                        in_=bank.bitcast(BF16)[:, 0:128]))(r, ri2, bank),
                        waits=[tt] + ([wqt_free[ri2]] if wqt_free[ri2] else []),
                        tag=tagp + "te")
                    self.ppool.release(j, ev)
                    dm = p.dma(SYNC, scratch[r * 128:(r + 1) * 128,
                                             c * 128:(c + 1) * 128],
                               wqt[:, ri2 * 128:(ri2 + 1) * 128], waits=[ev],
                               tag=tagp + "sp")
                    wqt_free[ri2] = dm
                    last_tt = tt
                wq_free[0] = last_tt
        wq_free = [None]
        wqt_free = [None] * 4
        wqt = k.sb("wqt6", [128, 4 * 128], BF16)
        quant_spill(gw, self.gT, FI, D, 6, gcols, tk_g, "gq")
        quant_spill(uw, self.uT, FI, D, 9, gcols, tk_u, "uq")
        quant_spill(dw, self.dT, D, FI, 0, gcols2, tk_dw, "dq")
        self._p6_sin_cols = (gcols, tk_x, tk_h)
        self._p6_done_w = p.fence()
        k.pop_scope()

    def p6_xq(self):
        k, p = self.k, self.k.p
        self.xqT = k.sb("xqT", [128, NCH * S], BF16)
        gcols = self._p6_sin_cols[0]
        tk_x = self._p6_sin_cols[1]
        k.push_scope()
        bcvq = self.strm[:, :].bitcast(BF16)
        xn_tok = bcvq[:, 0:NT * D]
        xq_tok = bcvq[:, NT * D:2 * NT * D]
        mirf = k.sb("mirf6", [128, 512], F32)
        mirb = k.sb("mirb6", [128, 512], BF16)
        tok_done = {}
        mfree = None
        for c in range(NCH):
            for ts4 in range(NT4):
                a0 = ts4 * 512
                mt = self.xn1_tile(mirf, mirb[:, :], c, a0, 512,
                                   [mfree] if mfree else [])
                evs = []
                for q4 in range(4):
                    tt = ts4 * 4 + q4
                    j, bank, wb = self.ppool.acquire()
                    tk = p.emit(TENSOR, (lambda q4, bank: lambda te: te.transpose(
                        out=bank.bitcast(BF16)[:, 0:128],
                        in_=mirb[:, q4 * 128:(q4 + 1) * 128],
                        identity=self.idn_bf[:, :]))(q4, bank),
                        waits=[mt, self.idn_bf_tok] + wb, tag="x6tp")
                    ev = p.emit(VECTOR, (lambda tt, c, bank: lambda v: v.tensor_copy(
                        out=xn_tok[:, tt * D + c * 128: tt * D + (c + 1) * 128],
                        in_=bank.bitcast(BF16)[:, 0:128]))(tt, c, bank),
                        waits=[tk], tag="x6ev")
                    self.ppool.release(j, ev)
                    evs.append(ev)
                    tok_done[(tt, c)] = ev
                mfree = evs[-1]
        q_toks = []
        for tt in range(NT):
            w_ = [tok_done[(tt, c)] for c in range(NCH)]
            qt = self.quant_tile(xn_tok[:, tt * D:(tt + 1) * D],
                                 xq_tok[:, tt * D:(tt + 1) * D],
                                 gcols[:, 0:1], gcols[:, 1:2], gcols[:, 2:3],
                                 [tk_x], w_, tag=f"xq{tt}")
            q_toks.append(qt)
        for tt in range(NT):
            for c in range(NCH):
                j, bank, wb = self.ppool.acquire()
                tk = p.emit(TENSOR, (lambda tt, c, bank: lambda te: te.transpose(
                    out=bank.bitcast(BF16)[:, 0:128],
                    in_=xq_tok[:, tt * D + c * 128: tt * D + (c + 1) * 128],
                    identity=self.idn_bf[:, :]))(tt, c, bank),
                    waits=[q_toks[tt], self.idn_bf_tok] + wb, tag="xq6tp")
                ev = p.emit(VECTOR, (lambda tt, c, bank: lambda v: v.tensor_copy(
                    out=self.xqT[:, c * S + tt * 128: c * S + (tt + 1) * 128],
                    in_=bank.bitcast(BF16)[:, 0:128]))(tt, c, bank),
                    waits=[tk], tag="xq6ev")
                self.ppool.release(j, ev)
                self._xqT_tok = ev
        k.pop_scope()

    def p6_main(self):
        k, p = self.k, self.k.p
        gcols = self._p6_sin_cols[0]
        tk_h = self._p6_sin_cols[2]
        k.push_scope()
        gsl = k.sb("gsl", [128, 8 * 512], BF16)
        h_tok = None
        bcv6 = self.strm[:, :].bitcast(BF16)
        h_tok = bcv6[:, 0:4 * FI]
        hqT = bcv6[:, 4 * FI:4 * FI + 32 * 512]
        hq = k.sb("hq6", [128, FI], BF16)
        dsl = k.sb("dsl", [128, 2 * D], BF16)
        sil = k.sb("sil", [128, 4 * 512], BF16)
        gsl_free = [None]
        hq_free = None
        hqT_free = None
        dsl_free = [None, None]
        for t5 in range(NT4):
            t0 = t5 * 512
            h_toks = {}
            for osl in range(8):
                o0 = osl * 512
                g_toks = []
                for hh in range(NCH):
                    ld = p.dma(SYNC, gsl[:, hh * 512:(hh + 1) * 512],
                               self.gT[hh * 128:(hh + 1) * 128, o0:o0 + 512],
                               waits=(self._p6_done_w if (t5 == 0 and osl == 0
                                      and hh == 0) else []) +
                                     ([gsl_free[0]] if (hh == 0 and gsl_free[0])
                                      else []),
                               tag="gl")
                    g_toks.append(ld)
                gbanks = []
                for q4 in range(4):
                    tt = t5 * 4 + q4
                    jg, bg, wbg = self.ppool.acquire()
                    mmg = None
                    for hh in range(NCH):
                        mmg = p.emit(TENSOR, (lambda hh, tt, bg, first, last:
                            lambda te: te.matmul(
                                out=bg[:, :],
                                lhsT=self.xqT[:, hh * S + tt * 128: hh * S + (tt + 1) * 128],
                                rhs=gsl[:, hh * 512:(hh + 1) * 512],
                                start=first, stop=last))(hh, tt, bg, hh == 0,
                                                         hh == NCH - 1),
                            waits=(wbg + [g_toks[-1], self._xqT_tok]
                                   if hh == 0 else []),
                            tag=f"g6{t5}_{osl}_{q4}")
                    gbanks.append((jg, bg, mmg))
                sv_toks = []
                for q4 in range(4):
                    jg, bg, mmg = gbanks[q4]
                    sv = p.emit(SCALAR, (lambda q4, bg: lambda s: s.activation(
                        out=sil[:, q4 * 512:(q4 + 1) * 512], in_=bg[:, :],
                        func=AF.Silu))(q4, bg),
                        waits=[mmg], tag=f"sl6{t5}_{osl}_{q4}")
                    self.ppool.release(jg, sv)
                    sv_toks.append(sv)
                u_toks = []
                for hh in range(NCH):
                    ld = p.dma(SYNC, gsl[:, hh * 512:(hh + 1) * 512],
                               self.uT[hh * 128:(hh + 1) * 128, o0:o0 + 512],
                               waits=([gbanks[-1][2]] if hh == 0 else []),
                               tag="ul")
                    u_toks.append(ld)
                last_mmu = None
                for q4 in range(4):
                    tt = t5 * 4 + q4
                    ju, bu, wbu = self.ppool.acquire()
                    mmu = None
                    for hh in range(NCH):
                        mmu = p.emit(TENSOR, (lambda hh, tt, bu, first, last:
                            lambda te: te.matmul(
                                out=bu[:, :],
                                lhsT=self.xqT[:, hh * S + tt * 128: hh * S + (tt + 1) * 128],
                                rhs=gsl[:, hh * 512:(hh + 1) * 512],
                                start=first, stop=last))(hh, tt, bu, hh == 0,
                                                         hh == NCH - 1),
                            waits=(wbu + [u_toks[-1]] if hh == 0 else []),
                            tag=f"u6{t5}_{osl}_{q4}")
                    last_mmu = mmu
                    he = p.emit(VECTOR, (lambda q4, osl, bu: lambda v: v.tensor_tensor(
                        out=h_tok[:, q4 * FI + osl * 512: q4 * FI + osl * 512 + 512],
                        in0=bu[:, :], in1=sil[:, q4 * 512:(q4 + 1) * 512],
                        op=ALU.mult))(q4, osl, bu),
                        waits=[sv_toks[q4], mmu], tag=f"h6{t5}_{osl}_{q4}")
                    self.ppool.release(ju, he)
                    h_toks[(q4, osl)] = he
                gsl_free[0] = last_mmu
            hqT_toks = []
            for q4 in range(4):
                qts = []
                for f0 in range(0, FI, 1024):
                    qt = self.quant_tile(
                        h_tok[:, q4 * FI + f0: q4 * FI + f0 + 1024],
                        hq[:, f0:f0 + 1024],
                        gcols[:, 3:4], gcols[:, 4:5], gcols[:, 5:6], [tk_h],
                        [h_toks[(q4, o)] for o in range(8)] +
                        ([hq_free] if hq_free else []), tag=f"hq{t5}_{q4}")
                    qts.append(qt)
                last_tt = None
                for ic in range(32):
                    j, bank, wb = self.ppool.acquire()
                    tk = p.emit(TENSOR, (lambda ic, bank: lambda te: te.transpose(
                        out=bank.bitcast(BF16)[:, 0:128],
                        in_=hq[:, ic * 128:(ic + 1) * 128],
                        identity=self.idn_bf[:, :]))(ic, bank),
                        waits=[qts[-1], self.idn_bf_tok] + wb +
                              ([hqT_free] if (q4 == 0 and ic == 0 and hqT_free)
                               else []),
                        tag="h6tp")
                    ev = p.emit(VECTOR, (lambda ic, q4, bank: lambda v: v.tensor_copy(
                        out=hqT[:, ic * 512 + q4 * 128: ic * 512 + (q4 + 1) * 128],
                        in_=bank.bitcast(BF16)[:, 0:128]))(ic, q4, bank),
                        waits=[tk], tag="h6te")
                    self.ppool.release(j, ev)
                    hqT_toks.append(ev)
                    last_tt = tk
                hq_free = last_tt
            banks_o = []
            for o2 in range(8):
                j, bank, wb = self.ppool.acquire()
                banks_o.append((j, bank, wb))
            mm_last = [None] * 8
            for ic in range(32):
                ri = ic % 2
                ld = p.dma(SYNC, dsl[:, ri * D:(ri + 1) * D],
                           self.dT[ic * 128:(ic + 1) * 128, :],
                           waits=[dsl_free[ri]] if dsl_free[ri] else [], tag="dld")
                for o2 in range(8):
                    jj, bank, wb = banks_o[o2]
                    mm_last[o2] = p.emit(TENSOR, (lambda ic, o2, ri, bank, first, last:
                        lambda te: te.matmul(
                            out=bank[:, :],
                            lhsT=dsl[:, ri * D + o2 * 128: ri * D + (o2 + 1) * 128],
                            rhs=hqT[:, ic * 512:(ic + 1) * 512],
                            start=first, stop=last))(ic, o2, ri, bank,
                                                     ic == 0, ic == 31),
                        waits=[ld, hqT_toks[-1]] + (wb if ic == 0 else []),
                        tag=f"d6{t5}_{ic}_{o2}")
                dsl_free[ri] = mm_last[-1]
            for o2 in range(8):
                jj, bank, wb = banks_o[o2]
                ev = p.emit(VECTOR, (lambda o2, t0, bank: lambda v: v.tensor_tensor(
                    out=self.xTc[o2][:, t0:t0 + 512],
                    in0=self.xTc[o2][:, t0:t0 + 512], in1=bank[:, :],
                    op=ALU.add))(o2, t0, bank),
                    waits=[mm_last[o2]] +
                          ([self.xTc_tok[o2]] if self.xTc_tok[o2] else []),
                    tag=f"d6ev{t5}_{o2}")
                self.ppool.release(jj, ev)
                self.xTc_tok[o2] = ev
            hqT_free = mm_last[-1]
        k.pop_scope()

    def p7_output(self):
        k, p = self.k, self.k.p
        k.push_scope()
        oring = k.sb("oring", [128, 2 * D], F32)
        o_free = [None, None]
        last_dma = None
        for tt in range(NT):
            ri = tt % 2
            evs = []
            for c in range(NCH):
                j, bank, wb = self.ppool.acquire()
                tk = p.emit(TENSOR, (lambda tt, c, bank: lambda te: te.transpose(
                    out=bank[:, 0:128],
                    in_=self.xTc[c][:, tt * 128:(tt + 1) * 128],
                    identity=self.idn_f32[:, :]))(tt, c, bank),
                    waits=[self.idn_f32_tok, self.xTc_tok[c]] + wb +
                          ([o_free[ri]] if (c == 0 and o_free[ri]) else []),
                    tag="otp")
                ev = p.emit(VECTOR, (lambda c, ri, bank: lambda v: v.tensor_copy(
                    out=oring[:, ri * D + c * 128: ri * D + (c + 1) * 128],
                    in_=bank[:, 0:128]))(c, ri, bank),
                    waits=[tk], tag="oev")
                self.ppool.release(j, ev)
                evs.append(ev)
            dm = p.dma(SYNC, self.y_out[tt * 128:(tt + 1) * 128, :],
                       oring[:, ri * D:(ri + 1) * D], waits=evs, tag="odma")
            o_free[ri] = dm
            last_dma = dm
        p.emit(SYNC, lambda e: e.nop(), waits=[last_dma], inc=False, tag="fin")
        k.pop_scope()



def build_full():
    mk = MK()
    mk.p0_load()
    mk.close_p0()
    mk.p1_retention()
    mk.p2_stream_init()
    mk.p2_spill_xT()
    mk.k.pop_scope()
    mk.p2_layer(0)
    mk.p2_layer(1)
    mk.p2_apply()
    mk.p3_ln1()
    mk.p4_l1attn()
    mk.p6_prepare()
    mk.p6_xq()
    mk.p5_l1attn_apply()
    mk.p6_main()
    mk.k.pop_scope()
    mk.p7_output()
    return mk


def kernel(**inputs):
    x = np.ascontiguousarray(inputs["L0_attn_x"]).astype(np.float32, copy=False)
    mk = build_full()
    mk.k.p.install(mk.k.stack)
    base = {
        "l0a_wq": inputs["L0_attn_Wq"], "l0a_bq": inputs["L0_attn_bq"].reshape(1, -1),
        "l0a_wk": inputs["L0_attn_Wk"], "l0a_bk": inputs["L0_attn_bk"].reshape(1, -1),
        "l0a_wv": inputs["L0_attn_Wv"],
        "l1a_wq": inputs["L1_attn_Wq"], "l1a_bq": inputs["L1_attn_bq"].reshape(1, -1),
        "l1a_wk": inputs["L1_attn_Wk"], "l1a_bk": inputs["L1_attn_bk"].reshape(1, -1),
        "l1a_wv": inputs["L1_attn_Wv"], "l1a_bv": inputs["L1_attn_bv"].reshape(1, -1),
        "l1a_g": inputs["L1_attn_gamma"].reshape(1, -1),
        "l1a_b": inputs["L1_attn_beta"].reshape(1, -1),
        "l1f_gw": inputs["L1_ffn_gate_w"], "l1f_uw": inputs["L1_ffn_up_w"],
        "l1f_dw": inputs["L1_ffn_down_w"],
        "l1f_sin": inputs["L1_ffn_s_in"].reshape(1, 1),
        "l1f_sind": inputs["L1_ffn_s_in_down"].reshape(1, 1),
    }
    for l in range(2):
        base[f"f{l}_wqkv"] = inputs["L0_ffn_wqkv"][l]
        base[f"f{l}_qs"] = inputs["L0_ffn_q_scale"][l].reshape(1, -1)
        base[f"f{l}_ks"] = inputs["L0_ffn_k_scale"][l].reshape(1, -1)
        base[f"f{l}_wo"] = inputs["L0_ffn_wo"][l]
        base[f"f{l}_lng"] = inputs["L0_ffn_ln_g"][l].reshape(1, -1)
        base[f"f{l}_lnb"] = inputs["L0_ffn_ln_b"][l].reshape(1, -1)
        base[f"f{l}_w1"] = inputs["L0_ffn_w1"][l]
        base[f"f{l}_b1"] = inputs["L0_ffn_b1"][l].reshape(1, -1)
        base[f"f{l}_w2"] = inputs["L0_ffn_w2"][l]
        base[f"f{l}_b2"] = inputs["L0_ffn_b2"][l].reshape(1, -1)
    base = {kk: np.ascontiguousarray(np.asarray(v, dtype=np.float32))
            for kk, v in base.items()}
    in_maps = []
    for b in range(8):
        m = dict(base)
        m["x"] = np.ascontiguousarray(x[b])
        in_maps.append(m)
    res = run_bass_kernel_spmd(mk.k.nc, in_maps, core_ids=list(range(8)))
    out = np.stack([np.asarray(r["y"], dtype=np.float32) for r in res.results])
    try:
        mk.k.stack.close()
    except Exception:
        pass
    return out


# version 03
# speedup vs baseline: 1.0743x; 1.0743x over previous
import contextlib
import math
import numpy as np
import concourse.bass as bass
import concourse.mybir as mybir
from concourse.bass_utils import run_bass_kernel_spmd

F32 = mybir.dt.float32
BF16 = mybir.dt.bfloat16
AF = mybir.ActivationFunctionType
ALU = mybir.AluOpType
AX = mybir.AxisListType

TENSOR, VECTOR, SCALAR, GPSIMD, SYNC = "tensor", "vector", "scalar", "gpsimd", "sync"
COMPUTE_ENGINES = (TENSOR, VECTOR, SCALAR, GPSIMD)
DMA_QUEUES = (SYNC, GPSIMD, SCALAR)

S = 2048
D = 1024
NCH = D // 128
NT = S // 128
NT4 = S // 512
GAMMA = 0.96875
FH = 8
FDH = 64
FWIN = 512
FHID = 2730
FI = 4096
AH = 16
ADH = 64
LNG = math.log(GAMMA)
LN2 = math.log(2.0)
NEGB = -1.0e38



class Step:
    __slots__ = ("eng", "fn", "waits", "inc", "tok", "dma", "tag")

    def __init__(self, eng, fn, waits, inc, dma, tag):
        self.eng, self.fn, self.waits, self.inc, self.dma, self.tag = (
            eng, fn, list(waits), inc, dma, tag)
        self.tok = None


class Prog:
    def __init__(self, nc):
        self.nc = nc
        self.steps = []
        self.counts = {e: 0 for e in COMPUTE_ENGINES}
        self.dcounts = {q: 0 for q in DMA_QUEUES}

    pending_fence = None
    _fence_seen = None

    def _merge_fence(self, stream, waits):
        if self.pending_fence is None:
            return waits
        if stream in self._fence_seen:
            return waits
        self._fence_seen.add(stream)
        own = stream if stream in self.counts else None
        extra = [t for t in self.pending_fence
                 if t[0] != stream and t[0] != "dma_" + stream]
        if len(self._fence_seen) >= 5:
            self.pending_fence = None
        return list(waits) + extra

    def phase_fence(self):
        self.pending_fence = self.fence()
        self._fence_seen = set()

    def emit(self, eng, fn, waits=(), inc=True, tag=""):
        waits = self._merge_fence(eng, waits)
        s = Step(eng, fn, [t for t in waits if t], inc, False, tag)
        if inc:
            self.counts[eng] += 1
            s.tok = (eng, self.counts[eng])
        self.steps.append(s)
        return s.tok

    def dma(self, queue, out, in_, waits=(), inc=True, tag=""):
        def fn(eng):
            return eng.dma_start(out=out, in_=in_)
        return self.dma_fn(queue, fn, waits, inc, tag)

    def dma_fn(self, queue, fn, waits=(), inc=True, tag=""):
        waits = self._merge_fence(queue, waits)
        s = Step(queue, fn, [t for t in waits if t], inc, True, tag)
        if inc:
            self.dcounts[queue] += 16
            s.tok = ("dma_" + queue, self.dcounts[queue])
        self.steps.append(s)
        return s.tok

    def fence(self):
        toks = [(e, self.counts[e]) for e in COMPUTE_ENGINES if self.counts[e]]
        toks += [("dma_" + q, self.dcounts[q]) for q in DMA_QUEUES if self.dcounts[q]]
        return toks

    def check_deadlock(self):
        qs = {}
        for s in self.steps:
            qs.setdefault(s.eng, []).append(s)
        pos = {e: 0 for e in qs}
        counts = {}
        progress = True
        while progress:
            progress = False
            for e, lst in qs.items():
                while pos[e] < len(lst):
                    s = lst[pos[e]]
                    if all(counts.get(sm, 0) >= v for sm, v in s.waits):
                        if s.tok is not None:
                            counts[s.tok[0]] = s.tok[1]
                        pos[e] += 1
                        progress = True
                    else:
                        break
        bad = []
        for e, lst in qs.items():
            if pos[e] < len(lst):
                s = lst[pos[e]]
                bad.append(f"{e}[{pos[e]}/{len(lst)}] tag={s.tag} waits={s.waits}")
        if bad:
            raise RuntimeError("DEADLOCK:\n" + "\n".join(bad))

    def install(self, stack):
        self.check_deadlock()
        nc = self.nc
        sems = {}
        for e in COMPUTE_ENGINES:
            sems[e] = stack.enter_context(nc.semaphore("sem_" + e))
        for q in DMA_QUEUES:
            sems["dma_" + q] = stack.enter_context(nc.semaphore("sdma_" + q))
        block = stack.enter_context(nc.Block())
        bysec = {}
        for s in self.steps:
            bysec.setdefault(s.eng, []).append(s)

        def emitter(name):
            def run(eng):
                for s in bysec.get(name, []):
                    for sm, val in s.waits:
                        eng.wait_ge(sems[sm], val)
                    inst = s.fn(eng)
                    if s.tok is not None:
                        inst.then_inc(sems[s.tok[0]], 16 if s.dma else 1)
            return run

        block.tensor(emitter(TENSOR))
        block.vector(emitter(VECTOR))
        block.scalar(emitter(SCALAR))
        block.gpsimd(emitter(GPSIMD))
        block.sync(emitter(SYNC))


class Pool:

    def __init__(self, items):
        self.items = list(items)
        self.free_tok = [None] * len(self.items)
        self.i = 0

    def acquire(self):
        j = self.i % len(self.items)
        self.i += 1
        tok = self.free_tok[j]
        return j, self.items[j], ([tok] if tok else [])

    def release(self, j, tok):
        self.free_tok[j] = tok


class K:

    def __init__(self):
        self.nc = bass.Bass("TRN2", target_bir_lowering=False, debug=False, detect_race_conditions=False)
        self.stack = contextlib.ExitStack()
        self.scopes = [self.stack]
        self.p = Prog(self.nc)
        self._uid = 0
        self.dbg = {}

    def uid(self):
        self._uid += 1
        return self._uid

    def push_scope(self):
        sc = contextlib.ExitStack()
        self.scopes.append(sc)
        return sc

    def pop_scope(self):
        sc = self.scopes.pop()
        sc.close()
        self.p.phase_fence()

    def sb(self, name, shape, dt):
        import os
        h = self.scopes[-1].enter_context(
            self.nc.sbuf_tensor(f"{name}{self.uid()}", list(shape), dt))
        if os.environ.get("SBUF_LOG"):
            import numpy as _np
            nb = int(_np.prod(shape[1:])) * (2 if dt == BF16 else 4)
            print(f"SBUF {name} {shape} {nb}B rem={self.nc.sbuf_bytes_remaining}")
        return h

    _consts = None

    def c(self, value):
        if self._consts is None:
            self._consts = {}
        if value not in self._consts:
            t = self.scopes[0].enter_context(
                self.nc.sbuf_tensor(f"cst{self.uid()}", [128, 1], F32))
            tok = self.p.emit(GPSIMD, (lambda t, value: lambda g: g.memset(
                t[:, :], float(value)))(t, value), tag="const")
            self._consts[value] = (t[:, 0:1], tok)
        return self._consts[value]

    def din(self, name, shape, dt=F32):
        return self.nc.dram_tensor(name, list(shape), dt, kind="ExternalInput").ap()

    def dout(self, name, shape, dt=F32):
        return self.nc.dram_tensor(name, list(shape), dt, kind="ExternalOutput").ap()

    def dram(self, name, shape, dt):
        return self.nc.dram_tensor(f"{name}{self.uid()}", list(shape), dt,
                                   kind="Internal").ap()

    def dbg_dump(self, name, aps, dt=F32):
        total = sum(a.shape[1] for a in aps)
        d = self.dout("dbg_" + name, [128, total], dt)
        toks = self.p.fence()
        off = 0
        dts = []
        for a in aps:
            dts.append(self.p.dma(SYNC, d[0:a.shape[0], off:off + a.shape[1]], a,
                                  waits=toks, tag="dbg"))
            off += a.shape[1]
        self.p.emit(SYNC, lambda e: e.nop(), waits=[dts[-1]], inc=False, tag="dbgw")
        self.dbg[name] = d



def affine_mask(k, t, ncols, compare_op, base, cmul, fill, set_to=1.0, tag="am",
                piece=512, step=1):
    p = k.p
    def fn(g):
        last = None
        for c0 in range(0, ncols, piece):
            wdt = min(piece, ncols - c0)
            g.memset(t[:, c0:c0 + wdt], float(set_to))
            last = g.affine_select(out=t[:, c0:c0 + wdt], in_=t[:, c0:c0 + wdt],
                                   pattern=[[step, wdt]], compare_op=compare_op,
                                   fill=float(fill), base=int(base + step * c0),
                                   channel_multiplier=int(cmul))
        return last
    return p.emit(GPSIMD, fn, tag=tag)


def build_identity(k, dt, tag):
    idn = k.sb("idn" + tag, [128, 128], dt)
    def fn(g):
        g.memset(idn[:, :], 1.0)
        return g.affine_select(out=idn[:, :], in_=idn[:, :], pattern=[[-1, 128]],
                               compare_op=ALU.is_equal, fill=0.0, base=0,
                               channel_multiplier=1)
    tok = k.p.emit(GPSIMD, fn, tag="idn")
    return idn, tok


def bcast_row(k, row_ap, out_ap, psum_pool, waits, tag, dtype_bf=False):
    p = k.p
    if not hasattr(k, "_ones_row"):
        k._ones_row = k.scopes[0].enter_context(
            k.nc.sbuf_tensor(f"onesr{k.uid()}", [1, 128], F32))
        k._ones_row_tok = p.emit(GPSIMD, lambda g: g.memset(k._ones_row[:, :], 1.0),
                                 tag="onesr")
    n = out_ap.shape[1]
    last = None
    for t0 in range(0, n, 512):
        wdt = min(512, n - t0)
        j, bank, wb = psum_pool.acquire()
        mm = p.emit(TENSOR, (lambda bank, t0, wdt: lambda te: te.matmul(
            out=bank[:, 0:wdt], lhsT=k._ones_row[0:1, :],
            rhs=row_ap[0:1, t0:t0 + wdt], start=True, stop=True))(bank, t0, wdt),
            waits=list(waits) + wb + [k._ones_row_tok], tag=tag + "_mm")
        ev = p.emit(VECTOR, (lambda bank, t0, wdt: lambda v: v.tensor_copy(
            out=out_ap[:, t0:t0 + wdt], in_=bank[:, 0:wdt]))(bank, t0, wdt),
            waits=[mm], tag=tag + "_ev")
        psum_pool.release(j, ev)
        last = ev
    return last


def load_bias_cols(k, hbm_row, n=D, waits=(), tag="bias"):
    ncol = n // 128
    t = k.sb(tag, [128, ncol], F32)
    def fn(eng):
        return eng.dma_start(out=t[:, :],
                             in_=hbm_row[0:1, :].rearrange("o (c q) -> (o q) c", q=128),
                             allow_slow_non_contiguous=True)
    tok = k.p.dma_fn(SYNC, fn, waits, True, tag)
    return t, tok


class LNStats:

    def __init__(self, rho_bc, om_bc, toks, mu_row=None, r_row=None):
        self.rho_bc, self.om_bc, self.toks = rho_bc, om_bc, toks
        self.mu_row, self.r_row = mu_row, r_row


def ln_stats(k, chunk_aps, chunk_toks, ones1, ones1_tok, psum_pool, eps=1e-5,
             tag="ln", out=None, upto=99, scratch=None):
    p = k.p
    nc = NCH
    if scratch is not None:
        sq_ring, sum_ps, sq_ps, om_row = scratch
    else:
        sum_ps = k.sb(tag + "_sum", [1, S], F32)
        sq_ps = k.sb(tag + "_sq", [1, S], F32)
        sq_ring = k.sb(tag + "_sqr", [128, 2 * 512], F32)
    sq_ring_toks = [None, None]
    ones_c, ones_c_tok = ones1, ones1_tok
    if chunk_aps[0].dtype == BF16:
        ones_c, ones_c_tok = k._ones1b, k._ones1b_tok
    ev_toks = []
    for ts in range(NT4):
        j, bank, wb = psum_pool.acquire()
        mm = None
        for c in range(nc):
            mm = p.emit(TENSOR, (lambda bank, c, ts, first, last: lambda te: te.matmul(
                out=bank[0:1, :], lhsT=ones_c[:, 0:1],
                rhs=chunk_aps[c][:, ts * 512:(ts + 1) * 512],
                start=first, stop=last))(bank, c, ts, c == 0, c == nc - 1),
                waits=([ones_c_tok] + wb + list(chunk_toks) if c == 0 else []),
                tag=f"{tag}_sum{ts}")
        ev = p.emit(VECTOR, (lambda bank, ts: lambda v: v.tensor_copy(
            out=sum_ps[0:1, ts * 512:(ts + 1) * 512], in_=bank[0:1, :]))(bank, ts),
            waits=[mm], tag=f"{tag}_sumev{ts}")
        ev_toks.append(ev)
        psum_pool.release(j, ev)
        import os
        if os.environ.get("SKIPSQ"):
            continue
        j2, bank2, wb2 = psum_pool.acquire()
        mm2 = None
        for c in range(nc):
            ri = c % 2
            sqt = p.emit(SCALAR, (lambda c, ts, ri: lambda s: s.activation(
                out=sq_ring[:, ri * 512:(ri + 1) * 512],
                in_=chunk_aps[c][:, ts * 512:(ts + 1) * 512], func=AF.Square))(c, ts, ri),
                waits=(list(chunk_toks) if (ts == 0 and c < 2) else []) +
                      ([sq_ring_toks[ri]] if sq_ring_toks[ri] else []),
                tag=f"{tag}_sq{ts}_{c}")
            mm2 = p.emit(TENSOR, (lambda bank2, ri, first, last: lambda te: te.matmul(
                out=bank2[0:1, :], lhsT=ones1[:, 0:1],
                rhs=sq_ring[:, ri * 512:(ri + 1) * 512],
                start=first, stop=last))(bank2, ri, c == 0, c == nc - 1),
                waits=[sqt] + (wb2 + [ones1_tok] if c == 0 else []),
                tag=f"{tag}_sqmm{ts}")
            sq_ring_toks[ri] = mm2
        ev2 = p.emit(VECTOR, (lambda bank2, ts: lambda v: v.tensor_copy(
            out=sq_ps[0:1, ts * 512:(ts + 1) * 512], in_=bank2[0:1, :]))(bank2, ts),
            waits=[mm2], tag=f"{tag}_sqev{ts}")
        ev_toks.append(ev2)
        psum_pool.release(j2, ev2)
    mu = sum_ps
    r = sq_ps
    om = om_row if scratch is not None else k.sb(tag + "_om", [1, S], F32)
    res0 = LNStats(None, None, [], mu, r)
    res0.sum_ps, res0.sq_ps, res0.om = sum_ps, sq_ps, om
    if upto <= 0:
        return res0
    t1 = p.emit(VECTOR, lambda v: v.tensor_scalar(
        out=mu[0:1, :], in0=sum_ps[0:1, :], scalar1=1.0 / D, scalar2=None,
        op0=ALU.mult), waits=ev_toks, tag=tag + "_mu")
    t2 = p.emit(VECTOR, lambda v: v.tensor_tensor(
        out=om[0:1, :], in0=mu[0:1, :], in1=mu[0:1, :], op=ALU.mult),
        waits=[t1], tag=tag + "_mu2")
    t3 = p.emit(VECTOR, lambda v: v.scalar_tensor_tensor(
        out=r[0:1, :], in0=sq_ps[0:1, :], scalar=1.0 / D, in1=om[0:1, :],
        op0=ALU.mult, op1=ALU.subtract), waits=[t2], tag=tag + "_var")
    if upto <= 1:
        return res0
    epsc, epstok = k.c(float(eps))
    t4 = p.emit(SCALAR, lambda s: s.activation(
        out=r[0:1, :], in_=r[0:1, :], func=AF.Sqrt, bias=epsc[0:1, :]),
        waits=[t3, epstok], tag=tag + "_sqrt")
    t5 = p.emit(VECTOR, lambda v: v.reciprocal(out=r[0:1, :], in_=r[0:1, :]),
                waits=[t4], tag=tag + "_recip")
    t6 = p.emit(VECTOR, lambda v: v.scalar_tensor_tensor(
        out=om[0:1, :], in0=mu[0:1, :], scalar=-1.0, in1=r[0:1, :],
        op0=ALU.mult, op1=ALU.mult), waits=[t5, t1], tag=tag + "_om")
    if out is not None:
        rho_bc, om_bc = out
    else:
        rho_bc = k.sb(tag + "_rho_bc", [128, S], F32)
        om_bc = k.sb(tag + "_om_bc", [128, S], F32)
    if upto <= 2:
        res0.toks = [t6]
        return res0
    t7 = bcast_row(k, r[0:1, :], rho_bc[:, :], psum_pool, [t5], tag + "_bcr")
    t8 = bcast_row(k, om[0:1, :], om_bc[:, :], psum_pool, [t6], tag + "_bco")
    res = LNStats(rho_bc, om_bc, [t7, t8], mu, r)
    res.sum_ps, res.sq_ps, res.om = sum_ps, sq_ps, om
    return res



class MK:
    def __init__(self):
        self.k = K()
        k, p = self.k, self.k.p
        self.x_in = k.din("x", [S, D])
        self.y_out = k.dout("y", [S, D])
        self.win = {}
        self.banks = [k.stack.enter_context(
            k.nc.psum_tensor(f"bank{i}", [128, 512], F32)) for i in range(8)]
        self.ppool = Pool(self.banks)
        self.idn_f32, self.idn_f32_tok = build_identity(k, F32, "f")
        self.idn_bf, self.idn_bf_tok = build_identity(k, BF16, "b")
        self.ones1 = k.sb("ones1", [128, 1], F32)
        self.ones1_tok = p.emit(GPSIMD, lambda g: g.memset(self.ones1[:, :], 1.0),
                                tag="ones1")
        self.ones1b = k.sb("ones1b", [128, 1], BF16)
        self.ones1b_tok = p.emit(VECTOR, lambda v: v.memset(self.ones1b[:, :], 1.0),
                                 tag="ones1b")
        k._ones1b, k._ones1b_tok = self.ones1b, self.ones1b_tok
        for v in (1e-5, 1e-8, 0.5, -16.0, 1.0, -1.0, 25.6 / 89.6,
                  3.0 * LN2, -3.0 * LN2):
            k.c(v)
        k._ones_row = k.scopes[0].enter_context(
            k.nc.sbuf_tensor(f"onesr{k.uid()}", [1, 128], F32))
        k._ones_row_tok = p.emit(GPSIMD, lambda g: g.memset(k._ones_row[:, :], 1.0),
                                 tag="onesr")
        self.xT = k.sb("xT", [128, NCH * S], F32)
        self.xTc = [self.xT[:, c * S:(c + 1) * S] for c in range(NCH)]
        self.xTc_tok = [None] * NCH
        self.SW = S + 1
        self.strm = k.sb("strm", [128, NCH * self.SW], F32)
        self.strmc = [self.strm[:, c * self.SW:(c + 1) * self.SW] for c in range(NCH)]
        self.strm_tok = [None] * NCH

    def din(self, name, shape):
        ap = self.k.din(name, shape)
        self.win[name] = ap
        return ap

    def p0_load(self, do_ln=True, upto=99):
        k, p = self.k, self.k.p
        k.push_scope()
        rho_bc = k.sb("ln0_rho_bc", [128, S], BF16)
        om_bc = k.sb("ln0_om_bc", [128, S], BF16)
        k.push_scope()
        x_tok = self.strm[:, 0:NT * D]
        ld = []
        for tt in range(NT):
            ld.append(p.dma(SYNC, x_tok[:, tt * D:(tt + 1) * D],
                            self.x_in[tt * 128:(tt + 1) * 128, :], tag="ldx"))
        ch_done = [[] for _ in range(NCH)]
        for tt in range(NT):
            for c in range(NCH):
                j, bank, wb = self.ppool.acquire()
                src = x_tok[:, tt * D + c * 128: tt * D + (c + 1) * 128]
                tk = p.emit(TENSOR, (lambda src, bank: lambda te: te.transpose(
                    out=bank[:, 0:128], in_=src, identity=self.idn_f32[:, :]))(src, bank),
                    waits=[ld[tt], self.idn_f32_tok] + wb, tag=f"tpx{tt}_{c}")
                dst = self.xTc[c][:, tt * 128:(tt + 1) * 128]
                tv = p.emit(VECTOR, (lambda bank, dst: lambda v: v.tensor_copy(
                    out=dst, in_=bank[:, 0:128]))(bank, dst),
                    waits=[tk], tag=f"tpc{tt}_{c}")
                ch_done[c].append(tv)
                self.ppool.release(j, tv)
        self.x_ld_toks = [t for lst in ch_done for t in lst]
        for c in range(NCH):
            self.xTc_tok[c] = ch_done[c][-1]
        if not do_ln:
            self.ln0 = None
            return
        self.ln0 = ln_stats(k, self.xTc, self.x_ld_toks, self.ones1,
                            self.ones1_tok, self.ppool, tag="ln0",
                            out=(rho_bc, om_bc), upto=upto)

    def close_p0(self):
        self.k.pop_scope()

    def xn_tile(self, ring_f32, ring_bf, c, a0, wdt, waits, ln=None):
        k, p = self.k, self.k.p
        ln = ln or self.ln0
        t1 = p.emit(VECTOR, lambda v: v.tensor_tensor(
            out=ring_f32[:, 0:wdt], in0=self.xTc[c][:, a0:a0 + wdt],
            in1=ln.rho_bc[:, a0:a0 + wdt], op=ALU.mult),
            waits=list(waits) + ln.toks + ([self.xTc_tok[c]] if self.xTc_tok[c] else []),
            tag="xnt1")
        t2 = p.emit(VECTOR, lambda v: v.tensor_tensor(
            out=ring_bf[:, 0:wdt], in0=ring_f32[:, 0:wdt],
            in1=ln.om_bc[:, a0:a0 + wdt], op=ALU.add), waits=[t1], tag="xnt2")
        return t2

    def p1_retention(self):
        k, p = self.k, self.k.p
        import os as _os
        wq = k.din("l0a_wq", [D, D]); bq = k.din("l0a_bq", [1, D])
        wk = k.din("l0a_wk", [D, D]); bk = k.din("l0a_bk", [1, D])
        wv = k.din("l0a_wv", [D, D])
        self.attn0 = k.dram("attn0", [128, NCH * S], BF16)

        k.push_scope()
        if not _os.environ.get("SKIP_BIAS"):
            bq_col, bq_tok = load_bias_cols(k, bq, D, tag="bqc")
            bk_col, bk_tok = load_bias_cols(k, bk, D, tag="bkc")
        else:
            bq_col = k.sb("bqc", [128, 8], F32); bq_tok = None
            bk_col = k.sb("bkc", [128, 8], F32); bk_tok = None
        gi_bc = k.sb("gi_bc", [128, S], BF16)
        gj_bc = k.sb("gj_bc", [128, S], BF16)
        k.push_scope()
        ti = k.sb("ti", [1, S], F32)
        t_iota = p.emit(GPSIMD, lambda g: g.iota(
            out=ti[0:1, :], pattern=[[1, S]], base=0, channel_multiplier=0,
            allow_small_or_imprecise_dtypes=True), tag="ti")
        gi_row = k.sb("gi_row", [1, S], F32)
        gj_row = k.sb("gj_row", [1, S], F32)
        tg1 = p.emit(SCALAR, lambda s: s.activation(
            out=gi_row[0:1, :], in_=ti[0:1, :], func=AF.Exp, scale=LNG),
            waits=[t_iota], tag="gi")
        tg2 = p.emit(SCALAR, lambda s: s.activation(
            out=gj_row[0:1, :], in_=ti[0:1, :], func=AF.Exp, scale=-LNG),
            waits=[t_iota], tag="gj")
        if not _os.environ.get("SKIP_DECAYBC"):
            tg3 = bcast_row(k, gi_row[0:1, :], gi_bc[:, :], self.ppool, [tg1], "gibc")
            tg4 = bcast_row(k, gj_row[0:1, :], gj_bc[:, :], self.ppool, [tg2], "gjbc")
        else:
            tg3 = tg4 = tg2
        k.pop_scope()
        b01 = k.sb("b01", [128, 1024], F32)
        tb01 = affine_mask(k, b01, 1024, ALU.is_ge, -512, -1, 0.0, tag="b01")

        bcv = self.strm[:, :].bitcast(BF16)
        kT = bcv[:, 0:NCH * S]
        v_tok = bcv[:, NCH * S:2 * NCH * S]
        mir_f = k.sb("mir_f", [128, 512], F32)
        mir = k.sb("mir", [128, 8 * 512], BF16)
        wstg = k.sb("wstg", [128, D], F32)
        wbf = k.sb("wbf", [128, D], BF16)
        sT = k.sb("sT", [128, 16 * 512], BF16)
        q_slab = k.sb("q_slab", [128, 8 * 512], BF16)
        out_ring = k.sb("out_ring", [128, 2 * 512], BF16)
        out_free = [None] * 2
        self._mir_free = [None] * 8
        self._wstg_free = None

        def prep_wT(hbm, dst, dst_toks, tagp):
            for o in range(NCH):
                ld = p.dma(SYNC, wstg[:, :], hbm[o * 128:(o + 1) * 128, :],
                           waits=([self._wstg_free] if self._wstg_free else []),
                           tag=tagp + "ld")
                ct = p.emit(SCALAR, lambda s: s.copy(out=wbf[:, :], in_=wstg[:, :]),
                            waits=[ld], tag=tagp + "cast")
                last_ev = None
                for h in range(NCH):
                    j, bank, wb = self.ppool.acquire()
                    tt = p.emit(TENSOR, (lambda h, bank: lambda te: te.transpose(
                        out=bank.bitcast(BF16)[:, 0:128],
                        in_=wbf[:, h * 128:(h + 1) * 128],
                        identity=self.idn_bf[:, :]))(h, bank),
                        waits=[ct, self.idn_bf_tok] + wb, tag=tagp + "tp")
                    ev = p.emit(VECTOR, (lambda h, o, bank: lambda v: v.tensor_copy(
                        out=dst[:, h * D + o * 128: h * D + (o + 1) * 128],
                        in_=bank.bitcast(BF16)[:, 0:128]))(h, o, bank),
                        waits=[tt], tag=tagp + "ev")
                    self.ppool.release(j, ev)
                    dst_toks[h][o] = ev
                    last_ev = tt
                self._wstg_free = last_ev

        def build_mirror(a0):
            toks = []
            for h in range(NCH):
                w = [self._mir_free[h]] if self._mir_free[h] else []
                t2 = self.xn_tile(mir_f, mir[:, h * 512:(h + 1) * 512], h, a0,
                                  512, w)
                toks.append(t2)
            return toks

        kT_tok = [[None] * NT4 for _ in range(NCH)]
        v_tok_tok = [[None, None] for _ in range(NT)]
        import os
        _phases = os.environ.get("RET_PHASES", "kvq")
        _stop = os.environ.get("RET_STOP", "")

        k.push_scope()
        wkT = k.sb("wkT", [128, NCH * D], BF16)
        wkT_toks = [[None] * NCH for _ in range(NCH)]
        if _stop == "early1":
            self._ret_dbg = dict(kT=kT, v_tok=v_tok, sT=sT, q_slab=q_slab,
                                 gi_bc=gi_bc, gj_bc=gj_bc, mir=mir, b01=b01, wkT=wkT)
            self._p1_done = p.fence()
            return
        prep_wT(wk, wkT, wkT_toks, "wk")
        if _stop == "early2":
            self._ret_dbg = dict(kT=kT, v_tok=v_tok, sT=sT, q_slab=q_slab,
                                 gi_bc=gi_bc, gj_bc=gj_bc, mir=mir, b01=b01, wkT=wkT)
            self._p1_done = p.fence()
            return
        for ts in range(NT4):
            a0 = ts * 512
            mtoks = build_mirror(a0)
            last_mm = None
            for o in range(NCH):
                j, bank, wb = self.ppool.acquire()
                mm = None
                for h in range(NCH):
                    mm = p.emit(TENSOR, (lambda o, h, bank, first, last:
                        lambda te: te.matmul(
                            out=bank[:, :],
                            lhsT=wkT[:, h * D + o * 128: h * D + (o + 1) * 128],
                            rhs=mir[:, h * 512:(h + 1) * 512],
                            start=first, stop=last))(o, h, bank, h == 0, h == NCH - 1),
                        waits=(wb + mtoks + [wkT_toks[hh][o] for hh in range(NCH)]
                               if h == 0 else []),
                        tag=f"kmm{ts}_{o}")
                last_mm = mm
                ev = p.emit(VECTOR, (lambda o, a0, bank: lambda v:
                    v.scalar_tensor_tensor(
                        out=kT[:, o * S + a0: o * S + a0 + 512],
                        in0=bank[:, :], scalar=bk_col[:, o:o + 1],
                        in1=gj_bc[:, a0:a0 + 512],
                        op0=ALU.add, op1=ALU.mult))(o, a0, bank),
                    waits=[mm, bk_tok, tg4], tag=f"kev{ts}_{o}")
                self.ppool.release(j, ev)
                kT_tok[o][ts] = ev
            for h in range(NCH):
                self._mir_free[h] = last_mm
        k.pop_scope()

        if "v" not in _phases:
            self._ret_dbg = dict(kT=kT, v_tok=v_tok, sT=sT, q_slab=q_slab,
                                 gi_bc=gi_bc, gj_bc=gj_bc, mir=mir, b01=b01)
            self._p1_done = p.fence()
            return
        k.push_scope()
        wvT = k.sb("wvT", [128, NCH * D], BF16)
        wvT_toks = [[None] * NCH for _ in range(NCH)]
        prep_wT(wv, wvT, wvT_toks, "wv")
        for ts in range(NT4):
            a0 = ts * 512
            mtoks = build_mirror(a0)
            last_mm = None
            for q4 in range(4):
                tt_idx = ts * 4 + q4
                for oh in range(2):
                    j, bank, wb = self.ppool.acquire()
                    mm = None
                    for h in range(NCH):
                        mm = p.emit(TENSOR, (lambda h, q4, oh, bank, first, last:
                            lambda te: te.matmul(
                                out=bank[:, :],
                                lhsT=mir[:, h * 512 + q4 * 128: h * 512 + (q4 + 1) * 128],
                                rhs=wvT[:, h * D + oh * 512: h * D + (oh + 1) * 512],
                                start=first, stop=last))(h, q4, oh, bank, h == 0,
                                                         h == NCH - 1),
                            waits=(wb + mtoks +
                                   [wvT_toks[hh][o2] for hh in range(NCH)
                                    for o2 in range(oh * 4, oh * 4 + 4)]
                                   if h == 0 else []),
                            tag=f"vmm{tt_idx}_{oh}")
                    last_mm = mm
                    ev = p.emit(SCALAR, (lambda tt_idx, oh, bank: lambda s: s.copy(
                        out=v_tok[:, tt_idx * D + oh * 512: tt_idx * D + (oh + 1) * 512],
                        in_=bank[:, :]))(tt_idx, oh, bank),
                        waits=[mm], tag=f"vev{tt_idx}_{oh}")
                    self.ppool.release(j, ev)
                    v_tok_tok[tt_idx][oh] = ev
            for h in range(NCH):
                self._mir_free[h] = last_mm
        k.pop_scope()

        if "q" not in _phases:
            self._ret_dbg = dict(kT=kT, v_tok=v_tok, sT=sT, q_slab=q_slab,
                                 gi_bc=gi_bc, gj_bc=gj_bc, mir=mir, b01=b01)
            self._p1_done = p.fence()
            return
        k.push_scope()
        wqT = k.sb("wqT", [128, NCH * D], BF16)
        wqT_toks = [[None] * NCH for _ in range(NCH)]
        prep_wT(wq, wqT, wqT_toks, "wq")
        q_free = [None] * NCH
        sT_free = None
        for T in range(NT4):
            i0 = T * 512
            mtoks = build_mirror(i0)
            q_toks = []
            last_q_mm = None
            for o in range(NCH):
                j, bank, wb = self.ppool.acquire()
                mm = None
                for h in range(NCH):
                    mm = p.emit(TENSOR, (lambda o, h, bank, first, last:
                        lambda te: te.matmul(
                            out=bank[:, :],
                            lhsT=wqT[:, h * D + o * 128: h * D + (o + 1) * 128],
                            rhs=mir[:, h * 512:(h + 1) * 512],
                            start=first, stop=last))(o, h, bank, h == 0, h == NCH - 1),
                        waits=(wb + mtoks + [wqT_toks[hh][o] for hh in range(NCH)]
                               + ([q_free[o]] if q_free[o] else [])
                               if h == 0 else []),
                        tag=f"qmm{T}_{o}")
                last_q_mm = mm
                ev = p.emit(VECTOR, (lambda o, i0, bank: lambda v:
                    v.scalar_tensor_tensor(
                        out=q_slab[:, o * 512:(o + 1) * 512],
                        in0=bank[:, :], scalar=bq_col[:, o:o + 1],
                        in1=gi_bc[:, i0:i0 + 512],
                        op0=ALU.add, op1=ALU.mult))(o, i0, bank),
                    waits=[mm, bq_tok, tg3], tag=f"qev{T}_{o}")
                self.ppool.release(j, ev)
                q_toks.append(ev)
            for h in range(NCH):
                self._mir_free[h] = last_q_mm
            nj = 4 * T + 4
            s_toks = []
            for c in range(nj):
                ts_of_c = c // 4
                j, bank, wb = self.ppool.acquire()
                mm = None
                for o in range(NCH):
                    mm = p.emit(TENSOR, (lambda c, o, bank, first, last:
                        lambda te: te.matmul(
                            out=bank[:, :],
                            lhsT=kT[:, o * S + c * 128: o * S + (c + 1) * 128],
                            rhs=q_slab[:, o * 512:(o + 1) * 512],
                            start=first, stop=last))(c, o, bank, o == 0, o == NCH - 1),
                        waits=(wb + q_toks + [kT_tok[oo][ts_of_c] for oo in range(NCH)]
                               + ([sT_free] if sT_free else [])
                               if o == 0 else []),
                        tag=f"smm{T}_{c}")
                m = c - 4 * T
                if m >= 0:
                    ev = p.emit(VECTOR, (lambda c, m, bank: lambda v: v.tensor_tensor(
                        out=sT[:, c * 512:(c + 1) * 512], in0=bank[:, :],
                        in1=b01[:, 512 - 128 * m: 1024 - 128 * m], op=ALU.mult))(c, m, bank),
                        waits=[mm, tb01], tag=f"sev{T}_{c}")
                else:
                    ev = p.emit(SCALAR, (lambda c, bank: lambda s: s.copy(
                        out=sT[:, c * 512:(c + 1) * 512], in_=bank[:, :]))(c, bank),
                        waits=[mm], tag=f"sev{T}_{c}")
                self.ppool.release(j, ev)
                s_toks.append(ev)
            last_av = None
            for o in range(NCH):
                j, bank, wb = self.ppool.acquire()
                ri = o % 2
                mm = None
                for c in range(nj):
                    mm = p.emit(TENSOR, (lambda o, c, bank, first, last:
                        lambda te: te.matmul(
                            out=bank[:, :],
                            lhsT=v_tok[:, c * D + o * 128: c * D + o * 128 + 128],
                            rhs=sT[:, c * 512:(c + 1) * 512],
                            start=first, stop=last))(o, c, bank, c == 0, c == nj - 1),
                        waits=(wb + s_toks +
                               [v_tok_tok[c2][ohh] for c2 in range(nj) for ohh in range(2)]
                               if c == 0 else []),
                        tag=f"avmm{T}_{o}")
                last_av = mm
                w_or = [out_free[ri]] if out_free[ri] else []
                ev = p.emit(VECTOR, (lambda ri, bank: lambda v: v.tensor_copy(
                    out=out_ring[:, ri * 512:(ri + 1) * 512], in_=bank[:, :]))(ri, bank),
                    waits=[mm] + w_or, tag=f"avev{T}_{o}")
                self.ppool.release(j, ev)
                dm = p.dma(SYNC, self.attn0[:, o * S + i0: o * S + i0 + 512],
                           out_ring[:, ri * 512:(ri + 1) * 512], waits=[ev],
                           tag=f"avdma{T}_{o}")
                out_free[ri] = dm
            sT_free = last_av
            for o in range(NCH):
                q_free[o] = last_av
        if os.environ.get("KEEP_RET"):
            self._ret_dbg = dict(kT=kT, v_tok=v_tok, sT=sT, q_slab=q_slab,
                                 gi_bc=gi_bc, gj_bc=gj_bc, mir=mir, wqT=wqT, b01=b01)
            self._p1_done = p.fence()
            return
        k.pop_scope()
        self._p1_done = p.fence()
        k.pop_scope()

    def p1_apply_attn(self, extra_waits=()):
        k, p = self.k, self.k.p
        k.push_scope()
        ld_ring = k.sb("a0ld", [128, 2 * S], BF16)
        free_t = [None, None]
        for c in range(NCH):
            ri = c % 2
            ld = p.dma(SYNC, ld_ring[:, ri * S:(ri + 1) * S],
                       self.attn0[:, c * S:(c + 1) * S],
                       waits=list(extra_waits) + ([free_t[ri]] if free_t[ri] else []),
                       tag="a0ld")
            ad = p.emit(VECTOR, (lambda c, ri: lambda v: v.tensor_tensor(
                out=self.xTc[c], in0=self.xTc[c],
                in1=ld_ring[:, ri * S:(ri + 1) * S], op=ALU.add))(c, ri),
                waits=[ld] + ([self.xTc_tok[c]] if self.xTc_tok[c] else []),
                tag="a0add")
            free_t[ri] = ad
            self.xTc_tok[c] = ad
        k.pop_scope()

    SW = S + 1

    def p2_stream_init(self, waits=()):
        k, p = self.k, self.k.p
        k.push_scope()
        scr = k.sb("si_scr", [128, 512], F32)
        _w0 = list(waits)
        for c in range(NCH):
            zt = p.emit(VECTOR, (lambda c: lambda v: v.memset(
                self.strmc[c][:, 0:1], 0.0))(c), waits=_w0, tag="spad")
            last = zt
            for ts in range(NT4):
                a0 = ts * 512
                t1 = p.emit(VECTOR, (lambda c, a0: lambda v: v.tensor_tensor(
                    out=scr[:, :], in0=self.xTc[c][:, a0:a0 + 512],
                    in1=self.ln0.rho_bc[:, a0:a0 + 512], op=ALU.mult))(c, a0),
                    waits=self.ln0.toks + [self.xTc_tok[c]], tag="si1")
                t2 = p.emit(VECTOR, (lambda c, a0: lambda v: v.tensor_tensor(
                    out=self.strmc[c][:, a0 + 1:a0 + 513], in0=scr[:, :],
                    in1=self.ln0.om_bc[:, a0:a0 + 512], op=ALU.add))(c, a0),
                    waits=[t1], tag="si2")
                last = t2
            self.strm_tok[c] = last
        k.pop_scope()

    def p2_spill_xT(self):
        k, p = self.k, self.k.p
        self.xT_spill = k.dram("xTsp", [128, NCH * S], F32)
        self.xT_spill_toks = []
        for c in range(NCH):
            t = p.dma(SYNC, self.xT_spill[:, c * S:(c + 1) * S], self.xTc[c],
                      waits=[self.strm_tok[c]] + ([self.xTc_tok[c]] if self.xTc_tok[c] else []),
                      tag="spill")
            self.xT_spill_toks.append(t)

    def p2_reload_xT(self, waits=()):
        k, p = self.k, self.k.p
        for c in range(NCH):
            t = p.dma(SYNC, self.xTc[c], self.xT_spill[:, c * S:(c + 1) * S],
                      waits=list(waits) + [self.xT_spill_toks[c]], tag="unspill")
            self.xTc_tok[c] = t

    def p2_tables_global(self):
        k, p = self.k, self.k.p
        self.csq = k.sb("csq", [128, NT * 64], F32)
        self.ssq = k.sb("ssq", [128, NT * 64], F32)
        self.csk = k.sb("csk", [128, NT * 64], F32)
        self.ssk = k.sb("ssk", [128, NT * 64], F32)
        k.push_scope()
        invf_r = k.sb("invf", [1, 64], F32)
        lbase_r = k.sb("lbase", [1, 64], F32)
        t1 = p.emit(GPSIMD, lambda g: g.iota(
            out=invf_r[0:1, :], pattern=[[0, 2], [2, 32]], base=0,
            channel_multiplier=0, allow_small_or_imprecise_dtypes=True), tag="ivf0")
        t2 = p.emit(SCALAR, lambda s: s.activation(
            out=lbase_r[0:1, :], in_=invf_r[0:1, :], func=AF.Identity,
            scale=1.0 / 89.6, bias=k.c(25.6 / 89.6)[0][0:1, :]),
            waits=[t1, k.c(25.6 / 89.6)[1]], tag="base")
        t3 = p.emit(SCALAR, lambda s: s.activation(
            out=lbase_r[0:1, :], in_=lbase_r[0:1, :], func=AF.Ln), waits=[t2],
            tag="lbase")
        t4 = p.emit(SCALAR, lambda s: s.activation(
            out=invf_r[0:1, :], in_=invf_r[0:1, :], func=AF.Exp,
            scale=-math.log(10000.0) / 64.0), waits=[t1, t2], tag="invf")
        invf_bc = k.sb("invfbc", [128, 64], F32)
        lbase_bc = k.sb("lbasebc", [128, 64], F32)
        t5 = bcast_row(k, invf_r[0:1, :], invf_bc[:, :], self.ppool, [t4], "ivbc")
        t6 = bcast_row(k, lbase_r[0:1, :], lbase_bc[:, :], self.ppool, [t3], "lbbc")
        tcol = k.sb("tcol", [128, 2], F32)
        fr = k.sb("fr", [128, 64 * 5], F32)
        TWO_PI = 2.0 * math.pi
        prev_tls = None
        for tt in range(NT):
            w0 = [t5, t6] + ([prev_tls] if prev_tls else [])
            ti = p.emit(GPSIMD, (lambda tt: lambda g: g.iota(
                out=tcol[:, 0:1], pattern=[[0, 1]], base=tt * 128,
                channel_multiplier=1, allow_small_or_imprecise_dtypes=True))(tt),
                waits=w0, tag="tc")
            tpw = p.emit(VECTOR, lambda v: v.tensor_scalar(
                out=tcol[:, 1:2], in0=tcol[:, 0:1], scalar1=-1024.0, scalar2=1.0 / 256,
                op0=ALU.add, op1=ALU.mult), waits=[ti], tag="pw")
            tf = p.emit(VECTOR, lambda v: v.tensor_scalar(
                out=fr[:, 0:64], in0=invf_bc[:, :], scalar1=tcol[:, 0:1], scalar2=None,
                op0=ALU.mult), waits=[ti, t5], tag="fq")
            tu = p.emit(VECTOR, lambda v: v.tensor_scalar(
                out=fr[:, 64:128], in0=fr[:, 0:64], scalar1=1.0 / TWO_PI, scalar2=None,
                op0=ALU.mult), waits=[tf], tag="u")
            last = tu
            for kk in range(8, -1, -1):
                step = float(2 ** kk)
                m1 = p.emit(VECTOR, (lambda step: lambda v: v.tensor_scalar(
                    out=fr[:, 128:192], in0=fr[:, 64:128], scalar1=step, scalar2=None,
                    op0=ALU.is_ge))(step), waits=[last], tag="lad")
                last = p.emit(VECTOR, (lambda step: lambda v: v.scalar_tensor_tensor(
                    out=fr[:, 64:128], in0=fr[:, 128:192], scalar=-step,
                    in1=fr[:, 64:128], op0=ALU.mult, op1=ALU.add))(step),
                    waits=[m1], tag="lad2")
            m1 = p.emit(VECTOR, lambda v: v.tensor_scalar(
                out=fr[:, 128:192], in0=fr[:, 64:128], scalar1=0.5, scalar2=None,
                op0=ALU.is_ge), waits=[last], tag="sw")
            tsin = p.emit(VECTOR, lambda v: v.scalar_tensor_tensor(
                out=fr[:, 128:192], in0=fr[:, 128:192], scalar=-1.0,
                in1=fr[:, 64:128], op0=ALU.mult, op1=ALU.add), waits=[m1], tag="sa")
            tc1 = p.emit(VECTOR, lambda v: v.tensor_scalar(
                out=fr[:, 192:256], in0=fr[:, 64:128], scalar1=0.25, scalar2=None,
                op0=ALU.add), waits=[tsin], tag="ca")
            tc2 = p.emit(VECTOR, lambda v: v.tensor_scalar(
                out=fr[:, 0:64], in0=fr[:, 192:256], scalar1=0.5, scalar2=None,
                op0=ALU.is_ge), waits=[tc1], tag="cb")
            tcos = p.emit(VECTOR, lambda v: v.scalar_tensor_tensor(
                out=fr[:, 192:256], in0=fr[:, 0:64], scalar=-1.0,
                in1=fr[:, 192:256], op0=ALU.mult, op1=ALU.add), waits=[tc2], tag="cc")
            tsn = p.emit(SCALAR, (lambda tt: lambda s: s.activation(
                out=self.ssq[:, tt * 64:(tt + 1) * 64], in_=fr[:, 128:192],
                func=AF.Sin, scale=TWO_PI))(tt), waits=[tsin], tag="sin")
            tcs = p.emit(SCALAR, (lambda tt: lambda s: s.activation(
                out=self.csq[:, tt * 64:(tt + 1) * 64], in_=fr[:, 192:256],
                func=AF.Sin, scale=TWO_PI))(tt), waits=[tcos], tag="cos")
            tls = p.emit(VECTOR, lambda v: v.tensor_scalar(
                out=fr[:, 0:64], in0=lbase_bc[:, :], scalar1=tcol[:, 1:2],
                scalar2=None, op0=ALU.mult), waits=[tpw, t6, tcs], tag="ls")
            prev_tls = tls
            tsc = p.emit(SCALAR, lambda s: s.activation(
                out=fr[:, 64:128], in_=fr[:, 0:64], func=AF.Exp), waits=[tls],
                tag="sc")
            tisc = p.emit(SCALAR, lambda s: s.activation(
                out=fr[:, 256:320], in_=fr[:, 0:64], func=AF.Exp, scale=-1.0),
                waits=[tls], tag="isc")
            a = self.csq[:, tt * 64:(tt + 1) * 64]
            b = self.ssq[:, tt * 64:(tt + 1) * 64]
            c_ = self.csk[:, tt * 64:(tt + 1) * 64]
            d_ = self.ssk[:, tt * 64:(tt + 1) * 64]
            e1 = p.emit(VECTOR, (lambda a, c_: lambda v: v.tensor_tensor(
                out=c_, in0=a, in1=fr[:, 256:320], op=ALU.mult))(a, c_),
                waits=[tsc, tisc, tcs, tsn], tag="csk")
            e2 = p.emit(VECTOR, (lambda b, d_: lambda v: v.tensor_tensor(
                out=d_, in0=b, in1=fr[:, 256:320], op=ALU.mult))(b, d_),
                waits=[e1], tag="ssk")
            e3 = p.emit(VECTOR, (lambda a: lambda v: v.tensor_tensor(
                out=a, in0=a, in1=fr[:, 64:128], op=ALU.mult))(a),
                waits=[e2], tag="csq")
            e4 = p.emit(VECTOR, (lambda b: lambda v: v.tensor_tensor(
                out=b, in0=b, in1=fr[:, 64:128], op=ALU.mult))(b),
                waits=[e3], tag="ssq")
            self._tbl_tok = e4
        k.pop_scope()

    def p2_layer(self, l):
        k, p = self.k, self.k.p
        pre = f"f{l}_"
        wqkv = k.din(pre + "wqkv", [D, 1536])
        qs_h = k.din(pre + "qs", [1, 64]); ks_h = k.din(pre + "ks", [1, 64])
        wo_h = k.din(pre + "wo", [512, D])
        lng_h = k.din(pre + "lng", [1, D]); lnb_h = k.din(pre + "lnb", [1, D])
        w1_h = k.din(pre + "w1", [D, 5460]); b1_h = k.din(pre + "b1", [1, 5460])
        w2_h = k.din(pre + "w2", [2730, D]); b2_h = k.din(pre + "b2", [1, D])
        SW = self.SW
        k.push_scope()
        self.p2_tables_global()

        rows = k.sb("qr", [1, 4 * 64], F32)
        l1 = p.dma(SYNC, rows[0:1, 0:64], qs_h[0:1, :], tag="qsld")
        l2 = p.dma(SYNC, rows[0:1, 128:192], ks_h[0:1, :], tag="ksld")
        t1 = p.emit(VECTOR, lambda v: v.tensor_scalar(
            out=rows[0:1, 0:64], in0=rows[0:1, 0:64], scalar1=8.0, scalar2=None,
            op0=ALU.mult), waits=[l1], tag="qs8")
        t2 = p.emit(VECTOR, lambda v: v.tensor_scalar(
            out=rows[0:1, 64:96], in0=rows[0:1, 32:64], scalar1=-1.0, scalar2=None,
            op0=ALU.mult), waits=[t1], tag="qr1")
        t3 = p.emit(VECTOR, lambda v: v.tensor_copy(
            out=rows[0:1, 96:128], in_=rows[0:1, 0:32]), waits=[t2], tag="qr2")
        t4 = p.emit(VECTOR, lambda v: v.tensor_scalar(
            out=rows[0:1, 192:224], in0=rows[0:1, 160:192], scalar1=-1.0,
            scalar2=None, op0=ALU.mult), waits=[l2], tag="kr1")
        t5 = p.emit(VECTOR, lambda v: v.tensor_copy(
            out=rows[0:1, 224:256], in_=rows[0:1, 128:160]), waits=[t4], tag="kr2")
        nb = k.sb("nb", [1, 2], F32)
        t6 = p.emit(VECTOR, lambda v: v.reduce_max(
            out=nb[0:1, 0:1], in_=rows[0:1, 0:64], axis=AX.X,
            apply_absolute_value=True), waits=[t1], tag="m1")
        t7 = p.emit(VECTOR, lambda v: v.reduce_max(
            out=nb[0:1, 1:2], in_=rows[0:1, 128:192], axis=AX.X,
            apply_absolute_value=True), waits=[l2], tag="m2")
        t8 = p.emit(VECTOR, lambda v: v.scalar_tensor_tensor(
            out=nb[0:1, 0:1], in0=nb[0:1, 0:1], scalar=-1.0, in1=nb[0:1, 1:2],
            op0=ALU.mult, op1=ALU.mult), waits=[t6, t7], tag="nb0")
        nb0_col = k.sb("nb0c", [128, 1], F32)
        t9 = bcast_row(k, nb[0:1, 0:1], nb0_col[:, 0:1], self.ppool, [t8], "nbbc")
        rbc = k.sb("rbc", [128, 4 * 64], F32)
        rb_toks = []
        for i, tk in enumerate([t3, t3, t5, t5]):
            rb_toks.append(bcast_row(k, rows[0:1, i * 64:(i + 1) * 64],
                                     rbc[:, i * 64:(i + 1) * 64], self.ppool,
                                     [t3, t5], f"rbc{i}"))
        tbl = k.sb("tbl", [128, 4 * NT * 64], BF16)
        TL = NT * 64
        tbl_toks = []
        for i, (srct, rbi) in enumerate([(self.csq, 0), (self.ssq, 1),
                                         (self.csk, 2), (self.ssk, 3)]):
            tt_ = p.emit(VECTOR, (lambda i, srct, rbi: lambda v: v.tensor_tensor(
                out=tbl[:, i * TL:(i + 1) * TL].rearrange("p (t d) -> p t d", d=64),
                in0=srct[:, :].rearrange("p (t d) -> p t d", d=64),
                in1=rbc[:, rbi * 64:(rbi + 1) * 64].unsqueeze(1).broadcast_to(
                    [128, NT, 64]), op=ALU.mult))(i, srct, rbi),
                waits=rb_toks + [self._tbl_tok], tag=f"tbl{i}")
            tbl_toks.append(tt_)
        T1q = tbl[:, 0 * TL:1 * TL]; T2q = tbl[:, 1 * TL:2 * TL]
        T1k = tbl[:, 2 * TL:3 * TL]; T2k = tbl[:, 3 * TL:4 * TL]
        mAC = k.sb("mAC", [128, 256], F32)
        tmA = affine_mask(k, mAC[:, 0:128], 128, ALU.is_ge, 0, 1, NEGB, 0.0, "mA",
                          step=-1)
        tmC = affine_mask(k, mAC[:, 128:256], 128, ALU.is_ge, 0, -1, NEGB, 0.0, "mC",
                          step=1)
        mA = mAC[:, 0:128]; mC = mAC[:, 128:256]

        xbf = self.xT[:, :].bitcast(BF16)
        kT = xbf[:, 0:4 * S]
        qT = xbf[:, 4 * S:8 * S]
        v64 = xbf[:, 8 * S:8 * S + NT * 512]
        wo_sb = xbf[:, 8 * S + NT * 512:8 * S + NT * 512 + 8 * D]
        slab = k.sb("fslab", [128, 8 * 512], BF16)
        wstg = k.sb("fwstg", [128, D], F32)
        mir_f = k.sb("fmirf", [128, 512], F32)
        mir = k.sb("fmir", [128, 8 * 512], BF16)
        qn_sb = k.sb("fqn", [128, 512], F32)
        sq_sb = k.sb("fsq", [128, 512], F32)
        ra = k.sb("fra", [128, 512], F32)
        rot_bf = k.sb("frot", [128, 512], BF16)
        ssr = k.sb("fssr", [128, 24], F32)
        pT = k.sb("fpT", [128, 8 * 512], BF16)
        attn_sl = k.sb("fattn", [64, 8 * 512], BF16)
        den_sb = k.sb("fden", [64, 512], F32)
        den_r = k.sb("fdenr", [1, 512], F32)
        self._mirQ_free = [None] * 8
        wstg_free = [None]

        zt_toks = []
        def ztf(v):
            last = None
            for slot in range(1):
                for oo in range(1, 4):
                    last = v.memset(pT[:, slot * 4096 + (4 + oo) * 512:
                                       slot * 4096 + (4 + oo) * 512 + 128 * oo], 0.0)
                for pp in range(0, 3):
                    off = 128 * (pp + 1)
                    last = v.memset(pT[:, slot * 4096 + pp * 512 + off:
                                       slot * 4096 + (pp + 1) * 512], 0.0)
            return last
        zt = p.emit(VECTOR, ztf, tag="pTz")

        wo_toks = []
        for h in range(FH):
            ld = p.dma(SYNC, wstg[0:64, 0:D], wo_h[h * 64:(h + 1) * 64, :],
                       waits=[wstg_free[0]] if wstg_free[0] else [], tag="wold")
            ct = p.emit(SCALAR, (lambda h: lambda s: s.copy(
                out=wo_sb[0:64, h * D:(h + 1) * D], in_=wstg[0:64, 0:D]))(h),
                waits=[ld], tag="woc")
            wstg_free[0] = ct
            wo_toks.append(ct)

        def strm_shift_ap(c, a0, wdt):
            if c < 4:
                return self.strmc[c][:, a0 + 1:a0 + 1 + wdt]
            return self.strmc[c][:, a0:a0 + wdt]

        def build_mirror_q(a0):
            toks = []
            for h in range(NCH):
                w = [self._mirQ_free[h]] if self._mirQ_free[h] else []
                tq = p.emit(VECTOR, (lambda h, a0: lambda v: v.tensor_copy(
                    out=mir[:, h * 512:(h + 1) * 512],
                    in_=strm_shift_ap(h, a0, 512)))(h, a0),
                    waits=w + [self.strm_tok[h]], tag="mirq")
                toks.append(tq)
            return toks

        def rotv(ap512):
            return bass.AP(ap512.tensor, ap512.offset + 32,
                           [list(ap512.ap[0]), [64, 8], [-32, 2], [1, 32]])

        kT_tok = [None] * NT
        qT_tok = [None] * NT
        v_tok2 = [None] * NT
        self._slab_free = None
        def proj_pass(fpart, kind):
            sl_toks = []
            for hp in range(4):
                ld = p.dma(SYNC, wstg[:, 0:1024].rearrange("p (c w) -> p c w", c=2),
                           wqkv[hp * 256:(hp + 1) * 256,
                                fpart * 512:(fpart + 1) * 512].rearrange(
                                "(c p) w -> p c w", p=128),
                           waits=[wstg_free[0]] if wstg_free[0] else [], tag="qkvld")
                ct = p.emit(SCALAR, (lambda hp: lambda s: s.copy(
                    out=slab[:, hp * 1024:(hp + 1) * 1024], in_=wstg[:, 0:1024]))(hp),
                    waits=[ld] + ([self._slab_free] if (hp == 0 and self._slab_free)
                                  else []),
                    tag="qkvc")
                wstg_free[0] = ct
                sl_toks.append(ct)
            for ts4 in range(NT4):
                a0 = ts4 * 512
                mtoks = build_mirror_q(a0)
                last_mm_slice = None
                for q4 in range(4):
                    tt = ts4 * 4 + q4
                    j, bank, wb = self.ppool.acquire()
                    mm = None
                    for h in range(NCH):
                        mm = p.emit(TENSOR, (lambda h, q4, bank, first, last:
                            lambda te: te.matmul(
                                out=bank[:, :],
                                lhsT=mir[:, h * 512 + q4 * 128: h * 512 + (q4 + 1) * 128],
                                rhs=slab[:, h * 512:(h + 1) * 512],
                                start=first, stop=last))(h, q4, bank, h == 0,
                                                         h == NCH - 1),
                            waits=(wb + mtoks + sl_toks if h == 0 else []),
                            tag=f"pj{kind}{tt}")
                    last_mm_slice = mm
                    if kind == "v":
                        ev = p.emit(SCALAR, (lambda tt, bank: lambda s: s.copy(
                            out=v64[:, tt * 512:(tt + 1) * 512], in_=bank[:, :]))(tt, bank),
                            waits=[mm], tag=f"vev{tt}")
                        self.ppool.release(j, ev)
                        v_tok2[tt] = ev
                        continue
                    T1 = T1q if kind == "q" else T1k
                    T2 = T2q if kind == "q" else T2k
                    dstT = qT if kind == "q" else kT
                    s1 = p.emit(SCALAR, (lambda bank: lambda s: s.activation(
                        out=sq_sb[:, :], in_=bank[:, :], func=AF.Square))(bank),
                        waits=[mm], tag="l2sq")
                    s2 = p.emit(VECTOR, lambda v: v.tensor_reduce(
                        out=ssr[:, 0:8], in_=sq_sb[:, :].rearrange(
                            "p (h d) -> p h d", d=64), axis=AX.X, op=ALU.add),
                        waits=[s1], tag="l2red")
                    s3 = p.emit(SCALAR, lambda s: s.activation(
                        out=ssr[:, 8:16], in_=ssr[:, 0:8], func=AF.Sqrt),
                        waits=[s2], tag="l2rt")
                    s4 = p.emit(VECTOR, lambda v: v.tensor_scalar(
                        out=ssr[:, 8:16], in0=ssr[:, 8:16], scalar1=1e-12,
                        scalar2=None, op0=ALU.max), waits=[s3], tag="l2mx")
                    s5 = p.emit(VECTOR, lambda v: v.reciprocal(
                        out=ssr[:, 16:24], in_=ssr[:, 8:16]), waits=[s4], tag="l2rc")
                    s6 = p.emit(VECTOR, (lambda bank: lambda v: v.tensor_tensor(
                        out=qn_sb[:, :].rearrange("p (h d) -> p h d", d=64),
                        in0=bank[:, :].rearrange("p (h d) -> p h d", d=64),
                        in1=ssr[:, 16:24].unsqueeze(2).broadcast_to([128, 8, 64]),
                        op=ALU.mult))(bank), waits=[s5], tag="l2ap")
                    self.ppool.release(j, s6)
                    s7 = p.emit(VECTOR, (lambda tt, T1: lambda v: v.tensor_tensor(
                        out=ra[:, :].rearrange("p (h d) -> p h d", d=64),
                        in0=qn_sb[:, :].rearrange("p (h d) -> p h d", d=64),
                        in1=T1[:, tt * 64:(tt + 1) * 64].unsqueeze(1).broadcast_to(
                            [128, 8, 64]), op=ALU.mult))(tt, T1),
                        waits=[s6] + tbl_toks, tag="rt1")
                    s8 = p.emit(VECTOR, (lambda tt, T2: lambda v: v.tensor_tensor(
                        out=sq_sb[:, :].rearrange("p (h d) -> p h d", d=64),
                        in0=rotv(qn_sb[:, :]),
                        in1=T2[:, tt * 64:(tt + 1) * 64].unsqueeze(1).broadcast_to(
                            [128, 8, 64]), op=ALU.mult))(tt, T2),
                        waits=[s7], tag="rt2")
                    s9 = p.emit(VECTOR, lambda v: v.tensor_tensor(
                        out=rot_bf[:, :], in0=ra[:, :], in1=sq_sb[:, :], op=ALU.add),
                        waits=[s8], tag="rt3")
                    evs = []
                    for r in range(4):
                        j2, bank2, wb2 = self.ppool.acquire()
                        tt2 = p.emit(TENSOR, (lambda r, bank2: lambda te: te.transpose(
                            out=bank2.bitcast(BF16)[:, 0:128],
                            in_=rot_bf[:, r * 128:(r + 1) * 128],
                            identity=self.idn_bf[:, :]))(r, bank2),
                            waits=[s9, self.idn_bf_tok] + wb2, tag="rtp")
                        ev = p.emit(SCALAR, (lambda r, tt, bank2, dstT: lambda s:
                            s.copy(
                                out=dstT[:, r * S + tt * 128: r * S + (tt + 1) * 128],
                                in_=bank2.bitcast(BF16)[:, 0:128]))(r, tt, bank2, dstT),
                            waits=[tt2], tag="rtc")
                        self.ppool.release(j2, ev)
                        evs.append(ev)
                    if kind == "q":
                        qT_tok[tt] = evs[-1]
                    else:
                        kT_tok[tt] = evs[-1]
                for h in range(NCH):
                    self._mirQ_free[h] = last_mm_slice
                self._slab_free = last_mm_slice
        proj_pass(1, "k")
        proj_pass(2, "v")
        proj_pass(0, "q")

        pT_free = [None, None]
        attn_free = [None] * 8
        for w in range(4):
            wo_evs = []
            for h in range(FH):
                slot = 0
                pbase = 0
                chunks = []
                if w > 0:
                    for pp in range(4):
                        chunks.append(("prev", pp, 4 * (w - 1) + pp, pp))
                for oo in range(4):
                    chunks.append(("own", oo, 4 * w + oo, 4 + oo))
                hp = (h % 2) * 64
                r = h // 2
                exp_toks = []
                pw = [pT_free[slot]] if pT_free[slot] else []
                for kindc, idx, jtile, ci in chunks:
                    if kindc == "prev":
                        off, N = 0, 128 * (idx + 1)
                    else:
                        off, N = 128 * idx, 512 - 128 * idx
                    j, bank, wb = self.ppool.acquire()
                    mm = p.emit(TENSOR, (lambda hp, r, jtile, w, off, N, bank:
                        lambda te: te.matmul(
                            out=bank[:, off:off + N],
                            lhsT=kT[hp:hp + 64, r * S + jtile * 128: r * S + (jtile + 1) * 128],
                            rhs=qT[hp:hp + 64, r * S + w * 512 + off: r * S + w * 512 + off + N],
                            start=True, stop=True,
                            tile_position=(hp, 0)))(hp, r, jtile, w, off, N, bank),
                        waits=wb + [kT_tok[jtile], qT_tok[4 * w + 0], qT_tok[4 * w + 1],
                                    qT_tok[4 * w + 2], qT_tok[4 * w + 3]],
                        tag=f"sc{w}_{h}_{ci}")
                    dq = 128 * idx
                    md = p.emit(VECTOR, (lambda bank, dq, kindc: lambda v:
                        v.tensor_tensor(out=bank[:, dq:dq + 128],
                                        in0=bank[:, dq:dq + 128],
                                        in1=(mA if kindc == "prev" else mC),
                                        op=ALU.add))(bank, dq, kindc),
                        waits=[mm, tmA, tmC], tag=f"msk{w}_{h}_{ci}")
                    ex = p.emit(SCALAR, (lambda bank, pbase, ci, off, N: lambda s:
                        s.activation(out=pT[:, pbase + ci * 512 + off:
                                            pbase + ci * 512 + off + N],
                                     in_=bank[:, off:off + N], func=AF.Exp,
                                     bias=nb0_col[:, 0:1]))(bank, pbase, ci, off, N),
                        waits=[md, t9, zt] + pw, tag=f"ex{w}_{h}_{ci}")
                    self.ppool.release(j, ex)
                    exp_toks.append(ex)
                j, bank, wb = self.ppool.acquire()
                mm = None
                for ii, (kindc, idx, jtile, ci) in enumerate(chunks):
                    mm = p.emit(TENSOR, (lambda h, jtile, ci, pbase, bank, first, last:
                        lambda te: te.matmul(
                            out=bank[0:64, :],
                            lhsT=v64[:, jtile * 512 + h * 64: jtile * 512 + (h + 1) * 64],
                            rhs=pT[:, pbase + ci * 512: pbase + (ci + 1) * 512],
                            start=first, stop=last))(h, jtile, ci, pbase, bank,
                                                     ii == 0, ii == len(chunks) - 1),
                        waits=(wb + exp_toks + [v_tok2[jt] for _, _, jt, _ in chunks]
                               if ii == 0 else []),
                        tag=f"av{w}_{h}")
                j2, bank2, wb2 = self.ppool.acquire()
                dmm = None
                for ii, (kindc, idx, jtile, ci) in enumerate(chunks):
                    dmm = p.emit(TENSOR, (lambda ci, pbase, bank2, first, last:
                        lambda te: te.matmul(
                            out=bank2[0:1, :], lhsT=self.ones1b[:, 0:1],
                            rhs=pT[:, pbase + ci * 512: pbase + (ci + 1) * 512],
                            start=first, stop=last))(ci, pbase, bank2,
                                                     ii == 0, ii == len(chunks) - 1),
                        waits=(wb2 + exp_toks + [self.ones1b_tok] if ii == 0 else []),
                        tag=f"dn{w}_{h}")
                pT_free[slot] = dmm
                d1 = p.emit(VECTOR, (lambda bank2: lambda v: v.reciprocal(
                    out=den_r[0:1, :], in_=bank2[0:1, :]))(bank2),
                    waits=[dmm], tag=f"dr{w}_{h}")
                self.ppool.release(j2, d1)
                j3, bank3, wb3 = self.ppool.acquire()
                d2 = p.emit(TENSOR, (lambda bank3: lambda te: te.matmul(
                    out=bank3[0:64, :], lhsT=k._ones_row[0:1, 0:64],
                    rhs=den_r[0:1, :], start=True, stop=True))(bank3),
                    waits=wb3 + [d1, k._ones_row_tok], tag=f"db{w}_{h}")
                d3 = p.emit(VECTOR, (lambda bank3: lambda v: v.tensor_copy(
                    out=den_sb[0:64, :], in_=bank3[0:64, :]))(bank3),
                    waits=[d2], tag=f"dc{w}_{h}")
                self.ppool.release(j3, d3)
                aw = [attn_free[h]] if attn_free[h] else []
                a1 = p.emit(VECTOR, (lambda h, bank: lambda v: v.tensor_tensor(
                    out=attn_sl[0:64, h * 512:(h + 1) * 512], in0=bank[0:64, :],
                    in1=den_sb[0:64, :], op=ALU.mult))(h, bank),
                    waits=[d3] + aw, tag=f"at{w}_{h}")
                self.ppool.release(j, a1)
                attn_free[h] = a1
                wo_evs.append(a1)
            for f in range(NCH):
                j, bank, wb = self.ppool.acquire()
                mm = None
                for h in range(FH):
                    mm = p.emit(TENSOR, (lambda f, h, bank, first, last:
                        lambda te: te.matmul(
                            out=bank[:, :],
                            lhsT=wo_sb[0:64, h * D + f * 128: h * D + (f + 1) * 128],
                            rhs=attn_sl[0:64, h * 512:(h + 1) * 512],
                            start=first, stop=last))(f, h, bank, h == 0, h == FH - 1),
                        waits=(wb + wo_evs + wo_toks if h == 0 else []),
                        tag=f"wo{w}_{f}")
                ev = p.emit(VECTOR, (lambda f, w, bank: lambda v: v.tensor_tensor(
                    out=self.strmc[f][:, w * 512 + 1:(w + 1) * 512 + 1],
                    in0=self.strmc[f][:, w * 512 + 1:(w + 1) * 512 + 1],
                    in1=bank[:, :], op=ALU.add))(f, w, bank),
                    waits=[mm, self.strm_tok[f]], tag=f"woev{w}_{f}")
                self.ppool.release(j, ev)
                self.strm_tok[f] = ev
            attn_free = [mm] * 8
        self._p2a_done = p.fence()
        import os as _os
        if _os.environ.get("P2_KEEP"):
            self._p2_dbg = dict(kT=kT, qT=qT, v64=v64, tbl=tbl[:, :], pT=pT[:, :],
                                attn_sl=attn_sl[:, :], nb0_col=nb0_col[:, :],
                                rbc=rbc[:, :])
            return
        k.pop_scope()
        if _os.environ.get("P2_NO_FFN"):
            return
        self.p2_ffn(l, lng_h, lnb_h, w1_h, b1_h, w2_h, b2_h)

    def p2_ffn(self, l, lng_h, lnb_h, w1_h, b1_h, w2_h, b2_h):
        k, p = self.k, self.k.p
        SW = self.SW
        HID = 2730
        k.push_scope()
        sh_aps = [self.strmc[c][:, (1 if c < 4 else 0):(1 if c < 4 else 0) + S]
                  for c in range(NCH)]
        xbf = self.xT[:, :].bitcast(BF16)
        h_half = xbf[:, 0:22 * 1024]
        lnh_h = xbf[:, 22 * 1024:22 * 1024 + NCH * 1024]
        lng_col, lg_tok = load_bias_cols(k, lng_h, D, tag="lngc")
        lnb_col, lb_tok = load_bias_cols(k, lnb_h, D, tag="lnbc")
        b2_col, b2_tok = load_bias_cols(k, b2_h, D, tag="b2c")
        b1a = k.sb("b1a", [128, 22], F32)
        b1g = k.sb("b1g", [128, 22], F32)
        def ld_ragged(dst, base):
            t1_ = k.p.dma_fn(SYNC, (lambda dst, base: lambda eng: eng.dma_start(
                out=dst[:, 0:21],
                in_=b1_h[0:1, base:base + 2688].rearrange("o (c q) -> (o q) c", q=128),
                allow_slow_non_contiguous=True))(dst, base), (), True, "b1r")
            t2_ = k.p.dma_fn(SYNC, (lambda dst, base: lambda eng: eng.dma_start(
                out=dst[0:42, 21:22],
                in_=b1_h[0:1, base + 2688:base + 2730].rearrange("o q -> q o"),
                allow_slow_non_contiguous=True))(dst, base), (), True, "b1r2")
            return [t1_, t2_]
        b1a_tok = ld_ragged(b1a, 0)
        b1g_tok = ld_ragged(b1g, HID)
        sv = k.sb("svcol", [128, 8], F32)
        sv_toks = []
        for c in range(4, NCH):
            t = p.emit(VECTOR, (lambda c: lambda v: v.tensor_copy(
                out=sv[:, c:c + 1], in_=self.strmc[c][:, 1024:1025]))(c),
                waits=[self.strm_tok[c]], tag="svc")
            sv_toks.append(t)
        rho2 = k.sb("rho2", [128, S], BF16)
        om2 = k.sb("om2", [128, S], BF16)
        k.push_scope()
        ln2 = ln_stats(k, sh_aps, list(self.strm_tok), self.ones1, self.ones1_tok,
                       self.ppool, tag=f"l2s{l}", out=(rho2, om2))
        k.pop_scope()
        w2res = k.sb("w2res", [128, 22 * D], BF16)
        wstg = k.sb("w2stg", [128, D], F32)
        stg_free = None
        w2_toks = []
        for oc in range(22):
            M = 128 if oc < 21 else 42
            ld = p.dma(SYNC, wstg[0:M, :], w2_h[oc * 128: oc * 128 + M, :],
                       waits=[stg_free] if stg_free else [], tag="w2ld")
            ct = p.emit(SCALAR, (lambda oc, M: lambda s: s.copy(
                out=w2res[0:M, oc * D:(oc + 1) * D], in_=wstg[0:M, :]))(oc, M),
                waits=[ld], tag="w2c")
            stg_free = ct
            w2_toks.append(ct)
        w1a_sl = k.sb("w1a", [128, 8 * 256], BF16)
        w1g_sl = k.sb("w1g", [128, 8 * 256], BF16)
        gring = k.sb("gring", [128, 2 * 512], F32)
        gring_free = [None, None]
        lf = k.sb("lf", [128, 2 * 512], F32)
        slabs = [(i * 256, 256) for i in range(10)] + [(2560, 170)]
        w1_free = [None, None]
        h_toks = {}
        w2_last = None
        for th in range(2):
            lnh_last = None
            lf_free = [None, None]
            prev_th = [w2_last] if w2_last else []
            for c in range(NCH):
                for t5i in range(2):
                    a0 = th * 1024 + t5i * 512
                    ri = (c * 2 + t5i) % 2
                    w_ = [lf_free[ri]] if lf_free[ri] else []
                    u1 = p.emit(VECTOR, (lambda c, a0, ri: lambda v: v.tensor_tensor(
                        out=lf[:, ri * 512:(ri + 1) * 512],
                        in0=sh_aps[c][:, a0:a0 + 512],
                        in1=rho2[:, a0:a0 + 512], op=ALU.mult))(c, a0, ri),
                        waits=ln2.toks + [self.strm_tok[c]] + w_ + prev_th,
                        tag="lnh1")
                    u2 = p.emit(VECTOR, (lambda c, a0, ri: lambda v: v.tensor_tensor(
                        out=lf[:, ri * 512:(ri + 1) * 512],
                        in0=lf[:, ri * 512:(ri + 1) * 512],
                        in1=om2[:, a0:a0 + 512], op=ALU.add))(c, a0, ri),
                        waits=[u1], tag="lnh2")
                    u3 = p.emit(VECTOR, (lambda c, a0, t5i, ri: lambda v: v.tensor_scalar(
                        out=lnh_h[:, c * 1024 + t5i * 512: c * 1024 + t5i * 512 + 512],
                        in0=lf[:, ri * 512:(ri + 1) * 512],
                        scalar1=lng_col[:, c:c + 1], scalar2=lnb_col[:, c:c + 1],
                        op0=ALU.mult, op1=ALU.add))(c, a0, t5i, ri),
                        waits=[u2, lg_tok, lb_tok], tag="lnh3")
                    lf_free[ri] = u3
                    lnh_last = u3
                if th == 1 and c >= 4:
                    u1 = p.emit(VECTOR, (lambda c: lambda v: v.tensor_tensor(
                        out=lf[:, 0:1], in0=sv[:, c:c + 1],
                        in1=rho2[:, 1024:1025], op=ALU.mult))(c),
                        waits=sv_toks + [lnh_last], tag="lnp1")
                    u2 = p.emit(VECTOR, (lambda c: lambda v: v.tensor_tensor(
                        out=lf[:, 0:1], in0=lf[:, 0:1],
                        in1=om2[:, 1024:1025], op=ALU.add))(c),
                        waits=[u1], tag="lnp2")
                    u3 = p.emit(VECTOR, (lambda c: lambda v: v.tensor_scalar(
                        out=lnh_h[:, c * 1024: c * 1024 + 1], in0=lf[:, 0:1],
                        scalar1=lng_col[:, c:c + 1], scalar2=lnb_col[:, c:c + 1],
                        op0=ALU.mult, op1=ALU.add))(c),
                        waits=[u2], tag="lnp3")
                    lnh_last = u3
                    lf_free[0] = u3
            for snum, (aoff, wdt) in enumerate(slabs):
                a_toks, g_toks = [], []
                for half, (sl, coff) in enumerate([(w1a_sl, 0), (w1g_sl, HID)]):
                    w_ = [w1_free[half]] if w1_free[half] else []
                    for hp in range(2):
                        ld = p.dma(SYNC, wstg[:, 0:4 * wdt].rearrange(
                                        "p (c w) -> p c w", c=4),
                                   w1_h[hp * 512:(hp + 1) * 512,
                                        coff + aoff: coff + aoff + wdt].rearrange(
                                        "(c p) w -> p c w", p=128),
                                   waits=([stg_free] if stg_free else []) +
                                         (w_ if hp == 0 else []),
                                   tag="w1ld")
                        ct = p.emit(SCALAR, (lambda hp, sl, wdt: lambda s: s.copy(
                            out=sl[:, hp * 1024:(hp + 1) * 1024].rearrange(
                                "p (c w) -> p c w", c=4)[:, :, 0:wdt],
                            in_=wstg[:, 0:4 * wdt].rearrange(
                                "p (c w) -> p c w", c=4)))(hp, sl, wdt),
                            waits=[ld], tag="w1c")
                        stg_free = ct
                        (a_toks if half == 0 else g_toks).append(ct)
                nsub = (wdt + 127) // 128
                last_g_mm = None
                for t5i in range(2):
                    tcols = th * 1024 + t5i * 512
                    for sub in range(nsub):
                        M = min(128, wdt - 128 * sub)
                        ci = (aoff + 128 * sub) // 128
                        ja, banka, wba = self.ppool.acquire()
                        mma = None
                        for hh in range(NCH):
                            mma = p.emit(TENSOR, (lambda hh, sub, M, t5i, banka, first, last:
                                lambda te: te.matmul(
                                    out=banka[0:M, :],
                                    lhsT=w1a_sl[:, hh * 256 + sub * 128: hh * 256 + sub * 128 + M],
                                    rhs=lnh_h[:, hh * 1024 + t5i * 512: hh * 1024 + t5i * 512 + 512],
                                    start=first, stop=last))(hh, sub, M, t5i, banka,
                                                             hh == 0, hh == NCH - 1),
                                waits=(wba + [a_toks[-1], lnh_last] if hh == 0 else []),
                                tag=f"w1a{th}_{snum}_{t5i}_{sub}")
                        jg, bankg, wbg = self.ppool.acquire()
                        mmg = None
                        for hh in range(NCH):
                            mmg = p.emit(TENSOR, (lambda hh, sub, M, t5i, bankg, first, last:
                                lambda te: te.matmul(
                                    out=bankg[0:M, :],
                                    lhsT=w1g_sl[:, hh * 256 + sub * 128: hh * 256 + sub * 128 + M],
                                    rhs=lnh_h[:, hh * 1024 + t5i * 512: hh * 1024 + t5i * 512 + 512],
                                    start=first, stop=last))(hh, sub, M, t5i, bankg,
                                                             hh == 0, hh == NCH - 1),
                                waits=(wbg + [g_toks[-1]] if hh == 0 else []),
                                tag=f"w1g{th}_{snum}_{t5i}_{sub}")
                        last_g_mm = mmg
                        gr = t5i % 2
                        ge = p.emit(SCALAR, (lambda bankg, M, ci, gr: lambda s:
                            s.activation(out=gring[0:M, gr * 512:(gr + 1) * 512],
                                         in_=bankg[0:M, :], func=AF.Gelu,
                                         bias=b1g[0:M, ci:ci + 1]))(bankg, M, ci, gr),
                            waits=[mmg] + b1g_tok +
                                  ([gring_free[gr]] if gring_free[gr] else []),
                            tag="gelu")
                        self.ppool.release(jg, ge)
                        he = p.emit(VECTOR, (lambda banka, M, ci, gr, t5i: lambda v:
                            v.scalar_tensor_tensor(
                                out=h_half[0:M, ci * 1024 + t5i * 512:
                                           ci * 1024 + t5i * 512 + 512],
                                in0=banka[0:M, :], scalar=b1a[0:M, ci:ci + 1],
                                in1=gring[0:M, gr * 512:(gr + 1) * 512],
                                op0=ALU.add, op1=ALU.mult))(banka, M, ci, gr, t5i),
                            waits=[ge, mma] + b1a_tok + prev_th, tag="hmul")
                        self.ppool.release(ja, he)
                        gring_free[gr] = he
                        h_toks[(th, t5i)] = he
                w1_free[0] = last_g_mm
                w1_free[1] = last_g_mm
            for f in range(NCH):
                for t5i in range(2):
                    tcols = th * 1024 + t5i * 512
                    j, bank, wb = self.ppool.acquire()
                    mm = None
                    for oc in range(22):
                        M = 128 if oc < 21 else 42
                        mm = p.emit(TENSOR, (lambda f, oc, M, t5i, bank, first, last:
                            lambda te: te.matmul(
                                out=bank[:, :],
                                lhsT=w2res[0:M, oc * D + f * 128: oc * D + (f + 1) * 128],
                                rhs=h_half[0:M, oc * 1024 + t5i * 512:
                                           oc * 1024 + t5i * 512 + 512],
                                start=first, stop=last))(f, oc, M, t5i, bank,
                                                         oc == 0, oc == 21),
                            waits=(wb + [w2_toks[-1], h_toks[(th, t5i)]]
                                   if oc == 0 else []),
                            tag=f"w2{th}_{f}_{t5i}")
                    ev = p.emit(VECTOR, (lambda f, tcols, bank: lambda v:
                        v.scalar_tensor_tensor(
                            out=self.strmc[f][:, tcols + 1: tcols + 513],
                            in0=bank[:, :], scalar=b2_col[:, f:f + 1],
                            in1=self.strmc[f][:, tcols + 1: tcols + 513],
                            op0=ALU.add, op1=ALU.add))(f, tcols, bank),
                        waits=[mm, b2_tok, self.strm_tok[f]], tag=f"w2ev{th}_{f}")
                    self.ppool.release(j, ev)
                    self.strm_tok[f] = ev
                    w2_last = mm
        import os as _os
        if _os.environ.get("P2_KEEP_FFN") == str(l):
            self._ffn_dbg = dict(lnh=lnh_h, h=h_half, rho2=rho2[:, :], om2=om2[:, :])
            return
        k.pop_scope()

    def p2_apply(self):
        k, p = self.k, self.k.p
        self.p2_reload_xT()
        for c in range(NCH):
            t = p.emit(VECTOR, (lambda c: lambda v: v.tensor_tensor(
                out=self.xTc[c], in0=self.xTc[c],
                in1=self.strmc[c][:, 1:S + 1], op=ALU.add))(c),
                waits=[self.xTc_tok[c], self.strm_tok[c]], tag="xps")
            self.xTc_tok[c] = t
        self.p1_apply_attn()

    def p3_ln1(self):
        k, p = self.k, self.k.p
        k.push_scope()
        self.rho1 = k.sb("rho1", [128, S], BF16)
        self.om1 = k.sb("om1", [128, S], BF16)
        k.push_scope()
        self.ln1 = ln_stats(k, self.xTc, [t for t in self.xTc_tok if t],
                            self.ones1, self.ones1_tok, self.ppool, tag="ln1",
                            out=(self.rho1, self.om1))
        k.pop_scope()

    def xn1_tile(self, ring_f32, out_bf, c, a0, wdt, waits):
        k, p = self.k, self.k.p
        t1 = p.emit(VECTOR, lambda v: v.tensor_tensor(
            out=ring_f32[:, 0:wdt], in0=self.xTc[c][:, a0:a0 + wdt],
            in1=self.rho1[:, a0:a0 + wdt], op=ALU.mult),
            waits=list(waits) + self.ln1.toks +
                  ([self.xTc_tok[c]] if self.xTc_tok[c] else []), tag="x1a")
        t2 = p.emit(VECTOR, lambda v: v.tensor_tensor(
            out=out_bf, in0=ring_f32[:, 0:wdt],
            in1=self.om1[:, a0:a0 + wdt], op=ALU.add), waits=[t1], tag="x1b")
        return t2

    def p4_l1attn(self):
        k, p = self.k, self.k.p
        wq = k.din("l1a_wq", [D, D]); bq = k.din("l1a_bq", [1, D])
        wk = k.din("l1a_wk", [D, D]); bk = k.din("l1a_bk", [1, D])
        wv = k.din("l1a_wv", [D, D]); bv = k.din("l1a_bv", [1, D])
        self.attn1 = k.dram("attn1", [D, S], BF16)
        k.push_scope()
        bq_col, bq_tok = load_bias_cols(k, bq, D, tag="bq1")
        bk_col, bk_tok = load_bias_cols(k, bk, D, tag="bk1")
        bv_bc = k.sb("bv_bc", [128, D], BF16)
        k.push_scope()
        bv_row = k.sb("bv_row", [1, D], F32)
        tbv0 = p.dma(SYNC, bv_row[0:1, :], bv[0:1, :], tag="bvld")
        tbv = bcast_row(k, bv_row[0:1, :], bv_bc[:, :], self.ppool, [tbv0], "bvbc")
        k.pop_scope()
        mC_t = k.sb("mC1", [128, 128], F32)
        tmC = affine_mask(k, mC_t, 128, ALU.is_ge, 0, -1, NEGB, 0.0, "mC1", step=1)
        mC = mC_t[:, :]
        bcv = self.strm[:, :].bitcast(BF16)
        KT = bcv[:, 0:NCH * S]
        VT = bcv[:, NCH * S:2 * NCH * S]
        wsb = k.sb("wsb1", [128, NCH * D], BF16)
        wstg = k.sb("wstg1", [128, D], F32)
        mirf = k.sb("mirf1", [128, 512], F32)
        mir = k.sb("mir1", [128, 8 * 512], BF16)
        q_slab = k.sb("qs1", [128, 8 * 512], BF16)
        pT = k.sb("pT1", [128, 16 * 512], BF16)
        offs_col = k.sb("offs1", [128, 16], F32)
        km_row = k.sb("kmr1", [1, 2 * S], BF16)
        den_r = k.sb("denr1", [1, 512], F32)
        den_sb = k.sb("dens1", [64, 512], BF16)
        to_ring = k.sb("tor1", [64, 2 * 512], BF16)
        to_free = [None, None]
        self._mir1_free = [None] * 8
        self._w1stg_free = None
        self._w1sb_free = None

        def load_w(hbm, toks_out, tagp):
            for c in range(NCH):
                ld = p.dma(SYNC, wstg[:, :], hbm[c * 128:(c + 1) * 128, :],
                           waits=[self._w1stg_free] if self._w1stg_free else [],
                           tag=tagp + "ld")
                ct = p.emit(SCALAR, (lambda c: lambda s: s.copy(
                    out=wsb[:, c * D:(c + 1) * D], in_=wstg[:, :]))(c),
                    waits=[ld] + ([self._w1sb_free] if (c == 0 and self._w1sb_free)
                                  else []),
                    tag=tagp + "c")
                self._w1stg_free = ct
                toks_out.append(ct)

        def build_mir1(a0):
            toks = []
            for h in range(NCH):
                w_ = [self._mir1_free[h]] if self._mir1_free[h] else []
                toks.append(self.xn1_tile(mirf, mir[:, h * 512:(h + 1) * 512],
                                          h, a0, 512, w_))
            return toks

        KT_tok = [[None] * NT4 for _ in range(NCH)]
        km_toks = []
        wk_toks = []
        load_w(wk, wk_toks, "wk1")
        last_mm = None
        for ts4 in range(NT4):
            a0 = ts4 * 512
            mtoks = build_mir1(a0)
            for u in range(NCH):
                j, bank, wb = self.ppool.acquire()
                mm = None
                for c in range(NCH):
                    mm = p.emit(TENSOR, (lambda u, c, bank, first, last:
                        lambda te: te.matmul(
                            out=bank[:, :],
                            lhsT=wsb[:, c * D + u * 128: c * D + (u + 1) * 128],
                            rhs=mir[:, c * 512:(c + 1) * 512],
                            start=first, stop=last))(u, c, bank, c == 0, c == NCH - 1),
                        waits=(wb + mtoks + [wk_toks[-1]] if c == 0 else []),
                        tag=f"k1mm{ts4}_{u}")
                last_mm = mm
                ev = p.emit(SCALAR, (lambda u, a0, bank: lambda s: s.activation(
                    out=KT[:, u * S + a0: u * S + a0 + 512], in_=bank[:, :],
                    func=AF.Relu, bias=bk_col[:, u:u + 1]))(u, a0, bank),
                    waits=[mm, bk_tok], tag=f"k1ev{ts4}_{u}")
                self.ppool.release(j, ev)
                KT_tok[u][ts4] = ev
            j, bank, wb = self.ppool.acquire()
            mm = None
            for c in range(NCH):
                mm = p.emit(TENSOR, (lambda c, bank, first, last: lambda te: te.matmul(
                    out=bank[0:1, :], lhsT=self.ones1b[:, 0:1],
                    rhs=mir[:, c * 512:(c + 1) * 512],
                    start=first, stop=last))(c, bank, c == 0, c == NCH - 1),
                    waits=(wb + mtoks + [self.ones1b_tok] if c == 0 else []),
                    tag=f"km{ts4}")
            last_mm = mm
            ev = p.emit(VECTOR, (lambda a0, bank: lambda v: v.tensor_copy(
                out=km_row[0:1, a0:a0 + 512], in_=bank[0:1, :]))(a0, bank),
                waits=[mm], tag=f"kmev{ts4}")
            self.ppool.release(j, ev)
            km_toks.append(ev)
            for h in range(NCH):
                self._mir1_free[h] = last_mm
        self._w1sb_free = last_mm
        t1 = p.emit(VECTOR, lambda v: v.tensor_scalar(
            out=km_row[0:1, S:2 * S], in0=km_row[0:1, 0:S], scalar1=0.0,
            scalar2=None, op0=ALU.is_equal), waits=km_toks, tag="kmeq")
        t2 = p.emit(VECTOR, lambda v: v.tensor_scalar(
            out=km_row[0:1, 0:S], in0=km_row[0:1, S:2 * S],
            scalar1=-(2.0 ** 32 - 1.0), scalar2=-16.0, op0=ALU.mult, op1=ALU.add),
            waits=[t1], tag="kmoffs")
        t3 = p.emit(VECTOR, lambda v: v.tensor_scalar(
            out=km_row[0:1, S:2 * S], in0=km_row[0:1, S:2 * S], scalar1=-1.0,
            scalar2=1.0, op0=ALU.mult, op1=ALU.add), waits=[t2], tag="kmqm")
        off_toks = []
        for tt in range(NT):
            j, bank, wb = self.ppool.acquire()
            tk = p.emit(TENSOR, (lambda tt, bank: lambda te: te.transpose(
                out=bank.bitcast(BF16)[0:128, 0:1],
                in_=km_row[0:1, tt * 128:(tt + 1) * 128],
                identity=self.idn_bf[0:1, 0:1]))(tt, bank),
                waits=[t2, self.idn_bf_tok] + wb, tag=f"otp{tt}")
            ev = p.emit(VECTOR, (lambda tt, bank: lambda v: v.tensor_copy(
                out=offs_col[:, tt:tt + 1],
                in_=bank.bitcast(BF16)[0:128, 0:1]))(tt, bank),
                waits=[tk], tag=f"oev{tt}")
            self.ppool.release(j, ev)
            off_toks.append(ev)
        VT_tok = [None] * NT
        wv_toks = []
        load_w(wv, wv_toks, "wv1")
        for ts4 in range(NT4):
            a0 = ts4 * 512
            mtoks = build_mir1(a0)
            last_mm = None
            for q4 in range(4):
                tt = ts4 * 4 + q4
                for oh in range(2):
                    j, bank, wb = self.ppool.acquire()
                    mm = None
                    for c in range(NCH):
                        mm = p.emit(TENSOR, (lambda c, q4, oh, bank, first, last:
                            lambda te: te.matmul(
                                out=bank[:, :],
                                lhsT=mir[:, c * 512 + q4 * 128: c * 512 + (q4 + 1) * 128],
                                rhs=wsb[:, c * D + oh * 512: c * D + (oh + 1) * 512],
                                start=first, stop=last))(c, q4, oh, bank, c == 0,
                                                         c == NCH - 1),
                            waits=(wb + mtoks + [wv_toks[-1]] if c == 0 else []),
                            tag=f"v1mm{tt}_{oh}")
                    last_mm = mm
                    e1 = p.emit(VECTOR, (lambda oh, bank: lambda v: v.tensor_tensor(
                        out=bank[:, :], in0=bank[:, :],
                        in1=bv_bc[:, oh * 512:(oh + 1) * 512], op=ALU.add))(oh, bank),
                        waits=[mm, tbv], tag=f"v1b{tt}_{oh}")
                    e2 = p.emit(VECTOR, (lambda tt, oh, bank: lambda v: v.tensor_scalar(
                        out=VT[:, tt * D + oh * 512: tt * D + (oh + 1) * 512],
                        in0=bank[:, :], scalar1=0.0, scalar2=None,
                        op0=ALU.max))(tt, oh, bank), waits=[e1], tag=f"v1r{tt}_{oh}")
                    self.ppool.release(j, e2)
                    VT_tok[tt] = e2
            for h in range(NCH):
                self._mir1_free[h] = last_mm
        self._w1sb_free = last_mm
        wq_toks = []
        load_w(wq, wq_toks, "wq1")
        q_free = [None] * NCH
        pT_free = None
        for T in range(NT4):
            i0 = T * 512
            mtoks = build_mir1(i0)
            q_toks = []
            last_mm = None
            for u in range(NCH):
                j, bank, wb = self.ppool.acquire()
                mm = None
                for c in range(NCH):
                    mm = p.emit(TENSOR, (lambda u, c, bank, first, last:
                        lambda te: te.matmul(
                            out=bank[:, :],
                            lhsT=wsb[:, c * D + u * 128: c * D + (u + 1) * 128],
                            rhs=mir[:, c * 512:(c + 1) * 512],
                            start=first, stop=last))(u, c, bank, c == 0, c == NCH - 1),
                        waits=(wb + mtoks + [wq_toks[-1]] +
                               ([q_free[u]] if q_free[u] else [])
                               if c == 0 else []),
                        tag=f"q1mm{T}_{u}")
                last_mm = mm
                ev = p.emit(SCALAR, (lambda u, bank: lambda s: s.activation(
                    out=q_slab[:, u * 512:(u + 1) * 512], in_=bank[:, :],
                    func=AF.Relu, bias=bq_col[:, u:u + 1]))(u, bank),
                    waits=[mm, bq_tok], tag=f"q1ev{T}_{u}")
                self.ppool.release(j, ev)
                q_toks.append(ev)
            for h in range(NCH):
                self._mir1_free[h] = last_mm
            nj = 4 * T + 4
            zT = None
            for c in range(nj - 3, nj):
                trim = c - 4 * T
                if trim > 0:
                    zT = p.emit(VECTOR, (lambda c, trim: lambda v: v.memset(
                        pT[:, c * 512: c * 512 + 128 * trim], 0.0))(c, trim),
                        waits=([pT_free] if pT_free else []), tag=f"zT{T}_{c}")
            zTl = [zT] if zT else []
            for hh in range(AH):
                hp = (hh % 2) * 64
                r = hh // 2
                exp_toks = []
                pw = [pT_free] if pT_free else []
                for c in range(nj):
                    trim = c - 4 * T
                    off = 128 * trim if trim >= 0 else 0
                    N = 512 - off
                    j, bank, wb = self.ppool.acquire()
                    mm = p.emit(TENSOR, (lambda hp, r, c, i0, off, N, bank:
                        lambda te: te.matmul(
                            out=bank[:, off:off + N],
                            lhsT=KT[hp:hp + 64, r * S + c * 128: r * S + (c + 1) * 128],
                            rhs=q_slab[hp:hp + 64, r * 512 + off: r * 512 + off + N],
                            start=True, stop=True,
                            tile_position=(hp, 0)))(hp, r, c, i0, off, N, bank),
                        waits=wb + [q_toks[-1], KT_tok[NCH - 1][NT4 - 1]],
                        tag=f"s1{T}_{hh}_{c}")
                    w_ = [mm, tmC]
                    if trim >= 0:
                        md = p.emit(VECTOR, (lambda bank, off: lambda v:
                            v.tensor_tensor(out=bank[:, off:off + 128],
                                            in0=bank[:, off:off + 128],
                                            in1=mC, op=ALU.add))(bank, off),
                            waits=w_, tag=f"m1{T}_{hh}_{c}")
                    else:
                        md = mm
                    ex = p.emit(SCALAR, (lambda bank, c, off, N: lambda s:
                        s.activation(out=pT[:, c * 512 + off: c * 512 + off + N],
                                     in_=bank[:, off:off + N], func=AF.Exp,
                                     scale=0.125,
                                     bias=offs_col[:, c // 4 * 4 + c % 4:
                                                   c // 4 * 4 + c % 4 + 1][:, 0:1]
                                     if False else offs_col[:, c:c + 1]))(bank, c, off, N),
                        waits=[md, off_toks[c]] + pw, tag=f"e1{T}_{hh}_{c}")
                    self.ppool.release(j, ex)
                    exp_toks.append(ex)
                ztl = zTl
                j, bank, wb = self.ppool.acquire()
                mm = None
                for ii in range(nj):
                    mm = p.emit(TENSOR, (lambda hh, ii, bank, first, last:
                        lambda te: te.matmul(
                            out=bank[0:64, :],
                            lhsT=VT[:, ii * D + hh * 64: ii * D + (hh + 1) * 64],
                            rhs=pT[:, ii * 512:(ii + 1) * 512],
                            start=first, stop=last))(hh, ii, bank,
                                                     ii == 0, ii == nj - 1),
                        waits=(wb + [exp_toks[-1], VT_tok[NT - 1]] + ztl
                               if ii == 0 else []),
                        tag=f"av1{T}_{hh}")
                j2, bank2, wb2 = self.ppool.acquire()
                dmm = None
                for ii in range(nj):
                    dmm = p.emit(TENSOR, (lambda ii, bank2, first, last:
                        lambda te: te.matmul(
                            out=bank2[0:1, :], lhsT=self.ones1b[:, 0:1],
                            rhs=pT[:, ii * 512:(ii + 1) * 512],
                            start=first, stop=last))(ii, bank2, ii == 0, ii == nj - 1),
                        waits=(wb2 + [exp_toks[-1]] + ztl + [self.ones1b_tok]
                               if ii == 0 else []),
                        tag=f"dn1{T}_{hh}")
                pT_free = dmm
                d1 = p.emit(VECTOR, (lambda bank2: lambda v: v.reciprocal(
                    out=den_r[0:1, :], in_=bank2[0:1, :]))(bank2),
                    waits=[dmm], tag=f"dr1{T}_{hh}")
                self.ppool.release(j2, d1)
                d1b = p.emit(VECTOR, (lambda i0: lambda v: v.tensor_tensor(
                    out=den_r[0:1, :], in0=den_r[0:1, :],
                    in1=km_row[0:1, S + i0: S + i0 + 512], op=ALU.mult))(i0),
                    waits=[d1, t3], tag=f"dq1{T}_{hh}")
                j3, bank3, wb3 = self.ppool.acquire()
                d2 = p.emit(TENSOR, (lambda bank3: lambda te: te.matmul(
                    out=bank3[0:64, :], lhsT=k._ones_row[0:1, 0:64],
                    rhs=den_r[0:1, :], start=True, stop=True))(bank3),
                    waits=wb3 + [d1b, k._ones_row_tok], tag=f"db1{T}_{hh}")
                d3 = p.emit(VECTOR, (lambda bank3: lambda v: v.tensor_copy(
                    out=den_sb[0:64, :], in_=bank3[0:64, :]))(bank3),
                    waits=[d2], tag=f"dc1{T}_{hh}")
                self.ppool.release(j3, d3)
                ri = hh % 2
                w_ = [to_free[ri]] if to_free[ri] else []
                a1 = p.emit(VECTOR, (lambda ri, bank: lambda v: v.tensor_tensor(
                    out=to_ring[0:64, ri * 512:(ri + 1) * 512], in0=bank[0:64, :],
                    in1=den_sb[0:64, :], op=ALU.mult))(ri, bank),
                    waits=[d3] + w_, tag=f"ao1{T}_{hh}")
                self.ppool.release(j, a1)
                dm = p.dma(SYNC, self.attn1[hh * 64:(hh + 1) * 64, i0:i0 + 512],
                           to_ring[0:64, ri * 512:(ri + 1) * 512], waits=[a1],
                           tag=f"ad1{T}_{hh}")
                to_free[ri] = dm
            for u in range(NCH):
                q_free[u] = mm
        self._p4_done = p.fence()
        k.pop_scope()

    def quant_init_scratch(self):
        k = self.k
        self.qsc_blk = [k.sb("qblk", [128, 64 * 4], F32)[:, :] for _ in range(2)]
        self.qsc_el = [(k.sb("qely", [128, 1024], F32)[:, :],
                        k.sb("qela", [128, 1024], F32)[:, :]) for _ in range(2)]
        self._quant_free = [None, None]
        self._quant_slot = 0

    def quant_tile(self, src, dst, gs_col, g6_col, ginv2_col, gtoks, waits,
                   tag="qt"):
        k, p = self.k, self.k.p
        F = src.shape[-1]
        nb = F // 16
        slot = self._quant_slot
        self._quant_slot = 1 - slot
        slot = 0
        B = self.qsc_blk[slot]
        Y, A = self.qsc_el[slot]
        qf = [self._quant_free[slot]] if self._quant_free[slot] else []
        amax = B[:, 0:nb]; sc = B[:, nb:2 * nb]; ee = B[:, 2 * nb:3 * nb]
        yy = B[:, 3 * nb:4 * nb]; qq = amax; vv = sc
        y = Y[:, 0:F]; acc = A[:, 0:F]
        w0 = list(waits) + list(gtoks) + qf
        t = p.emit(VECTOR, lambda v: v.tensor_reduce(
            out=amax, in_=src.rearrange("p (b z) -> p b z", z=16), axis=AX.X,
            op=ALU.max, apply_absolute_value=True), waits=w0, tag=tag + "am")
        t = p.emit(VECTOR, lambda v: v.tensor_scalar(
            out=sc, in0=amax, scalar1=g6_col, scalar2=448.0, op0=ALU.mult,
            op1=ALU.min), waits=[t], tag=tag + "sc")
        t = p.emit(VECTOR, lambda v: v.tensor_scalar(
            out=sc, in0=sc, scalar1=1e-30, scalar2=None, op0=ALU.max),
            waits=[t], tag=tag + "scm")
        tl = p.emit(SCALAR, lambda s: s.activation(
            out=ee, in_=sc, func=AF.Ln), waits=[t], tag=tag + "ln")
        t = p.emit(VECTOR, lambda v: v.tensor_scalar(
            out=acc[:, 0:nb], in0=ee, scalar1=-5.0 * LN2, scalar2=-6.0,
            op0=ALU.is_ge, op1=ALU.add), waits=[tl], tag=tag + "e0")
        for kk in range(-4, 9):
            t = p.emit(VECTOR, (lambda kk: lambda v: v.scalar_tensor_tensor(
                out=acc[:, 0:nb], in0=ee, scalar=float(kk) * LN2,
                in1=acc[:, 0:nb], op0=ALU.is_ge, op1=ALU.add))(kk),
                waits=[t], tag=tag + "el")
        c3a, tk3a = k.c(3.0 * LN2)
        c3b, tk3b = k.c(-3.0 * LN2)
        tqi = p.emit(SCALAR, lambda s: s.activation(
            out=qq, in_=acc[:, 0:nb], func=AF.Exp, scale=-LN2, bias=c3a),
            waits=[t, tk3a], tag=tag + "qi")
        tq2 = p.emit(SCALAR, lambda s: s.activation(
            out=ee, in_=acc[:, 0:nb], func=AF.Exp, scale=LN2, bias=c3b),
            waits=[t, tk3b], tag=tag + "q2")
        t = p.emit(VECTOR, lambda v: v.tensor_tensor(
            out=vv, in0=sc, in1=qq, op=ALU.mult), waits=[tqi], tag=tag + "v")
        t = p.emit(VECTOR, lambda v: v.tensor_scalar(
            out=vv, in0=vv, scalar1=0.5, scalar2=None, op0=ALU.add),
            waits=[t], tag=tag + "v5")
        tz = p.emit(VECTOR, lambda v: v.tensor_scalar(
            out=yy, in0=vv, scalar1=1.0, scalar2=None, op0=ALU.is_ge),
            waits=[t], tag=tag + "f0")
        for kk in range(2, 17):
            tz = p.emit(VECTOR, (lambda kk: lambda v: v.scalar_tensor_tensor(
                out=yy, in0=vv, scalar=float(kk), in1=yy,
                op0=ALU.is_ge, op1=ALU.add))(kk), waits=[tz], tag=tag + "fl")
        ty = p.emit(VECTOR, lambda v: v.tensor_tensor(
            out=yy, in0=yy, in1=ee, op=ALU.mult), waits=[tz, tq2], tag=tag + "y")
        t = p.emit(VECTOR, lambda v: v.tensor_scalar(
            out=vv, in0=yy, scalar1=0.0, scalar2=None, op0=ALU.is_equal),
            waits=[ty], tag=tag + "z")
        t = p.emit(VECTOR, lambda v: v.tensor_tensor(
            out=vv, in0=vv, in1=yy, op=ALU.add), waits=[t], tag=tag + "zs")
        t = p.emit(VECTOR, lambda v: v.reciprocal(out=vv, in_=vv),
                   waits=[t], tag=tag + "rc")
        trdq = p.emit(VECTOR, lambda v: v.tensor_scalar(
            out=vv, in0=vv, scalar1=gs_col, scalar2=None, op0=ALU.mult),
            waits=[t], tag=tag + "rq")
        tdq = p.emit(VECTOR, lambda v: v.tensor_scalar(
            out=yy, in0=yy, scalar1=ginv2_col, scalar2=None, op0=ALU.mult),
            waits=[ty, trdq], tag=tag + "dq")
        def bc(col):
            return col.rearrange("p (b o) -> p b o", o=1).broadcast_to([128, nb, 16])
        sv3 = src.rearrange("p (b z) -> p b z", z=16)
        t = p.emit(VECTOR, lambda v: v.tensor_tensor(
            out=y.rearrange("p (b z) -> p b z", z=16), in0=sv3, in1=bc(vv),
            op=ALU.mult), waits=[trdq], tag=tag + "ye")
        t = p.emit(SCALAR, lambda s: s.activation(
            out=y, in_=y, func=AF.Abs), waits=[t], tag=tag + "abs")
        t0 = p.emit(VECTOR, lambda v: v.tensor_scalar(
            out=acc, in0=y, scalar1=0.25, scalar2=None, op0=ALU.is_ge),
            waits=[t], tag=tag + "n1")
        cum = t0
        for thr in [0.75, 1.25, 1.75, 2.5, 2.5, 3.5, 3.5,
                    5.0, 5.0, 5.0, 5.0]:
            cum = p.emit(VECTOR, (lambda thr: lambda v: v.scalar_tensor_tensor(
                out=acc, in0=y, scalar=float(thr), in1=acc,
                op0=ALU.is_ge, op1=ALU.add))(thr), waits=[cum], tag=tag + "na")
        tsg = p.emit(VECTOR, lambda v: v.tensor_scalar(
            out=y, in0=src, scalar1=0.0, scalar2=2.0, op0=ALU.is_ge,
            op1=ALU.mult), waits=[cum], tag=tag + "sg")
        tsg2 = p.emit(VECTOR, lambda v: v.scalar_tensor_tensor(
            out=y, in0=y, scalar=-1.0, in1=acc, op0=ALU.add,
            op1=ALU.mult), waits=[tsg], tag=tag + "sg2")
        tout = p.emit(VECTOR, lambda v: v.tensor_tensor(
            out=dst.rearrange("p (b z) -> p b z", z=16),
            in0=y.rearrange("p (b z) -> p b z", z=16), in1=bc(yy),
            op=ALU.mult), waits=[tsg2, tdq], tag=tag + "out")
        self._quant_free[slot] = tout
        return tout

    def p5_l1attn_apply(self):
        k, p = self.k, self.k.p
        g_h = k.din("l1a_g", [1, D]); b_h = k.din("l1a_b", [1, D])
        k.push_scope()
        g_col, g_tok = load_bias_cols(k, g_h, D, tag="g1c")
        b_col, b_tok = load_bias_cols(k, b_h, D, tag="b1c")
        tob = self.strm[:, :].bitcast(BF16)[:, 0:NCH * S]
        rho3 = k.sb("rho3", [128, S], BF16)
        om3 = k.sb("om3", [128, S], BF16)
        ldr = k.sb("ldr1", [128, S], BF16)
        fr = k.sb("fr1", [128, 512], F32)
        ld_free = [None]
        tob_toks = []
        for c in range(NCH):
            ri = 0
            ld = p.dma(SYNC, ldr[:, ri * S:(ri + 1) * S],
                       self.attn1[c * 128:(c + 1) * 128, :],
                       waits=self._p4_done + ([ld_free[ri]] if ld_free[ri] else []),
                       tag="a1ld")
            last = None
            for ts4 in range(NT4):
                a0 = ts4 * 512
                t1 = p.emit(VECTOR, (lambda c, a0: lambda v: v.tensor_tensor(
                    out=fr[:, :], in0=self.xTc[c][:, a0:a0 + 512],
                    in1=self.rho1[:, a0:a0 + 512], op=ALU.mult))(c, a0),
                    waits=self.ln1.toks +
                          ([self.xTc_tok[c]] if self.xTc_tok[c] else []) +
                          ([last] if last else []), tag="t1")
                t2 = p.emit(VECTOR, (lambda c, a0: lambda v: v.tensor_tensor(
                    out=fr[:, :], in0=fr[:, :],
                    in1=self.om1[:, a0:a0 + 512], op=ALU.add))(c, a0),
                    waits=[t1], tag="t2")
                t3 = p.emit(VECTOR, (lambda c, a0, ri: lambda v: v.tensor_tensor(
                    out=tob[:, c * S + a0: c * S + a0 + 512], in0=fr[:, :],
                    in1=ldr[:, ri * S + a0: ri * S + a0 + 512], op=ALU.add))(c, a0, ri),
                    waits=[t2, ld], tag="t3")
                last = t3
            ld_free[ri] = last
            tob_toks.append(last)
        tob_c = [tob[:, c * S:(c + 1) * S] for c in range(NCH)]
        scr = (self.strm[:, 8196:9220], self.strm[0:1, 9220:11268],
               self.strm[0:1, 11268:13316], self.strm[0:1, 13316:15364])
        ln3 = ln_stats(k, tob_c, tob_toks + [self._xq_done], self.ones1,
                       self.ones1_tok, self.ppool,
                       eps=1e-8, tag="ln3", out=(rho3, om3), scratch=scr)
        for c in range(NCH):
            last = None
            for ts4 in range(NT4):
                a0 = ts4 * 512
                t1 = p.emit(VECTOR, (lambda c, a0: lambda v: v.tensor_tensor(
                    out=fr[:, :], in0=tob[:, c * S + a0: c * S + a0 + 512],
                    in1=rho3[:, a0:a0 + 512], op=ALU.mult))(c, a0),
                    waits=ln3.toks + [tob_toks[c]] + ([last] if last else []),
                    tag="a3a")
                t2 = p.emit(VECTOR, (lambda c, a0: lambda v: v.tensor_tensor(
                    out=fr[:, :], in0=fr[:, :], in1=om3[:, a0:a0 + 512],
                    op=ALU.add))(c, a0), waits=[t1], tag="a3b")
                t3 = p.emit(VECTOR, (lambda c: lambda v: v.tensor_scalar(
                    out=fr[:, :], in0=fr[:, :], scalar1=g_col[:, c:c + 1],
                    scalar2=b_col[:, c:c + 1], op0=ALU.mult, op1=ALU.add))(c),
                    waits=[t2, g_tok, b_tok], tag="a3c")
                t4 = p.emit(VECTOR, (lambda c, a0: lambda v: v.tensor_tensor(
                    out=self.xTc[c][:, a0:a0 + 512],
                    in0=self.xTc[c][:, a0:a0 + 512], in1=fr[:, :],
                    op=ALU.add))(c, a0),
                    waits=[t3] + ([self.xTc_tok[c]] if self.xTc_tok[c] else []),
                    tag="a3d")
                last = t4
            self.xTc_tok[c] = last
        k.pop_scope()

    def p6_prepare(self):
        k, p = self.k, self.k.p
        gw = k.din("l1f_gw", [FI, D])
        uw = k.din("l1f_uw", [FI, D])
        dw = k.din("l1f_dw", [D, FI])
        sin_h = k.din("l1f_sin", [1, 1])
        sind_h = k.din("l1f_sind", [1, 1])
        self.gT = k.dram("gT", [D, FI], BF16)
        self.uT = k.dram("uT", [D, FI], BF16)
        self.dT = k.dram("dT", [FI, D], BF16)
        self.quant_init_scratch()
        gcols = k.sb("gcols", [128, 12], F32)
        gcols2 = k.sb("gcols2", [128, 3], F32)
        k.push_scope()
        PS = 2048
        wstg = k.sb("wstg6", [128, FI], F32)
        wq = k.sb("wq6", [128, FI], BF16)
        stg_free = [None, None]
        wq_free = None

        def scalar_cols(val_row, toks, idx, tagp):
            r2 = k.sb(tagp + "r", [1, 3], F32)
            t1 = p.emit(VECTOR, lambda v: v.tensor_copy(out=r2[0:1, 0:1],
                        in_=val_row[0:1, 0:1]), waits=toks, tag=tagp + "a")
            t2 = p.emit(VECTOR, lambda v: v.tensor_scalar(
                out=r2[0:1, 1:2], in0=r2[0:1, 0:1], scalar1=1.0 / 6.0,
                scalar2=None, op0=ALU.mult), waits=[t1], tag=tagp + "b")
            t3 = p.emit(VECTOR, lambda v: v.reciprocal(
                out=r2[0:1, 2:3], in_=r2[0:1, 0:1]), waits=[t1], tag=tagp + "c")
            t4 = p.emit(VECTOR, lambda v: v.tensor_scalar(
                out=r2[0:1, 2:3], in0=r2[0:1, 2:3], scalar1=0.5, scalar2=None,
                op0=ALU.mult), waits=[t3], tag=tagp + "d")
            tk = bcast_row(k, r2[0:1, 0:3], gcols[:, idx:idx + 3], self.ppool,
                           [t2, t4], tagp + "bc")
            return tk

        sin_r = k.sb("sinr", [1, 2], F32)
        tsi = p.dma(SYNC, sin_r[0:1, 0:1], sin_h[0:1, :], tag="sinld")
        tsd = p.dma(SYNC, sin_r[0:1, 1:2], sind_h[0:1, :], tag="sindld")
        tk_x = scalar_cols(sin_r[0:1, 0:1], [tsi], 0, "gx")
        tk_h = scalar_cols(sin_r[0:1, 1:2], [tsd], 3, "gh")

        mxrow = k.sb("mxrow", [1, 128], F32)

        def weight_gs2(hbm, nrow, ncol, idx, tagp):
            mx = k.sb(tagp + "mx2", [128, 2], F32)
            t0 = p.emit(VECTOR, lambda v: v.memset(mx[:, :], 0.0), tag=tagp + "z")
            last = t0
            nch = nrow // 128
            for c in range(nch):
                ri = c % 2
                ld = p.dma(SYNC, wstg[:, ri * PS:ri * PS + ncol],
                           hbm[c * 128:(c + 1) * 128, :],
                           waits=[stg_free[ri]] if stg_free[ri] else [],
                           tag=tagp + "ld")
                t1 = p.emit(VECTOR, (lambda ri, ncol: lambda v: v.tensor_reduce(
                    out=mx[:, 1:2], in_=wstg[:, ri * PS:ri * PS + ncol], axis=AX.X,
                    op=ALU.max, apply_absolute_value=True))(ri, ncol),
                    waits=[ld, last], tag=tagp + "r")
                t2 = p.emit(VECTOR, lambda v: v.tensor_tensor(
                    out=mx[:, 0:1], in0=mx[:, 0:1], in1=mx[:, 1:2], op=ALU.max),
                    waits=[t1], tag=tagp + "m")
                stg_free[ri] = t1
                last = t2
            j, bank, wb = self.ppool.acquire()
            tt = p.emit(TENSOR, (lambda bank: lambda te: te.transpose(
                out=bank[0:1, 0:128], in_=mx[:, 0:1],
                identity=self.idn_f32[:, :]))(bank),
                waits=[last, self.idn_f32_tok] + wb, tag=tagp + "tp")
            ev = p.emit(VECTOR, (lambda bank: lambda v: v.reduce_max(
                out=mxrow[0:1, 0:1], in_=bank[0:1, 0:128], axis=AX.X))(bank),
                waits=[tt], tag=tagp + "rx")
            self.ppool.release(j, ev)
            t3 = p.emit(VECTOR, lambda v: v.reciprocal(
                out=mxrow[0:1, 1:2], in_=mxrow[0:1, 0:1]), waits=[ev], tag=tagp + "rc")
            t4 = p.emit(VECTOR, lambda v: v.tensor_scalar(
                out=mxrow[0:1, 1:2], in0=mxrow[0:1, 1:2], scalar1=2688.0,
                scalar2=None, op0=ALU.mult), waits=[t3], tag=tagp + "gs")
            return scalar_cols(mxrow[0:1, 1:2], [t4], idx, tagp + "sc")

        tk_g = weight_gs2(gw, FI, D, 6, "gw")
        tk_u = weight_gs2(uw, FI, D, 9, "uw")
        def weight_gs_d(hbm, tagp):
            mx = k.sb(tagp + "mx2", [128, 2], F32)
            t0 = p.emit(VECTOR, lambda v: v.memset(mx[:, :], 0.0), tag=tagp + "z")
            last = t0
            for sc2 in range(2 * (D // 128)):
                c, h = sc2 // 2, sc2 % 2
                ri = sc2 % 2
                ld = p.dma(SYNC, wstg[:, ri * PS:(ri + 1) * PS],
                           hbm[c * 128:(c + 1) * 128, h * PS:(h + 1) * PS],
                           waits=[stg_free[ri]] if stg_free[ri] else [],
                           tag=tagp + "ld")
                t1 = p.emit(VECTOR, (lambda ri: lambda v: v.tensor_reduce(
                    out=mx[:, 1:2], in_=wstg[:, ri * PS:(ri + 1) * PS], axis=AX.X,
                    op=ALU.max, apply_absolute_value=True))(ri),
                    waits=[ld, last], tag=tagp + "r")
                t2 = p.emit(VECTOR, lambda v: v.tensor_tensor(
                    out=mx[:, 0:1], in0=mx[:, 0:1], in1=mx[:, 1:2], op=ALU.max),
                    waits=[t1], tag=tagp + "m")
                stg_free[ri] = t1
                last = t2
            j, bank, wb = self.ppool.acquire()
            tt = p.emit(TENSOR, (lambda bank: lambda te: te.transpose(
                out=bank[0:1, 0:128], in_=mx[:, 0:1],
                identity=self.idn_f32[:, :]))(bank),
                waits=[last, self.idn_f32_tok] + wb, tag=tagp + "tp")
            ev = p.emit(VECTOR, (lambda bank: lambda v: v.reduce_max(
                out=mxrow[0:1, 0:1], in_=bank[0:1, 0:128], axis=AX.X))(bank),
                waits=[tt], tag=tagp + "rx")
            self.ppool.release(j, ev)
            t3 = p.emit(VECTOR, lambda v: v.reciprocal(
                out=mxrow[0:1, 1:2], in_=mxrow[0:1, 0:1]), waits=[ev], tag=tagp + "rc")
            t4 = p.emit(VECTOR, lambda v: v.tensor_scalar(
                out=mxrow[0:1, 1:2], in0=mxrow[0:1, 1:2], scalar1=2688.0,
                scalar2=None, op0=ALU.mult), waits=[t3], tag=tagp + "gs")
            r2 = k.sb(tagp + "r", [1, 3], F32)
            t1b = p.emit(VECTOR, lambda v: v.tensor_copy(out=r2[0:1, 0:1],
                         in_=mxrow[0:1, 1:2]), waits=[t4], tag=tagp + "a")
            t2b = p.emit(VECTOR, lambda v: v.tensor_scalar(
                out=r2[0:1, 1:2], in0=r2[0:1, 0:1], scalar1=1.0 / 6.0,
                scalar2=None, op0=ALU.mult), waits=[t1b], tag=tagp + "b")
            t3b = p.emit(VECTOR, lambda v: v.reciprocal(
                out=r2[0:1, 2:3], in_=r2[0:1, 0:1]), waits=[t1b], tag=tagp + "c")
            t4b = p.emit(VECTOR, lambda v: v.tensor_scalar(
                out=r2[0:1, 2:3], in0=r2[0:1, 2:3], scalar1=0.5, scalar2=None,
                op0=ALU.mult), waits=[t3b], tag=tagp + "d")
            return bcast_row(k, r2[0:1, 0:3], gcols2[:, 0:3], self.ppool,
                             [t2b, t4b], tagp + "bc")
        tk_dw = weight_gs_d(dw, "dww")

        def quant_spill(hbm, scratch, nrow, ncol, gidx, gc, gtok, tagp):
            nch = nrow // 128
            sc2 = 0
            for c in range(nch):
                qts = []
                for h0 in range(0, ncol, PS):
                    w = min(PS, ncol - h0)
                    ri = sc2 % 2
                    sc2 += 1
                    ld = p.dma(SYNC, wstg[:, ri * PS:ri * PS + w],
                               hbm[c * 128:(c + 1) * 128, h0:h0 + w],
                               waits=[stg_free[ri]] if stg_free[ri] else [],
                               tag=tagp + "l2")
                    for f0 in range(0, w, 1024):
                        qt = self.quant_tile(
                            wstg[:, ri * PS + f0: ri * PS + f0 + 1024],
                            wq[:, h0 + f0:h0 + f0 + 1024],
                            gc[:, gidx:gidx + 1], gc[:, gidx + 1:gidx + 2],
                            gc[:, gidx + 2:gidx + 3], [gtok],
                            [ld] + ([wq_free[0]] if wq_free[0] else []),
                            tag=tagp + f"q{c}_{h0 + f0}")
                        qts.append(qt)
                    stg_free[ri] = qts[-1]
                last_tt = None
                for r in range(ncol // 128):
                    j, bank, wb = self.ppool.acquire()
                    tt = p.emit(TENSOR, (lambda r, bank: lambda te: te.transpose(
                        out=bank.bitcast(BF16)[:, 0:128],
                        in_=wq[:, r * 128:(r + 1) * 128],
                        identity=self.idn_bf[:, :]))(r, bank),
                        waits=[qts[r // 8], self.idn_bf_tok] + wb, tag=tagp + "tp")
                    ri2 = r % 4
                    ev = p.emit(SCALAR, (lambda r, ri2, bank: lambda s: s.copy(
                        out=wqt[:, ri2 * 128:(ri2 + 1) * 128],
                        in_=bank.bitcast(BF16)[:, 0:128]))(r, ri2, bank),
                        waits=[tt] + ([wqt_free[ri2]] if wqt_free[ri2] else []),
                        tag=tagp + "te")
                    self.ppool.release(j, ev)
                    dm = p.dma(SYNC, scratch[r * 128:(r + 1) * 128,
                                             c * 128:(c + 1) * 128],
                               wqt[:, ri2 * 128:(ri2 + 1) * 128], waits=[ev],
                               tag=tagp + "sp")
                    wqt_free[ri2] = dm
                    last_tt = tt
                wq_free[0] = last_tt
        wq_free = [None]
        wqt_free = [None] * 4
        wqt = k.sb("wqt6", [128, 4 * 128], BF16)
        quant_spill(gw, self.gT, FI, D, 6, gcols, tk_g, "gq")
        quant_spill(uw, self.uT, FI, D, 9, gcols, tk_u, "uq")
        quant_spill(dw, self.dT, D, FI, 0, gcols2, tk_dw, "dq")
        self._p6_sin_cols = (gcols, tk_x, tk_h)
        self._p6_done_w = p.fence()
        k.pop_scope()

    def p6_xq(self):
        k, p = self.k, self.k.p
        self.xqT = k.sb("xqT", [128, NCH * S], BF16)
        gcols = self._p6_sin_cols[0]
        tk_x = self._p6_sin_cols[1]
        k.push_scope()
        bcvq = self.strm[:, :].bitcast(BF16)
        xn_tok = bcvq[:, 0:NT * D]
        xq_tok = bcvq[:, NT * D:2 * NT * D]
        mirf = k.sb("mirf6", [128, 512], F32)
        mirb = k.sb("mirb6", [128, 512], BF16)
        tok_done = {}
        mfree = None
        for c in range(NCH):
            for ts4 in range(NT4):
                a0 = ts4 * 512
                mt = self.xn1_tile(mirf, mirb[:, :], c, a0, 512,
                                   [mfree] if mfree else [])
                evs = []
                for q4 in range(4):
                    tt = ts4 * 4 + q4
                    j, bank, wb = self.ppool.acquire()
                    tk = p.emit(TENSOR, (lambda q4, bank: lambda te: te.transpose(
                        out=bank.bitcast(BF16)[:, 0:128],
                        in_=mirb[:, q4 * 128:(q4 + 1) * 128],
                        identity=self.idn_bf[:, :]))(q4, bank),
                        waits=[mt, self.idn_bf_tok] + wb, tag="x6tp")
                    ev = p.emit(SCALAR, (lambda tt, c, bank: lambda s: s.copy(
                        out=xn_tok[:, tt * D + c * 128: tt * D + (c + 1) * 128],
                        in_=bank.bitcast(BF16)[:, 0:128]))(tt, c, bank),
                        waits=[tk], tag="x6ev")
                    self.ppool.release(j, ev)
                    evs.append(ev)
                    tok_done[(tt, c)] = ev
                mfree = evs[-1]
        q_toks = []
        for tt in range(NT):
            w_ = [tok_done[(tt, c)] for c in range(NCH)]
            qt = self.quant_tile(xn_tok[:, tt * D:(tt + 1) * D],
                                 xq_tok[:, tt * D:(tt + 1) * D],
                                 gcols[:, 0:1], gcols[:, 1:2], gcols[:, 2:3],
                                 [tk_x], w_, tag=f"xq{tt}")
            q_toks.append(qt)
        for tt in range(NT):
            for c in range(NCH):
                j, bank, wb = self.ppool.acquire()
                tk = p.emit(TENSOR, (lambda tt, c, bank: lambda te: te.transpose(
                    out=bank.bitcast(BF16)[:, 0:128],
                    in_=xq_tok[:, tt * D + c * 128: tt * D + (c + 1) * 128],
                    identity=self.idn_bf[:, :]))(tt, c, bank),
                    waits=[q_toks[tt], self.idn_bf_tok] + wb, tag="xq6tp")
                ev = p.emit(SCALAR, (lambda tt, c, bank: lambda s: s.copy(
                    out=self.xqT[:, c * S + tt * 128: c * S + (tt + 1) * 128],
                    in_=bank.bitcast(BF16)[:, 0:128]))(tt, c, bank),
                    waits=[tk], tag="xq6ev")
                self.ppool.release(j, ev)
                self._xqT_tok = ev
        self._xq_done = (TENSOR, p.counts[TENSOR])
        k.pop_scope()

    def p6_main(self):
        k, p = self.k, self.k.p
        gcols = self._p6_sin_cols[0]
        tk_h = self._p6_sin_cols[2]
        k.push_scope()
        gsl = k.sb("gsl", [128, 8 * 512], BF16)
        h_tok = None
        bcv6 = self.strm[:, :].bitcast(BF16)
        h_tok = bcv6[:, 0:4 * FI]
        hqT = bcv6[:, 4 * FI:4 * FI + 32 * 512]
        hq = k.sb("hq6", [128, FI // 2], BF16)
        dsl = k.sb("dsl", [128, 2 * D], BF16)
        sil = k.sb("sil", [128, 4 * 512], BF16)
        gsl_free = [None]
        hq_free = None
        hqT_free = None
        dsl_free = [None, None]
        for t5 in range(NT4):
            t0 = t5 * 512
            h_toks = {}
            for osl in range(8):
                o0 = osl * 512
                g_toks = [p.dma(SYNC, gsl[:, :].rearrange(
                                    "p (c w) -> p c w", c=8),
                                self.gT[:, o0:o0 + 512].rearrange(
                                    "(c p) w -> p c w", p=128),
                                waits=(self._p6_done_w if (t5 == 0 and osl == 0)
                                       else []) +
                                      ([gsl_free[0]] if gsl_free[0] else []),
                                tag="gl")]
                gbanks = []
                for q4 in range(4):
                    tt = t5 * 4 + q4
                    jg, bg, wbg = self.ppool.acquire()
                    mmg = None
                    for hh in range(NCH):
                        mmg = p.emit(TENSOR, (lambda hh, tt, bg, first, last:
                            lambda te: te.matmul(
                                out=bg[:, :],
                                lhsT=self.xqT[:, hh * S + tt * 128: hh * S + (tt + 1) * 128],
                                rhs=gsl[:, hh * 512:(hh + 1) * 512],
                                start=first, stop=last))(hh, tt, bg, hh == 0,
                                                         hh == NCH - 1),
                            waits=(wbg + [g_toks[-1], self._xqT_tok]
                                   if hh == 0 else []),
                            tag=f"g6{t5}_{osl}_{q4}")
                    gbanks.append((jg, bg, mmg))
                sv_toks = []
                for q4 in range(4):
                    jg, bg, mmg = gbanks[q4]
                    sv = p.emit(SCALAR, (lambda q4, bg: lambda s: s.activation(
                        out=sil[:, q4 * 512:(q4 + 1) * 512], in_=bg[:, :],
                        func=AF.Silu))(q4, bg),
                        waits=[mmg], tag=f"sl6{t5}_{osl}_{q4}")
                    self.ppool.release(jg, sv)
                    sv_toks.append(sv)
                u_toks = [p.dma(SYNC, gsl[:, :].rearrange(
                                    "p (c w) -> p c w", c=8),
                                self.uT[:, o0:o0 + 512].rearrange(
                                    "(c p) w -> p c w", p=128),
                                waits=[gbanks[-1][2]], tag="ul")]
                last_mmu = None
                for q4 in range(4):
                    tt = t5 * 4 + q4
                    ju, bu, wbu = self.ppool.acquire()
                    mmu = None
                    for hh in range(NCH):
                        mmu = p.emit(TENSOR, (lambda hh, tt, bu, first, last:
                            lambda te: te.matmul(
                                out=bu[:, :],
                                lhsT=self.xqT[:, hh * S + tt * 128: hh * S + (tt + 1) * 128],
                                rhs=gsl[:, hh * 512:(hh + 1) * 512],
                                start=first, stop=last))(hh, tt, bu, hh == 0,
                                                         hh == NCH - 1),
                            waits=(wbu + [u_toks[-1]] if hh == 0 else []),
                            tag=f"u6{t5}_{osl}_{q4}")
                    last_mmu = mmu
                    he = p.emit(VECTOR, (lambda q4, osl, bu: lambda v: v.tensor_tensor(
                        out=h_tok[:, q4 * FI + osl * 512: q4 * FI + osl * 512 + 512],
                        in0=bu[:, :], in1=sil[:, q4 * 512:(q4 + 1) * 512],
                        op=ALU.mult))(q4, osl, bu),
                        waits=[sv_toks[q4], mmu], tag=f"h6{t5}_{osl}_{q4}")
                    self.ppool.release(ju, he)
                    h_toks[(q4, osl)] = he
                gsl_free[0] = last_mmu
            hqT_toks = []
            for q4 in range(4):
                for hf in range(2):
                    qts = []
                    for f2 in range(2):
                        f0 = hf * 2048 + f2 * 1024
                        qt = self.quant_tile(
                            h_tok[:, q4 * FI + f0: q4 * FI + f0 + 1024],
                            hq[:, f2 * 1024:(f2 + 1) * 1024],
                            gcols[:, 3:4], gcols[:, 4:5], gcols[:, 5:6], [tk_h],
                            [h_toks[(q4, f0 // 512)], h_toks[(q4, f0 // 512 + 1)]] +
                            ([hq_free] if hq_free else []),
                            tag=f"hq{t5}_{q4}_{hf}")
                        qts.append(qt)
                    last_tt = None
                    for ic2 in range(16):
                        ic = hf * 16 + ic2
                        j, bank, wb = self.ppool.acquire()
                        tk = p.emit(TENSOR, (lambda ic2, bank: lambda te: te.transpose(
                            out=bank.bitcast(BF16)[:, 0:128],
                            in_=hq[:, ic2 * 128:(ic2 + 1) * 128],
                            identity=self.idn_bf[:, :]))(ic2, bank),
                            waits=[qts[ic2 // 8], self.idn_bf_tok] + wb +
                                  ([hqT_free] if (q4 == 0 and ic == 0 and hqT_free)
                                   else []),
                            tag="h6tp")
                        ev = p.emit(SCALAR, (lambda ic, q4, bank: lambda s: s.copy(
                            out=hqT[:, ic * 512 + q4 * 128: ic * 512 + (q4 + 1) * 128],
                            in_=bank.bitcast(BF16)[:, 0:128]))(ic, q4, bank),
                            waits=[tk], tag="h6te")
                        self.ppool.release(j, ev)
                        hqT_toks.append(ev)
                        last_tt = tk
                    hq_free = last_tt
            banks_o = []
            for o2 in range(8):
                j, bank, wb = self.ppool.acquire()
                banks_o.append((j, bank, wb))
            mm_last = [None] * 8
            for ic in range(32):
                ri = ic % 2
                ld = p.dma(SYNC, dsl[:, ri * D:(ri + 1) * D],
                           self.dT[ic * 128:(ic + 1) * 128, :],
                           waits=[dsl_free[ri]] if dsl_free[ri] else [], tag="dld")
                for o2 in range(8):
                    jj, bank, wb = banks_o[o2]
                    mm_last[o2] = p.emit(TENSOR, (lambda ic, o2, ri, bank, first, last:
                        lambda te: te.matmul(
                            out=bank[:, :],
                            lhsT=dsl[:, ri * D + o2 * 128: ri * D + (o2 + 1) * 128],
                            rhs=hqT[:, ic * 512:(ic + 1) * 512],
                            start=first, stop=last))(ic, o2, ri, bank,
                                                     ic == 0, ic == 31),
                        waits=[ld, hqT_toks[-1]] + (wb if ic == 0 else []),
                        tag=f"d6{t5}_{ic}_{o2}")
                dsl_free[ri] = mm_last[-1]
            for o2 in range(8):
                jj, bank, wb = banks_o[o2]
                ev = p.emit(VECTOR, (lambda o2, t0, bank: lambda v: v.tensor_tensor(
                    out=self.xTc[o2][:, t0:t0 + 512],
                    in0=self.xTc[o2][:, t0:t0 + 512], in1=bank[:, :],
                    op=ALU.add))(o2, t0, bank),
                    waits=[mm_last[o2]] +
                          ([self.xTc_tok[o2]] if self.xTc_tok[o2] else []),
                    tag=f"d6ev{t5}_{o2}")
                self.ppool.release(jj, ev)
                self.xTc_tok[o2] = ev
            hqT_free = mm_last[-1]
        k.pop_scope()

    def p7_output(self):
        k, p = self.k, self.k.p
        k.push_scope()
        oring = k.sb("oring", [128, 2 * D], F32)
        o_free = [None, None]
        last_dma = None
        for tt in range(NT):
            ri = tt % 2
            evs = []
            for c in range(NCH):
                j, bank, wb = self.ppool.acquire()
                tk = p.emit(TENSOR, (lambda tt, c, bank: lambda te: te.transpose(
                    out=bank[:, 0:128],
                    in_=self.xTc[c][:, tt * 128:(tt + 1) * 128],
                    identity=self.idn_f32[:, :]))(tt, c, bank),
                    waits=[self.idn_f32_tok, self.xTc_tok[c]] + wb +
                          ([o_free[ri]] if (c == 0 and o_free[ri]) else []),
                    tag="otp")
                ev = p.emit(SCALAR, (lambda c, ri, bank: lambda s: s.copy(
                    out=oring[:, ri * D + c * 128: ri * D + (c + 1) * 128],
                    in_=bank[:, 0:128]))(c, ri, bank),
                    waits=[tk], tag="oev")
                self.ppool.release(j, ev)
                evs.append(ev)
            dm = p.dma(SYNC, self.y_out[tt * 128:(tt + 1) * 128, :],
                       oring[:, ri * D:(ri + 1) * D], waits=evs, tag="odma")
            o_free[ri] = dm
            last_dma = dm
        p.emit(SYNC, lambda e: e.nop(), waits=[last_dma], inc=False, tag="fin")
        k.pop_scope()



def build_full():
    mk = MK()
    mk.p0_load()
    mk.close_p0()
    mk.p1_retention()
    mk.p2_stream_init()
    mk.p2_spill_xT()
    mk.k.pop_scope()
    mk.p2_layer(0)
    mk.p2_layer(1)
    mk.p2_apply()
    mk.p3_ln1()
    mk.p4_l1attn()
    mk.p6_prepare()
    mk.p6_xq()
    mk.p5_l1attn_apply()
    mk.p6_main()
    mk.k.pop_scope()
    mk.p7_output()
    return mk


def kernel(**inputs):
    x = np.ascontiguousarray(inputs["L0_attn_x"]).astype(np.float32, copy=False)
    mk = build_full()
    mk.k.p.install(mk.k.stack)
    base = {
        "l0a_wq": inputs["L0_attn_Wq"], "l0a_bq": inputs["L0_attn_bq"].reshape(1, -1),
        "l0a_wk": inputs["L0_attn_Wk"], "l0a_bk": inputs["L0_attn_bk"].reshape(1, -1),
        "l0a_wv": inputs["L0_attn_Wv"],
        "l1a_wq": inputs["L1_attn_Wq"], "l1a_bq": inputs["L1_attn_bq"].reshape(1, -1),
        "l1a_wk": inputs["L1_attn_Wk"], "l1a_bk": inputs["L1_attn_bk"].reshape(1, -1),
        "l1a_wv": inputs["L1_attn_Wv"], "l1a_bv": inputs["L1_attn_bv"].reshape(1, -1),
        "l1a_g": inputs["L1_attn_gamma"].reshape(1, -1),
        "l1a_b": inputs["L1_attn_beta"].reshape(1, -1),
        "l1f_gw": inputs["L1_ffn_gate_w"], "l1f_uw": inputs["L1_ffn_up_w"],
        "l1f_dw": inputs["L1_ffn_down_w"],
        "l1f_sin": inputs["L1_ffn_s_in"].reshape(1, 1),
        "l1f_sind": inputs["L1_ffn_s_in_down"].reshape(1, 1),
    }
    for l in range(2):
        base[f"f{l}_wqkv"] = inputs["L0_ffn_wqkv"][l]
        base[f"f{l}_qs"] = inputs["L0_ffn_q_scale"][l].reshape(1, -1)
        base[f"f{l}_ks"] = inputs["L0_ffn_k_scale"][l].reshape(1, -1)
        base[f"f{l}_wo"] = inputs["L0_ffn_wo"][l]
        base[f"f{l}_lng"] = inputs["L0_ffn_ln_g"][l].reshape(1, -1)
        base[f"f{l}_lnb"] = inputs["L0_ffn_ln_b"][l].reshape(1, -1)
        base[f"f{l}_w1"] = inputs["L0_ffn_w1"][l]
        base[f"f{l}_b1"] = inputs["L0_ffn_b1"][l].reshape(1, -1)
        base[f"f{l}_w2"] = inputs["L0_ffn_w2"][l]
        base[f"f{l}_b2"] = inputs["L0_ffn_b2"][l].reshape(1, -1)
    base = {kk: np.ascontiguousarray(np.asarray(v, dtype=np.float32))
            for kk, v in base.items()}
    in_maps = []
    for b in range(8):
        m = dict(base)
        m["x"] = np.ascontiguousarray(x[b])
        in_maps.append(m)
    res = run_bass_kernel_spmd(mk.k.nc, in_maps, core_ids=list(range(8)))
    out = np.stack([np.asarray(r["y"], dtype=np.float32) for r in res.results])
    try:
        mk.k.stack.close()
    except Exception:
        pass
    return out


# version 05 (final)
# speedup vs baseline: 1.0797x; 1.0050x over previous
import contextlib
import math
import numpy as np
import concourse.bass as bass
import concourse.mybir as mybir
from concourse.bass_utils import run_bass_kernel_spmd

F32 = mybir.dt.float32
BF16 = mybir.dt.bfloat16
AF = mybir.ActivationFunctionType
ALU = mybir.AluOpType
AX = mybir.AxisListType

TENSOR, VECTOR, SCALAR, GPSIMD, SYNC = "tensor", "vector", "scalar", "gpsimd", "sync"
COMPUTE_ENGINES = (TENSOR, VECTOR, SCALAR, GPSIMD)
DMA_QUEUES = (SYNC, GPSIMD, SCALAR)

S = 2048
D = 1024
NCH = D // 128
NT = S // 128
NT4 = S // 512
GAMMA = 0.96875
FH = 8
FDH = 64
FWIN = 512
FHID = 2730
FI = 4096
AH = 16
ADH = 64
LNG = math.log(GAMMA)
LN2 = math.log(2.0)
NEGB = -1.0e38



class Step:
    __slots__ = ("eng", "fn", "waits", "inc", "tok", "dma", "tag")

    def __init__(self, eng, fn, waits, inc, dma, tag):
        self.eng, self.fn, self.waits, self.inc, self.dma, self.tag = (
            eng, fn, list(waits), inc, dma, tag)
        self.tok = None


class Prog:
    def __init__(self, nc):
        self.nc = nc
        self.steps = []
        self.counts = {e: 0 for e in COMPUTE_ENGINES}
        self.dcounts = {q: 0 for q in DMA_QUEUES}

    pending_fence = None
    _fence_seen = None

    def _merge_fence(self, stream, waits):
        if self.pending_fence is None:
            return waits
        if stream in self._fence_seen:
            return waits
        self._fence_seen.add(stream)
        own = stream if stream in self.counts else None
        extra = [t for t in self.pending_fence
                 if t[0] != stream and t[0] != "dma_" + stream]
        if len(self._fence_seen) >= 5:
            self.pending_fence = None
        return list(waits) + extra

    def phase_fence(self):
        self.pending_fence = self.fence()
        self._fence_seen = set()

    def emit(self, eng, fn, waits=(), inc=True, tag=""):
        waits = self._merge_fence(eng, waits)
        s = Step(eng, fn, [t for t in waits if t], inc, False, tag)
        if inc:
            self.counts[eng] += 1
            s.tok = (eng, self.counts[eng])
        self.steps.append(s)
        return s.tok

    def dma(self, queue, out, in_, waits=(), inc=True, tag=""):
        def fn(eng):
            return eng.dma_start(out=out, in_=in_)
        return self.dma_fn(queue, fn, waits, inc, tag)

    def dma_fn(self, queue, fn, waits=(), inc=True, tag=""):
        waits = self._merge_fence(queue, waits)
        s = Step(queue, fn, [t for t in waits if t], inc, True, tag)
        if inc:
            self.dcounts[queue] += 16
            s.tok = ("dma_" + queue, self.dcounts[queue])
        self.steps.append(s)
        return s.tok

    def fence(self):
        toks = [(e, self.counts[e]) for e in COMPUTE_ENGINES if self.counts[e]]
        toks += [("dma_" + q, self.dcounts[q]) for q in DMA_QUEUES if self.dcounts[q]]
        return toks

    def check_deadlock(self):
        qs = {}
        for s in self.steps:
            qs.setdefault(s.eng, []).append(s)
        pos = {e: 0 for e in qs}
        counts = {}
        progress = True
        while progress:
            progress = False
            for e, lst in qs.items():
                while pos[e] < len(lst):
                    s = lst[pos[e]]
                    if all(counts.get(sm, 0) >= v for sm, v in s.waits):
                        if s.tok is not None:
                            counts[s.tok[0]] = s.tok[1]
                        pos[e] += 1
                        progress = True
                    else:
                        break
        bad = []
        for e, lst in qs.items():
            if pos[e] < len(lst):
                s = lst[pos[e]]
                bad.append(f"{e}[{pos[e]}/{len(lst)}] tag={s.tag} waits={s.waits}")
        if bad:
            raise RuntimeError("DEADLOCK:\n" + "\n".join(bad))

    def install(self, stack):
        self.check_deadlock()
        nc = self.nc
        sems = {}
        for e in COMPUTE_ENGINES:
            sems[e] = stack.enter_context(nc.semaphore("sem_" + e))
        for q in DMA_QUEUES:
            sems["dma_" + q] = stack.enter_context(nc.semaphore("sdma_" + q))
        block = stack.enter_context(nc.Block())
        bysec = {}
        for s in self.steps:
            bysec.setdefault(s.eng, []).append(s)

        def emitter(name):
            def run(eng):
                for s in bysec.get(name, []):
                    for sm, val in s.waits:
                        eng.wait_ge(sems[sm], val)
                    inst = s.fn(eng)
                    if s.tok is not None:
                        inst.then_inc(sems[s.tok[0]], 16 if s.dma else 1)
            return run

        block.tensor(emitter(TENSOR))
        block.vector(emitter(VECTOR))
        block.scalar(emitter(SCALAR))
        block.gpsimd(emitter(GPSIMD))
        block.sync(emitter(SYNC))


class Pool:

    def __init__(self, items):
        self.items = list(items)
        self.free_tok = [None] * len(self.items)
        self.i = 0

    def acquire(self):
        j = self.i % len(self.items)
        self.i += 1
        tok = self.free_tok[j]
        return j, self.items[j], ([tok] if tok else [])

    def release(self, j, tok):
        self.free_tok[j] = tok


class K:

    def __init__(self):
        self.nc = bass.Bass("TRN2", target_bir_lowering=False, debug=False, detect_race_conditions=False)
        self.stack = contextlib.ExitStack()
        self.scopes = [self.stack]
        self.p = Prog(self.nc)
        self._uid = 0
        self.dbg = {}

    def uid(self):
        self._uid += 1
        return self._uid

    def push_scope(self):
        sc = contextlib.ExitStack()
        self.scopes.append(sc)
        return sc

    def pop_scope(self):
        sc = self.scopes.pop()
        sc.close()
        self.p.phase_fence()

    def sb(self, name, shape, dt):
        import os
        h = self.scopes[-1].enter_context(
            self.nc.sbuf_tensor(f"{name}{self.uid()}", list(shape), dt))
        if os.environ.get("SBUF_LOG"):
            import numpy as _np
            nb = int(_np.prod(shape[1:])) * (2 if dt == BF16 else 4)
            print(f"SBUF {name} {shape} {nb}B rem={self.nc.sbuf_bytes_remaining}")
        return h

    _consts = None

    def c(self, value):
        if self._consts is None:
            self._consts = {}
        if value not in self._consts:
            t = self.scopes[0].enter_context(
                self.nc.sbuf_tensor(f"cst{self.uid()}", [128, 1], F32))
            tok = self.p.emit(GPSIMD, (lambda t, value: lambda g: g.memset(
                t[:, :], float(value)))(t, value), tag="const")
            self._consts[value] = (t[:, 0:1], tok)
        return self._consts[value]

    def din(self, name, shape, dt=F32):
        return self.nc.dram_tensor(name, list(shape), dt, kind="ExternalInput").ap()

    def dout(self, name, shape, dt=F32):
        return self.nc.dram_tensor(name, list(shape), dt, kind="ExternalOutput").ap()

    def dram(self, name, shape, dt):
        return self.nc.dram_tensor(f"{name}{self.uid()}", list(shape), dt,
                                   kind="Internal").ap()

    def dbg_dump(self, name, aps, dt=F32):
        total = sum(a.shape[1] for a in aps)
        d = self.dout("dbg_" + name, [128, total], dt)
        toks = self.p.fence()
        off = 0
        dts = []
        for a in aps:
            dts.append(self.p.dma(SYNC, d[0:a.shape[0], off:off + a.shape[1]], a,
                                  waits=toks, tag="dbg"))
            off += a.shape[1]
        self.p.emit(SYNC, lambda e: e.nop(), waits=[dts[-1]], inc=False, tag="dbgw")
        self.dbg[name] = d



def affine_mask(k, t, ncols, compare_op, base, cmul, fill, set_to=1.0, tag="am",
                piece=512, step=1):
    p = k.p
    def fn(g):
        last = None
        for c0 in range(0, ncols, piece):
            wdt = min(piece, ncols - c0)
            g.memset(t[:, c0:c0 + wdt], float(set_to))
            last = g.affine_select(out=t[:, c0:c0 + wdt], in_=t[:, c0:c0 + wdt],
                                   pattern=[[step, wdt]], compare_op=compare_op,
                                   fill=float(fill), base=int(base + step * c0),
                                   channel_multiplier=int(cmul))
        return last
    return p.emit(GPSIMD, fn, tag=tag)


def build_identity(k, dt, tag):
    idn = k.sb("idn" + tag, [128, 128], dt)
    def fn(g):
        g.memset(idn[:, :], 1.0)
        return g.affine_select(out=idn[:, :], in_=idn[:, :], pattern=[[-1, 128]],
                               compare_op=ALU.is_equal, fill=0.0, base=0,
                               channel_multiplier=1)
    tok = k.p.emit(GPSIMD, fn, tag="idn")
    return idn, tok


def bcast_row(k, row_ap, out_ap, psum_pool, waits, tag, dtype_bf=False):
    p = k.p
    if not hasattr(k, "_ones_row"):
        k._ones_row = k.scopes[0].enter_context(
            k.nc.sbuf_tensor(f"onesr{k.uid()}", [1, 128], F32))
        k._ones_row_tok = p.emit(GPSIMD, lambda g: g.memset(k._ones_row[:, :], 1.0),
                                 tag="onesr")
    n = out_ap.shape[1]
    last = None
    for t0 in range(0, n, 512):
        wdt = min(512, n - t0)
        j, bank, wb = psum_pool.acquire()
        mm = p.emit(TENSOR, (lambda bank, t0, wdt: lambda te: te.matmul(
            out=bank[:, 0:wdt], lhsT=k._ones_row[0:1, :],
            rhs=row_ap[0:1, t0:t0 + wdt], start=True, stop=True))(bank, t0, wdt),
            waits=list(waits) + wb + [k._ones_row_tok], tag=tag + "_mm")
        ev = p.emit(VECTOR, (lambda bank, t0, wdt: lambda v: v.tensor_copy(
            out=out_ap[:, t0:t0 + wdt], in_=bank[:, 0:wdt]))(bank, t0, wdt),
            waits=[mm], tag=tag + "_ev")
        psum_pool.release(j, ev)
        last = ev
    return last


def load_bias_cols(k, hbm_row, n=D, waits=(), tag="bias"):
    ncol = n // 128
    t = k.sb(tag, [128, ncol], F32)
    def fn(eng):
        return eng.dma_start(out=t[:, :],
                             in_=hbm_row[0:1, :].rearrange("o (c q) -> (o q) c", q=128),
                             allow_slow_non_contiguous=True)
    tok = k.p.dma_fn(SYNC, fn, waits, True, tag)
    return t, tok


class LNStats:

    def __init__(self, rho_bc, om_bc, toks, mu_row=None, r_row=None):
        self.rho_bc, self.om_bc, self.toks = rho_bc, om_bc, toks
        self.mu_row, self.r_row = mu_row, r_row


def ln_stats(k, chunk_aps, chunk_toks, ones1, ones1_tok, psum_pool, eps=1e-5,
             tag="ln", out=None, upto=99, scratch=None):
    p = k.p
    nc = NCH
    if scratch is not None:
        sq_ring, sum_ps, sq_ps, om_row = scratch
    else:
        sum_ps = k.sb(tag + "_sum", [1, S], F32)
        sq_ps = k.sb(tag + "_sq", [1, S], F32)
        sq_ring = k.sb(tag + "_sqr", [128, 2 * 512], F32)
    sq_ring_toks = [None, None]
    ones_c, ones_c_tok = ones1, ones1_tok
    if chunk_aps[0].dtype == BF16:
        ones_c, ones_c_tok = k._ones1b, k._ones1b_tok
    ev_toks = []
    for ts in range(NT4):
        j, bank, wb = psum_pool.acquire()
        mm = None
        for c in range(nc):
            mm = p.emit(TENSOR, (lambda bank, c, ts, first, last: lambda te: te.matmul(
                out=bank[0:1, :], lhsT=ones_c[:, 0:1],
                rhs=chunk_aps[c][:, ts * 512:(ts + 1) * 512],
                start=first, stop=last))(bank, c, ts, c == 0, c == nc - 1),
                waits=([ones_c_tok] + wb + list(chunk_toks) if c == 0 else []),
                tag=f"{tag}_sum{ts}")
        ev = p.emit(VECTOR, (lambda bank, ts: lambda v: v.tensor_copy(
            out=sum_ps[0:1, ts * 512:(ts + 1) * 512], in_=bank[0:1, :]))(bank, ts),
            waits=[mm], tag=f"{tag}_sumev{ts}")
        ev_toks.append(ev)
        psum_pool.release(j, ev)
        import os
        if os.environ.get("SKIPSQ"):
            continue
        j2, bank2, wb2 = psum_pool.acquire()
        mm2 = None
        for c in range(nc):
            ri = c % 2
            sqt = p.emit(SCALAR, (lambda c, ts, ri: lambda s: s.activation(
                out=sq_ring[:, ri * 512:(ri + 1) * 512],
                in_=chunk_aps[c][:, ts * 512:(ts + 1) * 512], func=AF.Square))(c, ts, ri),
                waits=(list(chunk_toks) if (ts == 0 and c < 2) else []) +
                      ([sq_ring_toks[ri]] if sq_ring_toks[ri] else []),
                tag=f"{tag}_sq{ts}_{c}")
            mm2 = p.emit(TENSOR, (lambda bank2, ri, first, last: lambda te: te.matmul(
                out=bank2[0:1, :], lhsT=ones1[:, 0:1],
                rhs=sq_ring[:, ri * 512:(ri + 1) * 512],
                start=first, stop=last))(bank2, ri, c == 0, c == nc - 1),
                waits=[sqt] + (wb2 + [ones1_tok] if c == 0 else []),
                tag=f"{tag}_sqmm{ts}")
            sq_ring_toks[ri] = mm2
        ev2 = p.emit(VECTOR, (lambda bank2, ts: lambda v: v.tensor_copy(
            out=sq_ps[0:1, ts * 512:(ts + 1) * 512], in_=bank2[0:1, :]))(bank2, ts),
            waits=[mm2], tag=f"{tag}_sqev{ts}")
        ev_toks.append(ev2)
        psum_pool.release(j2, ev2)
    mu = sum_ps
    r = sq_ps
    om = om_row if scratch is not None else k.sb(tag + "_om", [1, S], F32)
    res0 = LNStats(None, None, [], mu, r)
    res0.sum_ps, res0.sq_ps, res0.om = sum_ps, sq_ps, om
    if upto <= 0:
        return res0
    t1 = p.emit(VECTOR, lambda v: v.tensor_scalar(
        out=mu[0:1, :], in0=sum_ps[0:1, :], scalar1=1.0 / D, scalar2=None,
        op0=ALU.mult), waits=ev_toks, tag=tag + "_mu")
    t2 = p.emit(VECTOR, lambda v: v.tensor_tensor(
        out=om[0:1, :], in0=mu[0:1, :], in1=mu[0:1, :], op=ALU.mult),
        waits=[t1], tag=tag + "_mu2")
    t3 = p.emit(VECTOR, lambda v: v.scalar_tensor_tensor(
        out=r[0:1, :], in0=sq_ps[0:1, :], scalar=1.0 / D, in1=om[0:1, :],
        op0=ALU.mult, op1=ALU.subtract), waits=[t2], tag=tag + "_var")
    if upto <= 1:
        return res0
    epsc, epstok = k.c(float(eps))
    t4 = p.emit(SCALAR, lambda s: s.activation(
        out=r[0:1, :], in_=r[0:1, :], func=AF.Sqrt, bias=epsc[0:1, :]),
        waits=[t3, epstok], tag=tag + "_sqrt")
    t5 = p.emit(VECTOR, lambda v: v.reciprocal(out=r[0:1, :], in_=r[0:1, :]),
                waits=[t4], tag=tag + "_recip")
    t6 = p.emit(VECTOR, lambda v: v.scalar_tensor_tensor(
        out=om[0:1, :], in0=mu[0:1, :], scalar=-1.0, in1=r[0:1, :],
        op0=ALU.mult, op1=ALU.mult), waits=[t5, t1], tag=tag + "_om")
    if out is not None:
        rho_bc, om_bc = out
    else:
        rho_bc = k.sb(tag + "_rho_bc", [128, S], F32)
        om_bc = k.sb(tag + "_om_bc", [128, S], F32)
    if upto <= 2:
        res0.toks = [t6]
        return res0
    t7 = bcast_row(k, r[0:1, :], rho_bc[:, :], psum_pool, [t5], tag + "_bcr")
    t8 = bcast_row(k, om[0:1, :], om_bc[:, :], psum_pool, [t6], tag + "_bco")
    res = LNStats(rho_bc, om_bc, [t7, t8], mu, r)
    res.sum_ps, res.sq_ps, res.om = sum_ps, sq_ps, om
    return res



class MK:
    def __init__(self):
        self.k = K()
        k, p = self.k, self.k.p
        self.x_in = k.din("x", [S, D])
        self.y_out = k.dout("y", [S, D])
        self.win = {}
        self.banks = [k.stack.enter_context(
            k.nc.psum_tensor(f"bank{i}", [128, 512], F32)) for i in range(8)]
        self.ppool = Pool(self.banks)
        self.idn_f32, self.idn_f32_tok = build_identity(k, F32, "f")
        self.idn_bf, self.idn_bf_tok = build_identity(k, BF16, "b")
        self.ones1 = k.sb("ones1", [128, 1], F32)
        self.ones1_tok = p.emit(GPSIMD, lambda g: g.memset(self.ones1[:, :], 1.0),
                                tag="ones1")
        self.ones1b = k.sb("ones1b", [128, 1], BF16)
        self.ones1b_tok = p.emit(VECTOR, lambda v: v.memset(self.ones1b[:, :], 1.0),
                                 tag="ones1b")
        k._ones1b, k._ones1b_tok = self.ones1b, self.ones1b_tok
        for v in (1e-5, 1e-8, 0.5, -16.0, 1.0, -1.0, 25.6 / 89.6,
                  3.0 * LN2, -3.0 * LN2):
            k.c(v)
        k._ones_row = k.scopes[0].enter_context(
            k.nc.sbuf_tensor(f"onesr{k.uid()}", [1, 128], F32))
        k._ones_row_tok = p.emit(GPSIMD, lambda g: g.memset(k._ones_row[:, :], 1.0),
                                 tag="onesr")
        self.xT = k.sb("xT", [128, NCH * S], F32)
        self.xTc = [self.xT[:, c * S:(c + 1) * S] for c in range(NCH)]
        self.xTc_tok = [None] * NCH
        self.SW = S + 1
        self.strm = k.sb("strm", [128, NCH * self.SW], F32)
        self.strmc = [self.strm[:, c * self.SW:(c + 1) * self.SW] for c in range(NCH)]
        self.strm_tok = [None] * NCH

    def din(self, name, shape):
        ap = self.k.din(name, shape)
        self.win[name] = ap
        return ap

    def p0_load(self, do_ln=True, upto=99):
        k, p = self.k, self.k.p
        k.push_scope()
        rho_bc = k.sb("ln0_rho_bc", [128, S], BF16)
        om_bc = k.sb("ln0_om_bc", [128, S], BF16)
        k.push_scope()
        x_tok = self.strm[:, 0:NT * D]
        ld = []
        for tt in range(NT):
            ld.append(p.dma(SYNC, x_tok[:, tt * D:(tt + 1) * D],
                            self.x_in[tt * 128:(tt + 1) * 128, :], tag="ldx"))
        ch_done = [[] for _ in range(NCH)]
        for tt in range(NT):
            for c in range(NCH):
                j, bank, wb = self.ppool.acquire()
                src = x_tok[:, tt * D + c * 128: tt * D + (c + 1) * 128]
                tk = p.emit(TENSOR, (lambda src, bank: lambda te: te.transpose(
                    out=bank[:, 0:128], in_=src, identity=self.idn_f32[:, :]))(src, bank),
                    waits=[ld[tt], self.idn_f32_tok] + wb, tag=f"tpx{tt}_{c}")
                dst = self.xTc[c][:, tt * 128:(tt + 1) * 128]
                tv = p.emit(VECTOR, (lambda bank, dst: lambda v: v.tensor_copy(
                    out=dst, in_=bank[:, 0:128]))(bank, dst),
                    waits=[tk], tag=f"tpc{tt}_{c}")
                ch_done[c].append(tv)
                self.ppool.release(j, tv)
        self.x_ld_toks = [t for lst in ch_done for t in lst]
        for c in range(NCH):
            self.xTc_tok[c] = ch_done[c][-1]
        if not do_ln:
            self.ln0 = None
            return
        self.ln0 = ln_stats(k, self.xTc, self.x_ld_toks, self.ones1,
                            self.ones1_tok, self.ppool, tag="ln0",
                            out=(rho_bc, om_bc), upto=upto)

    def close_p0(self):
        self.k.pop_scope()

    def xn_tile(self, ring_f32, ring_bf, c, a0, wdt, waits, ln=None):
        k, p = self.k, self.k.p
        ln = ln or self.ln0
        t1 = p.emit(VECTOR, lambda v: v.tensor_tensor(
            out=ring_f32[:, 0:wdt], in0=self.xTc[c][:, a0:a0 + wdt],
            in1=ln.rho_bc[:, a0:a0 + wdt], op=ALU.mult),
            waits=list(waits) + ln.toks + ([self.xTc_tok[c]] if self.xTc_tok[c] else []),
            tag="xnt1")
        t2 = p.emit(VECTOR, lambda v: v.tensor_tensor(
            out=ring_bf[:, 0:wdt], in0=ring_f32[:, 0:wdt],
            in1=ln.om_bc[:, a0:a0 + wdt], op=ALU.add), waits=[t1], tag="xnt2")
        return t2

    def p1_retention(self):
        k, p = self.k, self.k.p
        import os as _os
        wq = k.din("l0a_wq", [D, D]); bq = k.din("l0a_bq", [1, D])
        wk = k.din("l0a_wk", [D, D]); bk = k.din("l0a_bk", [1, D])
        wv = k.din("l0a_wv", [D, D])
        self.attn0 = k.dram("attn0", [128, NCH * S], BF16)

        k.push_scope()
        if not _os.environ.get("SKIP_BIAS"):
            bq_col, bq_tok = load_bias_cols(k, bq, D, tag="bqc")
            bk_col, bk_tok = load_bias_cols(k, bk, D, tag="bkc")
        else:
            bq_col = k.sb("bqc", [128, 8], F32); bq_tok = None
            bk_col = k.sb("bkc", [128, 8], F32); bk_tok = None
        gi_bc = k.sb("gi_bc", [128, S], BF16)
        gj_bc = k.sb("gj_bc", [128, S], BF16)
        k.push_scope()
        ti = k.sb("ti", [1, S], F32)
        t_iota = p.emit(GPSIMD, lambda g: g.iota(
            out=ti[0:1, :], pattern=[[1, S]], base=0, channel_multiplier=0,
            allow_small_or_imprecise_dtypes=True), tag="ti")
        gi_row = k.sb("gi_row", [1, S], F32)
        gj_row = k.sb("gj_row", [1, S], F32)
        tg1 = p.emit(SCALAR, lambda s: s.activation(
            out=gi_row[0:1, :], in_=ti[0:1, :], func=AF.Exp, scale=LNG),
            waits=[t_iota], tag="gi")
        tg2 = p.emit(SCALAR, lambda s: s.activation(
            out=gj_row[0:1, :], in_=ti[0:1, :], func=AF.Exp, scale=-LNG),
            waits=[t_iota], tag="gj")
        if not _os.environ.get("SKIP_DECAYBC"):
            tg3 = bcast_row(k, gi_row[0:1, :], gi_bc[:, :], self.ppool, [tg1], "gibc")
            tg4 = bcast_row(k, gj_row[0:1, :], gj_bc[:, :], self.ppool, [tg2], "gjbc")
        else:
            tg3 = tg4 = tg2
        k.pop_scope()
        b01 = k.sb("b01", [128, 1024], F32)
        tb01 = affine_mask(k, b01, 1024, ALU.is_ge, -512, -1, 0.0, tag="b01")

        bcv = self.strm[:, :].bitcast(BF16)
        kT = bcv[:, 0:NCH * S]
        v_tok = bcv[:, NCH * S:2 * NCH * S]
        mir_f = k.sb("mir_f", [128, 512], F32)
        mir = k.sb("mir", [128, 8 * 512], BF16)
        wstg = k.sb("wstg", [128, D], F32)
        wbf = k.sb("wbf", [128, D], BF16)
        sT = k.sb("sT", [128, 16 * 512], BF16)
        q_slab = k.sb("q_slab", [128, 8 * 512], BF16)
        out_ring = k.sb("out_ring", [128, 2 * 512], BF16)
        out_free = [None] * 2
        self._mir_free = [None] * 8
        self._wstg_free = None

        def prep_wT(hbm, dst, dst_toks, tagp):
            for o in range(NCH):
                ld = p.dma(SYNC, wstg[:, :], hbm[o * 128:(o + 1) * 128, :],
                           waits=([self._wstg_free] if self._wstg_free else []),
                           tag=tagp + "ld")
                ct = p.emit(SCALAR, lambda s: s.copy(out=wbf[:, :], in_=wstg[:, :]),
                            waits=[ld], tag=tagp + "cast")
                last_ev = None
                for h in range(NCH):
                    j, bank, wb = self.ppool.acquire()
                    tt = p.emit(TENSOR, (lambda h, bank: lambda te: te.transpose(
                        out=bank.bitcast(BF16)[:, 0:128],
                        in_=wbf[:, h * 128:(h + 1) * 128],
                        identity=self.idn_bf[:, :]))(h, bank),
                        waits=[ct, self.idn_bf_tok] + wb, tag=tagp + "tp")
                    ev = p.emit(VECTOR, (lambda h, o, bank: lambda v: v.tensor_copy(
                        out=dst[:, h * D + o * 128: h * D + (o + 1) * 128],
                        in_=bank.bitcast(BF16)[:, 0:128]))(h, o, bank),
                        waits=[tt], tag=tagp + "ev")
                    self.ppool.release(j, ev)
                    dst_toks[h][o] = ev
                    last_ev = tt
                self._wstg_free = last_ev

        def build_mirror(a0):
            toks = []
            for h in range(NCH):
                w = [self._mir_free[h]] if self._mir_free[h] else []
                t2 = self.xn_tile(mir_f, mir[:, h * 512:(h + 1) * 512], h, a0,
                                  512, w)
                toks.append(t2)
            return toks

        kT_tok = [[None] * NT4 for _ in range(NCH)]
        v_tok_tok = [[None, None] for _ in range(NT)]
        import os
        _phases = os.environ.get("RET_PHASES", "kvq")
        _stop = os.environ.get("RET_STOP", "")

        k.push_scope()
        wkT = k.sb("wkT", [128, NCH * D], BF16)
        wkT_toks = [[None] * NCH for _ in range(NCH)]
        if _stop == "early1":
            self._ret_dbg = dict(kT=kT, v_tok=v_tok, sT=sT, q_slab=q_slab,
                                 gi_bc=gi_bc, gj_bc=gj_bc, mir=mir, b01=b01, wkT=wkT)
            self._p1_done = p.fence()
            return
        prep_wT(wk, wkT, wkT_toks, "wk")
        if _stop == "early2":
            self._ret_dbg = dict(kT=kT, v_tok=v_tok, sT=sT, q_slab=q_slab,
                                 gi_bc=gi_bc, gj_bc=gj_bc, mir=mir, b01=b01, wkT=wkT)
            self._p1_done = p.fence()
            return
        for ts in range(NT4):
            a0 = ts * 512
            mtoks = build_mirror(a0)
            last_mm = None
            for o in range(NCH):
                j, bank, wb = self.ppool.acquire()
                mm = None
                for h in range(NCH):
                    mm = p.emit(TENSOR, (lambda o, h, bank, first, last:
                        lambda te: te.matmul(
                            out=bank[:, :],
                            lhsT=wkT[:, h * D + o * 128: h * D + (o + 1) * 128],
                            rhs=mir[:, h * 512:(h + 1) * 512],
                            start=first, stop=last))(o, h, bank, h == 0, h == NCH - 1),
                        waits=(wb + mtoks + [wkT_toks[hh][o] for hh in range(NCH)]
                               if h == 0 else []),
                        tag=f"kmm{ts}_{o}")
                last_mm = mm
                ev = p.emit(VECTOR, (lambda o, a0, bank: lambda v:
                    v.scalar_tensor_tensor(
                        out=kT[:, o * S + a0: o * S + a0 + 512],
                        in0=bank[:, :], scalar=bk_col[:, o:o + 1],
                        in1=gj_bc[:, a0:a0 + 512],
                        op0=ALU.add, op1=ALU.mult))(o, a0, bank),
                    waits=[mm, bk_tok, tg4], tag=f"kev{ts}_{o}")
                self.ppool.release(j, ev)
                kT_tok[o][ts] = ev
            for h in range(NCH):
                self._mir_free[h] = last_mm
        k.pop_scope()

        if "v" not in _phases:
            self._ret_dbg = dict(kT=kT, v_tok=v_tok, sT=sT, q_slab=q_slab,
                                 gi_bc=gi_bc, gj_bc=gj_bc, mir=mir, b01=b01)
            self._p1_done = p.fence()
            return
        k.push_scope()
        wvT = k.sb("wvT", [128, NCH * D], BF16)
        wvT_toks = [[None] * NCH for _ in range(NCH)]
        prep_wT(wv, wvT, wvT_toks, "wv")
        for ts in range(NT4):
            a0 = ts * 512
            mtoks = build_mirror(a0)
            last_mm = None
            for q4 in range(4):
                tt_idx = ts * 4 + q4
                for oh in range(2):
                    j, bank, wb = self.ppool.acquire()
                    mm = None
                    for h in range(NCH):
                        mm = p.emit(TENSOR, (lambda h, q4, oh, bank, first, last:
                            lambda te: te.matmul(
                                out=bank[:, :],
                                lhsT=mir[:, h * 512 + q4 * 128: h * 512 + (q4 + 1) * 128],
                                rhs=wvT[:, h * D + oh * 512: h * D + (oh + 1) * 512],
                                start=first, stop=last))(h, q4, oh, bank, h == 0,
                                                         h == NCH - 1),
                            waits=(wb + mtoks +
                                   [wvT_toks[hh][o2] for hh in range(NCH)
                                    for o2 in range(oh * 4, oh * 4 + 4)]
                                   if h == 0 else []),
                            tag=f"vmm{tt_idx}_{oh}")
                    last_mm = mm
                    ev = p.emit(SCALAR, (lambda tt_idx, oh, bank: lambda s: s.copy(
                        out=v_tok[:, tt_idx * D + oh * 512: tt_idx * D + (oh + 1) * 512],
                        in_=bank[:, :]))(tt_idx, oh, bank),
                        waits=[mm], tag=f"vev{tt_idx}_{oh}")
                    self.ppool.release(j, ev)
                    v_tok_tok[tt_idx][oh] = ev
            for h in range(NCH):
                self._mir_free[h] = last_mm
        k.pop_scope()

        if "q" not in _phases:
            self._ret_dbg = dict(kT=kT, v_tok=v_tok, sT=sT, q_slab=q_slab,
                                 gi_bc=gi_bc, gj_bc=gj_bc, mir=mir, b01=b01)
            self._p1_done = p.fence()
            return
        k.push_scope()
        wqT = k.sb("wqT", [128, NCH * D], BF16)
        wqT_toks = [[None] * NCH for _ in range(NCH)]
        prep_wT(wq, wqT, wqT_toks, "wq")
        q_free = [None] * NCH
        sT_free = None
        for T in range(NT4):
            i0 = T * 512
            mtoks = build_mirror(i0)
            q_toks = []
            last_q_mm = None
            for o in range(NCH):
                j, bank, wb = self.ppool.acquire()
                mm = None
                for h in range(NCH):
                    mm = p.emit(TENSOR, (lambda o, h, bank, first, last:
                        lambda te: te.matmul(
                            out=bank[:, :],
                            lhsT=wqT[:, h * D + o * 128: h * D + (o + 1) * 128],
                            rhs=mir[:, h * 512:(h + 1) * 512],
                            start=first, stop=last))(o, h, bank, h == 0, h == NCH - 1),
                        waits=(wb + mtoks + [wqT_toks[hh][o] for hh in range(NCH)]
                               + ([q_free[o]] if q_free[o] else [])
                               if h == 0 else []),
                        tag=f"qmm{T}_{o}")
                last_q_mm = mm
                ev = p.emit(VECTOR, (lambda o, i0, bank: lambda v:
                    v.scalar_tensor_tensor(
                        out=q_slab[:, o * 512:(o + 1) * 512],
                        in0=bank[:, :], scalar=bq_col[:, o:o + 1],
                        in1=gi_bc[:, i0:i0 + 512],
                        op0=ALU.add, op1=ALU.mult))(o, i0, bank),
                    waits=[mm, bq_tok, tg3], tag=f"qev{T}_{o}")
                self.ppool.release(j, ev)
                q_toks.append(ev)
            for h in range(NCH):
                self._mir_free[h] = last_q_mm
            nj = 4 * T + 4
            s_toks = []
            for c in range(nj):
                ts_of_c = c // 4
                j, bank, wb = self.ppool.acquire()
                mm = None
                for o in range(NCH):
                    mm = p.emit(TENSOR, (lambda c, o, bank, first, last:
                        lambda te: te.matmul(
                            out=bank[:, :],
                            lhsT=kT[:, o * S + c * 128: o * S + (c + 1) * 128],
                            rhs=q_slab[:, o * 512:(o + 1) * 512],
                            start=first, stop=last))(c, o, bank, o == 0, o == NCH - 1),
                        waits=(wb + q_toks + [kT_tok[oo][ts_of_c] for oo in range(NCH)]
                               + ([sT_free] if sT_free else [])
                               if o == 0 else []),
                        tag=f"smm{T}_{c}")
                m = c - 4 * T
                if m >= 0:
                    ev = p.emit(VECTOR, (lambda c, m, bank: lambda v: v.tensor_tensor(
                        out=sT[:, c * 512:(c + 1) * 512], in0=bank[:, :],
                        in1=b01[:, 512 - 128 * m: 1024 - 128 * m], op=ALU.mult))(c, m, bank),
                        waits=[mm, tb01], tag=f"sev{T}_{c}")
                else:
                    ev = p.emit(SCALAR, (lambda c, bank: lambda s: s.copy(
                        out=sT[:, c * 512:(c + 1) * 512], in_=bank[:, :]))(c, bank),
                        waits=[mm], tag=f"sev{T}_{c}")
                self.ppool.release(j, ev)
                s_toks.append(ev)
            last_av = None
            for o in range(NCH):
                j, bank, wb = self.ppool.acquire()
                ri = o % 2
                mm = None
                for c in range(nj):
                    mm = p.emit(TENSOR, (lambda o, c, bank, first, last:
                        lambda te: te.matmul(
                            out=bank[:, :],
                            lhsT=v_tok[:, c * D + o * 128: c * D + o * 128 + 128],
                            rhs=sT[:, c * 512:(c + 1) * 512],
                            start=first, stop=last))(o, c, bank, c == 0, c == nj - 1),
                        waits=(wb + s_toks +
                               [v_tok_tok[c2][ohh] for c2 in range(nj) for ohh in range(2)]
                               if c == 0 else []),
                        tag=f"avmm{T}_{o}")
                last_av = mm
                w_or = [out_free[ri]] if out_free[ri] else []
                ev = p.emit(VECTOR, (lambda ri, bank: lambda v: v.tensor_copy(
                    out=out_ring[:, ri * 512:(ri + 1) * 512], in_=bank[:, :]))(ri, bank),
                    waits=[mm] + w_or, tag=f"avev{T}_{o}")
                self.ppool.release(j, ev)
                dm = p.dma(SYNC, self.attn0[:, o * S + i0: o * S + i0 + 512],
                           out_ring[:, ri * 512:(ri + 1) * 512], waits=[ev],
                           tag=f"avdma{T}_{o}")
                out_free[ri] = dm
            sT_free = last_av
            for o in range(NCH):
                q_free[o] = last_av
        if os.environ.get("KEEP_RET"):
            self._ret_dbg = dict(kT=kT, v_tok=v_tok, sT=sT, q_slab=q_slab,
                                 gi_bc=gi_bc, gj_bc=gj_bc, mir=mir, wqT=wqT, b01=b01)
            self._p1_done = p.fence()
            return
        k.pop_scope()
        self._p1_done = p.fence()
        k.pop_scope()

    def p1_apply_attn(self, extra_waits=()):
        k, p = self.k, self.k.p
        k.push_scope()
        ld_ring = k.sb("a0ld", [128, 2 * S], BF16)
        free_t = [None, None]
        for c in range(NCH):
            ri = c % 2
            ld = p.dma(SYNC, ld_ring[:, ri * S:(ri + 1) * S],
                       self.attn0[:, c * S:(c + 1) * S],
                       waits=list(extra_waits) + ([free_t[ri]] if free_t[ri] else []),
                       tag="a0ld")
            ad = p.emit(VECTOR, (lambda c, ri: lambda v: v.tensor_tensor(
                out=self.xTc[c], in0=self.xTc[c],
                in1=ld_ring[:, ri * S:(ri + 1) * S], op=ALU.add))(c, ri),
                waits=[ld] + ([self.xTc_tok[c]] if self.xTc_tok[c] else []),
                tag="a0add")
            free_t[ri] = ad
            self.xTc_tok[c] = ad
        k.pop_scope()

    SW = S + 1

    def p2_stream_init(self, waits=()):
        k, p = self.k, self.k.p
        k.push_scope()
        scr = k.sb("si_scr", [128, 512], F32)
        _w0 = list(waits)
        for c in range(NCH):
            zt = p.emit(VECTOR, (lambda c: lambda v: v.memset(
                self.strmc[c][:, 0:1], 0.0))(c), waits=_w0, tag="spad")
            last = zt
            for ts in range(NT4):
                a0 = ts * 512
                t1 = p.emit(VECTOR, (lambda c, a0: lambda v: v.tensor_tensor(
                    out=scr[:, :], in0=self.xTc[c][:, a0:a0 + 512],
                    in1=self.ln0.rho_bc[:, a0:a0 + 512], op=ALU.mult))(c, a0),
                    waits=self.ln0.toks + [self.xTc_tok[c]], tag="si1")
                t2 = p.emit(VECTOR, (lambda c, a0: lambda v: v.tensor_tensor(
                    out=self.strmc[c][:, a0 + 1:a0 + 513], in0=scr[:, :],
                    in1=self.ln0.om_bc[:, a0:a0 + 512], op=ALU.add))(c, a0),
                    waits=[t1], tag="si2")
                last = t2
            self.strm_tok[c] = last
        k.pop_scope()

    def p2_spill_xT(self):
        k, p = self.k, self.k.p
        self.xT_spill = k.dram("xTsp", [128, NCH * S], F32)
        self.xT_spill_toks = []
        for c in range(NCH):
            t = p.dma(SYNC, self.xT_spill[:, c * S:(c + 1) * S], self.xTc[c],
                      waits=[self.strm_tok[c]] + ([self.xTc_tok[c]] if self.xTc_tok[c] else []),
                      tag="spill")
            self.xT_spill_toks.append(t)

    def p2_reload_xT(self, waits=()):
        k, p = self.k, self.k.p
        for c in range(NCH):
            t = p.dma(SYNC, self.xTc[c], self.xT_spill[:, c * S:(c + 1) * S],
                      waits=list(waits) + [self.xT_spill_toks[c]], tag="unspill")
            self.xTc_tok[c] = t

    def p2_tables_global(self):
        k, p = self.k, self.k.p
        self.csq = k.sb("csq", [128, NT * 64], F32)
        self.ssq = k.sb("ssq", [128, NT * 64], F32)
        self.csk = k.sb("csk", [128, NT * 64], F32)
        self.ssk = k.sb("ssk", [128, NT * 64], F32)
        k.push_scope()
        invf_r = k.sb("invf", [1, 64], F32)
        lbase_r = k.sb("lbase", [1, 64], F32)
        t1 = p.emit(GPSIMD, lambda g: g.iota(
            out=invf_r[0:1, :], pattern=[[0, 2], [2, 32]], base=0,
            channel_multiplier=0, allow_small_or_imprecise_dtypes=True), tag="ivf0")
        t2 = p.emit(SCALAR, lambda s: s.activation(
            out=lbase_r[0:1, :], in_=invf_r[0:1, :], func=AF.Identity,
            scale=1.0 / 89.6, bias=k.c(25.6 / 89.6)[0][0:1, :]),
            waits=[t1, k.c(25.6 / 89.6)[1]], tag="base")
        t3 = p.emit(SCALAR, lambda s: s.activation(
            out=lbase_r[0:1, :], in_=lbase_r[0:1, :], func=AF.Ln), waits=[t2],
            tag="lbase")
        t4 = p.emit(SCALAR, lambda s: s.activation(
            out=invf_r[0:1, :], in_=invf_r[0:1, :], func=AF.Exp,
            scale=-math.log(10000.0) / 64.0), waits=[t1, t2], tag="invf")
        invf_bc = k.sb("invfbc", [128, 64], F32)
        lbase_bc = k.sb("lbasebc", [128, 64], F32)
        t5 = bcast_row(k, invf_r[0:1, :], invf_bc[:, :], self.ppool, [t4], "ivbc")
        t6 = bcast_row(k, lbase_r[0:1, :], lbase_bc[:, :], self.ppool, [t3], "lbbc")
        tcol = k.sb("tcol", [128, 2], F32)
        fr = k.sb("fr", [128, 64 * 5], F32)
        TWO_PI = 2.0 * math.pi
        prev_tls = None
        for tt in range(NT):
            w0 = [t5, t6] + ([prev_tls] if prev_tls else [])
            ti = p.emit(GPSIMD, (lambda tt: lambda g: g.iota(
                out=tcol[:, 0:1], pattern=[[0, 1]], base=tt * 128,
                channel_multiplier=1, allow_small_or_imprecise_dtypes=True))(tt),
                waits=w0, tag="tc")
            tpw = p.emit(VECTOR, lambda v: v.tensor_scalar(
                out=tcol[:, 1:2], in0=tcol[:, 0:1], scalar1=-1024.0, scalar2=1.0 / 256,
                op0=ALU.add, op1=ALU.mult), waits=[ti], tag="pw")
            tf = p.emit(VECTOR, lambda v: v.tensor_scalar(
                out=fr[:, 0:64], in0=invf_bc[:, :], scalar1=tcol[:, 0:1], scalar2=None,
                op0=ALU.mult), waits=[ti, t5], tag="fq")
            tu = p.emit(VECTOR, lambda v: v.tensor_scalar(
                out=fr[:, 64:128], in0=fr[:, 0:64], scalar1=1.0 / TWO_PI, scalar2=None,
                op0=ALU.mult), waits=[tf], tag="u")
            last = tu
            for kk in range(8, -1, -1):
                step = float(2 ** kk)
                m1 = p.emit(VECTOR, (lambda step: lambda v: v.tensor_scalar(
                    out=fr[:, 128:192], in0=fr[:, 64:128], scalar1=step, scalar2=None,
                    op0=ALU.is_ge))(step), waits=[last], tag="lad")
                last = p.emit(VECTOR, (lambda step: lambda v: v.scalar_tensor_tensor(
                    out=fr[:, 64:128], in0=fr[:, 128:192], scalar=-step,
                    in1=fr[:, 64:128], op0=ALU.mult, op1=ALU.add))(step),
                    waits=[m1], tag="lad2")
            m1 = p.emit(VECTOR, lambda v: v.tensor_scalar(
                out=fr[:, 128:192], in0=fr[:, 64:128], scalar1=0.5, scalar2=None,
                op0=ALU.is_ge), waits=[last], tag="sw")
            tsin = p.emit(VECTOR, lambda v: v.scalar_tensor_tensor(
                out=fr[:, 128:192], in0=fr[:, 128:192], scalar=-1.0,
                in1=fr[:, 64:128], op0=ALU.mult, op1=ALU.add), waits=[m1], tag="sa")
            tc1 = p.emit(VECTOR, lambda v: v.tensor_scalar(
                out=fr[:, 192:256], in0=fr[:, 64:128], scalar1=0.25, scalar2=None,
                op0=ALU.add), waits=[tsin], tag="ca")
            tc2 = p.emit(VECTOR, lambda v: v.tensor_scalar(
                out=fr[:, 0:64], in0=fr[:, 192:256], scalar1=0.5, scalar2=None,
                op0=ALU.is_ge), waits=[tc1], tag="cb")
            tcos = p.emit(VECTOR, lambda v: v.scalar_tensor_tensor(
                out=fr[:, 192:256], in0=fr[:, 0:64], scalar=-1.0,
                in1=fr[:, 192:256], op0=ALU.mult, op1=ALU.add), waits=[tc2], tag="cc")
            tsn = p.emit(SCALAR, (lambda tt: lambda s: s.activation(
                out=self.ssq[:, tt * 64:(tt + 1) * 64], in_=fr[:, 128:192],
                func=AF.Sin, scale=TWO_PI))(tt), waits=[tsin], tag="sin")
            tcs = p.emit(SCALAR, (lambda tt: lambda s: s.activation(
                out=self.csq[:, tt * 64:(tt + 1) * 64], in_=fr[:, 192:256],
                func=AF.Sin, scale=TWO_PI))(tt), waits=[tcos], tag="cos")
            tls = p.emit(VECTOR, lambda v: v.tensor_scalar(
                out=fr[:, 0:64], in0=lbase_bc[:, :], scalar1=tcol[:, 1:2],
                scalar2=None, op0=ALU.mult), waits=[tpw, t6, tcs], tag="ls")
            prev_tls = tls
            tsc = p.emit(SCALAR, lambda s: s.activation(
                out=fr[:, 64:128], in_=fr[:, 0:64], func=AF.Exp), waits=[tls],
                tag="sc")
            tisc = p.emit(SCALAR, lambda s: s.activation(
                out=fr[:, 256:320], in_=fr[:, 0:64], func=AF.Exp, scale=-1.0),
                waits=[tls], tag="isc")
            a = self.csq[:, tt * 64:(tt + 1) * 64]
            b = self.ssq[:, tt * 64:(tt + 1) * 64]
            c_ = self.csk[:, tt * 64:(tt + 1) * 64]
            d_ = self.ssk[:, tt * 64:(tt + 1) * 64]
            e1 = p.emit(VECTOR, (lambda a, c_: lambda v: v.tensor_tensor(
                out=c_, in0=a, in1=fr[:, 256:320], op=ALU.mult))(a, c_),
                waits=[tsc, tisc, tcs, tsn], tag="csk")
            e2 = p.emit(VECTOR, (lambda b, d_: lambda v: v.tensor_tensor(
                out=d_, in0=b, in1=fr[:, 256:320], op=ALU.mult))(b, d_),
                waits=[e1], tag="ssk")
            e3 = p.emit(VECTOR, (lambda a: lambda v: v.tensor_tensor(
                out=a, in0=a, in1=fr[:, 64:128], op=ALU.mult))(a),
                waits=[e2], tag="csq")
            e4 = p.emit(VECTOR, (lambda b: lambda v: v.tensor_tensor(
                out=b, in0=b, in1=fr[:, 64:128], op=ALU.mult))(b),
                waits=[e3], tag="ssq")
            self._tbl_tok = e4
        k.pop_scope()

    def p2_layer(self, l):
        k, p = self.k, self.k.p
        pre = f"f{l}_"
        wqkv = k.din(pre + "wqkv", [D, 1536])
        qs_h = k.din(pre + "qs", [1, 64]); ks_h = k.din(pre + "ks", [1, 64])
        wo_h = k.din(pre + "wo", [512, D])
        lng_h = k.din(pre + "lng", [1, D]); lnb_h = k.din(pre + "lnb", [1, D])
        w1_h = k.din(pre + "w1", [D, 5460]); b1_h = k.din(pre + "b1", [1, 5460])
        w2_h = k.din(pre + "w2", [2730, D]); b2_h = k.din(pre + "b2", [1, D])
        SW = self.SW
        k.push_scope()
        self.p2_tables_global()

        rows = k.sb("qr", [1, 4 * 64], F32)
        l1 = p.dma(SYNC, rows[0:1, 0:64], qs_h[0:1, :], tag="qsld")
        l2 = p.dma(SYNC, rows[0:1, 128:192], ks_h[0:1, :], tag="ksld")
        t1 = p.emit(VECTOR, lambda v: v.tensor_scalar(
            out=rows[0:1, 0:64], in0=rows[0:1, 0:64], scalar1=8.0, scalar2=None,
            op0=ALU.mult), waits=[l1], tag="qs8")
        t2 = p.emit(VECTOR, lambda v: v.tensor_scalar(
            out=rows[0:1, 64:96], in0=rows[0:1, 32:64], scalar1=-1.0, scalar2=None,
            op0=ALU.mult), waits=[t1], tag="qr1")
        t3 = p.emit(VECTOR, lambda v: v.tensor_copy(
            out=rows[0:1, 96:128], in_=rows[0:1, 0:32]), waits=[t2], tag="qr2")
        t4 = p.emit(VECTOR, lambda v: v.tensor_scalar(
            out=rows[0:1, 192:224], in0=rows[0:1, 160:192], scalar1=-1.0,
            scalar2=None, op0=ALU.mult), waits=[l2], tag="kr1")
        t5 = p.emit(VECTOR, lambda v: v.tensor_copy(
            out=rows[0:1, 224:256], in_=rows[0:1, 128:160]), waits=[t4], tag="kr2")
        nb = k.sb("nb", [1, 2], F32)
        t6 = p.emit(VECTOR, lambda v: v.reduce_max(
            out=nb[0:1, 0:1], in_=rows[0:1, 0:64], axis=AX.X,
            apply_absolute_value=True), waits=[t1], tag="m1")
        t7 = p.emit(VECTOR, lambda v: v.reduce_max(
            out=nb[0:1, 1:2], in_=rows[0:1, 128:192], axis=AX.X,
            apply_absolute_value=True), waits=[l2], tag="m2")
        t8 = p.emit(VECTOR, lambda v: v.scalar_tensor_tensor(
            out=nb[0:1, 0:1], in0=nb[0:1, 0:1], scalar=-1.0, in1=nb[0:1, 1:2],
            op0=ALU.mult, op1=ALU.mult), waits=[t6, t7], tag="nb0")
        nb0_col = k.sb("nb0c", [128, 1], F32)
        t9 = bcast_row(k, nb[0:1, 0:1], nb0_col[:, 0:1], self.ppool, [t8], "nbbc")
        rbc = k.sb("rbc", [128, 4 * 64], F32)
        rb_toks = []
        for i, tk in enumerate([t3, t3, t5, t5]):
            rb_toks.append(bcast_row(k, rows[0:1, i * 64:(i + 1) * 64],
                                     rbc[:, i * 64:(i + 1) * 64], self.ppool,
                                     [t3, t5], f"rbc{i}"))
        tbl = k.sb("tbl", [128, 4 * NT * 64], BF16)
        TL = NT * 64
        tbl_toks = []
        for i, (srct, rbi) in enumerate([(self.csq, 0), (self.ssq, 1),
                                         (self.csk, 2), (self.ssk, 3)]):
            tt_ = p.emit(VECTOR, (lambda i, srct, rbi: lambda v: v.tensor_tensor(
                out=tbl[:, i * TL:(i + 1) * TL].rearrange("p (t d) -> p t d", d=64),
                in0=srct[:, :].rearrange("p (t d) -> p t d", d=64),
                in1=rbc[:, rbi * 64:(rbi + 1) * 64].unsqueeze(1).broadcast_to(
                    [128, NT, 64]), op=ALU.mult))(i, srct, rbi),
                waits=rb_toks + [self._tbl_tok], tag=f"tbl{i}")
            tbl_toks.append(tt_)
        T1q = tbl[:, 0 * TL:1 * TL]; T2q = tbl[:, 1 * TL:2 * TL]
        T1k = tbl[:, 2 * TL:3 * TL]; T2k = tbl[:, 3 * TL:4 * TL]
        mAC = k.sb("mAC", [128, 256], F32)
        tmA = affine_mask(k, mAC[:, 0:128], 128, ALU.is_ge, 0, 1, NEGB, 0.0, "mA",
                          step=-1)
        tmC = affine_mask(k, mAC[:, 128:256], 128, ALU.is_ge, 0, -1, NEGB, 0.0, "mC",
                          step=1)
        mA = mAC[:, 0:128]; mC = mAC[:, 128:256]

        xbf = self.xT[:, :].bitcast(BF16)
        kT = xbf[:, 0:4 * S]
        qT = xbf[:, 4 * S:8 * S]
        v64 = xbf[:, 8 * S:8 * S + NT * 512]
        wo_sb = xbf[:, 8 * S + NT * 512:8 * S + NT * 512 + 8 * D]
        slab = k.sb("fslab", [128, 8 * 512], BF16)
        wstg = k.sb("fwstg", [128, D], F32)
        mir_f = k.sb("fmirf", [128, 512], F32)
        mir = k.sb("fmir", [128, 8 * 512], BF16)
        qn_sb = k.sb("fqn", [128, 512], F32)
        sq_sb = k.sb("fsq", [128, 512], F32)
        ra = k.sb("fra", [128, 512], F32)
        rot_bf = k.sb("frot", [128, 512], BF16)
        ssr = k.sb("fssr", [128, 24], F32)
        pT = k.sb("fpT", [128, 8 * 512], BF16)
        attn_sl = k.sb("fattn", [64, 8 * 512], BF16)
        den_sb = k.sb("fden", [64, 512], F32)
        den_r = k.sb("fdenr", [1, 512], F32)
        self._mirQ_free = [None] * 8
        wstg_free = [None]

        zt_toks = []
        def ztf(v):
            last = None
            for slot in range(1):
                for oo in range(1, 4):
                    last = v.memset(pT[:, slot * 4096 + (4 + oo) * 512:
                                       slot * 4096 + (4 + oo) * 512 + 128 * oo], 0.0)
                for pp in range(0, 3):
                    off = 128 * (pp + 1)
                    last = v.memset(pT[:, slot * 4096 + pp * 512 + off:
                                       slot * 4096 + (pp + 1) * 512], 0.0)
            return last
        zt = p.emit(VECTOR, ztf, tag="pTz")

        wo_toks = []
        for h in range(FH):
            ld = p.dma(SYNC, wstg[0:64, 0:D], wo_h[h * 64:(h + 1) * 64, :],
                       waits=[wstg_free[0]] if wstg_free[0] else [], tag="wold")
            ct = p.emit(SCALAR, (lambda h: lambda s: s.copy(
                out=wo_sb[0:64, h * D:(h + 1) * D], in_=wstg[0:64, 0:D]))(h),
                waits=[ld], tag="woc")
            wstg_free[0] = ct
            wo_toks.append(ct)

        def strm_shift_ap(c, a0, wdt):
            if c < 4:
                return self.strmc[c][:, a0 + 1:a0 + 1 + wdt]
            return self.strmc[c][:, a0:a0 + wdt]

        def build_mirror_q(a0):
            toks = []
            for h in range(NCH):
                w = [self._mirQ_free[h]] if self._mirQ_free[h] else []
                tq = p.emit(VECTOR, (lambda h, a0: lambda v: v.tensor_copy(
                    out=mir[:, h * 512:(h + 1) * 512],
                    in_=strm_shift_ap(h, a0, 512)))(h, a0),
                    waits=w + [self.strm_tok[h]], tag="mirq")
                toks.append(tq)
            return toks

        def rotv(ap512):
            return bass.AP(ap512.tensor, ap512.offset + 32,
                           [list(ap512.ap[0]), [64, 8], [-32, 2], [1, 32]])

        kT_tok = [None] * NT
        qT_tok = [None] * NT
        v_tok2 = [None] * NT
        self._slab_free = None
        def proj_pass(fpart, kind):
            sl_toks = []
            for hp in range(4):
                ld = p.dma(SYNC, wstg[:, 0:1024].rearrange("p (c w) -> p c w", c=2),
                           wqkv[hp * 256:(hp + 1) * 256,
                                fpart * 512:(fpart + 1) * 512].rearrange(
                                "(c p) w -> p c w", p=128),
                           waits=[wstg_free[0]] if wstg_free[0] else [], tag="qkvld")
                ct = p.emit(SCALAR, (lambda hp: lambda s: s.copy(
                    out=slab[:, hp * 1024:(hp + 1) * 1024], in_=wstg[:, 0:1024]))(hp),
                    waits=[ld] + ([self._slab_free] if (hp == 0 and self._slab_free)
                                  else []),
                    tag="qkvc")
                wstg_free[0] = ct
                sl_toks.append(ct)
            for ts4 in range(NT4):
                a0 = ts4 * 512
                mtoks = build_mirror_q(a0)
                last_mm_slice = None
                for q4 in range(4):
                    tt = ts4 * 4 + q4
                    j, bank, wb = self.ppool.acquire()
                    mm = None
                    for h in range(NCH):
                        mm = p.emit(TENSOR, (lambda h, q4, bank, first, last:
                            lambda te: te.matmul(
                                out=bank[:, :],
                                lhsT=mir[:, h * 512 + q4 * 128: h * 512 + (q4 + 1) * 128],
                                rhs=slab[:, h * 512:(h + 1) * 512],
                                start=first, stop=last))(h, q4, bank, h == 0,
                                                         h == NCH - 1),
                            waits=(wb + mtoks + sl_toks if h == 0 else []),
                            tag=f"pj{kind}{tt}")
                    last_mm_slice = mm
                    if kind == "v":
                        ev = p.emit(SCALAR, (lambda tt, bank: lambda s: s.copy(
                            out=v64[:, tt * 512:(tt + 1) * 512], in_=bank[:, :]))(tt, bank),
                            waits=[mm], tag=f"vev{tt}")
                        self.ppool.release(j, ev)
                        v_tok2[tt] = ev
                        continue
                    T1 = T1q if kind == "q" else T1k
                    T2 = T2q if kind == "q" else T2k
                    dstT = qT if kind == "q" else kT
                    s1 = p.emit(SCALAR, (lambda bank: lambda s: s.activation(
                        out=sq_sb[:, :], in_=bank[:, :], func=AF.Square))(bank),
                        waits=[mm], tag="l2sq")
                    s2 = p.emit(VECTOR, lambda v: v.tensor_reduce(
                        out=ssr[:, 0:8], in_=sq_sb[:, :].rearrange(
                            "p (h d) -> p h d", d=64), axis=AX.X, op=ALU.add),
                        waits=[s1], tag="l2red")
                    s3 = p.emit(SCALAR, lambda s: s.activation(
                        out=ssr[:, 8:16], in_=ssr[:, 0:8], func=AF.Sqrt),
                        waits=[s2], tag="l2rt")
                    s4 = p.emit(VECTOR, lambda v: v.tensor_scalar(
                        out=ssr[:, 8:16], in0=ssr[:, 8:16], scalar1=1e-12,
                        scalar2=None, op0=ALU.max), waits=[s3], tag="l2mx")
                    s5 = p.emit(VECTOR, lambda v: v.reciprocal(
                        out=ssr[:, 16:24], in_=ssr[:, 8:16]), waits=[s4], tag="l2rc")
                    s6 = p.emit(VECTOR, (lambda bank: lambda v: v.tensor_tensor(
                        out=qn_sb[:, :].rearrange("p (h d) -> p h d", d=64),
                        in0=bank[:, :].rearrange("p (h d) -> p h d", d=64),
                        in1=ssr[:, 16:24].unsqueeze(2).broadcast_to([128, 8, 64]),
                        op=ALU.mult))(bank), waits=[s5], tag="l2ap")
                    self.ppool.release(j, s6)
                    s7 = p.emit(VECTOR, (lambda tt, T1: lambda v: v.tensor_tensor(
                        out=ra[:, :].rearrange("p (h d) -> p h d", d=64),
                        in0=qn_sb[:, :].rearrange("p (h d) -> p h d", d=64),
                        in1=T1[:, tt * 64:(tt + 1) * 64].unsqueeze(1).broadcast_to(
                            [128, 8, 64]), op=ALU.mult))(tt, T1),
                        waits=[s6] + tbl_toks, tag="rt1")
                    s8 = p.emit(VECTOR, (lambda tt, T2: lambda v: v.tensor_tensor(
                        out=sq_sb[:, :].rearrange("p (h d) -> p h d", d=64),
                        in0=rotv(qn_sb[:, :]),
                        in1=T2[:, tt * 64:(tt + 1) * 64].unsqueeze(1).broadcast_to(
                            [128, 8, 64]), op=ALU.mult))(tt, T2),
                        waits=[s7], tag="rt2")
                    s9 = p.emit(VECTOR, lambda v: v.tensor_tensor(
                        out=rot_bf[:, :], in0=ra[:, :], in1=sq_sb[:, :], op=ALU.add),
                        waits=[s8], tag="rt3")
                    evs = []
                    for r in range(4):
                        j2, bank2, wb2 = self.ppool.acquire()
                        tt2 = p.emit(TENSOR, (lambda r, bank2: lambda te: te.transpose(
                            out=bank2.bitcast(BF16)[:, 0:128],
                            in_=rot_bf[:, r * 128:(r + 1) * 128],
                            identity=self.idn_bf[:, :]))(r, bank2),
                            waits=[s9, self.idn_bf_tok] + wb2, tag="rtp")
                        ev = p.emit(SCALAR, (lambda r, tt, bank2, dstT: lambda s:
                            s.copy(
                                out=dstT[:, r * S + tt * 128: r * S + (tt + 1) * 128],
                                in_=bank2.bitcast(BF16)[:, 0:128]))(r, tt, bank2, dstT),
                            waits=[tt2], tag="rtc")
                        self.ppool.release(j2, ev)
                        evs.append(ev)
                    if kind == "q":
                        qT_tok[tt] = evs[-1]
                    else:
                        kT_tok[tt] = evs[-1]
                for h in range(NCH):
                    self._mirQ_free[h] = last_mm_slice
                self._slab_free = last_mm_slice
        proj_pass(1, "k")
        proj_pass(2, "v")
        proj_pass(0, "q")

        pT_free = [None, None]
        attn_free = [None] * 8
        for w in range(4):
            wo_evs = []
            for h in range(FH):
                slot = 0
                pbase = 0
                chunks = []
                if w > 0:
                    for pp in range(4):
                        chunks.append(("prev", pp, 4 * (w - 1) + pp, pp))
                for oo in range(4):
                    chunks.append(("own", oo, 4 * w + oo, 4 + oo))
                hp = (h % 2) * 64
                r = h // 2
                exp_toks = []
                pw = [pT_free[slot]] if pT_free[slot] else []
                for kindc, idx, jtile, ci in chunks:
                    if kindc == "prev":
                        off, N = 0, 128 * (idx + 1)
                    else:
                        off, N = 128 * idx, 512 - 128 * idx
                    j, bank, wb = self.ppool.acquire()
                    mm = p.emit(TENSOR, (lambda hp, r, jtile, w, off, N, bank:
                        lambda te: te.matmul(
                            out=bank[:, off:off + N],
                            lhsT=kT[hp:hp + 64, r * S + jtile * 128: r * S + (jtile + 1) * 128],
                            rhs=qT[hp:hp + 64, r * S + w * 512 + off: r * S + w * 512 + off + N],
                            start=True, stop=True,
                            tile_position=(hp, 0)))(hp, r, jtile, w, off, N, bank),
                        waits=wb + [kT_tok[jtile], qT_tok[4 * w + 0], qT_tok[4 * w + 1],
                                    qT_tok[4 * w + 2], qT_tok[4 * w + 3]],
                        tag=f"sc{w}_{h}_{ci}")
                    dq = 128 * idx
                    md = p.emit(VECTOR, (lambda bank, dq, kindc: lambda v:
                        v.tensor_tensor(out=bank[:, dq:dq + 128],
                                        in0=bank[:, dq:dq + 128],
                                        in1=(mA if kindc == "prev" else mC),
                                        op=ALU.add))(bank, dq, kindc),
                        waits=[mm, tmA, tmC], tag=f"msk{w}_{h}_{ci}")
                    ex = p.emit(SCALAR, (lambda bank, pbase, ci, off, N: lambda s:
                        s.activation(out=pT[:, pbase + ci * 512 + off:
                                            pbase + ci * 512 + off + N],
                                     in_=bank[:, off:off + N], func=AF.Exp,
                                     bias=nb0_col[:, 0:1]))(bank, pbase, ci, off, N),
                        waits=[md, t9, zt] + pw, tag=f"ex{w}_{h}_{ci}")
                    self.ppool.release(j, ex)
                    exp_toks.append(ex)
                j, bank, wb = self.ppool.acquire()
                mm = None
                for ii, (kindc, idx, jtile, ci) in enumerate(chunks):
                    mm = p.emit(TENSOR, (lambda h, jtile, ci, pbase, bank, first, last:
                        lambda te: te.matmul(
                            out=bank[0:64, :],
                            lhsT=v64[:, jtile * 512 + h * 64: jtile * 512 + (h + 1) * 64],
                            rhs=pT[:, pbase + ci * 512: pbase + (ci + 1) * 512],
                            start=first, stop=last))(h, jtile, ci, pbase, bank,
                                                     ii == 0, ii == len(chunks) - 1),
                        waits=(wb + exp_toks + [v_tok2[jt] for _, _, jt, _ in chunks]
                               if ii == 0 else []),
                        tag=f"av{w}_{h}")
                j2, bank2, wb2 = self.ppool.acquire()
                dmm = None
                for ii, (kindc, idx, jtile, ci) in enumerate(chunks):
                    dmm = p.emit(TENSOR, (lambda ci, pbase, bank2, first, last:
                        lambda te: te.matmul(
                            out=bank2[0:1, :], lhsT=self.ones1b[:, 0:1],
                            rhs=pT[:, pbase + ci * 512: pbase + (ci + 1) * 512],
                            start=first, stop=last))(ci, pbase, bank2,
                                                     ii == 0, ii == len(chunks) - 1),
                        waits=(wb2 + exp_toks + [self.ones1b_tok] if ii == 0 else []),
                        tag=f"dn{w}_{h}")
                pT_free[slot] = dmm
                d1 = p.emit(VECTOR, (lambda bank2: lambda v: v.reciprocal(
                    out=den_r[0:1, :], in_=bank2[0:1, :]))(bank2),
                    waits=[dmm], tag=f"dr{w}_{h}")
                self.ppool.release(j2, d1)
                j3, bank3, wb3 = self.ppool.acquire()
                d2 = p.emit(TENSOR, (lambda bank3: lambda te: te.matmul(
                    out=bank3[0:64, :], lhsT=k._ones_row[0:1, 0:64],
                    rhs=den_r[0:1, :], start=True, stop=True))(bank3),
                    waits=wb3 + [d1, k._ones_row_tok], tag=f"db{w}_{h}")
                d3 = p.emit(VECTOR, (lambda bank3: lambda v: v.tensor_copy(
                    out=den_sb[0:64, :], in_=bank3[0:64, :]))(bank3),
                    waits=[d2], tag=f"dc{w}_{h}")
                self.ppool.release(j3, d3)
                aw = [attn_free[h]] if attn_free[h] else []
                a1 = p.emit(VECTOR, (lambda h, bank: lambda v: v.tensor_tensor(
                    out=attn_sl[0:64, h * 512:(h + 1) * 512], in0=bank[0:64, :],
                    in1=den_sb[0:64, :], op=ALU.mult))(h, bank),
                    waits=[d3] + aw, tag=f"at{w}_{h}")
                self.ppool.release(j, a1)
                attn_free[h] = a1
                wo_evs.append(a1)
            for f in range(NCH):
                j, bank, wb = self.ppool.acquire()
                mm = None
                for h in range(FH):
                    mm = p.emit(TENSOR, (lambda f, h, bank, first, last:
                        lambda te: te.matmul(
                            out=bank[:, :],
                            lhsT=wo_sb[0:64, h * D + f * 128: h * D + (f + 1) * 128],
                            rhs=attn_sl[0:64, h * 512:(h + 1) * 512],
                            start=first, stop=last))(f, h, bank, h == 0, h == FH - 1),
                        waits=(wb + wo_evs + wo_toks if h == 0 else []),
                        tag=f"wo{w}_{f}")
                ev = p.emit(VECTOR, (lambda f, w, bank: lambda v: v.tensor_tensor(
                    out=self.strmc[f][:, w * 512 + 1:(w + 1) * 512 + 1],
                    in0=self.strmc[f][:, w * 512 + 1:(w + 1) * 512 + 1],
                    in1=bank[:, :], op=ALU.add))(f, w, bank),
                    waits=[mm, self.strm_tok[f]], tag=f"woev{w}_{f}")
                self.ppool.release(j, ev)
                self.strm_tok[f] = ev
            attn_free = [mm] * 8
        self._p2a_done = p.fence()
        import os as _os
        if _os.environ.get("P2_KEEP"):
            self._p2_dbg = dict(kT=kT, qT=qT, v64=v64, tbl=tbl[:, :], pT=pT[:, :],
                                attn_sl=attn_sl[:, :], nb0_col=nb0_col[:, :],
                                rbc=rbc[:, :])
            return
        k.pop_scope()
        if _os.environ.get("P2_NO_FFN"):
            return
        self.p2_ffn(l, lng_h, lnb_h, w1_h, b1_h, w2_h, b2_h)

    def p2_ffn(self, l, lng_h, lnb_h, w1_h, b1_h, w2_h, b2_h):
        k, p = self.k, self.k.p
        SW = self.SW
        HID = 2730
        k.push_scope()
        sh_aps = [self.strmc[c][:, (1 if c < 4 else 0):(1 if c < 4 else 0) + S]
                  for c in range(NCH)]
        xbf = self.xT[:, :].bitcast(BF16)
        h_half = xbf[:, 0:22 * 1024]
        lnh_h = xbf[:, 22 * 1024:22 * 1024 + NCH * 1024]
        lng_col, lg_tok = load_bias_cols(k, lng_h, D, tag="lngc")
        lnb_col, lb_tok = load_bias_cols(k, lnb_h, D, tag="lnbc")
        b2_col, b2_tok = load_bias_cols(k, b2_h, D, tag="b2c")
        b1a = k.sb("b1a", [128, 22], F32)
        b1g = k.sb("b1g", [128, 22], F32)
        def ld_ragged(dst, base):
            t1_ = k.p.dma_fn(SYNC, (lambda dst, base: lambda eng: eng.dma_start(
                out=dst[:, 0:21],
                in_=b1_h[0:1, base:base + 2688].rearrange("o (c q) -> (o q) c", q=128),
                allow_slow_non_contiguous=True))(dst, base), (), True, "b1r")
            t2_ = k.p.dma_fn(SYNC, (lambda dst, base: lambda eng: eng.dma_start(
                out=dst[0:42, 21:22],
                in_=b1_h[0:1, base + 2688:base + 2730].rearrange("o q -> q o"),
                allow_slow_non_contiguous=True))(dst, base), (), True, "b1r2")
            return [t1_, t2_]
        b1a_tok = ld_ragged(b1a, 0)
        b1g_tok = ld_ragged(b1g, HID)
        sv = k.sb("svcol", [128, 8], F32)
        sv_toks = []
        for c in range(4, NCH):
            t = p.emit(VECTOR, (lambda c: lambda v: v.tensor_copy(
                out=sv[:, c:c + 1], in_=self.strmc[c][:, 1024:1025]))(c),
                waits=[self.strm_tok[c]], tag="svc")
            sv_toks.append(t)
        rho2 = k.sb("rho2", [128, S], BF16)
        om2 = k.sb("om2", [128, S], BF16)
        k.push_scope()
        ln2 = ln_stats(k, sh_aps, list(self.strm_tok), self.ones1, self.ones1_tok,
                       self.ppool, tag=f"l2s{l}", out=(rho2, om2))
        k.pop_scope()
        w2res = k.sb("w2res", [128, 22 * D], BF16)
        wstg = k.sb("w2stg", [128, D], F32)
        stg_free = None
        w2_toks = []
        for oc in range(22):
            M = 128 if oc < 21 else 42
            ld = p.dma(SYNC, wstg[0:M, :], w2_h[oc * 128: oc * 128 + M, :],
                       waits=[stg_free] if stg_free else [], tag="w2ld")
            ct = p.emit(SCALAR, (lambda oc, M: lambda s: s.copy(
                out=w2res[0:M, oc * D:(oc + 1) * D], in_=wstg[0:M, :]))(oc, M),
                waits=[ld], tag="w2c")
            stg_free = ct
            w2_toks.append(ct)
        w1a_sl = k.sb("w1a", [128, 8 * 256], BF16)
        w1g_sl = k.sb("w1g", [128, 8 * 256], BF16)
        gring = k.sb("gring", [128, 2 * 512], F32)
        gring_free = [None, None]
        lf = k.sb("lf", [128, 2 * 512], F32)
        slabs = [(i * 256, 256) for i in range(10)] + [(2560, 170)]
        w1_free = [None, None]
        h_toks = {}
        w2_last = None
        for th in range(2):
            lnh_last = None
            lf_free = [None, None]
            prev_th = [w2_last] if w2_last else []
            for c in range(NCH):
                for t5i in range(2):
                    a0 = th * 1024 + t5i * 512
                    ri = (c * 2 + t5i) % 2
                    w_ = [lf_free[ri]] if lf_free[ri] else []
                    u1 = p.emit(VECTOR, (lambda c, a0, ri: lambda v: v.tensor_tensor(
                        out=lf[:, ri * 512:(ri + 1) * 512],
                        in0=sh_aps[c][:, a0:a0 + 512],
                        in1=rho2[:, a0:a0 + 512], op=ALU.mult))(c, a0, ri),
                        waits=ln2.toks + [self.strm_tok[c]] + w_ + prev_th,
                        tag="lnh1")
                    u2 = p.emit(VECTOR, (lambda c, a0, ri: lambda v: v.tensor_tensor(
                        out=lf[:, ri * 512:(ri + 1) * 512],
                        in0=lf[:, ri * 512:(ri + 1) * 512],
                        in1=om2[:, a0:a0 + 512], op=ALU.add))(c, a0, ri),
                        waits=[u1], tag="lnh2")
                    u3 = p.emit(VECTOR, (lambda c, a0, t5i, ri: lambda v: v.tensor_scalar(
                        out=lnh_h[:, c * 1024 + t5i * 512: c * 1024 + t5i * 512 + 512],
                        in0=lf[:, ri * 512:(ri + 1) * 512],
                        scalar1=lng_col[:, c:c + 1], scalar2=lnb_col[:, c:c + 1],
                        op0=ALU.mult, op1=ALU.add))(c, a0, t5i, ri),
                        waits=[u2, lg_tok, lb_tok], tag="lnh3")
                    lf_free[ri] = u3
                    lnh_last = u3
                if th == 1 and c >= 4:
                    u1 = p.emit(VECTOR, (lambda c: lambda v: v.tensor_tensor(
                        out=lf[:, 0:1], in0=sv[:, c:c + 1],
                        in1=rho2[:, 1024:1025], op=ALU.mult))(c),
                        waits=sv_toks + [lnh_last], tag="lnp1")
                    u2 = p.emit(VECTOR, (lambda c: lambda v: v.tensor_tensor(
                        out=lf[:, 0:1], in0=lf[:, 0:1],
                        in1=om2[:, 1024:1025], op=ALU.add))(c),
                        waits=[u1], tag="lnp2")
                    u3 = p.emit(VECTOR, (lambda c: lambda v: v.tensor_scalar(
                        out=lnh_h[:, c * 1024: c * 1024 + 1], in0=lf[:, 0:1],
                        scalar1=lng_col[:, c:c + 1], scalar2=lnb_col[:, c:c + 1],
                        op0=ALU.mult, op1=ALU.add))(c),
                        waits=[u2], tag="lnp3")
                    lnh_last = u3
                    lf_free[0] = u3
            for snum, (aoff, wdt) in enumerate(slabs):
                a_toks, g_toks = [], []
                for half, (sl, coff) in enumerate([(w1a_sl, 0), (w1g_sl, HID)]):
                    w_ = [w1_free[half]] if w1_free[half] else []
                    for hp in range(2):
                        ld = p.dma(SYNC, wstg[:, 0:4 * wdt].rearrange(
                                        "p (c w) -> p c w", c=4),
                                   w1_h[hp * 512:(hp + 1) * 512,
                                        coff + aoff: coff + aoff + wdt].rearrange(
                                        "(c p) w -> p c w", p=128),
                                   waits=([stg_free] if stg_free else []) +
                                         (w_ if hp == 0 else []),
                                   tag="w1ld")
                        ct = p.emit(SCALAR, (lambda hp, sl, wdt: lambda s: s.copy(
                            out=sl[:, hp * 1024:(hp + 1) * 1024].rearrange(
                                "p (c w) -> p c w", c=4)[:, :, 0:wdt],
                            in_=wstg[:, 0:4 * wdt].rearrange(
                                "p (c w) -> p c w", c=4)))(hp, sl, wdt),
                            waits=[ld], tag="w1c")
                        stg_free = ct
                        (a_toks if half == 0 else g_toks).append(ct)
                nsub = (wdt + 127) // 128
                last_g_mm = None
                for t5i in range(2):
                    tcols = th * 1024 + t5i * 512
                    for sub in range(nsub):
                        M = min(128, wdt - 128 * sub)
                        ci = (aoff + 128 * sub) // 128
                        ja, banka, wba = self.ppool.acquire()
                        mma = None
                        for hh in range(NCH):
                            mma = p.emit(TENSOR, (lambda hh, sub, M, t5i, banka, first, last:
                                lambda te: te.matmul(
                                    out=banka[0:M, :],
                                    lhsT=w1a_sl[:, hh * 256 + sub * 128: hh * 256 + sub * 128 + M],
                                    rhs=lnh_h[:, hh * 1024 + t5i * 512: hh * 1024 + t5i * 512 + 512],
                                    start=first, stop=last))(hh, sub, M, t5i, banka,
                                                             hh == 0, hh == NCH - 1),
                                waits=(wba + [a_toks[-1], lnh_last] if hh == 0 else []),
                                tag=f"w1a{th}_{snum}_{t5i}_{sub}")
                        jg, bankg, wbg = self.ppool.acquire()
                        mmg = None
                        for hh in range(NCH):
                            mmg = p.emit(TENSOR, (lambda hh, sub, M, t5i, bankg, first, last:
                                lambda te: te.matmul(
                                    out=bankg[0:M, :],
                                    lhsT=w1g_sl[:, hh * 256 + sub * 128: hh * 256 + sub * 128 + M],
                                    rhs=lnh_h[:, hh * 1024 + t5i * 512: hh * 1024 + t5i * 512 + 512],
                                    start=first, stop=last))(hh, sub, M, t5i, bankg,
                                                             hh == 0, hh == NCH - 1),
                                waits=(wbg + [g_toks[-1]] if hh == 0 else []),
                                tag=f"w1g{th}_{snum}_{t5i}_{sub}")
                        last_g_mm = mmg
                        gr = t5i % 2
                        ge = p.emit(SCALAR, (lambda bankg, M, ci, gr: lambda s:
                            s.activation(out=gring[0:M, gr * 512:(gr + 1) * 512],
                                         in_=bankg[0:M, :], func=AF.Gelu,
                                         bias=b1g[0:M, ci:ci + 1]))(bankg, M, ci, gr),
                            waits=[mmg] + b1g_tok +
                                  ([gring_free[gr]] if gring_free[gr] else []),
                            tag="gelu")
                        self.ppool.release(jg, ge)
                        he = p.emit(VECTOR, (lambda banka, M, ci, gr, t5i: lambda v:
                            v.scalar_tensor_tensor(
                                out=h_half[0:M, ci * 1024 + t5i * 512:
                                           ci * 1024 + t5i * 512 + 512],
                                in0=banka[0:M, :], scalar=b1a[0:M, ci:ci + 1],
                                in1=gring[0:M, gr * 512:(gr + 1) * 512],
                                op0=ALU.add, op1=ALU.mult))(banka, M, ci, gr, t5i),
                            waits=[ge, mma] + b1a_tok + prev_th, tag="hmul")
                        self.ppool.release(ja, he)
                        gring_free[gr] = he
                        h_toks[(th, t5i)] = he
                w1_free[0] = last_g_mm
                w1_free[1] = last_g_mm
            for f in range(NCH):
                for t5i in range(2):
                    tcols = th * 1024 + t5i * 512
                    j, bank, wb = self.ppool.acquire()
                    mm = None
                    for oc in range(22):
                        M = 128 if oc < 21 else 42
                        mm = p.emit(TENSOR, (lambda f, oc, M, t5i, bank, first, last:
                            lambda te: te.matmul(
                                out=bank[:, :],
                                lhsT=w2res[0:M, oc * D + f * 128: oc * D + (f + 1) * 128],
                                rhs=h_half[0:M, oc * 1024 + t5i * 512:
                                           oc * 1024 + t5i * 512 + 512],
                                start=first, stop=last))(f, oc, M, t5i, bank,
                                                         oc == 0, oc == 21),
                            waits=(wb + [w2_toks[-1], h_toks[(th, t5i)]]
                                   if oc == 0 else []),
                            tag=f"w2{th}_{f}_{t5i}")
                    ev = p.emit(VECTOR, (lambda f, tcols, bank: lambda v:
                        v.scalar_tensor_tensor(
                            out=self.strmc[f][:, tcols + 1: tcols + 513],
                            in0=bank[:, :], scalar=b2_col[:, f:f + 1],
                            in1=self.strmc[f][:, tcols + 1: tcols + 513],
                            op0=ALU.add, op1=ALU.add))(f, tcols, bank),
                        waits=[mm, b2_tok, self.strm_tok[f]], tag=f"w2ev{th}_{f}")
                    self.ppool.release(j, ev)
                    self.strm_tok[f] = ev
                    w2_last = mm
        import os as _os
        if _os.environ.get("P2_KEEP_FFN") == str(l):
            self._ffn_dbg = dict(lnh=lnh_h, h=h_half, rho2=rho2[:, :], om2=om2[:, :])
            return
        k.pop_scope()

    def p2_apply(self):
        k, p = self.k, self.k.p
        self.p2_reload_xT()
        for c in range(NCH):
            t = p.emit(VECTOR, (lambda c: lambda v: v.tensor_tensor(
                out=self.xTc[c], in0=self.xTc[c],
                in1=self.strmc[c][:, 1:S + 1], op=ALU.add))(c),
                waits=[self.xTc_tok[c], self.strm_tok[c]], tag="xps")
            self.xTc_tok[c] = t
        self.p1_apply_attn()

    def p3_ln1(self):
        k, p = self.k, self.k.p
        k.push_scope()
        self.rho1 = k.sb("rho1", [128, S], BF16)
        self.om1 = k.sb("om1", [128, S], BF16)
        k.push_scope()
        self.ln1 = ln_stats(k, self.xTc, [t for t in self.xTc_tok if t],
                            self.ones1, self.ones1_tok, self.ppool, tag="ln1",
                            out=(self.rho1, self.om1))
        k.pop_scope()

    def xn1_tile(self, ring_f32, out_bf, c, a0, wdt, waits):
        k, p = self.k, self.k.p
        t1 = p.emit(VECTOR, lambda v: v.tensor_tensor(
            out=ring_f32[:, 0:wdt], in0=self.xTc[c][:, a0:a0 + wdt],
            in1=self.rho1[:, a0:a0 + wdt], op=ALU.mult),
            waits=list(waits) + self.ln1.toks +
                  ([self.xTc_tok[c]] if self.xTc_tok[c] else []), tag="x1a")
        t2 = p.emit(VECTOR, lambda v: v.tensor_tensor(
            out=out_bf, in0=ring_f32[:, 0:wdt],
            in1=self.om1[:, a0:a0 + wdt], op=ALU.add), waits=[t1], tag="x1b")
        return t2

    def p4_l1attn(self):
        k, p = self.k, self.k.p
        wq = k.din("l1a_wq", [D, D]); bq = k.din("l1a_bq", [1, D])
        wk = k.din("l1a_wk", [D, D]); bk = k.din("l1a_bk", [1, D])
        wv = k.din("l1a_wv", [D, D]); bv = k.din("l1a_bv", [1, D])
        self.attn1 = k.dram("attn1", [D, S], BF16)
        k.push_scope()
        bq_col, bq_tok = load_bias_cols(k, bq, D, tag="bq1")
        bk_col, bk_tok = load_bias_cols(k, bk, D, tag="bk1")
        bv_bc = k.sb("bv_bc", [128, D], BF16)
        k.push_scope()
        bv_row = k.sb("bv_row", [1, D], F32)
        tbv0 = p.dma(SYNC, bv_row[0:1, :], bv[0:1, :], tag="bvld")
        tbv = bcast_row(k, bv_row[0:1, :], bv_bc[:, :], self.ppool, [tbv0], "bvbc")
        k.pop_scope()
        mC_t = k.sb("mC1", [128, 128], F32)
        tmC = affine_mask(k, mC_t, 128, ALU.is_ge, 0, -1, NEGB, 0.0, "mC1", step=1)
        mC = mC_t[:, :]
        bcv = self.strm[:, :].bitcast(BF16)
        KT = bcv[:, 0:NCH * S]
        VT = bcv[:, NCH * S:2 * NCH * S]
        wsb = k.sb("wsb1", [128, NCH * D], BF16)
        wstg = k.sb("wstg1", [128, D], F32)
        mirf = k.sb("mirf1", [128, 512], F32)
        mir = k.sb("mir1", [128, 8 * 512], BF16)
        q_slab = k.sb("qs1", [128, 8 * 512], BF16)
        pT = k.sb("pT1", [128, 16 * 512], BF16)
        offs_col = k.sb("offs1", [128, 16], F32)
        km_row = k.sb("kmr1", [1, 2 * S], BF16)
        den_r = k.sb("denr1", [1, 512], F32)
        den_sb = k.sb("dens1", [64, 512], BF16)
        to_ring = k.sb("tor1", [64, 2 * 512], BF16)
        to_free = [None, None]
        self._mir1_free = [None] * 8
        self._w1stg_free = None
        self._w1sb_free = None

        def load_w(hbm, toks_out, tagp):
            for c in range(NCH):
                ld = p.dma(SYNC, wstg[:, :], hbm[c * 128:(c + 1) * 128, :],
                           waits=[self._w1stg_free] if self._w1stg_free else [],
                           tag=tagp + "ld")
                ct = p.emit(SCALAR, (lambda c: lambda s: s.copy(
                    out=wsb[:, c * D:(c + 1) * D], in_=wstg[:, :]))(c),
                    waits=[ld] + ([self._w1sb_free] if (c == 0 and self._w1sb_free)
                                  else []),
                    tag=tagp + "c")
                self._w1stg_free = ct
                toks_out.append(ct)

        def build_mir1(a0):
            toks = []
            for h in range(NCH):
                w_ = [self._mir1_free[h]] if self._mir1_free[h] else []
                toks.append(self.xn1_tile(mirf, mir[:, h * 512:(h + 1) * 512],
                                          h, a0, 512, w_))
            return toks

        KT_tok = [[None] * NT4 for _ in range(NCH)]
        km_toks = []
        wk_toks = []
        load_w(wk, wk_toks, "wk1")
        last_mm = None
        for ts4 in range(NT4):
            a0 = ts4 * 512
            mtoks = build_mir1(a0)
            for u in range(NCH):
                j, bank, wb = self.ppool.acquire()
                mm = None
                for c in range(NCH):
                    mm = p.emit(TENSOR, (lambda u, c, bank, first, last:
                        lambda te: te.matmul(
                            out=bank[:, :],
                            lhsT=wsb[:, c * D + u * 128: c * D + (u + 1) * 128],
                            rhs=mir[:, c * 512:(c + 1) * 512],
                            start=first, stop=last))(u, c, bank, c == 0, c == NCH - 1),
                        waits=(wb + mtoks + [wk_toks[-1]] if c == 0 else []),
                        tag=f"k1mm{ts4}_{u}")
                last_mm = mm
                ev = p.emit(SCALAR, (lambda u, a0, bank: lambda s: s.activation(
                    out=KT[:, u * S + a0: u * S + a0 + 512], in_=bank[:, :],
                    func=AF.Relu, bias=bk_col[:, u:u + 1]))(u, a0, bank),
                    waits=[mm, bk_tok], tag=f"k1ev{ts4}_{u}")
                self.ppool.release(j, ev)
                KT_tok[u][ts4] = ev
            j, bank, wb = self.ppool.acquire()
            mm = None
            for c in range(NCH):
                mm = p.emit(TENSOR, (lambda c, bank, first, last: lambda te: te.matmul(
                    out=bank[0:1, :], lhsT=self.ones1b[:, 0:1],
                    rhs=mir[:, c * 512:(c + 1) * 512],
                    start=first, stop=last))(c, bank, c == 0, c == NCH - 1),
                    waits=(wb + mtoks + [self.ones1b_tok] if c == 0 else []),
                    tag=f"km{ts4}")
            last_mm = mm
            ev = p.emit(VECTOR, (lambda a0, bank: lambda v: v.tensor_copy(
                out=km_row[0:1, a0:a0 + 512], in_=bank[0:1, :]))(a0, bank),
                waits=[mm], tag=f"kmev{ts4}")
            self.ppool.release(j, ev)
            km_toks.append(ev)
            for h in range(NCH):
                self._mir1_free[h] = last_mm
        self._w1sb_free = last_mm
        t1 = p.emit(VECTOR, lambda v: v.tensor_scalar(
            out=km_row[0:1, S:2 * S], in0=km_row[0:1, 0:S], scalar1=0.0,
            scalar2=None, op0=ALU.is_equal), waits=km_toks, tag="kmeq")
        t2 = p.emit(VECTOR, lambda v: v.tensor_scalar(
            out=km_row[0:1, 0:S], in0=km_row[0:1, S:2 * S],
            scalar1=-(2.0 ** 32 - 1.0), scalar2=-16.0, op0=ALU.mult, op1=ALU.add),
            waits=[t1], tag="kmoffs")
        t3 = p.emit(VECTOR, lambda v: v.tensor_scalar(
            out=km_row[0:1, S:2 * S], in0=km_row[0:1, S:2 * S], scalar1=-1.0,
            scalar2=1.0, op0=ALU.mult, op1=ALU.add), waits=[t2], tag="kmqm")
        off_toks = []
        for tt in range(NT):
            j, bank, wb = self.ppool.acquire()
            tk = p.emit(TENSOR, (lambda tt, bank: lambda te: te.transpose(
                out=bank.bitcast(BF16)[0:128, 0:1],
                in_=km_row[0:1, tt * 128:(tt + 1) * 128],
                identity=self.idn_bf[0:1, 0:1]))(tt, bank),
                waits=[t2, self.idn_bf_tok] + wb, tag=f"otp{tt}")
            ev = p.emit(VECTOR, (lambda tt, bank: lambda v: v.tensor_copy(
                out=offs_col[:, tt:tt + 1],
                in_=bank.bitcast(BF16)[0:128, 0:1]))(tt, bank),
                waits=[tk], tag=f"oev{tt}")
            self.ppool.release(j, ev)
            off_toks.append(ev)
        VT_tok = [None] * NT
        wv_toks = []
        load_w(wv, wv_toks, "wv1")
        for ts4 in range(NT4):
            a0 = ts4 * 512
            mtoks = build_mir1(a0)
            last_mm = None
            for q4 in range(4):
                tt = ts4 * 4 + q4
                for oh in range(2):
                    j, bank, wb = self.ppool.acquire()
                    mm = None
                    for c in range(NCH):
                        mm = p.emit(TENSOR, (lambda c, q4, oh, bank, first, last:
                            lambda te: te.matmul(
                                out=bank[:, :],
                                lhsT=mir[:, c * 512 + q4 * 128: c * 512 + (q4 + 1) * 128],
                                rhs=wsb[:, c * D + oh * 512: c * D + (oh + 1) * 512],
                                start=first, stop=last))(c, q4, oh, bank, c == 0,
                                                         c == NCH - 1),
                            waits=(wb + mtoks + [wv_toks[-1]] if c == 0 else []),
                            tag=f"v1mm{tt}_{oh}")
                    last_mm = mm
                    e1 = p.emit(VECTOR, (lambda oh, bank: lambda v: v.tensor_tensor(
                        out=bank[:, :], in0=bank[:, :],
                        in1=bv_bc[:, oh * 512:(oh + 1) * 512], op=ALU.add))(oh, bank),
                        waits=[mm, tbv], tag=f"v1b{tt}_{oh}")
                    e2 = p.emit(VECTOR, (lambda tt, oh, bank: lambda v: v.tensor_scalar(
                        out=VT[:, tt * D + oh * 512: tt * D + (oh + 1) * 512],
                        in0=bank[:, :], scalar1=0.0, scalar2=None,
                        op0=ALU.max))(tt, oh, bank), waits=[e1], tag=f"v1r{tt}_{oh}")
                    self.ppool.release(j, e2)
                    VT_tok[tt] = e2
            for h in range(NCH):
                self._mir1_free[h] = last_mm
        self._w1sb_free = last_mm
        wq_toks = []
        load_w(wq, wq_toks, "wq1")
        q_free = [None] * NCH
        pT_free = None
        for T in range(NT4):
            i0 = T * 512
            mtoks = build_mir1(i0)
            q_toks = []
            last_mm = None
            for u in range(NCH):
                j, bank, wb = self.ppool.acquire()
                mm = None
                for c in range(NCH):
                    mm = p.emit(TENSOR, (lambda u, c, bank, first, last:
                        lambda te: te.matmul(
                            out=bank[:, :],
                            lhsT=wsb[:, c * D + u * 128: c * D + (u + 1) * 128],
                            rhs=mir[:, c * 512:(c + 1) * 512],
                            start=first, stop=last))(u, c, bank, c == 0, c == NCH - 1),
                        waits=(wb + mtoks + [wq_toks[-1]] +
                               ([q_free[u]] if q_free[u] else [])
                               if c == 0 else []),
                        tag=f"q1mm{T}_{u}")
                last_mm = mm
                ev = p.emit(SCALAR, (lambda u, bank: lambda s: s.activation(
                    out=q_slab[:, u * 512:(u + 1) * 512], in_=bank[:, :],
                    func=AF.Relu, bias=bq_col[:, u:u + 1]))(u, bank),
                    waits=[mm, bq_tok], tag=f"q1ev{T}_{u}")
                self.ppool.release(j, ev)
                q_toks.append(ev)
            for h in range(NCH):
                self._mir1_free[h] = last_mm
            nj = 4 * T + 4
            zT = None
            for c in range(nj - 3, nj):
                trim = c - 4 * T
                if trim > 0:
                    zT = p.emit(VECTOR, (lambda c, trim: lambda v: v.memset(
                        pT[:, c * 512: c * 512 + 128 * trim], 0.0))(c, trim),
                        waits=([pT_free] if pT_free else []), tag=f"zT{T}_{c}")
            zTl = [zT] if zT else []
            for hh in range(AH):
                hp = (hh % 2) * 64
                r = hh // 2
                exp_toks = []
                pw = [pT_free] if pT_free else []
                for c in range(nj):
                    trim = c - 4 * T
                    off = 128 * trim if trim >= 0 else 0
                    N = 512 - off
                    j, bank, wb = self.ppool.acquire()
                    mm = p.emit(TENSOR, (lambda hp, r, c, i0, off, N, bank:
                        lambda te: te.matmul(
                            out=bank[:, off:off + N],
                            lhsT=KT[hp:hp + 64, r * S + c * 128: r * S + (c + 1) * 128],
                            rhs=q_slab[hp:hp + 64, r * 512 + off: r * 512 + off + N],
                            start=True, stop=True,
                            tile_position=(hp, 0)))(hp, r, c, i0, off, N, bank),
                        waits=wb + [q_toks[-1], KT_tok[NCH - 1][NT4 - 1]],
                        tag=f"s1{T}_{hh}_{c}")
                    w_ = [mm, tmC]
                    if trim >= 0:
                        md = p.emit(VECTOR, (lambda bank, off: lambda v:
                            v.tensor_tensor(out=bank[:, off:off + 128],
                                            in0=bank[:, off:off + 128],
                                            in1=mC, op=ALU.add))(bank, off),
                            waits=w_, tag=f"m1{T}_{hh}_{c}")
                    else:
                        md = mm
                    ex = p.emit(SCALAR, (lambda bank, c, off, N: lambda s:
                        s.activation(out=pT[:, c * 512 + off: c * 512 + off + N],
                                     in_=bank[:, off:off + N], func=AF.Exp,
                                     scale=0.125,
                                     bias=offs_col[:, c // 4 * 4 + c % 4:
                                                   c // 4 * 4 + c % 4 + 1][:, 0:1]
                                     if False else offs_col[:, c:c + 1]))(bank, c, off, N),
                        waits=[md, off_toks[c]] + pw, tag=f"e1{T}_{hh}_{c}")
                    self.ppool.release(j, ex)
                    exp_toks.append(ex)
                ztl = zTl
                j, bank, wb = self.ppool.acquire()
                mm = None
                for ii in range(nj):
                    mm = p.emit(TENSOR, (lambda hh, ii, bank, first, last:
                        lambda te: te.matmul(
                            out=bank[0:64, :],
                            lhsT=VT[:, ii * D + hh * 64: ii * D + (hh + 1) * 64],
                            rhs=pT[:, ii * 512:(ii + 1) * 512],
                            start=first, stop=last))(hh, ii, bank,
                                                     ii == 0, ii == nj - 1),
                        waits=(wb + [exp_toks[-1], VT_tok[NT - 1]] + ztl
                               if ii == 0 else []),
                        tag=f"av1{T}_{hh}")
                j2, bank2, wb2 = self.ppool.acquire()
                dmm = None
                for ii in range(nj):
                    dmm = p.emit(TENSOR, (lambda ii, bank2, first, last:
                        lambda te: te.matmul(
                            out=bank2[0:1, :], lhsT=self.ones1b[:, 0:1],
                            rhs=pT[:, ii * 512:(ii + 1) * 512],
                            start=first, stop=last))(ii, bank2, ii == 0, ii == nj - 1),
                        waits=(wb2 + [exp_toks[-1]] + ztl + [self.ones1b_tok]
                               if ii == 0 else []),
                        tag=f"dn1{T}_{hh}")
                pT_free = dmm
                d1 = p.emit(VECTOR, (lambda bank2: lambda v: v.reciprocal(
                    out=den_r[0:1, :], in_=bank2[0:1, :]))(bank2),
                    waits=[dmm], tag=f"dr1{T}_{hh}")
                self.ppool.release(j2, d1)
                d1b = p.emit(VECTOR, (lambda i0: lambda v: v.tensor_tensor(
                    out=den_r[0:1, :], in0=den_r[0:1, :],
                    in1=km_row[0:1, S + i0: S + i0 + 512], op=ALU.mult))(i0),
                    waits=[d1, t3], tag=f"dq1{T}_{hh}")
                j3, bank3, wb3 = self.ppool.acquire()
                d2 = p.emit(TENSOR, (lambda bank3: lambda te: te.matmul(
                    out=bank3[0:64, :], lhsT=k._ones_row[0:1, 0:64],
                    rhs=den_r[0:1, :], start=True, stop=True))(bank3),
                    waits=wb3 + [d1b, k._ones_row_tok], tag=f"db1{T}_{hh}")
                d3 = p.emit(VECTOR, (lambda bank3: lambda v: v.tensor_copy(
                    out=den_sb[0:64, :], in_=bank3[0:64, :]))(bank3),
                    waits=[d2], tag=f"dc1{T}_{hh}")
                self.ppool.release(j3, d3)
                ri = hh % 2
                w_ = [to_free[ri]] if to_free[ri] else []
                a1 = p.emit(VECTOR, (lambda ri, bank: lambda v: v.tensor_tensor(
                    out=to_ring[0:64, ri * 512:(ri + 1) * 512], in0=bank[0:64, :],
                    in1=den_sb[0:64, :], op=ALU.mult))(ri, bank),
                    waits=[d3] + w_, tag=f"ao1{T}_{hh}")
                self.ppool.release(j, a1)
                dm = p.dma(SYNC, self.attn1[hh * 64:(hh + 1) * 64, i0:i0 + 512],
                           to_ring[0:64, ri * 512:(ri + 1) * 512], waits=[a1],
                           tag=f"ad1{T}_{hh}")
                to_free[ri] = dm
            for u in range(NCH):
                q_free[u] = mm
        self._p4_done = p.fence()
        k.pop_scope()

    def quant_init_scratch(self):
        k = self.k
        self.qsc_blk = [k.sb("qblk", [128, 64 * 4], F32)[:, :] for _ in range(2)]
        self.qsc_el = [(k.sb("qely", [128, 1024], F32)[:, :],
                        k.sb("qela", [128, 1024], F32)[:, :]) for _ in range(2)]
        self._quant_free = [None, None]
        self._quant_slot = 0

    def quant_tile(self, src, dst, gs_col, g6_col, ginv2_col, gtoks, waits,
                   tag="qt"):
        k, p = self.k, self.k.p
        F = src.shape[-1]
        nb = F // 16
        slot = self._quant_slot
        self._quant_slot = 1 - slot
        B = self.qsc_blk[slot]
        Y, A = self.qsc_el[slot]
        qf = [self._quant_free[slot]] if self._quant_free[slot] else []
        amax = B[:, 0:nb]; sc = B[:, nb:2 * nb]; ee = B[:, 2 * nb:3 * nb]
        yy = B[:, 3 * nb:4 * nb]; qq = amax; vv = sc
        y = Y[:, 0:F]; acc = A[:, 0:F]
        w0 = list(waits) + list(gtoks) + qf
        t = p.emit(VECTOR, lambda v: v.tensor_reduce(
            out=amax, in_=src.rearrange("p (b z) -> p b z", z=16), axis=AX.X,
            op=ALU.max, apply_absolute_value=True), waits=w0, tag=tag + "am")
        t = p.emit(VECTOR, lambda v: v.tensor_scalar(
            out=sc, in0=amax, scalar1=g6_col, scalar2=448.0, op0=ALU.mult,
            op1=ALU.min), waits=[t], tag=tag + "sc")
        t = p.emit(VECTOR, lambda v: v.tensor_scalar(
            out=sc, in0=sc, scalar1=1e-30, scalar2=None, op0=ALU.max),
            waits=[t], tag=tag + "scm")
        tl = p.emit(SCALAR, lambda s: s.activation(
            out=ee, in_=sc, func=AF.Ln), waits=[t], tag=tag + "ln")
        t = p.emit(VECTOR, lambda v: v.tensor_scalar(
            out=acc[:, 0:nb], in0=ee, scalar1=-5.0 * LN2, scalar2=-6.0,
            op0=ALU.is_ge, op1=ALU.add), waits=[tl], tag=tag + "e0")
        for kk in range(-4, 9):
            t = p.emit(VECTOR, (lambda kk: lambda v: v.scalar_tensor_tensor(
                out=acc[:, 0:nb], in0=ee, scalar=float(kk) * LN2,
                in1=acc[:, 0:nb], op0=ALU.is_ge, op1=ALU.add))(kk),
                waits=[t], tag=tag + "el")
        c3a, tk3a = k.c(3.0 * LN2)
        c3b, tk3b = k.c(-3.0 * LN2)
        tqi = p.emit(SCALAR, lambda s: s.activation(
            out=qq, in_=acc[:, 0:nb], func=AF.Exp, scale=-LN2, bias=c3a),
            waits=[t, tk3a], tag=tag + "qi")
        tq2 = p.emit(SCALAR, lambda s: s.activation(
            out=ee, in_=acc[:, 0:nb], func=AF.Exp, scale=LN2, bias=c3b),
            waits=[t, tk3b], tag=tag + "q2")
        t = p.emit(VECTOR, lambda v: v.tensor_tensor(
            out=vv, in0=sc, in1=qq, op=ALU.mult), waits=[tqi], tag=tag + "v")
        t = p.emit(VECTOR, lambda v: v.tensor_scalar(
            out=vv, in0=vv, scalar1=0.5, scalar2=None, op0=ALU.add),
            waits=[t], tag=tag + "v5")
        tz = p.emit(VECTOR, lambda v: v.tensor_scalar(
            out=yy, in0=vv, scalar1=1.0, scalar2=None, op0=ALU.is_ge),
            waits=[t], tag=tag + "f0")
        for kk in range(2, 17):
            tz = p.emit(VECTOR, (lambda kk: lambda v: v.scalar_tensor_tensor(
                out=yy, in0=vv, scalar=float(kk), in1=yy,
                op0=ALU.is_ge, op1=ALU.add))(kk), waits=[tz], tag=tag + "fl")
        ty = p.emit(VECTOR, lambda v: v.tensor_tensor(
            out=yy, in0=yy, in1=ee, op=ALU.mult), waits=[tz, tq2], tag=tag + "y")
        t = p.emit(VECTOR, lambda v: v.tensor_scalar(
            out=vv, in0=yy, scalar1=0.0, scalar2=None, op0=ALU.is_equal),
            waits=[ty], tag=tag + "z")
        t = p.emit(VECTOR, lambda v: v.tensor_tensor(
            out=vv, in0=vv, in1=yy, op=ALU.add), waits=[t], tag=tag + "zs")
        t = p.emit(VECTOR, lambda v: v.reciprocal(out=vv, in_=vv),
                   waits=[t], tag=tag + "rc")
        trdq = p.emit(VECTOR, lambda v: v.tensor_scalar(
            out=vv, in0=vv, scalar1=gs_col, scalar2=None, op0=ALU.mult),
            waits=[t], tag=tag + "rq")
        tdq = p.emit(VECTOR, lambda v: v.tensor_scalar(
            out=yy, in0=yy, scalar1=ginv2_col, scalar2=None, op0=ALU.mult),
            waits=[ty, trdq], tag=tag + "dq")
        def bc(col):
            return col.rearrange("p (b o) -> p b o", o=1).broadcast_to([128, nb, 16])
        sv3 = src.rearrange("p (b z) -> p b z", z=16)
        t = p.emit(VECTOR, lambda v: v.tensor_tensor(
            out=y.rearrange("p (b z) -> p b z", z=16), in0=sv3, in1=bc(vv),
            op=ALU.mult), waits=[trdq], tag=tag + "ye")
        t = p.emit(SCALAR, lambda s: s.activation(
            out=y, in_=y, func=AF.Abs), waits=[t], tag=tag + "abs")
        t0 = p.emit(VECTOR, lambda v: v.tensor_scalar(
            out=acc, in0=y, scalar1=0.25, scalar2=None, op0=ALU.is_ge),
            waits=[t], tag=tag + "n1")
        cum = t0
        for thr in [0.75, 1.25, 1.75, 2.5, 2.5, 3.5, 3.5,
                    5.0, 5.0, 5.0, 5.0]:
            cum = p.emit(VECTOR, (lambda thr: lambda v: v.scalar_tensor_tensor(
                out=acc, in0=y, scalar=float(thr), in1=acc,
                op0=ALU.is_ge, op1=ALU.add))(thr), waits=[cum], tag=tag + "na")
        tsg = p.emit(VECTOR, lambda v: v.tensor_scalar(
            out=y, in0=src, scalar1=0.0, scalar2=2.0, op0=ALU.is_ge,
            op1=ALU.mult), waits=[cum], tag=tag + "sg")
        tsg2 = p.emit(VECTOR, lambda v: v.scalar_tensor_tensor(
            out=y, in0=y, scalar=-1.0, in1=acc, op0=ALU.add,
            op1=ALU.mult), waits=[tsg], tag=tag + "sg2")
        tout = p.emit(VECTOR, lambda v: v.tensor_tensor(
            out=dst.rearrange("p (b z) -> p b z", z=16),
            in0=y.rearrange("p (b z) -> p b z", z=16), in1=bc(yy),
            op=ALU.mult), waits=[tsg2, tdq], tag=tag + "out")
        self._quant_free[slot] = tout
        return tout

    def p5_l1attn_apply(self):
        k, p = self.k, self.k.p
        g_h = k.din("l1a_g", [1, D]); b_h = k.din("l1a_b", [1, D])
        k.push_scope()
        g_col, g_tok = load_bias_cols(k, g_h, D, tag="g1c")
        b_col, b_tok = load_bias_cols(k, b_h, D, tag="b1c")
        tob = self.strm[:, :].bitcast(BF16)[:, 0:NCH * S]
        rho3 = k.sb("rho3", [128, S], BF16)
        om3 = k.sb("om3", [128, S], BF16)
        ldr = k.sb("ldr1", [128, S], BF16)
        fr = k.sb("fr1", [128, 512], F32)
        ld_free = [None]
        tob_toks = []
        for c in range(NCH):
            ri = 0
            ld = p.dma(SYNC, ldr[:, ri * S:(ri + 1) * S],
                       self.attn1[c * 128:(c + 1) * 128, :],
                       waits=self._p4_done + ([ld_free[ri]] if ld_free[ri] else []),
                       tag="a1ld")
            last = None
            for ts4 in range(NT4):
                a0 = ts4 * 512
                t1 = p.emit(VECTOR, (lambda c, a0: lambda v: v.tensor_tensor(
                    out=fr[:, :], in0=self.xTc[c][:, a0:a0 + 512],
                    in1=self.rho1[:, a0:a0 + 512], op=ALU.mult))(c, a0),
                    waits=self.ln1.toks +
                          ([self.xTc_tok[c]] if self.xTc_tok[c] else []) +
                          ([last] if last else []), tag="t1")
                t2 = p.emit(VECTOR, (lambda c, a0: lambda v: v.tensor_tensor(
                    out=fr[:, :], in0=fr[:, :],
                    in1=self.om1[:, a0:a0 + 512], op=ALU.add))(c, a0),
                    waits=[t1], tag="t2")
                t3 = p.emit(VECTOR, (lambda c, a0, ri: lambda v: v.tensor_tensor(
                    out=tob[:, c * S + a0: c * S + a0 + 512], in0=fr[:, :],
                    in1=ldr[:, ri * S + a0: ri * S + a0 + 512], op=ALU.add))(c, a0, ri),
                    waits=[t2, ld], tag="t3")
                last = t3
            ld_free[ri] = last
            tob_toks.append(last)
        tob_c = [tob[:, c * S:(c + 1) * S] for c in range(NCH)]
        scr = (self.strm[:, 8196:9220], self.strm[0:1, 9220:11268],
               self.strm[0:1, 11268:13316], self.strm[0:1, 13316:15364])
        ln3 = ln_stats(k, tob_c, tob_toks + [self._xq_done], self.ones1,
                       self.ones1_tok, self.ppool,
                       eps=1e-8, tag="ln3", out=(rho3, om3), scratch=scr)
        for c in range(NCH):
            last = None
            for ts4 in range(NT4):
                a0 = ts4 * 512
                t1 = p.emit(VECTOR, (lambda c, a0: lambda v: v.tensor_tensor(
                    out=fr[:, :], in0=tob[:, c * S + a0: c * S + a0 + 512],
                    in1=rho3[:, a0:a0 + 512], op=ALU.mult))(c, a0),
                    waits=ln3.toks + [tob_toks[c]] + ([last] if last else []),
                    tag="a3a")
                t2 = p.emit(VECTOR, (lambda c, a0: lambda v: v.tensor_tensor(
                    out=fr[:, :], in0=fr[:, :], in1=om3[:, a0:a0 + 512],
                    op=ALU.add))(c, a0), waits=[t1], tag="a3b")
                t3 = p.emit(VECTOR, (lambda c: lambda v: v.tensor_scalar(
                    out=fr[:, :], in0=fr[:, :], scalar1=g_col[:, c:c + 1],
                    scalar2=b_col[:, c:c + 1], op0=ALU.mult, op1=ALU.add))(c),
                    waits=[t2, g_tok, b_tok], tag="a3c")
                t4 = p.emit(VECTOR, (lambda c, a0: lambda v: v.tensor_tensor(
                    out=self.xTc[c][:, a0:a0 + 512],
                    in0=self.xTc[c][:, a0:a0 + 512], in1=fr[:, :],
                    op=ALU.add))(c, a0),
                    waits=[t3] + ([self.xTc_tok[c]] if self.xTc_tok[c] else []),
                    tag="a3d")
                last = t4
            self.xTc_tok[c] = last
        k.pop_scope()

    def p6_prepare(self):
        k, p = self.k, self.k.p
        gw = k.din("l1f_gw", [FI, D])
        uw = k.din("l1f_uw", [FI, D])
        dw = k.din("l1f_dw", [D, FI])
        sin_h = k.din("l1f_sin", [1, 1])
        sind_h = k.din("l1f_sind", [1, 1])
        self.gT = k.dram("gT", [D, FI], BF16)
        self.uT = k.dram("uT", [D, FI], BF16)
        self.dT = k.dram("dT", [FI, D], BF16)
        self.quant_init_scratch()
        gcols = k.sb("gcols", [128, 12], F32)
        gcols2 = k.sb("gcols2", [128, 3], F32)
        k.push_scope()
        PS = 2048
        wstg = k.sb("wstg6", [128, FI], F32)
        wq = k.sb("wq6", [128, FI], BF16)
        stg_free = [None, None]
        wq_free = None

        def scalar_cols(val_row, toks, idx, tagp):
            r2 = k.sb(tagp + "r", [1, 3], F32)
            t1 = p.emit(VECTOR, lambda v: v.tensor_copy(out=r2[0:1, 0:1],
                        in_=val_row[0:1, 0:1]), waits=toks, tag=tagp + "a")
            t2 = p.emit(VECTOR, lambda v: v.tensor_scalar(
                out=r2[0:1, 1:2], in0=r2[0:1, 0:1], scalar1=1.0 / 6.0,
                scalar2=None, op0=ALU.mult), waits=[t1], tag=tagp + "b")
            t3 = p.emit(VECTOR, lambda v: v.reciprocal(
                out=r2[0:1, 2:3], in_=r2[0:1, 0:1]), waits=[t1], tag=tagp + "c")
            t4 = p.emit(VECTOR, lambda v: v.tensor_scalar(
                out=r2[0:1, 2:3], in0=r2[0:1, 2:3], scalar1=0.5, scalar2=None,
                op0=ALU.mult), waits=[t3], tag=tagp + "d")
            tk = bcast_row(k, r2[0:1, 0:3], gcols[:, idx:idx + 3], self.ppool,
                           [t2, t4], tagp + "bc")
            return tk

        sin_r = k.sb("sinr", [1, 2], F32)
        tsi = p.dma(SYNC, sin_r[0:1, 0:1], sin_h[0:1, :], tag="sinld")
        tsd = p.dma(SYNC, sin_r[0:1, 1:2], sind_h[0:1, :], tag="sindld")
        tk_x = scalar_cols(sin_r[0:1, 0:1], [tsi], 0, "gx")
        tk_h = scalar_cols(sin_r[0:1, 1:2], [tsd], 3, "gh")

        mxrow = k.sb("mxrow", [1, 128], F32)

        def weight_gs2(hbm, nrow, ncol, idx, tagp):
            mx = k.sb(tagp + "mx2", [128, 2], F32)
            t0 = p.emit(VECTOR, lambda v: v.memset(mx[:, :], 0.0), tag=tagp + "z")
            last = t0
            nch = nrow // 128
            for c in range(nch):
                ri = c % 2
                ld = p.dma(SYNC, wstg[:, ri * PS:ri * PS + ncol],
                           hbm[c * 128:(c + 1) * 128, :],
                           waits=[stg_free[ri]] if stg_free[ri] else [],
                           tag=tagp + "ld")
                t1 = p.emit(VECTOR, (lambda ri, ncol: lambda v: v.tensor_reduce(
                    out=mx[:, 1:2], in_=wstg[:, ri * PS:ri * PS + ncol], axis=AX.X,
                    op=ALU.max, apply_absolute_value=True))(ri, ncol),
                    waits=[ld, last], tag=tagp + "r")
                t2 = p.emit(VECTOR, lambda v: v.tensor_tensor(
                    out=mx[:, 0:1], in0=mx[:, 0:1], in1=mx[:, 1:2], op=ALU.max),
                    waits=[t1], tag=tagp + "m")
                stg_free[ri] = t1
                last = t2
            j, bank, wb = self.ppool.acquire()
            tt = p.emit(TENSOR, (lambda bank: lambda te: te.transpose(
                out=bank[0:1, 0:128], in_=mx[:, 0:1],
                identity=self.idn_f32[:, :]))(bank),
                waits=[last, self.idn_f32_tok] + wb, tag=tagp + "tp")
            ev = p.emit(VECTOR, (lambda bank: lambda v: v.reduce_max(
                out=mxrow[0:1, 0:1], in_=bank[0:1, 0:128], axis=AX.X))(bank),
                waits=[tt], tag=tagp + "rx")
            self.ppool.release(j, ev)
            t3 = p.emit(VECTOR, lambda v: v.reciprocal(
                out=mxrow[0:1, 1:2], in_=mxrow[0:1, 0:1]), waits=[ev], tag=tagp + "rc")
            t4 = p.emit(VECTOR, lambda v: v.tensor_scalar(
                out=mxrow[0:1, 1:2], in0=mxrow[0:1, 1:2], scalar1=2688.0,
                scalar2=None, op0=ALU.mult), waits=[t3], tag=tagp + "gs")
            return scalar_cols(mxrow[0:1, 1:2], [t4], idx, tagp + "sc")

        tk_g = weight_gs2(gw, FI, D, 6, "gw")
        tk_u = weight_gs2(uw, FI, D, 9, "uw")
        def weight_gs_d(hbm, tagp):
            mx = k.sb(tagp + "mx2", [128, 2], F32)
            t0 = p.emit(VECTOR, lambda v: v.memset(mx[:, :], 0.0), tag=tagp + "z")
            last = t0
            for sc2 in range(2 * (D // 128)):
                c, h = sc2 // 2, sc2 % 2
                ri = sc2 % 2
                ld = p.dma(SYNC, wstg[:, ri * PS:(ri + 1) * PS],
                           hbm[c * 128:(c + 1) * 128, h * PS:(h + 1) * PS],
                           waits=[stg_free[ri]] if stg_free[ri] else [],
                           tag=tagp + "ld")
                t1 = p.emit(VECTOR, (lambda ri: lambda v: v.tensor_reduce(
                    out=mx[:, 1:2], in_=wstg[:, ri * PS:(ri + 1) * PS], axis=AX.X,
                    op=ALU.max, apply_absolute_value=True))(ri),
                    waits=[ld, last], tag=tagp + "r")
                t2 = p.emit(VECTOR, lambda v: v.tensor_tensor(
                    out=mx[:, 0:1], in0=mx[:, 0:1], in1=mx[:, 1:2], op=ALU.max),
                    waits=[t1], tag=tagp + "m")
                stg_free[ri] = t1
                last = t2
            j, bank, wb = self.ppool.acquire()
            tt = p.emit(TENSOR, (lambda bank: lambda te: te.transpose(
                out=bank[0:1, 0:128], in_=mx[:, 0:1],
                identity=self.idn_f32[:, :]))(bank),
                waits=[last, self.idn_f32_tok] + wb, tag=tagp + "tp")
            ev = p.emit(VECTOR, (lambda bank: lambda v: v.reduce_max(
                out=mxrow[0:1, 0:1], in_=bank[0:1, 0:128], axis=AX.X))(bank),
                waits=[tt], tag=tagp + "rx")
            self.ppool.release(j, ev)
            t3 = p.emit(VECTOR, lambda v: v.reciprocal(
                out=mxrow[0:1, 1:2], in_=mxrow[0:1, 0:1]), waits=[ev], tag=tagp + "rc")
            t4 = p.emit(VECTOR, lambda v: v.tensor_scalar(
                out=mxrow[0:1, 1:2], in0=mxrow[0:1, 1:2], scalar1=2688.0,
                scalar2=None, op0=ALU.mult), waits=[t3], tag=tagp + "gs")
            r2 = k.sb(tagp + "r", [1, 3], F32)
            t1b = p.emit(VECTOR, lambda v: v.tensor_copy(out=r2[0:1, 0:1],
                         in_=mxrow[0:1, 1:2]), waits=[t4], tag=tagp + "a")
            t2b = p.emit(VECTOR, lambda v: v.tensor_scalar(
                out=r2[0:1, 1:2], in0=r2[0:1, 0:1], scalar1=1.0 / 6.0,
                scalar2=None, op0=ALU.mult), waits=[t1b], tag=tagp + "b")
            t3b = p.emit(VECTOR, lambda v: v.reciprocal(
                out=r2[0:1, 2:3], in_=r2[0:1, 0:1]), waits=[t1b], tag=tagp + "c")
            t4b = p.emit(VECTOR, lambda v: v.tensor_scalar(
                out=r2[0:1, 2:3], in0=r2[0:1, 2:3], scalar1=0.5, scalar2=None,
                op0=ALU.mult), waits=[t3b], tag=tagp + "d")
            return bcast_row(k, r2[0:1, 0:3], gcols2[:, 0:3], self.ppool,
                             [t2b, t4b], tagp + "bc")
        tk_dw = weight_gs_d(dw, "dww")

        def quant_spill(hbm, scratch, nrow, ncol, gidx, gc, gtok, tagp):
            nch = nrow // 128
            sc2 = 0
            for c in range(nch):
                qts = []
                for h0 in range(0, ncol, PS):
                    w = min(PS, ncol - h0)
                    ri = sc2 % 2
                    sc2 += 1
                    ld = p.dma(SYNC, wstg[:, ri * PS:ri * PS + w],
                               hbm[c * 128:(c + 1) * 128, h0:h0 + w],
                               waits=[stg_free[ri]] if stg_free[ri] else [],
                               tag=tagp + "l2")
                    for f0 in range(0, w, 1024):
                        qt = self.quant_tile(
                            wstg[:, ri * PS + f0: ri * PS + f0 + 1024],
                            wq[:, h0 + f0:h0 + f0 + 1024],
                            gc[:, gidx:gidx + 1], gc[:, gidx + 1:gidx + 2],
                            gc[:, gidx + 2:gidx + 3], [gtok],
                            [ld] + ([wq_free[0]] if wq_free[0] else []),
                            tag=tagp + f"q{c}_{h0 + f0}")
                        qts.append(qt)
                    stg_free[ri] = qts[-1]
                last_tt = None
                for r in range(ncol // 128):
                    j, bank, wb = self.ppool.acquire()
                    tt = p.emit(TENSOR, (lambda r, bank: lambda te: te.transpose(
                        out=bank.bitcast(BF16)[:, 0:128],
                        in_=wq[:, r * 128:(r + 1) * 128],
                        identity=self.idn_bf[:, :]))(r, bank),
                        waits=[qts[r // 8], self.idn_bf_tok] + wb, tag=tagp + "tp")
                    ri2 = r % 4
                    ev = p.emit(SCALAR, (lambda r, ri2, bank: lambda s: s.copy(
                        out=wqt[:, ri2 * 128:(ri2 + 1) * 128],
                        in_=bank.bitcast(BF16)[:, 0:128]))(r, ri2, bank),
                        waits=[tt] + ([wqt_free[ri2]] if wqt_free[ri2] else []),
                        tag=tagp + "te")
                    self.ppool.release(j, ev)
                    dm = p.dma(SYNC, scratch[r * 128:(r + 1) * 128,
                                             c * 128:(c + 1) * 128],
                               wqt[:, ri2 * 128:(ri2 + 1) * 128], waits=[ev],
                               tag=tagp + "sp")
                    wqt_free[ri2] = dm
                    last_tt = tt
                wq_free[0] = last_tt
        wq_free = [None]
        wqt_free = [None] * 4
        wqt = k.sb("wqt6", [128, 4 * 128], BF16)
        quant_spill(gw, self.gT, FI, D, 6, gcols, tk_g, "gq")
        quant_spill(uw, self.uT, FI, D, 9, gcols, tk_u, "uq")
        quant_spill(dw, self.dT, D, FI, 0, gcols2, tk_dw, "dq")
        self._p6_sin_cols = (gcols, tk_x, tk_h)
        self._p6_done_w = p.fence()
        k.pop_scope()

    def p6_xq(self):
        k, p = self.k, self.k.p
        self.xqT = k.sb("xqT", [128, NCH * S], BF16)
        gcols = self._p6_sin_cols[0]
        tk_x = self._p6_sin_cols[1]
        k.push_scope()
        bcvq = self.strm[:, :].bitcast(BF16)
        xn_tok = bcvq[:, 0:NT * D]
        xq_tok = bcvq[:, NT * D:2 * NT * D]
        mirf = k.sb("mirf6", [128, 512], F32)
        mirb = k.sb("mirb6", [128, 512], BF16)
        tok_done = {}
        mfree = None
        for c in range(NCH):
            for ts4 in range(NT4):
                a0 = ts4 * 512
                mt = self.xn1_tile(mirf, mirb[:, :], c, a0, 512,
                                   [mfree] if mfree else [])
                evs = []
                for q4 in range(4):
                    tt = ts4 * 4 + q4
                    j, bank, wb = self.ppool.acquire()
                    tk = p.emit(TENSOR, (lambda q4, bank: lambda te: te.transpose(
                        out=bank.bitcast(BF16)[:, 0:128],
                        in_=mirb[:, q4 * 128:(q4 + 1) * 128],
                        identity=self.idn_bf[:, :]))(q4, bank),
                        waits=[mt, self.idn_bf_tok] + wb, tag="x6tp")
                    ev = p.emit(SCALAR, (lambda tt, c, bank: lambda s: s.copy(
                        out=xn_tok[:, tt * D + c * 128: tt * D + (c + 1) * 128],
                        in_=bank.bitcast(BF16)[:, 0:128]))(tt, c, bank),
                        waits=[tk], tag="x6ev")
                    self.ppool.release(j, ev)
                    evs.append(ev)
                    tok_done[(tt, c)] = ev
                mfree = evs[-1]
        q_toks = []
        for tt in range(NT):
            w_ = [tok_done[(tt, c)] for c in range(NCH)]
            qt = self.quant_tile(xn_tok[:, tt * D:(tt + 1) * D],
                                 xq_tok[:, tt * D:(tt + 1) * D],
                                 gcols[:, 0:1], gcols[:, 1:2], gcols[:, 2:3],
                                 [tk_x], w_, tag=f"xq{tt}")
            q_toks.append(qt)
        for tt in range(NT):
            for c in range(NCH):
                j, bank, wb = self.ppool.acquire()
                tk = p.emit(TENSOR, (lambda tt, c, bank: lambda te: te.transpose(
                    out=bank.bitcast(BF16)[:, 0:128],
                    in_=xq_tok[:, tt * D + c * 128: tt * D + (c + 1) * 128],
                    identity=self.idn_bf[:, :]))(tt, c, bank),
                    waits=[q_toks[tt], self.idn_bf_tok] + wb, tag="xq6tp")
                ev = p.emit(SCALAR, (lambda tt, c, bank: lambda s: s.copy(
                    out=self.xqT[:, c * S + tt * 128: c * S + (tt + 1) * 128],
                    in_=bank.bitcast(BF16)[:, 0:128]))(tt, c, bank),
                    waits=[tk], tag="xq6ev")
                self.ppool.release(j, ev)
                self._xqT_tok = ev
        self._xq_done = (TENSOR, p.counts[TENSOR])
        k.pop_scope()

    def p6_main(self):
        k, p = self.k, self.k.p
        gcols = self._p6_sin_cols[0]
        tk_h = self._p6_sin_cols[2]
        k.push_scope()
        gsl = k.sb("gsl", [128, 8 * 512], BF16)
        h_tok = None
        bcv6 = self.strm[:, :].bitcast(BF16)
        h_tok = bcv6[:, 0:4 * FI]
        hqT = bcv6[:, 4 * FI:4 * FI + 32 * 512]
        hq = k.sb("hq6", [128, FI // 2], BF16)
        dsl = k.sb("dsl", [128, 2 * D], BF16)
        sil = k.sb("sil", [128, 4 * 512], BF16)
        gsl_free = [None]
        hq_free = None
        hqT_free = None
        dsl_free = [None, None]
        for t5 in range(NT4):
            t0 = t5 * 512
            h_toks = {}
            for osl in range(8):
                o0 = osl * 512
                g_toks = [p.dma(SYNC, gsl[:, :].rearrange(
                                    "p (c w) -> p c w", c=8),
                                self.gT[:, o0:o0 + 512].rearrange(
                                    "(c p) w -> p c w", p=128),
                                waits=(self._p6_done_w if (t5 == 0 and osl == 0)
                                       else []) +
                                      ([gsl_free[0]] if gsl_free[0] else []),
                                tag="gl")]
                gbanks = []
                for q4 in range(4):
                    tt = t5 * 4 + q4
                    jg, bg, wbg = self.ppool.acquire()
                    mmg = None
                    for hh in range(NCH):
                        mmg = p.emit(TENSOR, (lambda hh, tt, bg, first, last:
                            lambda te: te.matmul(
                                out=bg[:, :],
                                lhsT=self.xqT[:, hh * S + tt * 128: hh * S + (tt + 1) * 128],
                                rhs=gsl[:, hh * 512:(hh + 1) * 512],
                                start=first, stop=last))(hh, tt, bg, hh == 0,
                                                         hh == NCH - 1),
                            waits=(wbg + [g_toks[-1], self._xqT_tok]
                                   if hh == 0 else []),
                            tag=f"g6{t5}_{osl}_{q4}")
                    gbanks.append((jg, bg, mmg))
                sv_toks = []
                for q4 in range(4):
                    jg, bg, mmg = gbanks[q4]
                    sv = p.emit(SCALAR, (lambda q4, bg: lambda s: s.activation(
                        out=sil[:, q4 * 512:(q4 + 1) * 512], in_=bg[:, :],
                        func=AF.Silu))(q4, bg),
                        waits=[mmg], tag=f"sl6{t5}_{osl}_{q4}")
                    self.ppool.release(jg, sv)
                    sv_toks.append(sv)
                u_toks = [p.dma(SYNC, gsl[:, :].rearrange(
                                    "p (c w) -> p c w", c=8),
                                self.uT[:, o0:o0 + 512].rearrange(
                                    "(c p) w -> p c w", p=128),
                                waits=[gbanks[-1][2]], tag="ul")]
                last_mmu = None
                for q4 in range(4):
                    tt = t5 * 4 + q4
                    ju, bu, wbu = self.ppool.acquire()
                    mmu = None
                    for hh in range(NCH):
                        mmu = p.emit(TENSOR, (lambda hh, tt, bu, first, last:
                            lambda te: te.matmul(
                                out=bu[:, :],
                                lhsT=self.xqT[:, hh * S + tt * 128: hh * S + (tt + 1) * 128],
                                rhs=gsl[:, hh * 512:(hh + 1) * 512],
                                start=first, stop=last))(hh, tt, bu, hh == 0,
                                                         hh == NCH - 1),
                            waits=(wbu + [u_toks[-1]] if hh == 0 else []),
                            tag=f"u6{t5}_{osl}_{q4}")
                    last_mmu = mmu
                    he = p.emit(VECTOR, (lambda q4, osl, bu: lambda v: v.tensor_tensor(
                        out=h_tok[:, q4 * FI + osl * 512: q4 * FI + osl * 512 + 512],
                        in0=bu[:, :], in1=sil[:, q4 * 512:(q4 + 1) * 512],
                        op=ALU.mult))(q4, osl, bu),
                        waits=[sv_toks[q4], mmu], tag=f"h6{t5}_{osl}_{q4}")
                    self.ppool.release(ju, he)
                    h_toks[(q4, osl)] = he
                gsl_free[0] = last_mmu
            hqT_toks = []
            for q4 in range(4):
                for hf in range(2):
                    qts = []
                    for f2 in range(2):
                        f0 = hf * 2048 + f2 * 1024
                        qt = self.quant_tile(
                            h_tok[:, q4 * FI + f0: q4 * FI + f0 + 1024],
                            hq[:, f2 * 1024:(f2 + 1) * 1024],
                            gcols[:, 3:4], gcols[:, 4:5], gcols[:, 5:6], [tk_h],
                            [h_toks[(q4, f0 // 512)], h_toks[(q4, f0 // 512 + 1)]] +
                            ([hq_free] if hq_free else []),
                            tag=f"hq{t5}_{q4}_{hf}")
                        qts.append(qt)
                    last_tt = None
                    for ic2 in range(16):
                        ic = hf * 16 + ic2
                        j, bank, wb = self.ppool.acquire()
                        tk = p.emit(TENSOR, (lambda ic2, bank: lambda te: te.transpose(
                            out=bank.bitcast(BF16)[:, 0:128],
                            in_=hq[:, ic2 * 128:(ic2 + 1) * 128],
                            identity=self.idn_bf[:, :]))(ic2, bank),
                            waits=[qts[ic2 // 8], self.idn_bf_tok] + wb +
                                  ([hqT_free] if (q4 == 0 and ic == 0 and hqT_free)
                                   else []),
                            tag="h6tp")
                        ev = p.emit(SCALAR, (lambda ic, q4, bank: lambda s: s.copy(
                            out=hqT[:, ic * 512 + q4 * 128: ic * 512 + (q4 + 1) * 128],
                            in_=bank.bitcast(BF16)[:, 0:128]))(ic, q4, bank),
                            waits=[tk], tag="h6te")
                        self.ppool.release(j, ev)
                        hqT_toks.append(ev)
                        last_tt = tk
                    hq_free = last_tt
            banks_o = []
            for o2 in range(8):
                j, bank, wb = self.ppool.acquire()
                banks_o.append((j, bank, wb))
            mm_last = [None] * 8
            for ic in range(32):
                ri = ic % 2
                ld = p.dma(SYNC, dsl[:, ri * D:(ri + 1) * D],
                           self.dT[ic * 128:(ic + 1) * 128, :],
                           waits=[dsl_free[ri]] if dsl_free[ri] else [], tag="dld")
                for o2 in range(8):
                    jj, bank, wb = banks_o[o2]
                    mm_last[o2] = p.emit(TENSOR, (lambda ic, o2, ri, bank, first, last:
                        lambda te: te.matmul(
                            out=bank[:, :],
                            lhsT=dsl[:, ri * D + o2 * 128: ri * D + (o2 + 1) * 128],
                            rhs=hqT[:, ic * 512:(ic + 1) * 512],
                            start=first, stop=last))(ic, o2, ri, bank,
                                                     ic == 0, ic == 31),
                        waits=[ld, hqT_toks[-1]] + (wb if ic == 0 else []),
                        tag=f"d6{t5}_{ic}_{o2}")
                dsl_free[ri] = mm_last[-1]
            for o2 in range(8):
                jj, bank, wb = banks_o[o2]
                ev = p.emit(VECTOR, (lambda o2, t0, bank: lambda v: v.tensor_tensor(
                    out=self.xTc[o2][:, t0:t0 + 512],
                    in0=self.xTc[o2][:, t0:t0 + 512], in1=bank[:, :],
                    op=ALU.add))(o2, t0, bank),
                    waits=[mm_last[o2]] +
                          ([self.xTc_tok[o2]] if self.xTc_tok[o2] else []),
                    tag=f"d6ev{t5}_{o2}")
                self.ppool.release(jj, ev)
                self.xTc_tok[o2] = ev
            hqT_free = mm_last[-1]
        k.pop_scope()

    def p7_output(self):
        k, p = self.k, self.k.p
        k.push_scope()
        oring = k.sb("oring", [128, 2 * D], F32)
        o_free = [None, None]
        last_dma = None
        for tt in range(NT):
            ri = tt % 2
            evs = []
            for c in range(NCH):
                j, bank, wb = self.ppool.acquire()
                tk = p.emit(TENSOR, (lambda tt, c, bank: lambda te: te.transpose(
                    out=bank[:, 0:128],
                    in_=self.xTc[c][:, tt * 128:(tt + 1) * 128],
                    identity=self.idn_f32[:, :]))(tt, c, bank),
                    waits=[self.idn_f32_tok, self.xTc_tok[c]] + wb +
                          ([o_free[ri]] if (c == 0 and o_free[ri]) else []),
                    tag="otp")
                ev = p.emit(SCALAR, (lambda c, ri, bank: lambda s: s.copy(
                    out=oring[:, ri * D + c * 128: ri * D + (c + 1) * 128],
                    in_=bank[:, 0:128]))(c, ri, bank),
                    waits=[tk], tag="oev")
                self.ppool.release(j, ev)
                evs.append(ev)
            dm = p.dma(SYNC, self.y_out[tt * 128:(tt + 1) * 128, :],
                       oring[:, ri * D:(ri + 1) * D], waits=evs, tag="odma")
            o_free[ri] = dm
            last_dma = dm
        p.emit(SYNC, lambda e: e.nop(), waits=[last_dma], inc=False, tag="fin")
        k.pop_scope()



def build_full():
    mk = MK()
    mk.p0_load()
    mk.close_p0()
    mk.p1_retention()
    mk.p2_stream_init()
    mk.p2_spill_xT()
    mk.k.pop_scope()
    mk.p2_layer(0)
    mk.p2_layer(1)
    mk.p2_apply()
    mk.p3_ln1()
    mk.p4_l1attn()
    mk.p6_prepare()
    mk.p6_xq()
    mk.p5_l1attn_apply()
    mk.p6_main()
    mk.k.pop_scope()
    mk.p7_output()
    return mk


def kernel(**inputs):
    x = np.ascontiguousarray(inputs["L0_attn_x"]).astype(np.float32, copy=False)
    mk = build_full()
    mk.k.p.install(mk.k.stack)
    base = {
        "l0a_wq": inputs["L0_attn_Wq"], "l0a_bq": inputs["L0_attn_bq"].reshape(1, -1),
        "l0a_wk": inputs["L0_attn_Wk"], "l0a_bk": inputs["L0_attn_bk"].reshape(1, -1),
        "l0a_wv": inputs["L0_attn_Wv"],
        "l1a_wq": inputs["L1_attn_Wq"], "l1a_bq": inputs["L1_attn_bq"].reshape(1, -1),
        "l1a_wk": inputs["L1_attn_Wk"], "l1a_bk": inputs["L1_attn_bk"].reshape(1, -1),
        "l1a_wv": inputs["L1_attn_Wv"], "l1a_bv": inputs["L1_attn_bv"].reshape(1, -1),
        "l1a_g": inputs["L1_attn_gamma"].reshape(1, -1),
        "l1a_b": inputs["L1_attn_beta"].reshape(1, -1),
        "l1f_gw": inputs["L1_ffn_gate_w"], "l1f_uw": inputs["L1_ffn_up_w"],
        "l1f_dw": inputs["L1_ffn_down_w"],
        "l1f_sin": inputs["L1_ffn_s_in"].reshape(1, 1),
        "l1f_sind": inputs["L1_ffn_s_in_down"].reshape(1, 1),
    }
    for l in range(2):
        base[f"f{l}_wqkv"] = inputs["L0_ffn_wqkv"][l]
        base[f"f{l}_qs"] = inputs["L0_ffn_q_scale"][l].reshape(1, -1)
        base[f"f{l}_ks"] = inputs["L0_ffn_k_scale"][l].reshape(1, -1)
        base[f"f{l}_wo"] = inputs["L0_ffn_wo"][l]
        base[f"f{l}_lng"] = inputs["L0_ffn_ln_g"][l].reshape(1, -1)
        base[f"f{l}_lnb"] = inputs["L0_ffn_ln_b"][l].reshape(1, -1)
        base[f"f{l}_w1"] = inputs["L0_ffn_w1"][l]
        base[f"f{l}_b1"] = inputs["L0_ffn_b1"][l].reshape(1, -1)
        base[f"f{l}_w2"] = inputs["L0_ffn_w2"][l]
        base[f"f{l}_b2"] = inputs["L0_ffn_b2"][l].reshape(1, -1)
    base = {kk: np.ascontiguousarray(np.asarray(v, dtype=np.float32))
            for kk, v in base.items()}
    in_maps = []
    for b in range(8):
        m = dict(base)
        m["x"] = np.ascontiguousarray(x[b])
        in_maps.append(m)
    res = run_bass_kernel_spmd(mk.k.nc, in_maps, core_ids=list(range(8)))
    out = np.stack([np.asarray(r["y"], dtype=np.float32) for r in res.results])
    try:
        mk.k.stack.close()
    except Exception:
        pass
    return out
